# version 02
# baseline (speedup 1.0000x reference)
import os
import numpy as np
import ml_dtypes
from contextlib import ExitStack

import concourse.bass as bass
import concourse.mybir as mybir
from concourse.bass_utils import run_bass_kernel_spmd

F32 = mybir.dt.float32
BF16 = mybir.dt.bfloat16
I8 = mybir.dt.int8
AF = mybir.ActivationFunctionType
ALU = mybir.AluOpType
AX = mybir.AxisListType
BF = ml_dtypes.bfloat16

B, TSEQ, DM = 4, 2048, 2048
T = 1024
NCH = DM // 128
H, D, RD = 16, 128, 64
M, NB = 32, 64
G, DG, NIH, CI = 4, 512, 4, 64
SWIN = 512
INTER = 8192
NI = INTER // 128
FT = 512
NQT = T // 128
KW = 12
EPS_LN, EPS_RMS = 1e-5, 1e-6

SEC_K = 0
SEC_V = SEC_K + T * D
SEC_CKV = SEC_V + T * D
SEC_KCI = SEC_CKV + 128 * 32
BLOB_N = SEC_KCI + 64 * 32

STAGE = int(os.environ.get("MK_STAGE", "4"))
SUB = os.environ.get("MK_SUB", "full")
TAPS = tuple(t for t in os.environ.get("MK_TAPS", "").split(",") if t)


class Prog:
    ENGS = ("sync", "tensor", "vector", "scalar", "gpsimd")

    def __init__(self):
        self.items = {e: [] for e in self.ENGS}
        self.cnt = {}

    def val(self, sem):
        return self.cnt.get(sem, 0)

    def emit(self, eng, fn=None, waits=(), inc=None, amt=1):
        w = [(s, v) for (s, v) in waits if v and v > 0]
        self.items[eng].append((w, fn, inc, amt))
        if inc is not None:
            self.cnt[inc] = self.cnt.get(inc, 0) + amt
            return self.cnt[inc]
        return None

    def replay(self, eng_name, eng, sems):
        for w, fn, inc, amt in self.items[eng_name]:
            for s, v in w:
                eng.wait_ge(sems[s], v)
            ins = fn(eng) if fn is not None else None
            if inc is not None:
                if ins is None:
                    ins = eng.nop()
                ins.then_inc(sems[inc], amt)


def build_program(stage, taps, sub="full"):
    nc = bass.Bass("TRN2", target_bir_lowering=False)

    def din(name, shape, dtype):
        return nc.dram_tensor(name, shape, dtype, kind="ExternalInput").ap()

    xin = din("xin", [DM, T], F32)
    wg_r = din("wg_r", [128, 32, 16, 256], I8)
    wu_r = din("wu_r", [128, 32, 16, 256], I8)
    wd_r = din("wd_r", [128, 4, 16, 4, 512], I8)
    sg_r = din("sg_r", [128, 64], F32)
    su_r = din("su_r", [128, 64], F32)
    sd_r = din("sd_r", [128, 16], F32)
    wqc_r = din("wqc_r", [128, 16, 1024], F32)
    wqup_r = din("wqup_r", [128, 8, 2048], F32)
    wkvb_r = din("wkvb_r", [128, 16, 128], F32)
    wzb_r = din("wzb_r", [128, 16, 128], F32)
    wk_r = din("wk_r", [128, 16, 128], F32)
    wv_r = din("wv_r", [128, 16, 128], F32)
    wikv_r = din("wikv_r", [128, 16, 64], F32)
    wiz_r = din("wiz_r", [128, 16, 64], F32)
    wdq_r = din("wdq_r", [128, 16, 64], F32)
    ww_r = din("ww_r", [128, 16, 4], F32)
    wiuq_r = din("wiuq_r", [64, 256], F32)
    gw_r = din("gw_r", [128, 16, 512], F32)
    fin_r = din("fin_r", [128, 16, 2048], F32)
    qnw_r = din("qnw_r", [1, 128], F32)
    knw_r = din("knw_r", [1, 128], F32)
    sink_r = din("sink_r", [1, 16], F32)
    biasb_r = din("biasb_r", [32, 128], F32)
    biasi_r = din("biasi_r", [32, 64], F32)
    ropeC_r = din("ropeC_r", [128, 1024], BF16)
    ropeS_r = din("ropeS_r", [128, 1024], BF16)
    swp_r = din("swp_r", [128, 128], BF16)
    coskt_r = din("coskt_r", [128, 8, 32], BF16)
    sinkt_r = din("sinkt_r", [128, 8, 32], BF16)
    maskjm_r = din("maskjm_r", [128, 8, 5, 128], BF16)
    valid01_r = din("valid01_r", [128, 8, 64], BF16)
    validadd_r = din("validadd_r", [128, 8, 64], BF16)
    tbrow_r = din("tbrow_r", [128, 64], F32)
    idf_r = din("idf_r", [128, 128], F32)
    hsel_r = din("hsel_r", [128, 16, 16], BF16)
    qsel_r = din("qsel_r", [16, 16, 128], BF16)

    y = nc.dram_tensor("y", [T, DM], F32, kind="ExternalOutput").ap()
    dbg = {}
    tap_shapes = {"h": [128, NCH * 1024], "q": [128, 16 * 1024],
                  "kwin": [128, KW * 128], "vwin": [128, KW * 128],
                  "ckv": [128, 64], "kci": [64, 64], "selT": [64, NQT * 128],
                  "attn": [128, NCH * 1024], "og": [128, 16 * 1024],
                  "isc": [128, 64], "x2": [128, NCH * 1024], "h3": [128, NCH * 1024], "r0": [1, 512], "r1": [1, 512]}
    for tp in taps:
        dbg[tp] = nc.dram_tensor("dbg_" + tp, tap_shapes[tp], F32,
                                 kind="ExternalOutput").ap()

    cc_in = nc.dram_tensor("cc_in", [1, BLOB_N], BF16)
    cc_out = nc.dram_tensor("cc_out", [2, BLOB_N], BF16)

    P = Prog()
    em = P.emit
    ctx = ExitStack()
    sb = lambda n, s, d: ctx.enter_context(nc.sbuf_tensor(n, s, d))
    psT = lambda n, s, d: ctx.enter_context(nc.psum_tensor(n, s, d))

    with ctx:
        R = sb("R", [128, NCH, 1024], F32)
        Hb = sb("Hb", [128, NCH, 1024], BF16)
        A = sb("A", [128, 32768], BF16)
        W = sb("W", [128, 16384], BF16)
        fv = A[:, :].rearrange("p (c t) -> p c t", c=NI)
        qv = A[:, 0:16384].rearrange("p (c t) -> p c t", c=16)
        qcv = A[:, 16384:24576].rearrange("p (c t) -> p c t", c=8)
        ropeC = A[:, 16384:17408]
        ropeS = A[:, 17408:18432]
        tA = A[:, 18432:19456]
        qsel = A[0:16, 19456:21504].rearrange("p (h t) -> p h t", h=16)
        stats32 = A[0:16, 21504:23552].bitcast(F32)
        hsel = A[:, 23552:23808].rearrange("p (c h) -> p c h", c=16)
        rstd_hs = A[0:16, 23808:24832]
        zb32 = A[:, 16384:18432].bitcast(F32)
        cbf = A[:, 18432:19456]
        qi_v = A[0:64, 19456:23552].rearrange("p (c t) -> p c t", c=4)
        dq_v = A[0:64, 23552:24576]
        maskjm = A[:, 24832:29952].rearrange("p (q k t) -> p q k t", q=8, k=5)
        esw = W[:, 0:1280].rearrange("p (a k t) -> p a k t", a=2, k=5)
        esp = W[0:64, 1280:1536].rearrange("p (a t) -> p a t", a=2)
        avs = W[:, 1536:1792].rearrange("p (a t) -> p a t", a=2)
        attn_tm = W[:, 1792:2304].bitcast(F32).rearrange("p (a t) -> p a t", a=2)
        kwin_fm = W[:, 2304:3840]
        selT = W[0:64, 5376:6400]
        kwin_tm = W[:, 11328:14400].bitcast(F32).rearrange("p (k d) -> p k d", k=KW)
        vwin = W[:, 14400:15936].rearrange("p (k d) -> p k d", k=KW)
        outbuf = W[:, 0:8192].bitcast(F32).rearrange("p (a t) -> p a t", a=2)
        sq = sb("sq", [128, 512], BF16)
        usb = sb("usb", [128, 2, 512], BF16)
        sel32 = sb("sel32", [128, 64], F32)
        wgt_sb = sb("wgt_sb", [128, NQT, 4], F32)
        isc = sb("isc", [128, 64], F32)
        work = sb("work", [128, 64], F32)
        m01 = sb("m01", [128, 64], F32)
        mx8 = sb("mx8", [128, 8], F32)
        sgc = sb("sgc", [128, 64], F32)
        suc = sb("suc", [128, 64], F32)
        sdh = sb("sdh", [128, 16], F32)
        coskt = sb("coskt", [128, 8, 32], BF16)
        sinkt = sb("sinkt", [128, 8, 32], BF16)
        valid01 = A[:, 29952:30464].rearrange("p (q n) -> p q n", q=8)
        validadd = A[:, 30464:30976].rearrange("p (q n) -> p q n", q=8)
        tbrow = sb("tbrow", [128, 64], F32)
        idf = sb("idf", [128, 128], F32)
        swp = sb("swp", [128, 128], BF16)
        onescf = sb("onescf", [128, 1], F32)
        onescb = sb("onescb", [128, 1], BF16)
        onesrf = sb("onesrf", [1, 128], F32)
        onesrb = sb("onesrb", [1, 128], BF16)
        qnw_row = sb("qnw_row", [1, 128], F32)
        knw_row = sb("knw_row", [1, 128], F32)
        qnw_col = sb("qnw_col", [128, 1], F32)
        knw_bc = sb("knw_bc", [128, 128], F32)
        knw_col = sb("knw_col", [128, 1], F32)
        sink_row = sb("sink_row", [1, 16], F32)
        sink_cols = sb("sink_cols", [128, 16], F32)
        biasbT = sb("biasbT", [128, 32], F32)
        biasiT = sb("biasiT", [64, 32], F32)
        biasb_sb = A[0:32, 0:256].bitcast(F32)
        biasi_sb = A[0:32, 256:384].bitcast(F32)
        epsln = sb("epsln", [1, 1], F32)
        epsrms = sb("epsrms", [128, 1], F32)
        lnC = sb("lnC", [128, 1], F32)
        wiuq_sb = W[0:64, 15936:16192]
        r0row = sb("r0row", [1, 512], BF16)
        r1row = sb("r1row", [1, 512], BF16)
        varrow = sb("varrow", [1, 512], F32)
        rstd_r32 = sb("rstd_r32", [1, 32], F32)
        cmp32 = sb("cmp32", [128, 32], F32)
        gmx = sb("gmx", [128, 32], F32)
        gsm = sb("gsm", [128, 32], F32)
        grec = sb("grec", [128, 32], F32)
        rms_c = sb("rms_c", [128, 1], F32)
        ckv_fm = sb("ckv_fm", [128, 64], BF16)
        ckv_nm = sb("ckv_nm", [64, 128], BF16)
        kci_cm = sb("kci_cm", [64, 64], BF16)
        m11 = sb("m11", [1, 8], F32)
        srow = W[0:1, 6400:7424].bitcast(F32)
        nSW = sb("nSW", [128, 16], F32)
        nSP = sb("nSP", [128, 16], F32)
        sinkexp = sb("sinkexp", [128, 16], F32)
        hsel2 = A[:, 30976:31232].rearrange("p (c h) -> p c h", c=16)
        ssw_c = sb("ssw_c", [128, 2], F32)
        ssp_c = sb("ssp_c", [128, 2], F32)
        rsw_c = sb("rsw_c", [128, 2], F32)
        rsp_c = sb("rsp_c", [128, 2], F32)
        red1 = sb("red1", [128, 2], F32)
        red2 = sb("red2", [128, 2], F32)

        pb = [psT(f"pb{i}", [128, 512], F32) for i in range(8)]

        semctx = ExitStack()
        sems = {n: semctx.enter_context(nc.semaphore(n))
                for n in ("dS", "dG", "mm", "vv", "aa", "pp", "cc")}

        with semctx:
            def Wv(off, shape):
                n = int(np.prod(shape))
                v = W[:, off:off + n]
                if len(shape) > 1:
                    lets = "abcdef"[: len(shape)]
                    pat = f"p ({' '.join(lets)}) -> p {' '.join(lets)}"
                    v = v.rearrange(pat, **{lets[i]: shape[i] for i in range(len(shape))})
                return v

            def tap(name, src_ap):
                if name in dbg:
                    waits = [("aa", P.val("aa")), ("vv", P.val("vv")),
                             ("mm", P.val("mm")), ("pp", P.val("pp"))]
                    em("gpsimd", lambda e, s=src_ap, d=dbg[name]: e.dma_start(out=d, in_=s),
                       waits=waits, inc="dG", amt=16)

            for o, i in ((sgc[:], sg_r), (suc[:], su_r), (sdh[:], sd_r),
                         (coskt[:], coskt_r), (sinkt[:], sinkt_r),
                         (tbrow[:], tbrow_r), (idf[:], idf_r),
                         (swp[:], swp_r),
                         (qnw_row[:], qnw_r), (knw_row[:], knw_r),
                         (sink_row[:], sink_r),
                         (biasb_sb, biasb_r), (biasi_sb, biasi_r)):
                em("sync", lambda e, o=o, i=i: e.dma_start(out=o, in_=i), inc="dS", amt=16)
            em("sync", lambda e: e.dma_start(
                out=R[:], in_=xin.rearrange("(c p) t -> p c t", p=128)), inc="dS", amt=16)
            ld_done = P.val("dS")

            def memsets(e):
                e.memset(onescf[:], 1.0)
                e.memset(onescb[:], 1.0)
                e.memset(onesrf[:], 1.0)
                e.memset(onesrb[:], 1.0)
                e.memset(epsln[:], EPS_LN)
                e.memset(epsrms[:], EPS_RMS)
                return e.memset(lnC[:], float(np.log(1.0 / np.sqrt(128.0))))
            vmark = em("vector", memsets, inc="vv")
            vmark = em("vector", lambda e: e.tensor_scalar(
                out=sdh[:], in0=sdh[:], scalar1=0.5, scalar2=None, op0=ALU.mult),
                waits=[("dS", ld_done)], inc="vv")
            mk = em("tensor", lambda e: e.transpose(pb[6][:, 0:1], qnw_row[:], idf[0:1, 0:1]),
                    waits=[("dS", ld_done), ("vv", vmark)], inc="mm")
            mk = em("tensor", lambda e: e.transpose(pb[6][:, 128:129], knw_row[:], idf[0:1, 0:1]),
                    inc="mm")
            mk = em("tensor", lambda e: e.transpose(pb[6][:, 256:288], biasb_sb, idf[0:32, 0:32]),
                    inc="mm")
            mk = em("tensor", lambda e: e.transpose(pb[6][0:64, 288:320], biasi_sb, idf[0:32, 0:32]),
                    inc="mm")
            mk = em("tensor", lambda e: e.matmul(pb[6][:, 320:448], onesrf[:], knw_row[:],
                                                 start=True, stop=True), inc="mm")
            mk = em("tensor", lambda e: e.matmul(pb[6][:, 448:464], onesrf[:], sink_row[:],
                                                 start=True, stop=True), inc="mm")
            vmark = em("vector", lambda e: e.tensor_copy(out=qnw_col[:], in_=pb[6][:, 0:1]),
                       waits=[("mm", mk)], inc="vv")
            vmark = em("vector", lambda e: e.tensor_copy(out=knw_col[:], in_=pb[6][:, 128:129]), inc="vv")
            vmark = em("vector", lambda e: e.tensor_copy(out=biasbT[:, :], in_=pb[6][:, 256:288]), inc="vv")
            vmark = em("vector", lambda e: e.tensor_copy(out=biasiT[:, :], in_=pb[6][0:64, 288:320]), inc="vv")
            vmark = em("vector", lambda e: e.tensor_copy(out=knw_bc[:], in_=pb[6][:, 320:448]), inc="vv")
            init_done = em("vector", lambda e: e.tensor_copy(out=sink_cols[:], in_=pb[6][:, 448:464]), inc="vv")

            def ln_full(tau, pre_mm=(), pre_vv=(), pre_aa=()):
                ts = slice(tau * FT, tau * FT + FT)
                auto = [("vv", P.val("vv")), ("pp", P.val("pp")), ("aa", P.val("aa"))]
                entry_mm = P.val("mm")
                for c in range(NCH):
                    em("tensor", lambda e, c=c, ts=ts: e.matmul(
                        pb[4][0:1, :], onescf[:], R[:, c, ts],
                        start=(c == 0), stop=(c == NCH - 1)),
                        waits=(list(pre_mm) + auto if c == 0 else []),
                        inc="mm" if c == NCH - 1 else None)
                sqmk = entry_mm
                for c in range(NCH):
                    aw = em("scalar", lambda e, c=c, ts=ts: e.activation(
                        out=sq[:], in_=R[:, c, ts], func=AF.Square),
                        waits=(list(pre_aa) if c == 0 else []) + [("mm", sqmk)],
                        inc="aa")
                    sqmk = em("tensor", lambda e, c=c: e.matmul(
                        pb[5][0:1, :], onescb[:], sq[:],
                        start=(c == 0), stop=(c == NCH - 1)),
                        waits=[("aa", aw)], inc="mm")
                mm_sums = P.val("mm")
                vw = em("vector", lambda e: e.tensor_scalar(
                    out=r0row[:], in0=pb[4][0:1, :], scalar1=1.0 / DM, scalar2=None,
                    op0=ALU.mult), waits=[("mm", mm_sums)] + list(pre_vv), inc="vv")
                vw = em("vector", lambda e: e.tensor_mul(out=r1row[:], in0=r0row[:], in1=r0row[:]),
                        waits=[("vv", vw)], inc="vv")
                vw = em("vector", lambda e: e.scalar_tensor_tensor(
                    out=varrow[:], in0=pb[5][0:1, :], scalar=1.0 / DM, in1=r1row[:],
                    op0=ALU.mult, op1=ALU.subtract), waits=[("vv", vw)], inc="vv")
                aw = em("scalar", lambda e: e.activation(
                    out=varrow[:], in_=varrow[:], func=AF.Ln, bias=epsln[:]),
                    waits=[("vv", vw)], inc="aa")
                aw = em("scalar", lambda e: e.activation(
                    out=varrow[:], in_=varrow[:], func=AF.Exp, scale=-0.5),
                    waits=[("aa", aw)], inc="aa")
                mk1 = em("tensor", lambda e: e.matmul(pb[6][:, :], onesrb[:], r0row[:],
                                                      start=True, stop=True),
                         waits=[("vv", vw)], inc="mm")
                mk1 = em("tensor", lambda e: e.matmul(pb[7][:, :], onesrf[:], varrow[:],
                                                      start=True, stop=True),
                         waits=[("aa", aw)], inc="mm")
                for c in range(NCH):
                    em("vector", lambda e, c=c, ts=ts: e.tensor_sub(
                        out=Hb[:, c, ts], in0=R[:, c, ts], in1=pb[6][:, :]),
                        waits=[("mm", mk1)] if c == 0 else [], inc="vv")
                    em("vector", lambda e, c=c, ts=ts: e.tensor_mul(
                        out=Hb[:, c, ts], in0=Hb[:, c, ts], in1=pb[7][:, :]), inc="vv")
                return P.val("vv"), P.val("pp")

            def ffn_pass(ln_marks):
                prev_tile_vv = None
                prev_tile_aa = None
                ntau = 1 if SUB.startswith("z") else 2
                for tau in range(ntau):
                    ts = slice(tau * FT, tau * FT + FT)
                    lnvv, lnpp = ln_marks[tau]
                    pe_umark = {}
                    evac = {}
                    dma_mark = {}
                    def emit_wdma(blk):
                        par = blk % 2
                        if blk >= 2:
                            wwaits = [("mm", pe_umark[2 * (blk - 2) + 1])]
                        else:
                            wwaits = [("mm", P.val("mm"))]
                        em("gpsimd", lambda e, blk=blk, par=par: e.dma_start(
                            out=Wv(8192 * par, (16, 256)), in_=wg_r[:, blk]),
                            waits=wwaits, inc="dG", amt=16)
                        dma_mark[blk] = em("gpsimd", lambda e, blk=blk, par=par: e.dma_start(
                            out=Wv(8192 * par + 4096, (16, 256)), in_=wu_r[:, blk]),
                            inc="dG", amt=16)
                    emit_wdma(0)
                    emit_wdma(1)
                    if SUB == "dma":
                        for blk in range(2, 32):
                            par = blk % 2
                            dw = [("dG", P.val("dG") - 32)] if blk >= 4 else []
                            em("gpsimd", lambda e, blk=blk, par=par: e.dma_start(
                                out=Wv(8192 * par, (16, 256)), in_=wg_r[:, blk]),
                                waits=dw, inc="dG", amt=16)
                            em("gpsimd", lambda e, blk=blk, par=par: e.dma_start(
                                out=Wv(8192 * par + 4096, (16, 256)), in_=wu_r[:, blk]),
                                inc="dG", amt=16)
                        em("gpsimd", lambda e: None, waits=[("dG", P.val("dG"))])
                        ffn_tile_done = (P.val("vv"), P.val("mm"))
                        continue
                    nblk_eff = 1 if (SUB.startswith("a1") or SUB.startswith("z")) else 32
                    for blk in range(nblk_eff):
                        par = blk % 2
                        for sub in range(2):
                            oc = blk * 2 + sub
                            pg = pb[0 + (oc % 2)]
                            pu = pb[2 + (oc % 2)]
                            pw = [("dG", dma_mark[blk])]
                            if oc >= 2:
                                ag, au, vf = evac[oc - 2]
                                pw += [("aa", au), ("vv", vf)]
                            if oc in (0, 1):
                                if oc == 0:
                                    pw += [("vv", lnvv), ("pp", lnpp)]
                                if prev_tile_vv is not None:
                                    pw += [("vv", prev_tile_vv), ("aa", prev_tile_aa)]
                            for c in range(NCH):
                                mk_g = em("tensor", lambda e, c=c, sub=sub, par=par, pg=pg, ts=ts: e.matmul(
                                    pg[:, 0:FT],
                                    Wv(8192 * par, (16, 2, 128))[:, c, sub],
                                    Hb[:, c, ts], start=(c == 0), stop=(c == NCH - 1)),
                                    waits=pw if c == 0 else [],
                                    inc="mm" if c == NCH - 1 else None)
                            for c in range(NCH):
                                mk_u = em("tensor", lambda e, c=c, sub=sub, par=par, pu=pu, ts=ts: e.matmul(
                                    pu[:, 0:FT],
                                    Wv(8192 * par + 4096, (16, 2, 128))[:, c, sub],
                                    Hb[:, c, ts], start=(c == 0), stop=(c == NCH - 1)),
                                    inc="mm" if c == NCH - 1 else None)
                            pe_umark[oc] = mk_u
                            if SUB.startswith("z") or SUB in ("a1s", "a1su"):
                                if SUB == "z0":
                                    ag = em("scalar", lambda e, oc=oc, pg=pg: e.activation(
                                        out=fv[:, oc, :], in_=pg[:, 0:FT], func=AF.Silu),
                                        waits=[("mm", mk_g)], inc="aa")
                                elif SUB == "z2":
                                    ag = em("scalar", lambda e, oc=oc, pg=pg: e.activation(
                                        out=usb[:, oc % 2, :], in_=pg[:, 0:FT], func=AF.Silu,
                                        scale=sgc[:, oc:oc + 1]), waits=[("mm", mk_g)], inc="aa")
                                elif SUB == "z3":
                                    ag = em("scalar", lambda e, oc=oc, pg=pg: e.activation(
                                        out=fv[:, oc, :], in_=pg[:, 0:FT], func=AF.Copy,
                                        scale=sgc[:, oc:oc + 1]), waits=[("mm", mk_g)], inc="aa")
                                else:
                                    ag = em("scalar", lambda e, oc=oc, pg=pg: e.activation(
                                        out=fv[:, oc, :], in_=pg[:, 0:FT], func=AF.Silu,
                                        scale=sgc[:, oc:oc + 1]), waits=[("mm", mk_g)], inc="aa")
                                if SUB == "a1su":
                                    au = em("scalar", lambda e, oc=oc, pu=pu: e.activation(
                                        out=usb[:, oc % 2, :], in_=pu[:, 0:FT], func=AF.Copy,
                                        scale=suc[:, oc:oc + 1]), waits=[("mm", mk_u)], inc="aa")
                                else:
                                    au = em("scalar", lambda e: None, waits=[("mm", mk_u)], inc="aa")
                                vf = em("vector", lambda e: None, waits=[("aa", au)], inc="vv")
                                evac[oc] = (ag, au, vf)
                                continue
                            if SUB == "a1m":
                                ag = em("scalar", lambda e: None, waits=[("mm", mk_g)], inc="aa")
                                au = em("scalar", lambda e: None, waits=[("mm", mk_u)], inc="aa")
                                vf = em("vector", lambda e: None, waits=[("aa", au)], inc="vv")
                                evac[oc] = (ag, au, vf)
                                continue
                            ag = em("scalar", lambda e, oc=oc, pg=pg: e.activation(
                                out=fv[:, oc, :], in_=pg[:, 0:FT], func=AF.Silu,
                                scale=sgc[:, oc:oc + 1]), waits=[("mm", mk_g)], inc="aa")
                            au = em("scalar", lambda e, oc=oc, pu=pu: e.activation(
                                out=usb[:, oc % 2, :], in_=pu[:, 0:FT], func=AF.Copy,
                                scale=suc[:, oc:oc + 1]), waits=[("mm", mk_u)], inc="aa")
                            vf = em("vector", lambda e, oc=oc: e.tensor_mul(
                                out=fv[:, oc, :], in0=fv[:, oc, :], in1=usb[:, oc % 2, :]),
                                waits=[("aa", au)], inc="vv")
                            evac[oc] = (ag, au, vf)
                        if blk + 2 < nblk_eff:
                            emit_wdma(blk + 2)
                    f_done = P.val("vv")
                    if SUB == "a" or SUB.startswith("a1") or SUB.startswith("z"):
                        prev_tile_vv = P.val("vv")
                        prev_tile_aa = P.val("aa")
                        ffn_tile_done = (P.val("vv"), P.val("mm"))
                        continue
                    slab_mark = {}
                    yev = {}
                    for oq in range(4):
                        base = 0 if oq % 2 == 0 else 4
                        accs = [pb[base + j] for j in range(4)]
                        for s in range(16):
                            par = s % 2
                            if oq == 0 and s < 2:
                                wwaits = [("vv", f_done)]
                            else:
                                prev_use = (oq, s - 2) if s >= 2 else (oq - 1, 14 + s)
                                wwaits = [("mm", slab_mark[prev_use])]
                            em("gpsimd", lambda e, oq=oq, s=s, par=par: e.dma_start(
                                out=Wv(4096 * par, (4, 512)), in_=wd_r[:, oq, s]),
                                waits=wwaits, inc="dG", amt=16)
                            dmk = P.val("dG")
                            pw = [("dG", dmk)]
                            if s == 0:
                                if oq >= 2:
                                    pw.append(("vv", yev[oq - 2]))
                                else:
                                    pw.append(("vv", f_done))
                            for c in range(4):
                                for j in range(4):
                                    slab_mark[(oq, s)] = em("tensor", lambda e, c=c, j=j, s=s, par=par, accs=accs: e.matmul(
                                        accs[j][:, 0:FT],
                                        Wv(4096 * par, (4, 4, 128))[:, c, j],
                                        fv[:, s * 4 + c, :],
                                        start=(s == 0 and c == 0), stop=(s == 15 and c == 3)),
                                        waits=pw if (c == 0 and j == 0) else [],
                                        inc="mm" if (c == 3 and j == 3) else None)
                        mk = slab_mark[(oq, 15)]
                        for j in range(4):
                            oc = oq * 4 + j
                            vw = em("vector", lambda e, oc=oc, accs=accs, j=j, ts=ts: e.scalar_tensor_tensor(
                                out=R[:, oc, ts], in0=accs[j][:, 0:FT],
                                scalar=sdh[:, oc:oc + 1], in1=R[:, oc, ts],
                                op0=ALU.mult, op1=ALU.add),
                                waits=[("mm", mk)] if j == 0 else [], inc="vv")
                        yev[oq] = vw
                    prev_tile_vv = P.val("vv")
                    prev_tile_aa = P.val("aa")
                    ffn_tile_done = (P.val("vv"), P.val("mm"))
                return ffn_tile_done

            if stage >= 1:
                ln1 = {0: ln_full(0, pre_mm=[("dS", ld_done)], pre_vv=[("vv", init_done)]),
                       1: None}
                if SUB.startswith("z"):
                    ln1[1] = ln1[0]
                else:
                    ln1[1] = ln_full(1)
                if sub == "ln":
                    final_marks = {"vv": ln1[1][0], "aa": P.val("aa")}
                else:
                    f1_vv, f1_mm = ffn_pass(ln1)
                    final_marks = {"vv": f1_vv, "aa": P.val("aa")}
            else:
                final_marks = {"vv": init_done, "aa": P.val("aa")}
                if stage >= -1:
                    pass

            if stage >= 2:
                ln2 = {0: ln_full(0, pre_mm=[("vv", f1_vv)]), 1: None}
                ln2[1] = ln_full(1)
                h_vv, h_pp = ln2[1]
                tap("h", Hb[:, :, :].rearrange("p c t -> p (c t)"))

                em("gpsimd", lambda e: e.dma_start(out=Wv(0, (16, 1024)), in_=wqc_r),
                   waits=[("mm", P.val("mm"))], inc="dG", amt=16)
                wqc_ld = P.val("dG")
                qc_ev = {}
                for co in range(8):
                    for tt in range(2):
                        it2 = co * 2 + tt
                        tsl = slice(tt * 512, tt * 512 + 512)
                        pband = pb[it2 % 2]
                        pw = [("dG", wqc_ld), ("vv", h_vv), ("pp", h_pp)]
                        if it2 >= 2:
                            pw.append(("aa", qc_ev[it2 - 2]))
                        for c in range(NCH):
                            mk = em("tensor", lambda e, c=c, co=co, tsl=tsl, pband=pband: e.matmul(
                                pband[:, 0:512], Wv(0, (16, 8, 128))[:, c, co],
                                Hb[:, c, tsl], start=(c == 0), stop=(c == NCH - 1)),
                                waits=pw if c == 0 else [],
                                inc="mm" if c == NCH - 1 else None)
                        qc_ev[it2] = em("scalar", lambda e, co=co, tsl=tsl, pband=pband: e.activation(
                            out=qcv[:, co, tsl], in_=pband[:, 0:512], func=AF.Copy),
                            waits=[("mm", mk)], inc="aa")
                qc_done = P.val("aa")

                em("gpsimd", lambda e: e.dma_start(out=Wv(0, (8, 2048)), in_=wqup_r),
                   waits=[("aa", qc_done)], inc="dG", amt=16)
                wqup_ld = P.val("dG")
                q_ev = {}
                for hh in range(16):
                    for tt in range(2):
                        it2 = hh * 2 + tt
                        tsl = slice(tt * 512, tt * 512 + 512)
                        pband = pb[2 + it2 % 2]
                        pw = [("dG", wqup_ld)]
                        if it2 >= 2:
                            pw.append(("aa", q_ev[it2 - 2]))
                        for c in range(8):
                            mk = em("tensor", lambda e, c=c, hh=hh, tsl=tsl, pband=pband: e.matmul(
                                pband[:, 0:512], Wv(0, (8, 16, 128))[:, c, hh],
                                qcv[:, c, tsl], start=(c == 0), stop=(c == 7)),
                                waits=pw if c == 0 else [],
                                inc="mm" if c == 7 else None)
                        q_ev[it2] = em("scalar", lambda e, hh=hh, tsl=tsl, pband=pband: e.activation(
                            out=qv[:, hh, tsl], in_=pband[:, 0:512], func=AF.Copy),
                            waits=[("mm", mk)], inc="aa")
                q_done = P.val("aa")

                hs_ld = em("sync", lambda e: e.dma_start(out=hsel[:, :, :], in_=hsel_r),
                           waits=[("aa", q_done)], inc="dS", amt=16)
                for hh in range(16):
                    aw = em("scalar", lambda e, hh=hh: e.activation(
                        out=sq[:, 0:512], in_=qv[:, hh, 0:512], func=AF.Square),
                        waits=[("aa", q_done), ("dS", hs_ld)] if hh == 0 else [("mm", P.val("mm"))], inc="aa")
                    em("tensor", lambda e, hh=hh: e.matmul(
                        pb[4][0:16, 0:512], hsel[:, hh, :], sq[:, 0:512],
                        start=(hh == 0), stop=(hh == 15)), waits=[("aa", aw)], inc="mm")
                    aw = em("scalar", lambda e, hh=hh: e.activation(
                        out=sq[:, 0:512], in_=qv[:, hh, 512:1024], func=AF.Square),
                        waits=[("mm", P.val("mm"))], inc="aa")
                    em("tensor", lambda e, hh=hh: e.matmul(
                        pb[5][0:16, 0:512], hsel[:, hh, :], sq[:, 0:512],
                        start=(hh == 0), stop=(hh == 15)), waits=[("aa", aw)], inc="mm")
                mkq = P.val("mm")
                aw = em("scalar", lambda e: e.activation(
                    out=stats32[:, 0:512], in_=pb[4][0:16, 0:512], func=AF.Ln,
                    bias=epsrms[0:16, :], scale=1.0 / 128.0), waits=[("mm", mkq)], inc="aa")
                aw = em("scalar", lambda e: e.activation(
                    out=stats32[:, 512:1024], in_=pb[5][0:16, 0:512], func=AF.Ln,
                    bias=epsrms[0:16, :], scale=1.0 / 128.0), inc="aa")
                rstd_done = em("scalar", lambda e: e.activation(
                    out=rstd_hs[:, :], in_=stats32[:, :], func=AF.Exp,
                    bias=lnC[0:16, :], scale=-0.5), waits=[("aa", aw)], inc="aa")

                em("sync", lambda e: e.dma_start(out=ropeC, in_=ropeC_r),
                   waits=[("aa", q_done)], inc="dS", amt=16)
                rp_ld = em("sync", lambda e: e.dma_start(out=ropeS, in_=ropeS_r),
                           inc="dS", amt=16)
                qs_ld = em("sync", lambda e: e.dma_start(out=qsel[:, :, :], in_=qsel_r),
                           inc="dS", amt=16)
                qn_prev = None
                qn_prev2 = {}
                for it2 in range(32):
                    hh, half = it2 // 2, it2 % 2
                    hsl = slice(half * 512, half * 512 + 512)
                    pA_ = pb[4 + it2 % 2]
                    pB_ = pb[6 + it2 % 2]
                    if it2 == 0:
                        pw = [("dS", qs_ld), ("aa", rstd_done), ("dS", rp_ld)]
                    elif it2 == 1:
                        pw = []
                    else:
                        pw = [("vv", qn_prev2[it2 - 2])]
                    mka = em("tensor", lambda e, hh=hh, hsl=hsl, pA_=pA_: e.matmul(
                        pA_[:, 0:512], swp[:], qv[:, hh, hsl],
                        start=True, stop=True), waits=pw, inc="mm")
                    mkb = em("tensor", lambda e, hh=hh, hsl=hsl, pB_=pB_: e.matmul(
                        pB_[:, 0:512], qsel[:, hh, :], rstd_hs[:, hsl],
                        start=True, stop=True), inc="mm")
                    ts_ = tA[:, 0:512] if it2 % 2 == 0 else tA[:, 512:1024]
                    vw = em("vector", lambda e, hh=hh, hsl=hsl, ts_=ts_: e.tensor_mul(
                        out=ts_, in0=qv[:, hh, hsl], in1=ropeC[:, hsl]),
                        waits=[("mm", mkb)], inc="vv")
                    vw = em("vector", lambda e, hh=hh, hsl=hsl, pA_=pA_: e.tensor_tensor(
                        out=qv[:, hh, hsl], in0=pA_[:, 0:512], in1=ropeS[:, hsl],
                        op=ALU.mult), inc="vv")
                    vw = em("vector", lambda e, hh=hh, hsl=hsl, ts_=ts_: e.tensor_add(
                        out=qv[:, hh, hsl], in0=qv[:, hh, hsl], in1=ts_), inc="vv")
                    vw = em("vector", lambda e, hh=hh, hsl=hsl, pB_=pB_: e.scalar_tensor_tensor(
                        out=qv[:, hh, hsl], in0=qv[:, hh, hsl], scalar=qnw_col[:],
                        in1=pB_[:, 0:512], op0=ALU.mult, op1=ALU.mult), inc="vv")
                    qn_prev2[it2] = vw
                    qn_prev = vw
                qn_done = qn_prev
                tap("q", qv[:, :, :].rearrange("p c t -> p (c t)"))
                em("sync", lambda e: e.dma_start(out=maskjm[:, :, :, :], in_=maskjm_r),
                   waits=[("vv", qn_done)], inc="dS", amt=16)
                em("sync", lambda e: e.dma_start(out=valid01[:, :, :], in_=valid01_r),
                   inc="dS", amt=16)
                valid_ld = em("sync", lambda e: e.dma_start(out=validadd[:, :, :], in_=validadd_r),
                              inc="dS", amt=16)

                offs = {}
                off = 0
                first = True
                for nm, src, csz in (("wkvb", wkvb_r, 128), ("wzb", wzb_r, 128),
                                     ("wk", wk_r, 128), ("wv", wv_r, 128),
                                     ("wikv", wikv_r, 64), ("wiz", wiz_r, 64),
                                     ("wdq", wdq_r, 64), ("ww", ww_r, 4)):
                    offs[nm] = off
                    em("gpsimd", lambda e, src=src, off=off, csz=csz: e.dma_start(
                        out=Wv(off, (16, csz)), in_=src),
                        waits=[("vv", qn_done)] if first else [], inc="dG", amt=16)
                    first = False
                    off += 16 * csz
                em("gpsimd", lambda e: e.dma_start(out=wiuq_sb, in_=wiuq_r), inc="dG", amt=16)
                smalls_ld = P.val("dG")

                vev = None
                kev = None
                for i in range(8):
                    tsl = slice(i * 128, i * 128 + 128)
                    for c in range(NCH):
                        pw = []
                        if c == 0:
                            pw = [("dG", smalls_ld)]
                            if vev is not None:
                                pw.append(("aa", vev))
                        mkv = em("tensor", lambda e, c=c, tsl=tsl: e.matmul(
                            pb[0][:, 0:128], Hb[:, c, tsl], Wv(offs["wv"], (16, 128))[:, c, :],
                            start=(c == 0), stop=(c == NCH - 1)), waits=pw,
                            inc="mm" if c == NCH - 1 else None)
                    vev = em("scalar", lambda e, i=i: e.activation(
                        out=vwin[:, 4 + i, :], in_=pb[0][:, 0:128], func=AF.Copy),
                        waits=[("mm", mkv)], inc="aa")
                    for c in range(NCH):
                        pw = [("vv", kev)] if (c == 0 and kev is not None) else []
                        mkk = em("tensor", lambda e, c=c, tsl=tsl: e.matmul(
                            pb[1][:, 0:128], Hb[:, c, tsl], Wv(offs["wk"], (16, 128))[:, c, :],
                            start=(c == 0), stop=(c == NCH - 1)), waits=pw,
                            inc="mm" if c == NCH - 1 else None)
                    aw = em("scalar", lambda e: e.activation(
                        out=sq[:, 0:128], in_=pb[1][:, 0:128], func=AF.Square),
                        waits=[("mm", mkk)], inc="aa")
                    vw = em("vector", lambda e: e.reduce_sum(
                        out=rms_c[:], in_=sq[:, 0:128], axis=AX.X), waits=[("aa", aw)], inc="vv")
                    aw = em("scalar", lambda e: e.activation(
                        out=rms_c[:], in_=rms_c[:], func=AF.Ln, bias=epsrms[:], scale=1.0 / 128.0),
                        waits=[("vv", vw)], inc="aa")
                    aw = em("scalar", lambda e: e.activation(
                        out=rms_c[:], in_=rms_c[:], func=AF.Exp, scale=-0.5),
                        waits=[("aa", aw)], inc="aa")
                    vw = em("vector", lambda e: e.scalar_tensor_tensor(
                        out=cbf[:, 0:128], in0=pb[1][:, 0:128], scalar=rms_c[:],
                        in1=knw_bc[:], op0=ALU.mult, op1=ALU.mult),
                        waits=[("aa", aw)], inc="vv")
                    ci_ = coskt[:, i, :]
                    si_ = sinkt[:, i, :]
                    em("vector", lambda e, ci_=ci_: e.tensor_mul(
                        out=sel32[:, 0:32], in0=cbf[:, 0:32], in1=ci_), inc="vv")
                    em("vector", lambda e, si_=si_: e.tensor_mul(
                        out=sel32[:, 32:64], in0=cbf[:, 32:64], in1=si_), inc="vv")
                    em("vector", lambda e, i=i: e.tensor_sub(
                        out=kwin_tm[:, 4 + i, 0:32], in0=sel32[:, 0:32], in1=sel32[:, 32:64]), inc="vv")
                    em("vector", lambda e, si_=si_: e.tensor_mul(
                        out=sel32[:, 0:32], in0=cbf[:, 0:32], in1=si_), inc="vv")
                    em("vector", lambda e, ci_=ci_: e.tensor_mul(
                        out=sel32[:, 32:64], in0=cbf[:, 32:64], in1=ci_), inc="vv")
                    em("vector", lambda e, i=i: e.tensor_add(
                        out=kwin_tm[:, 4 + i, 32:64], in0=sel32[:, 0:32], in1=sel32[:, 32:64]), inc="vv")
                    kev = em("vector", lambda e, i=i: e.tensor_copy(
                        out=kwin_tm[:, 4 + i, 64:128], in_=cbf[:, 64:128]), inc="vv")
                kv_done_vv = P.val("vv")
                kv_done_aa = P.val("aa")

                def compress(wc, wz, odim, biasT_ap, out_ap, do_rms, pre_vv):
                    last = pre_vv
                    for tt in range(2):
                        tsl = slice(tt * 512, tt * 512 + 512)
                        for c in range(NCH):
                            pw = [("vv", last)] if c == 0 else []
                            mk = em("tensor", lambda e, c=c, tsl=tsl, wc=wc, odim=odim: e.matmul(
                                pb[2][0:odim, 0:512], Wv(wc, (16, odim))[:, c, :],
                                Hb[:, c, tsl], start=(c == 0), stop=(c == NCH - 1)), waits=pw,
                                inc="mm" if c == NCH - 1 else None)
                        aw = em("scalar", lambda e, tt=tt, odim=odim: e.activation(
                            out=cbf[0:odim, tt * 512:tt * 512 + 512], in_=pb[2][0:odim, 0:512],
                            func=AF.Copy), waits=[("mm", mk)], inc="aa")
                        for c in range(NCH):
                            mk = em("tensor", lambda e, c=c, tsl=tsl, wz=wz, odim=odim: e.matmul(
                                pb[3][0:odim, 0:512], Wv(wz, (16, odim))[:, c, :],
                                Hb[:, c, tsl], start=(c == 0), stop=(c == NCH - 1)),
                                waits=[("aa", aw)] if c == 0 else [],
                                inc="mm" if c == NCH - 1 else None)
                        last = em("vector", lambda e, tt=tt, odim=odim: e.tensor_copy(
                            out=zb32[0:odim, tt * 512:tt * 512 + 512], in_=pb[3][0:odim, 0:512]),
                            waits=[("mm", mk)], inc="vv")
                    z3 = lambda ap, od: ap[0:od, :].rearrange("p (n m) -> p n m", m=32)
                    vw = em("vector", lambda e, odim=odim, biasT_ap=biasT_ap: e.tensor_tensor(
                        out=z3(zb32, odim), in0=z3(zb32, odim),
                        in1=biasT_ap.rearrange("p (o m) -> p o m", o=1).broadcast_to([odim, 32, 32]),
                        op=ALU.add), inc="vv")
                    vw = em("vector", lambda e, odim=odim: e.tensor_reduce(
                        out=gmx[0:odim, :], in_=z3(zb32, odim), axis=AX.X, op=ALU.max,
                        negate=True), inc="vv")
                    vw = em("vector", lambda e, odim=odim: e.tensor_tensor(
                        out=z3(zb32, odim), in0=z3(zb32, odim),
                        in1=gmx[0:odim, :].rearrange("p (n o) -> p n o", o=1).broadcast_to([odim, 32, 32]),
                        op=ALU.add), waits=[("vv", vw)], inc="vv")
                    aw = em("scalar", lambda e, odim=odim: e.activation(
                        out=zb32[0:odim, :], in_=zb32[0:odim, :], func=AF.Exp),
                        waits=[("vv", vw)], inc="aa")
                    vw = em("vector", lambda e, odim=odim: e.tensor_reduce(
                        out=gsm[0:odim, :], in_=z3(zb32, odim), axis=AX.X, op=ALU.add),
                        waits=[("aa", aw)], inc="vv")
                    vw = em("vector", lambda e, odim=odim: e.tensor_mul(
                        out=zb32[0:odim, :], in0=zb32[0:odim, :], in1=cbf[0:odim, :]), inc="vv")
                    vw = em("vector", lambda e, odim=odim: e.tensor_reduce(
                        out=cmp32[0:odim, :], in_=z3(zb32, odim), axis=AX.X, op=ALU.add), inc="vv")
                    vw = em("vector", lambda e, odim=odim: e.reciprocal(
                        out=grec[0:odim, :], in_=gsm[0:odim, :]), waits=[("vv", vw)], inc="vv")
                    vw = em("vector", lambda e, odim=odim: e.tensor_mul(
                        out=cmp32[0:odim, :], in0=cmp32[0:odim, :], in1=grec[0:odim, :]),
                        waits=[("vv", vw)], inc="vv")
                    if do_rms:
                        aw = em("scalar", lambda e: e.activation(
                            out=sq[:, 0:32], in_=cmp32[:, :], func=AF.Square),
                            waits=[("vv", vw)], inc="aa")
                        mk = em("tensor", lambda e: e.matmul(
                            pb[2][0:1, 0:32], onescb[:], sq[:, 0:32], start=True, stop=True),
                            waits=[("aa", aw)], inc="mm")
                        aw = em("scalar", lambda e: e.activation(
                            out=rstd_r32[:], in_=pb[2][0:1, 0:32], func=AF.Ln,
                            bias=epsrms[0:1, :], scale=1.0 / 128.0),
                            waits=[("mm", mk)], inc="aa")
                        aw = em("scalar", lambda e: e.activation(
                            out=rstd_r32[:], in_=rstd_r32[:], func=AF.Exp, scale=-0.5), inc="aa")
                        mk = em("tensor", lambda e: e.matmul(
                            pb[3][:, 0:32], onesrf[:], rstd_r32[:], start=True, stop=True),
                            waits=[("aa", aw)], inc="mm")
                        vw = em("vector", lambda e, out_ap=out_ap: e.scalar_tensor_tensor(
                            out=out_ap, in0=cmp32[:, :], scalar=knw_col[:],
                            in1=pb[3][:, 0:32], op0=ALU.mult, op1=ALU.mult),
                            waits=[("mm", mk)], inc="vv")
                    else:
                        vw = em("vector", lambda e, odim=odim, out_ap=out_ap: e.tensor_copy(
                            out=out_ap, in_=cmp32[0:odim, :]), waits=[("vv", vw)], inc="vv")
                    return vw

                ckv_loc = compress(offs["wkvb"], offs["wzb"], 128, biasbT[:, :],
                                   ckv_fm[:, 0:32], True, kv_done_vv)
                kci_loc = compress(offs["wikv"], offs["wiz"], 64, biasiT[:, :],
                                   kci_cm[0:64, 0:32], False, ckv_loc)

                for tt in range(2):
                    tsl = slice(tt * 512, tt * 512 + 512)
                    for c in range(NCH):
                        pw = []
                        if c == 0:
                            pw = [("vv", kci_loc)] if tt == 0 else [("aa", P.val("aa"))]
                        mk = em("tensor", lambda e, c=c, tsl=tsl: e.matmul(
                            pb[2][0:64, 0:512], Wv(offs["wdq"], (16, 64))[:, c, :],
                            Hb[:, c, tsl], start=(c == 0), stop=(c == NCH - 1)),
                            waits=pw, inc="mm" if c == NCH - 1 else None)
                    em("scalar", lambda e, tt=tt: e.activation(
                        out=dq_v[0:64, tt * 512:tt * 512 + 512], in_=pb[2][0:64, 0:512],
                        func=AF.Copy), waits=[("mm", mk)], inc="aa")
                dq_done = P.val("aa")
                for chn in range(2):
                    for tt in range(2):
                        mk = em("tensor", lambda e, chn=chn, tt=tt: e.matmul(
                            pb[3][:, 0:512], wiuq_sb[0:64, chn * 128:chn * 128 + 128],
                            dq_v[0:64, tt * 512:tt * 512 + 512], start=True, stop=True),
                            waits=[("aa", P.val("aa"))], inc="mm")
                        em("scalar", lambda e, chn=chn, tt=tt: e.activation(
                            out=qi_v[0:64, 2 * chn, tt * 512:tt * 512 + 512],
                            in_=pb[3][0:64, 0:512], func=AF.Copy),
                            waits=[("mm", mk)], inc="aa")
                        em("scalar", lambda e, chn=chn, tt=tt: e.activation(
                            out=qi_v[0:64, 2 * chn + 1, tt * 512:tt * 512 + 512],
                            in_=pb[3][64:128, 0:512], func=AF.Copy), inc="aa")
                qi_done = P.val("aa")
                wgev = None
                for i in range(8):
                    tsl = slice(i * 128, i * 128 + 128)
                    for c in range(NCH):
                        pw = []
                        if c == 0:
                            pw = [("aa", qi_done)] if i == 0 else [("vv", wgev)]
                        mk = em("tensor", lambda e, c=c, tsl=tsl: e.matmul(
                            pb[2][:, 0:4], Hb[:, c, tsl], Wv(offs["ww"], (16, 4))[:, c, :],
                            start=(c == 0), stop=(c == NCH - 1)), waits=pw,
                            inc="mm" if c == NCH - 1 else None)
                    wgev = em("vector", lambda e, i=i: e.tensor_copy(
                        out=wgt_sb[:, i, :], in_=pb[2][:, 0:4]), waits=[("mm", mk)], inc="vv")
                wgt_done = P.val("vv")

                ccK = cc_in.ap()[0:1, SEC_K:SEC_K + T * D].rearrange(
                    "o (n p d) -> (o n) p d", p=128, d=128).rearrange("n p d -> p n d")
                ccV = cc_in.ap()[0:1, SEC_V:SEC_V + T * D].rearrange(
                    "o (n p d) -> (o n) p d", p=128, d=128).rearrange("n p d -> p n d")
                ccC = cc_in.ap()[0:1, SEC_CKV:SEC_CKV + 128 * 32].rearrange(
                    "o (p n) -> (o p) n", n=32)
                ccI = cc_in.ap()[0:1, SEC_KCI:SEC_KCI + 64 * 32].rearrange(
                    "o (p n) -> (o p) n", n=32)
                em("gpsimd", lambda e, ccK=ccK: e.dma_start(out=ccK, in_=kwin_tm[:, 4:12, :]),
                   waits=[("vv", kv_done_vv)], inc="dG", amt=16)
                em("gpsimd", lambda e, ccV=ccV: e.dma_start(out=ccV, in_=vwin[:, 4:12, :]),
                   waits=[("aa", kv_done_aa)], inc="dG", amt=16)
                em("gpsimd", lambda e, ccC=ccC: e.dma_start(out=ccC, in_=ckv_fm[:, 0:32]),
                   waits=[("vv", ckv_loc)], inc="dG", amt=16)
                blob_w = em("gpsimd", lambda e, ccI=ccI: e.dma_start(out=ccI, in_=kci_cm[0:64, 0:32]),
                            waits=[("vv", kci_loc)], inc="dG", amt=16)
                ccdone = em("gpsimd", lambda e: e.collective_compute(
                    "AllGather", ALU.bypass,
                    replica_groups=[[0, 1], [2, 3], [4, 5], [6, 7]],
                    ins=[cc_in.ap().opt()], outs=[cc_out.ap().opt()]),
                    waits=[("dG", blob_w)], inc="cc")
                rbK = cc_out.ap()[0:1, SEC_K:SEC_K + T * D].rearrange(
                    "o (n p d) -> (o n) p d", p=128, d=128).rearrange("n p d -> p n d")
                rbV = cc_out.ap()[0:1, SEC_V:SEC_V + T * D].rearrange(
                    "o (n p d) -> (o n) p d", p=128, d=128).rearrange("n p d -> p n d")
                em("gpsimd", lambda e, rbK=rbK: e.dma_start(
                    out=kwin_tm[:, 0:4, :], in_=rbK[:, 4:8, :]),
                    waits=[("cc", ccdone)], inc="dG", amt=16)
                em("gpsimd", lambda e, rbV=rbV: e.dma_start(
                    out=vwin[:, 0:4, :], in_=rbV[:, 4:8, :]), inc="dG", amt=16)
                for r_ in range(2):
                    cC = cc_out.ap()[r_:r_ + 1, SEC_CKV:SEC_CKV + 128 * 32].rearrange(
                        "o (p n) -> (o p) n", n=32)
                    cI = cc_out.ap()[r_:r_ + 1, SEC_KCI:SEC_KCI + 64 * 32].rearrange(
                        "o (p n) -> (o p) n", n=32)
                    em("gpsimd", lambda e, cC=cC, r_=r_: e.dma_start(
                        out=ckv_fm[:, r_ * 32:r_ * 32 + 32], in_=cC), inc="dG", amt=16)
                    em("gpsimd", lambda e, cI=cI, r_=r_: e.dma_start(
                        out=kci_cm[0:64, r_ * 32:r_ * 32 + 32], in_=cI), inc="dG", amt=16)
                rb_done = P.val("dG")

                ktv = None
                for k in range(KW):
                    pw = [("dG", rb_done)] if k == 0 else [("aa", ktv)]
                    mk = em("tensor", lambda e, k=k: e.transpose(
                        pb[2][:, 0:128], kwin_tm[:, k, :], idf[:]), waits=pw, inc="mm")
                    ktv = em("scalar", lambda e, k=k: e.activation(
                        out=kwin_fm[:, k * 128:k * 128 + 128], in_=pb[2][:, 0:128],
                        func=AF.Copy), waits=[("mm", mk)], inc="aa")
                vk = em("vector", lambda e: e.tensor_copy(out=sel32[:, 0:64], in_=ckv_fm[:, 0:64]),
                        waits=[("dG", rb_done)], inc="vv")
                mk = em("tensor", lambda e: e.transpose(pb[3][0:64, 0:128], sel32[:, 0:64], idf[:]),
                        waits=[("vv", vk)], inc="mm")
                em("scalar", lambda e: e.activation(
                    out=ckv_nm[0:64, :], in_=pb[3][0:64, 0:128], func=AF.Copy),
                    waits=[("mm", mk)], inc="aa")
                tap("kwin", kwin_tm[:, :, :].rearrange("p n d -> p (n d)"))
                tap("vwin", vwin[:, :, :].rearrange("p n d -> p (n d)"))
                tap("ckv", ckv_fm[:, :])
                tap("kci", kci_cm[:, :])

                selv = None
                for qt in range(NQT):
                    tsl = slice(qt * 128, qt * 128 + 128)
                    for ih in range(4):
                        pd = pb[4 + ih % 2]
                        lhs = qi_v[0:64, ih, tsl]
                        pw = [("dG", rb_done), ("vv", P.val("vv"))]
                        mk = em("tensor", lambda e, lhs=lhs, pd=pd: e.matmul(
                            pd[:, 0:64], lhs, kci_cm[0:64, 0:64], start=True, stop=True),
                            waits=pw, inc="mm")
                        vw = em("vector", lambda e, pd=pd: e.tensor_scalar_max(
                            out=m01[:], in0=pd[:, 0:64], scalar1=0.0),
                            waits=[("mm", mk)], inc="vv")
                        if ih == 0:
                            vw = em("vector", lambda e, qt=qt, ih=ih: e.tensor_scalar(
                                out=isc[:], in0=m01[:], scalar1=wgt_sb[:, qt, ih:ih + 1],
                                scalar2=None, op0=ALU.mult),
                                waits=[("vv", wgt_done)], inc="vv")
                        else:
                            vw = em("vector", lambda e, qt=qt, ih=ih: e.scalar_tensor_tensor(
                                out=isc[:], in0=m01[:], scalar=wgt_sb[:, qt, ih:ih + 1],
                                in1=isc[:], op0=ALU.mult, op1=ALU.add), inc="vv")
                    if "isc" in dbg and qt == 0:
                        tap("isc", isc[:, :])
                    vw = em("vector", lambda e, qt=qt: e.tensor_add(
                        out=work[:], in0=isc[:], in1=validadd[:, qt, :]),
                        waits=[("dS", valid_ld)] if qt == 0 else [], inc="vv")
                    vw = em("vector", lambda e: e.tensor_add(
                        out=work[:], in0=work[:], in1=tbrow[:]), inc="vv")
                    for r_ in range(4):
                        vw = em("vector", lambda e: e.max(out=mx8[:], in_=work[:]), inc="vv")
                        if r_ < 3:
                            vw = em("vector", lambda e: e.tensor_single_scalar(
                                out=m01[:], in_=work[:], scalar=mx8[:, 7:8], op=ALU.is_ge),
                                waits=[("vv", vw)], inc="vv")
                            vw = em("vector", lambda e: e.scalar_tensor_tensor(
                                out=work[:], in0=m01[:], scalar=-1e38, in1=work[:],
                                op0=ALU.mult, op1=ALU.add), inc="vv")
                    vw = em("vector", lambda e, qt=qt: e.tensor_add(
                        out=work[:], in0=isc[:], in1=validadd[:, qt, :]),
                        waits=[("vv", vw)], inc="vv")
                    vw = em("vector", lambda e: e.tensor_add(
                        out=work[:], in0=work[:], in1=tbrow[:]), inc="vv")
                    vw = em("vector", lambda e: e.tensor_single_scalar(
                        out=m01[:], in_=work[:], scalar=mx8[:, 7:8], op=ALU.is_ge), inc="vv")
                    vw = em("vector", lambda e, qt=qt: e.tensor_mul(
                        out=sel32[:, 0:64], in0=m01[:], in1=valid01[:, qt, :]), inc="vv")
                    mk = em("tensor", lambda e: e.transpose(pb[6][0:64, 0:128], sel32[:, 0:64], idf[:]),
                            waits=[("vv", vw)], inc="mm")
                    selv = em("scalar", lambda e, qt=qt: e.activation(
                        out=selT[:, qt * 128:qt * 128 + 128], in_=pb[6][0:64, 0:128],
                        func=AF.Copy), waits=[("mm", mk)], inc="aa")
                sel_done = P.val("aa")
                tap("selT", selT[:, :])
                final_marks = {"vv": P.val("vv"), "aa": P.val("aa")}

            if stage >= 3:
                hs_ld2 = em("sync", lambda e: e.dma_start(out=hsel2[:, :, :], in_=hsel_r),
                            waits=[("vv", qn_done)], inc="dS", amt=16)
                for hh in range(16):
                    aw = em("scalar", lambda e, hh=hh: e.activation(
                        out=sq[:, 0:512], in_=qv[:, hh, 0:512], func=AF.Square),
                        waits=[("vv", qn_done), ("dS", hs_ld2)] if hh == 0 else
                        [("mm", P.val("mm"))], inc="aa")
                    em("tensor", lambda e, hh=hh: e.matmul(
                        pb[6][0:16, 0:512], hsel2[:, hh, :], sq[:, 0:512],
                        start=(hh == 0), stop=(hh == 15)), waits=[("aa", aw)], inc="mm")
                    aw = em("scalar", lambda e, hh=hh: e.activation(
                        out=sq[:, 0:512], in_=qv[:, hh, 512:1024], func=AF.Square),
                        waits=[("mm", P.val("mm"))], inc="aa")
                    em("tensor", lambda e, hh=hh: e.matmul(
                        pb[7][0:16, 0:512], hsel2[:, hh, :], sq[:, 0:512],
                        start=(hh == 0), stop=(hh == 15)), waits=[("aa", aw)], inc="mm")
                mkq2 = P.val("mm")
                v1 = em("vector", lambda e: e.reduce_max(
                    out=red1[0:16, 0:1], in_=pb[6][0:16, 0:512], axis=AX.X),
                    waits=[("mm", mkq2)], inc="vv")
                v1 = em("vector", lambda e: e.reduce_max(
                    out=red2[0:16, 0:1], in_=pb[7][0:16, 0:512], axis=AX.X), inc="vv")
                v1 = em("vector", lambda e: e.tensor_max(
                    out=red1[0:16, 0:1], in0=red1[0:16, 0:1], in1=red2[0:16, 0:1]),
                    waits=[("vv", v1)], inc="vv")
                kv1 = None
                for k in range(KW):
                    aw = em("scalar", lambda e, k=k: e.activation(
                        out=sq[:, 0:128], in_=kwin_tm[:, k, :], func=AF.Square),
                        waits=[("vv", v1)] if k == 0 else [("vv", kv1)], inc="aa")
                    kv1 = em("vector", lambda e, k=k: e.reduce_sum(
                        out=red2[:, 0:1], in_=sq[:, 0:128], axis=AX.X),
                        waits=[("aa", aw)], inc="vv")
                    if k == 0:
                        kv1 = em("vector", lambda e: e.tensor_copy(
                            out=red2[:, 1:2], in_=red2[:, 0:1]), waits=[("vv", kv1)], inc="vv")
                    else:
                        kv1 = em("vector", lambda e: e.tensor_max(
                            out=red2[:, 1:2], in0=red2[:, 1:2], in1=red2[:, 0:1]),
                            waits=[("vv", kv1)], inc="vv")
                aw = em("scalar", lambda e: e.activation(
                    out=sq[0:64, 0:128], in_=ckv_nm[0:64, :], func=AF.Square),
                    waits=[("vv", kv1)], inc="aa")
                cv1 = em("vector", lambda e: e.reduce_sum(
                    out=red1[0:64, 1:2], in_=sq[0:64, 0:128], axis=AX.X),
                    waits=[("aa", aw)], inc="vv")
                mk = em("tensor", lambda e: e.transpose(
                    pb[6][0:1, 0:16], red1[0:16, 0:1], idf[0:16, 0:16]),
                    waits=[("vv", cv1)], inc="mm")
                mk = em("tensor", lambda e: e.transpose(
                    pb[6][0:1, 16:144], red2[:, 1:2], idf[:]), inc="mm")
                mk = em("tensor", lambda e: e.transpose(
                    pb[6][0:1, 144:208], red1[0:64, 1:2], idf[0:64, 0:64]), inc="mm")
                v1 = em("vector", lambda e: e.tensor_copy(
                    out=srow[0:1, 0:16], in_=pb[6][0:1, 0:16]),
                    waits=[("mm", mk)], inc="vv")
                v1 = em("vector", lambda e: e.reduce_max(
                    out=srow[0:1, 16:17], in_=pb[6][0:1, 16:144], axis=AX.X), inc="vv")
                v1 = em("vector", lambda e: e.reduce_max(
                    out=srow[0:1, 17:18], in_=pb[6][0:1, 144:208], axis=AX.X),
                    waits=[("vv", v1)], inc="vv")
                v1 = em("vector", lambda e: e.tensor_single_scalar(
                    out=srow[0:1, 32:48], in_=srow[0:1, 0:16], scalar=srow[0:1, 16:17],
                    op=ALU.mult), waits=[("vv", v1)], inc="vv")
                v1 = em("vector", lambda e: e.tensor_single_scalar(
                    out=srow[0:1, 48:64], in_=srow[0:1, 0:16], scalar=srow[0:1, 17:18],
                    op=ALU.mult), waits=[("vv", v1)], inc="vv")
                aw = em("scalar", lambda e: e.activation(
                    out=srow[0:1, 32:64], in_=srow[0:1, 32:64], func=AF.Ln,
                    bias=epsln[:]), waits=[("vv", v1)], inc="aa")
                aw = em("scalar", lambda e: e.activation(
                    out=srow[0:1, 32:64], in_=srow[0:1, 32:64], func=AF.Exp, scale=0.5),
                    waits=[("aa", aw)], inc="aa")
                v1 = em("vector", lambda e: e.tensor_max(
                    out=srow[0:1, 32:48], in0=srow[0:1, 32:48], in1=sink_row[0:1, :]),
                    waits=[("aa", aw)], inc="vv")
                v1 = em("vector", lambda e: e.tensor_scalar(
                    out=srow[0:1, 32:64], in0=srow[0:1, 32:64], scalar1=-1.0,
                    scalar2=None, op0=ALU.mult), waits=[("vv", v1)], inc="vv")
                for hh in range(16):
                    em("tensor", lambda e, hh=hh: e.matmul(
                        pb[7][:, hh:hh + 1], onesrf[:], srow[0:1, 32 + hh:33 + hh],
                        start=True, stop=True),
                        waits=[("vv", v1)] if hh == 0 else [])
                    em("tensor", lambda e, hh=hh: e.matmul(
                        pb[7][:, 16 + hh:17 + hh], onesrf[:], srow[0:1, 48 + hh:49 + hh],
                        start=True, stop=True), inc="mm" if hh == 15 else None)
                mkb = P.val("mm")
                v1 = em("vector", lambda e: e.tensor_copy(out=nSW[:, :], in_=pb[7][:, 0:16]),
                        waits=[("mm", mkb)], inc="vv")
                v1 = em("vector", lambda e: e.tensor_copy(out=nSP[:, :], in_=pb[7][:, 16:32]),
                        inc="vv")
                v1 = em("vector", lambda e: e.tensor_add(
                    out=sinkexp[:, :], in0=nSW[:, :], in1=sink_cols[:, :]),
                    waits=[("vv", v1)], inc="vv")
                preM_aa = em("scalar", lambda e: e.activation(
                    out=sinkexp[:, :], in_=sinkexp[:, :], func=AF.Exp),
                    waits=[("vv", v1)], inc="aa")

                evac_marks = {}
                recip_marks = {}
                for it in range(H * NQT):
                    qt, hh = it // H, it % H
                    par = it % 2
                    X = pb[0 + 2 * par]
                    Y = pb[1 + 2 * par]
                    Z = pb[4 + par]
                    tsl = slice(qt * 128, qt * 128 + 128)
                    qap = qv[:, hh, tsl]
                    if it >= 2:
                        pw0 = [("aa", evac_marks[it - 2]), ("vv", recip_marks[it - 2])]
                    else:
                        pw0 = [("aa", max(sel_done, preM_aa))]
                    for krel in range(5):
                        dst = X[:, krel * 128:krel * 128 + 128] if krel < 4 else Y[:, 0:128]
                        em("tensor", lambda e, krel=krel, qt=qt, dst=dst, qap=qap: e.matmul(
                            dst, kwin_fm[:, (qt + krel) * 128:(qt + krel) * 128 + 128], qap,
                            start=True, stop=True), waits=pw0 if krel == 0 else [])
                    mk_sc = em("tensor", lambda e, qap=qap, Y=Y: e.matmul(
                        Y[0:64, 128:256], ckv_fm[:, 0:64], qap, start=True, stop=True),
                        inc="mm")
                    a1 = em("scalar", lambda e, X=X, par=par, hh=hh: e.activation(
                        out=esw[:, par, 0:4, :], in_=X[:, :].rearrange("p (a b) -> p a b", b=128),
                        func=AF.Exp, bias=nSW[:, hh:hh + 1]),
                        waits=[("mm", mk_sc)], inc="aa")
                    a1 = em("scalar", lambda e, Y=Y, par=par, hh=hh: e.activation(
                        out=esw[:, par, 4, :], in_=Y[:, 0:128], func=AF.Exp,
                        bias=nSW[:, hh:hh + 1]), inc="aa")
                    a1 = em("scalar", lambda e, Y=Y, par=par, hh=hh: e.activation(
                        out=esp[:, par, :], in_=Y[0:64, 128:256], func=AF.Exp,
                        bias=nSP[0:64, hh:hh + 1]), inc="aa")
                    v1 = em("vector", lambda e, par=par, qt=qt: e.tensor_mul(
                        out=esw[:, par, :, :], in0=esw[:, par, :, :], in1=maskjm[:, qt, :, :]),
                        waits=[("aa", a1)], inc="vv")
                    mask_done = em("vector", lambda e, par=par, qt=qt: e.tensor_mul(
                        out=esp[:, par, :], in0=esp[:, par, :],
                        in1=selT[:, qt * 128:qt * 128 + 128]), inc="vv")
                    for krel in range(5):
                        em("tensor", lambda e, krel=krel, par=par, qt=qt, Z=Z: e.matmul(
                            Z[:, 128:256], esw[:, par, krel, :], vwin[:, qt + krel, :],
                            start=(krel == 0), stop=(krel == 4)),
                            waits=[("vv", mask_done)] if krel == 0 else [])
                        em("tensor", lambda e, krel=krel, par=par, Z=Z: e.matmul(
                            Z[:, 256:257], esw[:, par, krel, :], onescb[:],
                            start=(krel == 0), stop=(krel == 4)))
                    em("tensor", lambda e, par=par, Z=Z: e.matmul(
                        Z[:, 0:128], esp[:, par, :], ckv_nm[0:64, :],
                        start=True, stop=True))
                    mk_av = em("tensor", lambda e, par=par, Z=Z: e.matmul(
                        Z[:, 257:258], esp[:, par, :], onescb[0:64, :],
                        start=True, stop=True), inc="mm")
                    v1 = em("vector", lambda e, Z=Z, par=par, hh=hh: e.tensor_add(
                        out=ssw_c[:, par:par + 1], in0=Z[:, 256:257],
                        in1=sinkexp[:, hh:hh + 1]), waits=[("mm", mk_av)], inc="vv")
                    v1 = em("vector", lambda e, par=par: e.reciprocal(
                        out=rsw_c[:, par:par + 1], in_=ssw_c[:, par:par + 1]),
                        waits=[("vv", v1)], inc="vv")
                    v1 = em("vector", lambda e, Z=Z, par=par: e.tensor_scalar(
                        out=ssp_c[:, par:par + 1], in0=Z[:, 257:258], scalar1=1e-30,
                        scalar2=None, op0=ALU.add), inc="vv")
                    rec = em("vector", lambda e, par=par: e.reciprocal(
                        out=rsp_c[:, par:par + 1], in_=ssp_c[:, par:par + 1]),
                        waits=[("vv", v1)], inc="vv")
                    recip_marks[it] = rec
                    a1 = em("scalar", lambda e, Z=Z, par=par: e.activation(
                        out=avs[:, par, :], in_=Z[:, 128:256], func=AF.Copy,
                        scale=rsw_c[:, par:par + 1]), waits=[("vv", rec)], inc="aa")
                    v1 = em("vector", lambda e, Z=Z, par=par: e.scalar_tensor_tensor(
                        out=attn_tm[:, par, :], in0=Z[:, 0:128],
                        scalar=rsp_c[:, par:par + 1], in1=avs[:, par, :],
                        op0=ALU.mult, op1=ALU.add), waits=[("aa", a1)], inc="vv")
                    mk = em("tensor", lambda e, Z=Z, par=par: e.transpose(
                        Z[:, 384:512], attn_tm[:, par, :], idf[:]),
                        waits=[("vv", v1)], inc="mm")
                    evac_marks[it] = em("scalar", lambda e, Z=Z, hh=hh, tsl=tsl: e.activation(
                        out=Hb[:, hh, tsl], in_=Z[:, 384:512], func=AF.Copy),
                        waits=[("mm", mk)], inc="aa")
                attn_done_aa = P.val("aa")
                tap("attn", Hb[:, :, :].rearrange("p c t -> p (c t)"))

                em("gpsimd", lambda e: e.dma_start(out=Wv(0, (16, 512)), in_=gw_r),
                   waits=[("mm", P.val("mm")), ("aa", attn_done_aa),
                          ("vv", P.val("vv"))], inc="dG", amt=16)
                gw_ld = P.val("dG")
                og_ev = {}
                for g_ in range(4):
                    for ocg in range(4):
                        for tt in range(2):
                            it3 = (g_ * 4 + ocg) * 2 + tt
                            tsl = slice(tt * 512, tt * 512 + 512)
                            pband = pb[it3 % 2]
                            pw = [("dG", gw_ld), ("aa", attn_done_aa)]
                            if it3 >= 2:
                                pw.append(("aa", og_ev[it3 - 2]))
                            for ic in range(4):
                                mk = em("tensor", lambda e, g_=g_, ic=ic, ocg=ocg, tsl=tsl, pband=pband: e.matmul(
                                    pband[:, 0:512],
                                    Wv(0, (16, 4, 128))[:, g_ * 4 + ic, ocg],
                                    Hb[:, g_ * 4 + ic, tsl],
                                    start=(ic == 0), stop=(ic == 3)),
                                    waits=pw if ic == 0 else [],
                                    inc="mm" if ic == 3 else None)
                            og_ev[it3] = em("scalar", lambda e, g_=g_, ocg=ocg, tsl=tsl, pband=pband: e.activation(
                                out=qv[:, g_ * 4 + ocg, tsl], in_=pband[:, 0:512],
                                func=AF.Copy), waits=[("mm", mk)], inc="aa")
                og_done = P.val("aa")
                tap("og", qv[:, :, :].rearrange("p c t -> p (c t)"))

                fin_ev = {}
                for hf in range(2):
                    em("gpsimd", lambda e, hf=hf: e.dma_start(
                        out=Wv(0, (8, 2048)), in_=fin_r[:, hf * 8:hf * 8 + 8, :]),
                        waits=[("aa", og_done), ("mm", P.val("mm"))], inc="dG", amt=16)
                    fin_ld = P.val("dG")
                    for oc in range(16):
                        for tt in range(2):
                            it4 = oc * 2 + tt
                            tsl = slice(tt * 512, tt * 512 + 512)
                            pband = pb[it4 % 4]
                            pw2 = [("dG", fin_ld)]
                            if it4 >= 4:
                                pw2.append(("vv", fin_ev[(hf, it4 - 4)]))
                            elif hf == 1:
                                pw2.append(("vv", fin_ev[(0, 28 + it4)]))
                            for kc in range(8):
                                mk = em("tensor", lambda e, kc=kc, oc=oc, tsl=tsl, pband=pband, hf=hf: e.matmul(
                                    pband[:, 0:512],
                                    Wv(0, (8, 16, 128))[:, kc, oc],
                                    qv[:, hf * 8 + kc, tsl],
                                    start=(kc == 0), stop=(kc == 7)),
                                    waits=pw2 if kc == 0 else [],
                                    inc="mm" if kc == 7 else None)
                            fin_ev[(hf, it4)] = em("vector", lambda e, oc=oc, tsl=tsl, pband=pband: e.tensor_add(
                                out=R[:, oc, tsl], in0=R[:, oc, tsl], in1=pband[:, 0:512]),
                                waits=[("mm", mk)], inc="vv")
                final_marks = {"vv": P.val("vv"), "aa": P.val("aa")}

            if stage >= 4:
                x2_vv = final_marks["vv"]
                tap("x2", R[:, :, :].rearrange("p c t -> p (c t)"))
                ln3 = {0: ln_full(0, pre_mm=[("vv", x2_vv)]), 1: None}
                ln3[1] = ln_full(1)
                tap("r0", r0row[:])
                tap("r1", varrow[:])
                f2_vv, f2_mm = ffn_pass(ln3)
                tap("h3", Hb[:, :, :].rearrange("p c t -> p (c t)"))
                final_marks = {"vv": f2_vv, "aa": P.val("aa")}

            out_vv = final_marks["vv"]
            dS_out0 = P.val("dS")
            for i in range(8):
                for c in range(NCH):
                    pw = []
                    if i == 0 and c == 0:
                        pw = [("vv", out_vv), ("mm", P.val("mm")),
                              ("dG", P.val("dG"))]
                    else:
                        pw = [("aa", P.val("aa"))]
                    if c == 0 and i >= 2:
                        pw.append(("dS", dS_out0 + 16 * (i - 1)))
                    if c == 0 and i > 0:
                        pw.append(("vv", out_vv))
                    mk = em("tensor", lambda e, i=i, c=c: e.transpose(
                        pb[7][:, 0:128], R[:, c, i * 128:i * 128 + 128], idf[:]),
                        waits=pw, inc="mm")
                    em("scalar", lambda e, i=i, c=c: e.activation(
                        out=outbuf[:, i % 2, c * 128:c * 128 + 128], in_=pb[7][:, 0:128],
                        func=AF.Copy), waits=[("mm", mk)], inc="aa")
                aw = P.val("aa")
                em("sync", lambda e, i=i: e.dma_start(
                    out=y[i * 128:i * 128 + 128, :], in_=outbuf[:, i % 2, :]),
                    waits=[("aa", aw)], inc="dS", amt=16)
            st_done = P.val("dS")
            em("sync", None, waits=[("dS", st_done)])

            with nc.Block() as block:
                @block.sync
                def _(e):
                    P.replay("sync", e, sems)

                @block.tensor
                def _(e):
                    P.replay("tensor", e, sems)

                @block.vector
                def _(e):
                    P.replay("vector", e, sems)

                @block.scalar
                def _(e):
                    P.replay("scalar", e, sems)

                @block.gpsimd
                def _(e):
                    P.replay("gpsimd", e, sems)

    return nc



def _arrange(inputs):
    g = {}
    wg = np.asarray(inputs["L0_ffn_w_gate"])
    wu = np.asarray(inputs["L0_ffn_w_up"])
    wd = np.asarray(inputs["L0_ffn_w_down"])
    g["wg_r"] = np.ascontiguousarray(wg.reshape(16, 128, 32, 256).transpose(1, 2, 0, 3))
    g["wu_r"] = np.ascontiguousarray(wu.reshape(16, 128, 32, 256).transpose(1, 2, 0, 3))
    g["wd_r"] = np.ascontiguousarray(wd.reshape(16, 4, 128, 4, 512).transpose(2, 3, 0, 1, 4))
    g["sg_r"] = np.ascontiguousarray(np.asarray(inputs["L0_ffn_s_gate"]).reshape(64, 128).T)
    g["su_r"] = np.ascontiguousarray(np.asarray(inputs["L0_ffn_s_up"]).reshape(64, 128).T)
    g["sd_r"] = np.ascontiguousarray(np.asarray(inputs["L0_ffn_s_down"]).reshape(16, 128).T)
    g["wqc_r"] = np.ascontiguousarray(
        np.asarray(inputs["L0_attn_w_qc"]).reshape(16, 128, 1024).transpose(1, 0, 2))
    g["wqup_r"] = np.ascontiguousarray(
        np.asarray(inputs["L0_attn_w_qup"]).reshape(8, 128, 2048).transpose(1, 0, 2))
    for nm, key in (("wkvb_r", "L0_attn_w_kv_b"), ("wzb_r", "L0_attn_w_z_b"),
                    ("wk_r", "L0_attn_w_k"), ("wv_r", "L0_attn_w_v"),
                    ("wikv_r", "L0_attn_wi_kv"), ("wiz_r", "L0_attn_wi_z"),
                    ("wdq_r", "L0_attn_w_dq"), ("ww_r", "L0_attn_w_w")):
        w = np.asarray(inputs[key])
        g[nm] = np.ascontiguousarray(w.reshape(16, 128, w.shape[1]).transpose(1, 0, 2))
    g["wiuq_r"] = np.ascontiguousarray(np.asarray(inputs["L0_attn_w_iuq"]))
    g["gw_r"] = np.ascontiguousarray(
        np.asarray(inputs["L0_attn_group_w"]).reshape(4, 4, 128, 512)
        .transpose(2, 0, 1, 3).reshape(128, 16, 512))
    g["fin_r"] = np.ascontiguousarray(
        np.asarray(inputs["L0_attn_final_w"]).reshape(16, 128, 2048).transpose(1, 0, 2))
    g["qnw_r"] = np.ascontiguousarray(np.asarray(inputs["L0_attn_q_norm_w"]).reshape(1, 128))
    g["knw_r"] = np.ascontiguousarray(np.asarray(inputs["L0_attn_k_norm_w"]).reshape(1, 128))
    g["sink_r"] = np.ascontiguousarray(np.asarray(inputs["L0_attn_sink_logit"]).reshape(1, 16))
    g["biasb_r"] = np.ascontiguousarray(np.asarray(inputs["L0_attn_bias_b"]))
    g["biasi_r"] = np.ascontiguousarray(np.asarray(inputs["L0_attn_bias_i"]))
    g["idf_r"] = np.eye(128, dtype=np.float32)
    sg_ = np.arange(128)
    sig = np.where(sg_ < 32, sg_ + 32, np.where(sg_ < 64, sg_ - 32, sg_))
    sw_ = np.zeros((128, 128), BF)
    sw_[sig, np.arange(128)] = 1
    g["swp_r"] = sw_
    hs = np.zeros((128, 16, 16), BF)
    for c in range(16):
        hs[:, c, c] = 1
    g["hsel_r"] = hs
    qs = np.zeros((16, 16, 128), BF)
    for h_ in range(16):
        qs[h_, h_, :] = 1
    g["qsel_r"] = qs
    return g


def _percore_consts(hc):
    t0 = hc * T
    p = np.arange(t0, t0 + T)
    c = {}
    inv = 1.0 / (10000.0 ** (np.arange(0, RD, 2, dtype=np.float64) / RD))
    ang = p[:, None] * inv
    cos = np.cos(ang).astype(np.float32)
    sin = np.sin(ang).astype(np.float32)
    Cm = np.ones((128, 1024), np.float32)
    Sm = np.zeros((128, 1024), np.float32)
    Cm[0:32] = cos.T
    Cm[32:64] = cos.T
    Sm[0:32] = -sin.T
    Sm[32:64] = sin.T
    c["ropeC_r"] = Cm.astype(BF)
    c["ropeS_r"] = Sm.astype(BF)
    c["coskt_r"] = np.ascontiguousarray(cos.reshape(8, 128, 32).transpose(1, 0, 2)).astype(BF)
    c["sinkt_r"] = np.ascontiguousarray(sin.reshape(8, 128, 32).transpose(1, 0, 2)).astype(BF)
    jglob = np.concatenate([np.arange(t0 - 512, t0), np.arange(t0, t0 + T)])
    mk = np.zeros((128, NQT, 5, 128), np.float32)
    for qt in range(NQT):
        qpos = p[qt * 128:(qt + 1) * 128]
        for krel in range(5):
            kpos = jglob[(qt + krel) * 128:(qt + krel) * 128 + 128]
            valid = (kpos[:, None] >= 0) & (kpos[:, None] <= qpos[None, :]) & \
                    (qpos[None, :] - kpos[:, None] < SWIN)
            mk[:, qt, krel, :] = valid
    c["maskjm_r"] = mk.astype(BF)
    val = ((np.arange(NB) * M + (M - 1))[None, :] < p[:, None]).astype(np.float32)
    v01 = np.ascontiguousarray(val.reshape(NQT, 128, NB).transpose(1, 0, 2))
    c["valid01_r"] = v01.astype(BF)
    c["validadd_r"] = ((v01 - 1.0) * 1e30).astype(BF)
    return c


_TBROW = np.broadcast_to(
    (-np.arange(NB, dtype=np.float64) * 1e-20).astype(np.float32), (128, NB)).copy()

_NC_CACHE = {}


def kernel(**inputs):
    key = (STAGE, TAPS, SUB)
    if key not in _NC_CACHE:
        _NC_CACHE[key] = build_program(STAGE, TAPS, SUB)
    nc = _NC_CACHE[key]
    shared = _arrange(inputs)
    shared["tbrow_r"] = _TBROW
    x = np.asarray(inputs["L0_attn_h"])
    pc = [_percore_consts(0), _percore_consts(1)]
    in_maps = []
    for core in range(8):
        b, hc = core // 2, core % 2
        mmap = dict(shared)
        mmap.update(pc[hc])
        mmap["xin"] = np.ascontiguousarray(x[b, hc * T:(hc + 1) * T].T)
        in_maps.append(mmap)
    res = run_bass_kernel_spmd(nc, in_maps, core_ids=list(range(8)))
    out = np.empty((B, TSEQ, DM), np.float32)
    for core in range(8):
        b, hc = core // 2, core % 2
        out[b, hc * T:(hc + 1) * T] = res.results[core]["y"]
    return out


# version 03
# speedup vs baseline: 1.0490x; 1.0490x over previous
import os
import numpy as np
import ml_dtypes
from contextlib import ExitStack

import concourse.bass as bass
import concourse.mybir as mybir
from concourse.bass_utils import run_bass_kernel_spmd

F32 = mybir.dt.float32
BF16 = mybir.dt.bfloat16
I8 = mybir.dt.int8
AF = mybir.ActivationFunctionType
ALU = mybir.AluOpType
AX = mybir.AxisListType
BF = ml_dtypes.bfloat16

B, TSEQ, DM = 4, 2048, 2048
T = 1024
NCH = DM // 128
H, D, RD = 16, 128, 64
M, NB = 32, 64
G, DG, NIH, CI = 4, 512, 4, 64
SWIN = 512
INTER = 8192
NI = INTER // 128
FT = 512
NQT = T // 128
KW = 12
EPS_LN, EPS_RMS = 1e-5, 1e-6

SEC_K = 0
SEC_V = SEC_K + T * D
SEC_CKV = SEC_V + T * D
SEC_KCI = SEC_CKV + 128 * 32
BLOB_N = SEC_KCI + 64 * 32

STAGE = int(os.environ.get("MK_STAGE", "4"))
SUB = os.environ.get("MK_SUB", "full")
TAPS = tuple(t for t in os.environ.get("MK_TAPS", "").split(",") if t)


class Prog:
    ENGS = ("sync", "tensor", "vector", "scalar", "gpsimd")

    def __init__(self):
        self.items = {e: [] for e in self.ENGS}
        self.cnt = {}

    def val(self, sem):
        return self.cnt.get(sem, 0)

    def emit(self, eng, fn=None, waits=(), inc=None, amt=1):
        w = [(s, v) for (s, v) in waits if v and v > 0]
        self.items[eng].append((w, fn, inc, amt))
        if inc is not None:
            self.cnt[inc] = self.cnt.get(inc, 0) + amt
            return self.cnt[inc]
        return None

    def replay(self, eng_name, eng, sems):
        for w, fn, inc, amt in self.items[eng_name]:
            for s, v in w:
                eng.wait_ge(sems[s], v)
            ins = fn(eng) if fn is not None else None
            if inc is not None:
                if ins is None:
                    ins = eng.nop()
                ins.then_inc(sems[inc], amt)


def build_program(stage, taps, sub="full"):
    nc = bass.Bass("TRN2", target_bir_lowering=False)

    def din(name, shape, dtype):
        return nc.dram_tensor(name, shape, dtype, kind="ExternalInput").ap()

    xin = din("xin", [DM, T], F32)
    wg_r = din("wg_r", [128, 32, 16, 256], I8)
    wu_r = din("wu_r", [128, 32, 16, 256], I8)
    wd_r = din("wd_r", [128, 4, 16, 4, 512], I8)
    sg_r = din("sg_r", [128, 64], F32)
    su_r = din("su_r", [128, 64], F32)
    sd_r = din("sd_r", [128, 16], F32)
    wqc_r = din("wqc_r", [128, 16, 1024], F32)
    wqup_r = din("wqup_r", [128, 8, 2048], F32)
    wkvb_r = din("wkvb_r", [128, 16, 128], F32)
    wzb_r = din("wzb_r", [128, 16, 128], F32)
    wk_r = din("wk_r", [128, 16, 128], F32)
    wv_r = din("wv_r", [128, 16, 128], F32)
    wikv_r = din("wikv_r", [128, 16, 64], F32)
    wiz_r = din("wiz_r", [128, 16, 64], F32)
    wdq_r = din("wdq_r", [128, 16, 64], F32)
    ww_r = din("ww_r", [128, 16, 4], F32)
    wiuq_r = din("wiuq_r", [64, 256], F32)
    gw_r = din("gw_r", [128, 16, 512], F32)
    fin_r = din("fin_r", [128, 16, 2048], F32)
    qnw_r = din("qnw_r", [1, 128], F32)
    knw_r = din("knw_r", [1, 128], F32)
    sink_r = din("sink_r", [1, 16], F32)
    biasb_r = din("biasb_r", [32, 128], F32)
    biasi_r = din("biasi_r", [32, 64], F32)
    ropeC_r = din("ropeC_r", [128, 1024], BF16)
    ropeS_r = din("ropeS_r", [128, 1024], BF16)
    swp_r = din("swp_r", [128, 128], BF16)
    coskt_r = din("coskt_r", [128, 8, 32], BF16)
    sinkt_r = din("sinkt_r", [128, 8, 32], BF16)
    maskjm_r = din("maskjm_r", [128, 8, 5, 128], BF16)
    valid01_r = din("valid01_r", [128, 8, 64], BF16)
    validadd_r = din("validadd_r", [128, 8, 64], BF16)
    tbrow_r = din("tbrow_r", [128, 64], F32)
    idf_r = din("idf_r", [128, 128], F32)
    hsel_r = din("hsel_r", [128, 16, 16], BF16)
    qsel_r = din("qsel_r", [16, 16, 128], BF16)

    y = nc.dram_tensor("y", [T, DM], F32, kind="ExternalOutput").ap()
    dbg = {}
    tap_shapes = {"h": [128, NCH * 1024], "q": [128, 16 * 1024],
                  "kwin": [128, KW * 128], "vwin": [128, KW * 128],
                  "ckv": [128, 64], "kci": [64, 64], "selT": [64, NQT * 128],
                  "attn": [128, NCH * 1024], "og": [128, 16 * 1024],
                  "isc": [128, 64], "x2": [128, NCH * 1024], "h3": [128, NCH * 1024], "r0": [1, 512], "r1": [1, 512]}
    for tp in taps:
        dbg[tp] = nc.dram_tensor("dbg_" + tp, tap_shapes[tp], F32,
                                 kind="ExternalOutput").ap()

    cc_in = nc.dram_tensor("cc_in", [1, BLOB_N], BF16)
    cc_out = nc.dram_tensor("cc_out", [2, BLOB_N], BF16)

    P = Prog()
    em = P.emit
    ctx = ExitStack()
    sb = lambda n, s, d: ctx.enter_context(nc.sbuf_tensor(n, s, d))
    psT = lambda n, s, d: ctx.enter_context(nc.psum_tensor(n, s, d))

    with ctx:
        R = sb("R", [128, NCH, 1024], F32)
        Hb = sb("Hb", [128, NCH, 1024], BF16)
        A = sb("A", [128, 32768], BF16)
        W = sb("W", [128, 16384], BF16)
        fv = A[:, :].rearrange("p (c t) -> p c t", c=NI)
        qv = A[:, 0:16384].rearrange("p (c t) -> p c t", c=16)
        qcv = A[:, 16384:24576].rearrange("p (c t) -> p c t", c=8)
        ropeC = A[:, 16384:17408]
        ropeS = A[:, 17408:18432]
        tA = A[:, 18432:19456]
        qsel = A[0:16, 19456:21504].rearrange("p (h t) -> p h t", h=16)
        stats32 = A[0:16, 21504:23552].bitcast(F32)
        hsel = A[:, 23552:23808].rearrange("p (c h) -> p c h", c=16)
        rstd_hs = A[0:16, 23808:24832]
        zb32 = A[:, 16384:18432].bitcast(F32)
        cbf = A[:, 18432:19456]
        qi_v = A[0:64, 19456:23552].rearrange("p (c t) -> p c t", c=4)
        dq_v = A[0:64, 23552:24576]
        maskjm = A[:, 24832:29952].rearrange("p (q k t) -> p q k t", q=8, k=5)
        esw = W[:, 0:1280].rearrange("p (a k t) -> p a k t", a=2, k=5)
        esp = W[0:64, 1280:1536].rearrange("p (a t) -> p a t", a=2)
        avs = W[:, 1536:1792].rearrange("p (a t) -> p a t", a=2)
        attn_tm = W[:, 1792:2304].bitcast(F32).rearrange("p (a t) -> p a t", a=2)
        kwin_fm = W[:, 2304:3840]
        selT = W[0:64, 5376:6400]
        kwin_tm = W[:, 11328:14400].bitcast(F32).rearrange("p (k d) -> p k d", k=KW)
        vwin = W[:, 14400:15936].rearrange("p (k d) -> p k d", k=KW)
        outbuf = W[:, 0:8192].bitcast(F32).rearrange("p (a t) -> p a t", a=2)
        sq = sb("sq", [128, 512], BF16)
        usb = sb("usb", [128, 2, 512], BF16)
        sel32 = sb("sel32", [128, 64], F32)
        wgt_sb = sb("wgt_sb", [128, NQT, 4], F32)
        isc = sb("isc", [128, 64], F32)
        work = sb("work", [128, 64], F32)
        m01 = sb("m01", [128, 64], F32)
        mx8 = sb("mx8", [128, 8], F32)
        sgc = sb("sgc", [128, 64], F32)
        suc = sb("suc", [128, 64], F32)
        sdh = sb("sdh", [128, 16], F32)
        coskt = sb("coskt", [128, 8, 32], BF16)
        sinkt = sb("sinkt", [128, 8, 32], BF16)
        valid01 = A[:, 29952:30464].rearrange("p (q n) -> p q n", q=8)
        validadd = A[:, 30464:30976].rearrange("p (q n) -> p q n", q=8)
        tbrow = sb("tbrow", [128, 64], F32)
        idf = sb("idf", [128, 128], F32)
        swp = sb("swp", [128, 128], BF16)
        onescf = sb("onescf", [128, 1], F32)
        onescb = sb("onescb", [128, 1], BF16)
        onesrf = sb("onesrf", [1, 128], F32)
        onesrb = sb("onesrb", [1, 128], BF16)
        qnw_row = sb("qnw_row", [1, 128], F32)
        knw_row = sb("knw_row", [1, 128], F32)
        qnw_col = sb("qnw_col", [128, 1], F32)
        knw_bc = sb("knw_bc", [128, 128], F32)
        knw_col = sb("knw_col", [128, 1], F32)
        sink_row = sb("sink_row", [1, 16], F32)
        sink_cols = sb("sink_cols", [128, 16], F32)
        biasbT = sb("biasbT", [128, 32], F32)
        biasiT = sb("biasiT", [64, 32], F32)
        biasb_sb = A[0:32, 0:256].bitcast(F32)
        biasi_sb = A[0:32, 256:384].bitcast(F32)
        epsln = sb("epsln", [1, 1], F32)
        epsrms = sb("epsrms", [128, 1], F32)
        lnC = sb("lnC", [128, 1], F32)
        wiuq_sb = W[0:64, 15936:16192]
        r0row = sb("r0row", [1, 512], BF16)
        r1row = sb("r1row", [1, 512], BF16)
        varrow = sb("varrow", [1, 512], F32)
        rstd_r32 = sb("rstd_r32", [1, 32], F32)
        cmp32 = sb("cmp32", [128, 32], F32)
        gmx = sb("gmx", [128, 32], F32)
        gsm = sb("gsm", [128, 32], F32)
        grec = sb("grec", [128, 32], F32)
        rms_c = sb("rms_c", [128, 1], F32)
        ckv_fm = sb("ckv_fm", [128, 64], BF16)
        ckv_nm = sb("ckv_nm", [64, 128], BF16)
        kci_cm = sb("kci_cm", [64, 64], BF16)
        m11 = sb("m11", [1, 8], F32)
        srow = W[0:1, 6400:7424].bitcast(F32)
        nSW = sb("nSW", [128, 16], F32)
        nSP = sb("nSP", [128, 16], F32)
        sinkexp = sb("sinkexp", [128, 16], F32)
        hsel2 = A[:, 30976:31232].rearrange("p (c h) -> p c h", c=16)
        ssw_c = sb("ssw_c", [128, 2], F32)
        ssp_c = sb("ssp_c", [128, 2], F32)
        rsw_c = sb("rsw_c", [128, 2], F32)
        rsp_c = sb("rsp_c", [128, 2], F32)
        red1 = sb("red1", [128, 2], F32)
        red2 = sb("red2", [128, 2], F32)

        pb = [psT(f"pb{i}", [128, 512], F32) for i in range(8)]

        semctx = ExitStack()
        sems = {n: semctx.enter_context(nc.semaphore(n))
                for n in ("dS", "dG", "mm", "vv", "aa", "pp", "cc")}

        with semctx:
            def Wv(off, shape):
                n = int(np.prod(shape))
                v = W[:, off:off + n]
                if len(shape) > 1:
                    lets = "abcdef"[: len(shape)]
                    pat = f"p ({' '.join(lets)}) -> p {' '.join(lets)}"
                    v = v.rearrange(pat, **{lets[i]: shape[i] for i in range(len(shape))})
                return v

            def tap(name, src_ap):
                if name in dbg:
                    waits = [("aa", P.val("aa")), ("vv", P.val("vv")),
                             ("mm", P.val("mm")), ("pp", P.val("pp"))]
                    em("gpsimd", lambda e, s=src_ap, d=dbg[name]: e.dma_start(out=d, in_=s),
                       waits=waits, inc="dG", amt=16)

            for o, i in ((sgc[:], sg_r), (suc[:], su_r), (sdh[:], sd_r),
                         (coskt[:], coskt_r), (sinkt[:], sinkt_r),
                         (tbrow[:], tbrow_r), (idf[:], idf_r),
                         (swp[:], swp_r),
                         (qnw_row[:], qnw_r), (knw_row[:], knw_r),
                         (sink_row[:], sink_r),
                         (biasb_sb, biasb_r), (biasi_sb, biasi_r)):
                em("sync", lambda e, o=o, i=i: e.dma_start(out=o, in_=i), inc="dS", amt=16)
            em("sync", lambda e: e.dma_start(
                out=R[:], in_=xin.rearrange("(c p) t -> p c t", p=128)), inc="dS", amt=16)
            ld_done = P.val("dS")

            def memsets(e):
                e.memset(onescf[:], 1.0)
                e.memset(onescb[:], 1.0)
                e.memset(onesrf[:], 1.0)
                e.memset(onesrb[:], 1.0)
                e.memset(epsln[:], EPS_LN)
                e.memset(epsrms[:], EPS_RMS)
                return e.memset(lnC[:], float(np.log(1.0 / np.sqrt(128.0))))
            vmark = em("vector", memsets, inc="vv")
            vmark = em("vector", lambda e: e.tensor_scalar(
                out=sdh[:], in0=sdh[:], scalar1=0.5, scalar2=None, op0=ALU.mult),
                waits=[("dS", ld_done)], inc="vv")
            mk = em("tensor", lambda e: e.transpose(pb[6][:, 0:1], qnw_row[:], idf[0:1, 0:1]),
                    waits=[("dS", ld_done), ("vv", vmark)], inc="mm")
            mk = em("tensor", lambda e: e.transpose(pb[6][:, 128:129], knw_row[:], idf[0:1, 0:1]),
                    inc="mm")
            mk = em("tensor", lambda e: e.transpose(pb[6][:, 256:288], biasb_sb, idf[0:32, 0:32]),
                    inc="mm")
            mk = em("tensor", lambda e: e.transpose(pb[6][0:64, 288:320], biasi_sb, idf[0:32, 0:32]),
                    inc="mm")
            mk = em("tensor", lambda e: e.matmul(pb[6][:, 320:448], onesrf[:], knw_row[:],
                                                 start=True, stop=True), inc="mm")
            mk = em("tensor", lambda e: e.matmul(pb[6][:, 448:464], onesrf[:], sink_row[:],
                                                 start=True, stop=True), inc="mm")
            vmark = em("vector", lambda e: e.tensor_copy(out=qnw_col[:], in_=pb[6][:, 0:1]),
                       waits=[("mm", mk)], inc="vv")
            vmark = em("vector", lambda e: e.tensor_copy(out=knw_col[:], in_=pb[6][:, 128:129]), inc="vv")
            vmark = em("vector", lambda e: e.tensor_copy(out=biasbT[:, :], in_=pb[6][:, 256:288]), inc="vv")
            vmark = em("vector", lambda e: e.tensor_copy(out=biasiT[:, :], in_=pb[6][0:64, 288:320]), inc="vv")
            vmark = em("vector", lambda e: e.tensor_copy(out=knw_bc[:], in_=pb[6][:, 320:448]), inc="vv")
            init_done = em("vector", lambda e: e.tensor_copy(out=sink_cols[:], in_=pb[6][:, 448:464]), inc="vv")

            def ln_full(tau, pre_mm=(), pre_vv=(), pre_aa=()):
                ts = slice(tau * FT, tau * FT + FT)
                auto = [("vv", P.val("vv")), ("pp", P.val("pp")), ("aa", P.val("aa"))]
                entry_mm = P.val("mm")
                for c in range(NCH):
                    em("tensor", lambda e, c=c, ts=ts: e.matmul(
                        pb[4][0:1, :], onescf[:], R[:, c, ts],
                        start=(c == 0), stop=(c == NCH - 1)),
                        waits=(list(pre_mm) + auto if c == 0 else []),
                        inc="mm" if c == NCH - 1 else None)
                sqmk = entry_mm
                for c in range(NCH):
                    aw = em("scalar", lambda e, c=c, ts=ts: e.activation(
                        out=sq[:], in_=R[:, c, ts], func=AF.Square),
                        waits=(list(pre_aa) if c == 0 else []) + [("mm", sqmk)],
                        inc="aa")
                    sqmk = em("tensor", lambda e, c=c: e.matmul(
                        pb[5][0:1, :], onescb[:], sq[:],
                        start=(c == 0), stop=(c == NCH - 1)),
                        waits=[("aa", aw)], inc="mm")
                mm_sums = P.val("mm")
                vw = em("vector", lambda e: e.tensor_scalar(
                    out=r0row[:], in0=pb[4][0:1, :], scalar1=1.0 / DM, scalar2=None,
                    op0=ALU.mult), waits=[("mm", mm_sums)] + list(pre_vv), inc="vv")
                vw = em("vector", lambda e: e.tensor_mul(out=r1row[:], in0=r0row[:], in1=r0row[:]),
                        waits=[("vv", vw)], inc="vv")
                vw = em("vector", lambda e: e.scalar_tensor_tensor(
                    out=varrow[:], in0=pb[5][0:1, :], scalar=1.0 / DM, in1=r1row[:],
                    op0=ALU.mult, op1=ALU.subtract), waits=[("vv", vw)], inc="vv")
                aw = em("scalar", lambda e: e.activation(
                    out=varrow[:], in_=varrow[:], func=AF.Ln, bias=epsln[:]),
                    waits=[("vv", vw)], inc="aa")
                aw = em("scalar", lambda e: e.activation(
                    out=varrow[:], in_=varrow[:], func=AF.Exp, scale=-0.5),
                    waits=[("aa", aw)], inc="aa")
                mk1 = em("tensor", lambda e: e.matmul(pb[6][:, :], onesrb[:], r0row[:],
                                                      start=True, stop=True),
                         waits=[("vv", vw)], inc="mm")
                mk1 = em("tensor", lambda e: e.matmul(pb[7][:, :], onesrf[:], varrow[:],
                                                      start=True, stop=True),
                         waits=[("aa", aw)], inc="mm")
                for c in range(NCH):
                    em("vector", lambda e, c=c, ts=ts: e.tensor_sub(
                        out=Hb[:, c, ts], in0=R[:, c, ts], in1=pb[6][:, :]),
                        waits=[("mm", mk1)] if c == 0 else [], inc="vv")
                    em("vector", lambda e, c=c, ts=ts: e.tensor_mul(
                        out=Hb[:, c, ts], in0=Hb[:, c, ts], in1=pb[7][:, :]), inc="vv")
                return P.val("vv"), P.val("pp")

            def ffn_pass(ln_marks):
                prev_tile_vv = None
                prev_tile_aa = None
                ntau = 1 if SUB.startswith("z") else 2
                for tau in range(ntau):
                    ts = slice(tau * FT, tau * FT + FT)
                    lnvv, lnpp = ln_marks[tau]
                    pe_umark = {}
                    evac = {}
                    dma_mark = {}
                    def emit_wdma(blk):
                        par = blk % 2
                        if blk >= 2:
                            wwaits = [("mm", pe_umark[2 * (blk - 2) + 1])]
                        else:
                            wwaits = [("mm", P.val("mm"))]
                        em("gpsimd", lambda e, blk=blk, par=par: e.dma_start(
                            out=Wv(8192 * par, (16, 256)), in_=wg_r[:, blk]),
                            waits=wwaits, inc="dG", amt=16)
                        dma_mark[blk] = em("gpsimd", lambda e, blk=blk, par=par: e.dma_start(
                            out=Wv(8192 * par + 4096, (16, 256)), in_=wu_r[:, blk]),
                            inc="dG", amt=16)
                    emit_wdma(0)
                    emit_wdma(1)
                    if SUB == "dma":
                        for blk in range(2, 32):
                            par = blk % 2
                            dw = [("dG", P.val("dG") - 32)] if blk >= 4 else []
                            em("gpsimd", lambda e, blk=blk, par=par: e.dma_start(
                                out=Wv(8192 * par, (16, 256)), in_=wg_r[:, blk]),
                                waits=dw, inc="dG", amt=16)
                            em("gpsimd", lambda e, blk=blk, par=par: e.dma_start(
                                out=Wv(8192 * par + 4096, (16, 256)), in_=wu_r[:, blk]),
                                inc="dG", amt=16)
                        em("gpsimd", lambda e: None, waits=[("dG", P.val("dG"))])
                        ffn_tile_done = (P.val("vv"), P.val("mm"))
                        continue
                    nblk_eff = 1 if (SUB.startswith("a1") or SUB.startswith("z")) else 32
                    for blk in range(nblk_eff):
                        par = blk % 2
                        for sub in range(2):
                            oc = blk * 2 + sub
                            pg = pb[0 + (oc % 2)]
                            pu = pb[2 + (oc % 2)]
                            pw = [("dG", dma_mark[blk])]
                            if oc >= 2:
                                ag, au, vf = evac[oc - 2]
                                pw += [("aa", au), ("vv", vf)]
                            if oc in (0, 1):
                                if oc == 0:
                                    pw += [("vv", lnvv), ("pp", lnpp)]
                                if prev_tile_vv is not None:
                                    pw += [("vv", prev_tile_vv), ("aa", prev_tile_aa)]
                            for c in range(NCH):
                                mk_g = em("tensor", lambda e, c=c, sub=sub, par=par, pg=pg, ts=ts: e.matmul(
                                    pg[:, 0:FT],
                                    Wv(8192 * par, (16, 2, 128))[:, c, sub],
                                    Hb[:, c, ts], start=(c == 0), stop=(c == NCH - 1)),
                                    waits=pw if c == 0 else [],
                                    inc="mm" if c == NCH - 1 else None)
                            for c in range(NCH):
                                mk_u = em("tensor", lambda e, c=c, sub=sub, par=par, pu=pu, ts=ts: e.matmul(
                                    pu[:, 0:FT],
                                    Wv(8192 * par + 4096, (16, 2, 128))[:, c, sub],
                                    Hb[:, c, ts], start=(c == 0), stop=(c == NCH - 1)),
                                    inc="mm" if c == NCH - 1 else None)
                            pe_umark[oc] = mk_u
                            if SUB.startswith("z") or SUB in ("a1s", "a1su"):
                                if SUB == "z0":
                                    ag = em("scalar", lambda e, oc=oc, pg=pg: e.activation(
                                        out=fv[:, oc, :], in_=pg[:, 0:FT], func=AF.Silu),
                                        waits=[("mm", mk_g)], inc="aa")
                                elif SUB == "z2":
                                    ag = em("scalar", lambda e, oc=oc, pg=pg: e.activation(
                                        out=usb[:, oc % 2, :], in_=pg[:, 0:FT], func=AF.Silu,
                                        scale=sgc[:, oc:oc + 1]), waits=[("mm", mk_g)], inc="aa")
                                elif SUB == "z3":
                                    ag = em("scalar", lambda e, oc=oc, pg=pg: e.activation(
                                        out=fv[:, oc, :], in_=pg[:, 0:FT], func=AF.Copy,
                                        scale=sgc[:, oc:oc + 1]), waits=[("mm", mk_g)], inc="aa")
                                else:
                                    ag = em("scalar", lambda e, oc=oc, pg=pg: e.activation(
                                        out=fv[:, oc, :], in_=pg[:, 0:FT], func=AF.Silu,
                                        scale=sgc[:, oc:oc + 1]), waits=[("mm", mk_g)], inc="aa")
                                if SUB == "a1su":
                                    au = em("scalar", lambda e, oc=oc, pu=pu: e.activation(
                                        out=usb[:, oc % 2, :], in_=pu[:, 0:FT], func=AF.Copy,
                                        scale=suc[:, oc:oc + 1]), waits=[("mm", mk_u)], inc="aa")
                                else:
                                    au = em("scalar", lambda e: None, waits=[("mm", mk_u)], inc="aa")
                                vf = em("vector", lambda e: None, waits=[("aa", au)], inc="vv")
                                evac[oc] = (ag, au, vf)
                                continue
                            if SUB == "a1m":
                                ag = em("scalar", lambda e: None, waits=[("mm", mk_g)], inc="aa")
                                au = em("scalar", lambda e: None, waits=[("mm", mk_u)], inc="aa")
                                vf = em("vector", lambda e: None, waits=[("aa", au)], inc="vv")
                                evac[oc] = (ag, au, vf)
                                continue
                            ag = em("scalar", lambda e, oc=oc, pg=pg: e.activation(
                                out=fv[:, oc, :], in_=pg[:, 0:FT], func=AF.Silu,
                                scale=sgc[:, oc:oc + 1]), waits=[("mm", mk_g)], inc="aa")
                            au = em("scalar", lambda e, oc=oc, pu=pu: e.activation(
                                out=usb[:, oc % 2, :], in_=pu[:, 0:FT], func=AF.Copy,
                                scale=suc[:, oc:oc + 1]), waits=[("mm", mk_u)], inc="aa")
                            vf = em("vector", lambda e, oc=oc: e.tensor_mul(
                                out=fv[:, oc, :], in0=fv[:, oc, :], in1=usb[:, oc % 2, :]),
                                waits=[("aa", au)], inc="vv")
                            evac[oc] = (ag, au, vf)
                        if blk + 2 < nblk_eff:
                            emit_wdma(blk + 2)
                    f_done = P.val("vv")
                    if SUB == "a" or SUB.startswith("a1") or SUB.startswith("z"):
                        prev_tile_vv = P.val("vv")
                        prev_tile_aa = P.val("aa")
                        ffn_tile_done = (P.val("vv"), P.val("mm"))
                        continue
                    slab_mark = {}
                    yev = {}
                    for oq in range(4):
                        base = 0 if oq % 2 == 0 else 4
                        accs = [pb[base + j] for j in range(4)]
                        for s in range(16):
                            par = s % 2
                            if oq == 0 and s < 2:
                                wwaits = [("vv", f_done)]
                            else:
                                prev_use = (oq, s - 2) if s >= 2 else (oq - 1, 14 + s)
                                wwaits = [("mm", slab_mark[prev_use])]
                            em("gpsimd", lambda e, oq=oq, s=s, par=par: e.dma_start(
                                out=Wv(4096 * par, (4, 512)), in_=wd_r[:, oq, s]),
                                waits=wwaits, inc="dG", amt=16)
                            dmk = P.val("dG")
                            pw = [("dG", dmk)]
                            if s == 0:
                                if oq >= 2:
                                    pw.append(("vv", yev[oq - 2]))
                                else:
                                    pw.append(("vv", f_done))
                            for c in range(4):
                                for j in range(4):
                                    slab_mark[(oq, s)] = em("tensor", lambda e, c=c, j=j, s=s, par=par, accs=accs: e.matmul(
                                        accs[j][:, 0:FT],
                                        Wv(4096 * par, (4, 4, 128))[:, c, j],
                                        fv[:, s * 4 + c, :],
                                        start=(s == 0 and c == 0), stop=(s == 15 and c == 3)),
                                        waits=pw if (c == 0 and j == 0) else [],
                                        inc="mm" if (c == 3 and j == 3) else None)
                        mk = slab_mark[(oq, 15)]
                        for j in range(4):
                            oc = oq * 4 + j
                            vw = em("vector", lambda e, oc=oc, accs=accs, j=j, ts=ts: e.scalar_tensor_tensor(
                                out=R[:, oc, ts], in0=accs[j][:, 0:FT],
                                scalar=sdh[:, oc:oc + 1], in1=R[:, oc, ts],
                                op0=ALU.mult, op1=ALU.add),
                                waits=[("mm", mk)] if j == 0 else [], inc="vv")
                        yev[oq] = vw
                    prev_tile_vv = P.val("vv")
                    prev_tile_aa = P.val("aa")
                    ffn_tile_done = (P.val("vv"), P.val("mm"))
                return ffn_tile_done

            if stage >= 1:
                ln1 = {0: ln_full(0, pre_mm=[("dS", ld_done)], pre_vv=[("vv", init_done)]),
                       1: None}
                if SUB.startswith("z"):
                    ln1[1] = ln1[0]
                else:
                    ln1[1] = ln_full(1)
                if sub == "ln":
                    final_marks = {"vv": ln1[1][0], "aa": P.val("aa")}
                else:
                    f1_vv, f1_mm = ffn_pass(ln1)
                    final_marks = {"vv": f1_vv, "aa": P.val("aa")}
            else:
                final_marks = {"vv": init_done, "aa": P.val("aa")}
                if stage >= -1:
                    pass

            if stage >= 2:
                ln2 = {0: ln_full(0, pre_mm=[("vv", f1_vv)]), 1: None}
                ln2[1] = ln_full(1)
                h_vv, h_pp = ln2[1]
                tap("h", Hb[:, :, :].rearrange("p c t -> p (c t)"))

                em("gpsimd", lambda e: e.dma_start(out=Wv(0, (16, 1024)), in_=wqc_r),
                   waits=[("mm", P.val("mm"))], inc="dG", amt=16)
                wqc_ld = P.val("dG")
                qc_ev = {}
                for co in range(8):
                    for tt in range(2):
                        it2 = co * 2 + tt
                        tsl = slice(tt * 512, tt * 512 + 512)
                        pband = pb[it2 % 2]
                        pw = [("dG", wqc_ld), ("vv", h_vv), ("pp", h_pp)]
                        if it2 >= 2:
                            pw.append(("aa", qc_ev[it2 - 2]))
                        for c in range(NCH):
                            mk = em("tensor", lambda e, c=c, co=co, tsl=tsl, pband=pband: e.matmul(
                                pband[:, 0:512], Wv(0, (16, 8, 128))[:, c, co],
                                Hb[:, c, tsl], start=(c == 0), stop=(c == NCH - 1)),
                                waits=pw if c == 0 else [],
                                inc="mm" if c == NCH - 1 else None)
                        qc_ev[it2] = em("scalar", lambda e, co=co, tsl=tsl, pband=pband: e.activation(
                            out=qcv[:, co, tsl], in_=pband[:, 0:512], func=AF.Copy),
                            waits=[("mm", mk)], inc="aa")
                qc_done = P.val("aa")

                em("gpsimd", lambda e: e.dma_start(out=Wv(0, (8, 2048)), in_=wqup_r),
                   waits=[("aa", qc_done)], inc="dG", amt=16)
                wqup_ld = P.val("dG")
                q_ev = {}
                for hh in range(16):
                    for tt in range(2):
                        it2 = hh * 2 + tt
                        tsl = slice(tt * 512, tt * 512 + 512)
                        pband = pb[2 + it2 % 2]
                        pw = [("dG", wqup_ld)]
                        if it2 >= 2:
                            pw.append(("aa", q_ev[it2 - 2]))
                        for c in range(8):
                            mk = em("tensor", lambda e, c=c, hh=hh, tsl=tsl, pband=pband: e.matmul(
                                pband[:, 0:512], Wv(0, (8, 16, 128))[:, c, hh],
                                qcv[:, c, tsl], start=(c == 0), stop=(c == 7)),
                                waits=pw if c == 0 else [],
                                inc="mm" if c == 7 else None)
                        q_ev[it2] = em("scalar", lambda e, hh=hh, tsl=tsl, pband=pband: e.activation(
                            out=qv[:, hh, tsl], in_=pband[:, 0:512], func=AF.Copy),
                            waits=[("mm", mk)], inc="aa")
                q_done = P.val("aa")

                hs_ld = em("sync", lambda e: e.dma_start(out=hsel[:, :, :], in_=hsel_r),
                           waits=[("aa", q_done)], inc="dS", amt=16)
                for hh in range(16):
                    aw = em("scalar", lambda e, hh=hh: e.activation(
                        out=sq[:, 0:512], in_=qv[:, hh, 0:512], func=AF.Square),
                        waits=[("aa", q_done), ("dS", hs_ld)] if hh == 0 else [("mm", P.val("mm"))], inc="aa")
                    em("tensor", lambda e, hh=hh: e.matmul(
                        pb[4][0:16, 0:512], hsel[:, hh, :], sq[:, 0:512],
                        start=(hh == 0), stop=(hh == 15)), waits=[("aa", aw)], inc="mm")
                    aw = em("scalar", lambda e, hh=hh: e.activation(
                        out=sq[:, 0:512], in_=qv[:, hh, 512:1024], func=AF.Square),
                        waits=[("mm", P.val("mm"))], inc="aa")
                    em("tensor", lambda e, hh=hh: e.matmul(
                        pb[5][0:16, 0:512], hsel[:, hh, :], sq[:, 0:512],
                        start=(hh == 0), stop=(hh == 15)), waits=[("aa", aw)], inc="mm")
                mkq = P.val("mm")
                aw = em("scalar", lambda e: e.activation(
                    out=stats32[:, 0:512], in_=pb[4][0:16, 0:512], func=AF.Ln,
                    bias=epsrms[0:16, :], scale=1.0 / 128.0), waits=[("mm", mkq)], inc="aa")
                aw = em("scalar", lambda e: e.activation(
                    out=stats32[:, 512:1024], in_=pb[5][0:16, 0:512], func=AF.Ln,
                    bias=epsrms[0:16, :], scale=1.0 / 128.0), inc="aa")
                rstd_done = em("scalar", lambda e: e.activation(
                    out=rstd_hs[:, :], in_=stats32[:, :], func=AF.Exp,
                    bias=lnC[0:16, :], scale=-0.5), waits=[("aa", aw)], inc="aa")

                em("sync", lambda e: e.dma_start(out=ropeC, in_=ropeC_r),
                   waits=[("aa", q_done)], inc="dS", amt=16)
                rp_ld = em("sync", lambda e: e.dma_start(out=ropeS, in_=ropeS_r),
                           inc="dS", amt=16)
                qs_ld = em("sync", lambda e: e.dma_start(out=qsel[:, :, :], in_=qsel_r),
                           inc="dS", amt=16)
                qn_prev = None
                qn_prev2 = {}
                for it2 in range(32):
                    hh, half = it2 // 2, it2 % 2
                    hsl = slice(half * 512, half * 512 + 512)
                    pA_ = pb[4 + it2 % 2]
                    pB_ = pb[6 + it2 % 2]
                    if it2 == 0:
                        pw = [("dS", qs_ld), ("aa", rstd_done), ("dS", rp_ld)]
                    elif it2 == 1:
                        pw = []
                    else:
                        pw = [("vv", qn_prev2[it2 - 2])]
                    mka = em("tensor", lambda e, hh=hh, hsl=hsl, pA_=pA_: e.matmul(
                        pA_[:, 0:512], swp[:], qv[:, hh, hsl],
                        start=True, stop=True), waits=pw, inc="mm")
                    mkb = em("tensor", lambda e, hh=hh, hsl=hsl, pB_=pB_: e.matmul(
                        pB_[:, 0:512], qsel[:, hh, :], rstd_hs[:, hsl],
                        start=True, stop=True), inc="mm")
                    ts_ = tA[:, 0:512] if it2 % 2 == 0 else tA[:, 512:1024]
                    vw = em("vector", lambda e, hh=hh, hsl=hsl, ts_=ts_: e.tensor_mul(
                        out=ts_, in0=qv[:, hh, hsl], in1=ropeC[:, hsl]),
                        waits=[("mm", mkb)], inc="vv")
                    vw = em("vector", lambda e, hh=hh, hsl=hsl, pA_=pA_: e.tensor_tensor(
                        out=qv[:, hh, hsl], in0=pA_[:, 0:512], in1=ropeS[:, hsl],
                        op=ALU.mult), inc="vv")
                    vw = em("vector", lambda e, hh=hh, hsl=hsl, ts_=ts_: e.tensor_add(
                        out=qv[:, hh, hsl], in0=qv[:, hh, hsl], in1=ts_), inc="vv")
                    vw = em("vector", lambda e, hh=hh, hsl=hsl, pB_=pB_: e.scalar_tensor_tensor(
                        out=qv[:, hh, hsl], in0=qv[:, hh, hsl], scalar=qnw_col[:],
                        in1=pB_[:, 0:512], op0=ALU.mult, op1=ALU.mult), inc="vv")
                    qn_prev2[it2] = vw
                    qn_prev = vw
                qn_done = qn_prev
                tap("q", qv[:, :, :].rearrange("p c t -> p (c t)"))
                em("sync", lambda e: e.dma_start(out=maskjm[:, :, :, :], in_=maskjm_r),
                   waits=[("vv", qn_done)], inc="dS", amt=16)
                em("sync", lambda e: e.dma_start(out=valid01[:, :, :], in_=valid01_r),
                   inc="dS", amt=16)
                valid_ld = em("sync", lambda e: e.dma_start(out=validadd[:, :, :], in_=validadd_r),
                              inc="dS", amt=16)

                offs = {}
                off = 0
                first = True
                for nm, src, csz in (("wkvb", wkvb_r, 128), ("wzb", wzb_r, 128),
                                     ("wk", wk_r, 128), ("wv", wv_r, 128),
                                     ("wikv", wikv_r, 64), ("wiz", wiz_r, 64),
                                     ("wdq", wdq_r, 64), ("ww", ww_r, 4)):
                    offs[nm] = off
                    em("gpsimd", lambda e, src=src, off=off, csz=csz: e.dma_start(
                        out=Wv(off, (16, csz)), in_=src),
                        waits=[("vv", qn_done)] if first else [], inc="dG", amt=16)
                    first = False
                    off += 16 * csz
                em("gpsimd", lambda e: e.dma_start(out=wiuq_sb, in_=wiuq_r), inc="dG", amt=16)
                smalls_ld = P.val("dG")

                vev = None
                kev = None
                for i in range(8):
                    tsl = slice(i * 128, i * 128 + 128)
                    for c in range(NCH):
                        pw = []
                        if c == 0:
                            pw = [("dG", smalls_ld)]
                            if vev is not None:
                                pw.append(("aa", vev))
                        mkv = em("tensor", lambda e, c=c, tsl=tsl: e.matmul(
                            pb[0][:, 0:128], Hb[:, c, tsl], Wv(offs["wv"], (16, 128))[:, c, :],
                            start=(c == 0), stop=(c == NCH - 1)), waits=pw,
                            inc="mm" if c == NCH - 1 else None)
                    vev = em("scalar", lambda e, i=i: e.activation(
                        out=vwin[:, 4 + i, :], in_=pb[0][:, 0:128], func=AF.Copy),
                        waits=[("mm", mkv)], inc="aa")
                    for c in range(NCH):
                        pw = [("vv", kev)] if (c == 0 and kev is not None) else []
                        mkk = em("tensor", lambda e, c=c, tsl=tsl: e.matmul(
                            pb[1][:, 0:128], Hb[:, c, tsl], Wv(offs["wk"], (16, 128))[:, c, :],
                            start=(c == 0), stop=(c == NCH - 1)), waits=pw,
                            inc="mm" if c == NCH - 1 else None)
                    aw = em("scalar", lambda e: e.activation(
                        out=sq[:, 0:128], in_=pb[1][:, 0:128], func=AF.Square),
                        waits=[("mm", mkk)], inc="aa")
                    vw = em("vector", lambda e: e.reduce_sum(
                        out=rms_c[:], in_=sq[:, 0:128], axis=AX.X), waits=[("aa", aw)], inc="vv")
                    aw = em("scalar", lambda e: e.activation(
                        out=rms_c[:], in_=rms_c[:], func=AF.Ln, bias=epsrms[:], scale=1.0 / 128.0),
                        waits=[("vv", vw)], inc="aa")
                    aw = em("scalar", lambda e: e.activation(
                        out=rms_c[:], in_=rms_c[:], func=AF.Exp, scale=-0.5),
                        waits=[("aa", aw)], inc="aa")
                    vw = em("vector", lambda e: e.scalar_tensor_tensor(
                        out=cbf[:, 0:128], in0=pb[1][:, 0:128], scalar=rms_c[:],
                        in1=knw_bc[:], op0=ALU.mult, op1=ALU.mult),
                        waits=[("aa", aw)], inc="vv")
                    ci_ = coskt[:, i, :]
                    si_ = sinkt[:, i, :]
                    em("vector", lambda e, ci_=ci_: e.tensor_mul(
                        out=sel32[:, 0:32], in0=cbf[:, 0:32], in1=ci_), inc="vv")
                    em("vector", lambda e, si_=si_: e.tensor_mul(
                        out=sel32[:, 32:64], in0=cbf[:, 32:64], in1=si_), inc="vv")
                    em("vector", lambda e, i=i: e.tensor_sub(
                        out=kwin_tm[:, 4 + i, 0:32], in0=sel32[:, 0:32], in1=sel32[:, 32:64]), inc="vv")
                    em("vector", lambda e, si_=si_: e.tensor_mul(
                        out=sel32[:, 0:32], in0=cbf[:, 0:32], in1=si_), inc="vv")
                    em("vector", lambda e, ci_=ci_: e.tensor_mul(
                        out=sel32[:, 32:64], in0=cbf[:, 32:64], in1=ci_), inc="vv")
                    em("vector", lambda e, i=i: e.tensor_add(
                        out=kwin_tm[:, 4 + i, 32:64], in0=sel32[:, 0:32], in1=sel32[:, 32:64]), inc="vv")
                    kev = em("vector", lambda e, i=i: e.tensor_copy(
                        out=kwin_tm[:, 4 + i, 64:128], in_=cbf[:, 64:128]), inc="vv")
                kv_done_vv = P.val("vv")
                kv_done_aa = P.val("aa")

                def compress(wc, wz, odim, biasT_ap, out_ap, do_rms, pre_vv):
                    last = pre_vv
                    for tt in range(2):
                        tsl = slice(tt * 512, tt * 512 + 512)
                        for c in range(NCH):
                            pw = [("vv", last)] if c == 0 else []
                            mk = em("tensor", lambda e, c=c, tsl=tsl, wc=wc, odim=odim: e.matmul(
                                pb[2][0:odim, 0:512], Wv(wc, (16, odim))[:, c, :],
                                Hb[:, c, tsl], start=(c == 0), stop=(c == NCH - 1)), waits=pw,
                                inc="mm" if c == NCH - 1 else None)
                        aw = em("scalar", lambda e, tt=tt, odim=odim: e.activation(
                            out=cbf[0:odim, tt * 512:tt * 512 + 512], in_=pb[2][0:odim, 0:512],
                            func=AF.Copy), waits=[("mm", mk)], inc="aa")
                        for c in range(NCH):
                            mk = em("tensor", lambda e, c=c, tsl=tsl, wz=wz, odim=odim: e.matmul(
                                pb[3][0:odim, 0:512], Wv(wz, (16, odim))[:, c, :],
                                Hb[:, c, tsl], start=(c == 0), stop=(c == NCH - 1)),
                                waits=[("aa", aw)] if c == 0 else [],
                                inc="mm" if c == NCH - 1 else None)
                        last = em("vector", lambda e, tt=tt, odim=odim: e.tensor_copy(
                            out=zb32[0:odim, tt * 512:tt * 512 + 512], in_=pb[3][0:odim, 0:512]),
                            waits=[("mm", mk)], inc="vv")
                    z3 = lambda ap, od: ap[0:od, :].rearrange("p (n m) -> p n m", m=32)
                    vw = em("vector", lambda e, odim=odim, biasT_ap=biasT_ap: e.tensor_tensor(
                        out=z3(zb32, odim), in0=z3(zb32, odim),
                        in1=biasT_ap.rearrange("p (o m) -> p o m", o=1).broadcast_to([odim, 32, 32]),
                        op=ALU.add), inc="vv")
                    vw = em("vector", lambda e, odim=odim: e.tensor_reduce(
                        out=gmx[0:odim, :], in_=z3(zb32, odim), axis=AX.X, op=ALU.max,
                        negate=True), inc="vv")
                    vw = em("vector", lambda e, odim=odim: e.tensor_tensor(
                        out=z3(zb32, odim), in0=z3(zb32, odim),
                        in1=gmx[0:odim, :].rearrange("p (n o) -> p n o", o=1).broadcast_to([odim, 32, 32]),
                        op=ALU.add), waits=[("vv", vw)], inc="vv")
                    aw = em("scalar", lambda e, odim=odim: e.activation(
                        out=zb32[0:odim, :], in_=zb32[0:odim, :], func=AF.Exp),
                        waits=[("vv", vw)], inc="aa")
                    vw = em("vector", lambda e, odim=odim: e.tensor_reduce(
                        out=gsm[0:odim, :], in_=z3(zb32, odim), axis=AX.X, op=ALU.add),
                        waits=[("aa", aw)], inc="vv")
                    vw = em("vector", lambda e, odim=odim: e.tensor_mul(
                        out=zb32[0:odim, :], in0=zb32[0:odim, :], in1=cbf[0:odim, :]), inc="vv")
                    vw = em("vector", lambda e, odim=odim: e.tensor_reduce(
                        out=cmp32[0:odim, :], in_=z3(zb32, odim), axis=AX.X, op=ALU.add), inc="vv")
                    vw = em("vector", lambda e, odim=odim: e.reciprocal(
                        out=grec[0:odim, :], in_=gsm[0:odim, :]), waits=[("vv", vw)], inc="vv")
                    vw = em("vector", lambda e, odim=odim: e.tensor_mul(
                        out=cmp32[0:odim, :], in0=cmp32[0:odim, :], in1=grec[0:odim, :]),
                        waits=[("vv", vw)], inc="vv")
                    if do_rms:
                        aw = em("scalar", lambda e: e.activation(
                            out=sq[:, 0:32], in_=cmp32[:, :], func=AF.Square),
                            waits=[("vv", vw)], inc="aa")
                        mk = em("tensor", lambda e: e.matmul(
                            pb[2][0:1, 0:32], onescb[:], sq[:, 0:32], start=True, stop=True),
                            waits=[("aa", aw)], inc="mm")
                        aw = em("scalar", lambda e: e.activation(
                            out=rstd_r32[:], in_=pb[2][0:1, 0:32], func=AF.Ln,
                            bias=epsrms[0:1, :], scale=1.0 / 128.0),
                            waits=[("mm", mk)], inc="aa")
                        aw = em("scalar", lambda e: e.activation(
                            out=rstd_r32[:], in_=rstd_r32[:], func=AF.Exp, scale=-0.5), inc="aa")
                        mk = em("tensor", lambda e: e.matmul(
                            pb[3][:, 0:32], onesrf[:], rstd_r32[:], start=True, stop=True),
                            waits=[("aa", aw)], inc="mm")
                        vw = em("vector", lambda e, out_ap=out_ap: e.scalar_tensor_tensor(
                            out=out_ap, in0=cmp32[:, :], scalar=knw_col[:],
                            in1=pb[3][:, 0:32], op0=ALU.mult, op1=ALU.mult),
                            waits=[("mm", mk)], inc="vv")
                    else:
                        vw = em("vector", lambda e, odim=odim, out_ap=out_ap: e.tensor_copy(
                            out=out_ap, in_=cmp32[0:odim, :]), waits=[("vv", vw)], inc="vv")
                    return vw

                ckv_loc = compress(offs["wkvb"], offs["wzb"], 128, biasbT[:, :],
                                   ckv_fm[:, 0:32], True, kv_done_vv)
                kci_loc = compress(offs["wikv"], offs["wiz"], 64, biasiT[:, :],
                                   kci_cm[0:64, 0:32], False, ckv_loc)

                for tt in range(2):
                    tsl = slice(tt * 512, tt * 512 + 512)
                    for c in range(NCH):
                        pw = []
                        if c == 0:
                            pw = [("vv", kci_loc)] if tt == 0 else [("aa", P.val("aa"))]
                        mk = em("tensor", lambda e, c=c, tsl=tsl: e.matmul(
                            pb[2][0:64, 0:512], Wv(offs["wdq"], (16, 64))[:, c, :],
                            Hb[:, c, tsl], start=(c == 0), stop=(c == NCH - 1)),
                            waits=pw, inc="mm" if c == NCH - 1 else None)
                    em("scalar", lambda e, tt=tt: e.activation(
                        out=dq_v[0:64, tt * 512:tt * 512 + 512], in_=pb[2][0:64, 0:512],
                        func=AF.Copy), waits=[("mm", mk)], inc="aa")
                dq_done = P.val("aa")
                for chn in range(2):
                    for tt in range(2):
                        mk = em("tensor", lambda e, chn=chn, tt=tt: e.matmul(
                            pb[3][:, 0:512], wiuq_sb[0:64, chn * 128:chn * 128 + 128],
                            dq_v[0:64, tt * 512:tt * 512 + 512], start=True, stop=True),
                            waits=[("aa", P.val("aa"))], inc="mm")
                        em("scalar", lambda e, chn=chn, tt=tt: e.activation(
                            out=qi_v[0:64, 2 * chn, tt * 512:tt * 512 + 512],
                            in_=pb[3][0:64, 0:512], func=AF.Copy),
                            waits=[("mm", mk)], inc="aa")
                        em("scalar", lambda e, chn=chn, tt=tt: e.activation(
                            out=qi_v[0:64, 2 * chn + 1, tt * 512:tt * 512 + 512],
                            in_=pb[3][64:128, 0:512], func=AF.Copy), inc="aa")
                qi_done = P.val("aa")
                wgev = None
                for i in range(8):
                    tsl = slice(i * 128, i * 128 + 128)
                    for c in range(NCH):
                        pw = []
                        if c == 0:
                            pw = [("aa", qi_done)] if i == 0 else [("vv", wgev)]
                        mk = em("tensor", lambda e, c=c, tsl=tsl: e.matmul(
                            pb[2][:, 0:4], Hb[:, c, tsl], Wv(offs["ww"], (16, 4))[:, c, :],
                            start=(c == 0), stop=(c == NCH - 1)), waits=pw,
                            inc="mm" if c == NCH - 1 else None)
                    wgev = em("vector", lambda e, i=i: e.tensor_copy(
                        out=wgt_sb[:, i, :], in_=pb[2][:, 0:4]), waits=[("mm", mk)], inc="vv")
                wgt_done = P.val("vv")

                ccK = cc_in.ap()[0:1, SEC_K:SEC_K + T * D].rearrange(
                    "o (n p d) -> (o n) p d", p=128, d=128).rearrange("n p d -> p n d")
                ccV = cc_in.ap()[0:1, SEC_V:SEC_V + T * D].rearrange(
                    "o (n p d) -> (o n) p d", p=128, d=128).rearrange("n p d -> p n d")
                ccC = cc_in.ap()[0:1, SEC_CKV:SEC_CKV + 128 * 32].rearrange(
                    "o (p n) -> (o p) n", n=32)
                ccI = cc_in.ap()[0:1, SEC_KCI:SEC_KCI + 64 * 32].rearrange(
                    "o (p n) -> (o p) n", n=32)
                em("gpsimd", lambda e, ccK=ccK: e.dma_start(out=ccK, in_=kwin_tm[:, 4:12, :]),
                   waits=[("vv", kv_done_vv)], inc="dG", amt=16)
                em("gpsimd", lambda e, ccV=ccV: e.dma_start(out=ccV, in_=vwin[:, 4:12, :]),
                   waits=[("aa", kv_done_aa)], inc="dG", amt=16)
                em("gpsimd", lambda e, ccC=ccC: e.dma_start(out=ccC, in_=ckv_fm[:, 0:32]),
                   waits=[("vv", ckv_loc)], inc="dG", amt=16)
                blob_w = em("gpsimd", lambda e, ccI=ccI: e.dma_start(out=ccI, in_=kci_cm[0:64, 0:32]),
                            waits=[("vv", kci_loc)], inc="dG", amt=16)
                ccdone = em("gpsimd", lambda e: e.collective_compute(
                    "AllGather", ALU.bypass,
                    replica_groups=[[0, 1], [2, 3], [4, 5], [6, 7]],
                    ins=[cc_in.ap().opt()], outs=[cc_out.ap().opt()]),
                    waits=[("dG", blob_w)], inc="cc")
                rbK = cc_out.ap()[0:1, SEC_K:SEC_K + T * D].rearrange(
                    "o (n p d) -> (o n) p d", p=128, d=128).rearrange("n p d -> p n d")
                rbV = cc_out.ap()[0:1, SEC_V:SEC_V + T * D].rearrange(
                    "o (n p d) -> (o n) p d", p=128, d=128).rearrange("n p d -> p n d")
                em("gpsimd", lambda e, rbK=rbK: e.dma_start(
                    out=kwin_tm[:, 0:4, :], in_=rbK[:, 4:8, :]),
                    waits=[("cc", ccdone)], inc="dG", amt=16)
                em("gpsimd", lambda e, rbV=rbV: e.dma_start(
                    out=vwin[:, 0:4, :], in_=rbV[:, 4:8, :]), inc="dG", amt=16)
                for r_ in range(2):
                    cC = cc_out.ap()[r_:r_ + 1, SEC_CKV:SEC_CKV + 128 * 32].rearrange(
                        "o (p n) -> (o p) n", n=32)
                    cI = cc_out.ap()[r_:r_ + 1, SEC_KCI:SEC_KCI + 64 * 32].rearrange(
                        "o (p n) -> (o p) n", n=32)
                    em("gpsimd", lambda e, cC=cC, r_=r_: e.dma_start(
                        out=ckv_fm[:, r_ * 32:r_ * 32 + 32], in_=cC), inc="dG", amt=16)
                    em("gpsimd", lambda e, cI=cI, r_=r_: e.dma_start(
                        out=kci_cm[0:64, r_ * 32:r_ * 32 + 32], in_=cI), inc="dG", amt=16)
                rb_done = P.val("dG")

                ktv = None
                for k in range(KW):
                    pw = [("dG", rb_done)] if k == 0 else [("aa", ktv)]
                    mk = em("tensor", lambda e, k=k: e.transpose(
                        pb[2][:, 0:128], kwin_tm[:, k, :], idf[:]), waits=pw, inc="mm")
                    ktv = em("scalar", lambda e, k=k: e.activation(
                        out=kwin_fm[:, k * 128:k * 128 + 128], in_=pb[2][:, 0:128],
                        func=AF.Copy), waits=[("mm", mk)], inc="aa")
                vk = em("vector", lambda e: e.tensor_copy(out=sel32[:, 0:64], in_=ckv_fm[:, 0:64]),
                        waits=[("dG", rb_done)], inc="vv")
                mk = em("tensor", lambda e: e.transpose(pb[3][0:64, 0:128], sel32[:, 0:64], idf[:]),
                        waits=[("vv", vk)], inc="mm")
                em("scalar", lambda e: e.activation(
                    out=ckv_nm[0:64, :], in_=pb[3][0:64, 0:128], func=AF.Copy),
                    waits=[("mm", mk)], inc="aa")
                tap("kwin", kwin_tm[:, :, :].rearrange("p n d -> p (n d)"))
                tap("vwin", vwin[:, :, :].rearrange("p n d -> p (n d)"))
                tap("ckv", ckv_fm[:, :])
                tap("kci", kci_cm[:, :])

                selv = None
                for qt in range(NQT):
                    tsl = slice(qt * 128, qt * 128 + 128)
                    for ih in range(4):
                        pd = pb[4 + ih % 2]
                        lhs = qi_v[0:64, ih, tsl]
                        pw = [("dG", rb_done), ("vv", P.val("vv"))]
                        mk = em("tensor", lambda e, lhs=lhs, pd=pd: e.matmul(
                            pd[:, 0:64], lhs, kci_cm[0:64, 0:64], start=True, stop=True),
                            waits=pw, inc="mm")
                        vw = em("vector", lambda e, pd=pd: e.tensor_scalar_max(
                            out=m01[:], in0=pd[:, 0:64], scalar1=0.0),
                            waits=[("mm", mk)], inc="vv")
                        if ih == 0:
                            vw = em("vector", lambda e, qt=qt, ih=ih: e.tensor_scalar(
                                out=isc[:], in0=m01[:], scalar1=wgt_sb[:, qt, ih:ih + 1],
                                scalar2=None, op0=ALU.mult),
                                waits=[("vv", wgt_done)], inc="vv")
                        else:
                            vw = em("vector", lambda e, qt=qt, ih=ih: e.scalar_tensor_tensor(
                                out=isc[:], in0=m01[:], scalar=wgt_sb[:, qt, ih:ih + 1],
                                in1=isc[:], op0=ALU.mult, op1=ALU.add), inc="vv")
                    if "isc" in dbg and qt == 0:
                        tap("isc", isc[:, :])
                    vw = em("vector", lambda e, qt=qt: e.tensor_add(
                        out=work[:], in0=isc[:], in1=validadd[:, qt, :]),
                        waits=[("dS", valid_ld)] if qt == 0 else [], inc="vv")
                    vw = em("vector", lambda e: e.tensor_add(
                        out=work[:], in0=work[:], in1=tbrow[:]), inc="vv")
                    for r_ in range(4):
                        vw = em("vector", lambda e: e.max(out=mx8[:], in_=work[:]), inc="vv")
                        if r_ < 3:
                            vw = em("vector", lambda e: e.tensor_single_scalar(
                                out=m01[:], in_=work[:], scalar=mx8[:, 7:8], op=ALU.is_ge),
                                waits=[("vv", vw)], inc="vv")
                            vw = em("vector", lambda e: e.scalar_tensor_tensor(
                                out=work[:], in0=m01[:], scalar=-1e38, in1=work[:],
                                op0=ALU.mult, op1=ALU.add), inc="vv")
                    vw = em("vector", lambda e, qt=qt: e.tensor_add(
                        out=work[:], in0=isc[:], in1=validadd[:, qt, :]),
                        waits=[("vv", vw)], inc="vv")
                    vw = em("vector", lambda e: e.tensor_add(
                        out=work[:], in0=work[:], in1=tbrow[:]), inc="vv")
                    vw = em("vector", lambda e: e.tensor_single_scalar(
                        out=m01[:], in_=work[:], scalar=mx8[:, 7:8], op=ALU.is_ge), inc="vv")
                    vw = em("vector", lambda e, qt=qt: e.tensor_mul(
                        out=sel32[:, 0:64], in0=m01[:], in1=valid01[:, qt, :]), inc="vv")
                    mk = em("tensor", lambda e: e.transpose(pb[6][0:64, 0:128], sel32[:, 0:64], idf[:]),
                            waits=[("vv", vw)], inc="mm")
                    selv = em("scalar", lambda e, qt=qt: e.activation(
                        out=selT[:, qt * 128:qt * 128 + 128], in_=pb[6][0:64, 0:128],
                        func=AF.Copy), waits=[("mm", mk)], inc="aa")
                sel_done = P.val("aa")
                tap("selT", selT[:, :])
                final_marks = {"vv": P.val("vv"), "aa": P.val("aa")}

            if stage >= 3:
                hs_ld2 = em("sync", lambda e: e.dma_start(out=hsel2[:, :, :], in_=hsel_r),
                            waits=[("vv", qn_done)], inc="dS", amt=16)
                for hh in range(16):
                    aw = em("scalar", lambda e, hh=hh: e.activation(
                        out=sq[:, 0:512], in_=qv[:, hh, 0:512], func=AF.Square),
                        waits=[("vv", qn_done), ("dS", hs_ld2)] if hh == 0 else
                        [("mm", P.val("mm"))], inc="aa")
                    em("tensor", lambda e, hh=hh: e.matmul(
                        pb[6][0:16, 0:512], hsel2[:, hh, :], sq[:, 0:512],
                        start=(hh == 0), stop=(hh == 15)), waits=[("aa", aw)], inc="mm")
                    aw = em("scalar", lambda e, hh=hh: e.activation(
                        out=sq[:, 0:512], in_=qv[:, hh, 512:1024], func=AF.Square),
                        waits=[("mm", P.val("mm"))], inc="aa")
                    em("tensor", lambda e, hh=hh: e.matmul(
                        pb[7][0:16, 0:512], hsel2[:, hh, :], sq[:, 0:512],
                        start=(hh == 0), stop=(hh == 15)), waits=[("aa", aw)], inc="mm")
                mkq2 = P.val("mm")
                v1 = em("vector", lambda e: e.reduce_max(
                    out=red1[0:16, 0:1], in_=pb[6][0:16, 0:512], axis=AX.X),
                    waits=[("mm", mkq2)], inc="vv")
                v1 = em("vector", lambda e: e.reduce_max(
                    out=red2[0:16, 0:1], in_=pb[7][0:16, 0:512], axis=AX.X), inc="vv")
                v1 = em("vector", lambda e: e.tensor_max(
                    out=red1[0:16, 0:1], in0=red1[0:16, 0:1], in1=red2[0:16, 0:1]),
                    waits=[("vv", v1)], inc="vv")
                kv1 = None
                for k in range(KW):
                    aw = em("scalar", lambda e, k=k: e.activation(
                        out=sq[:, 0:128], in_=kwin_tm[:, k, :], func=AF.Square),
                        waits=[("vv", v1)] if k == 0 else [("vv", kv1)], inc="aa")
                    kv1 = em("vector", lambda e, k=k: e.reduce_sum(
                        out=red2[:, 0:1], in_=sq[:, 0:128], axis=AX.X),
                        waits=[("aa", aw)], inc="vv")
                    if k == 0:
                        kv1 = em("vector", lambda e: e.tensor_copy(
                            out=red2[:, 1:2], in_=red2[:, 0:1]), waits=[("vv", kv1)], inc="vv")
                    else:
                        kv1 = em("vector", lambda e: e.tensor_max(
                            out=red2[:, 1:2], in0=red2[:, 1:2], in1=red2[:, 0:1]),
                            waits=[("vv", kv1)], inc="vv")
                aw = em("scalar", lambda e: e.activation(
                    out=sq[0:64, 0:128], in_=ckv_nm[0:64, :], func=AF.Square),
                    waits=[("vv", kv1)], inc="aa")
                cv1 = em("vector", lambda e: e.reduce_sum(
                    out=red1[0:64, 1:2], in_=sq[0:64, 0:128], axis=AX.X),
                    waits=[("aa", aw)], inc="vv")
                mk = em("tensor", lambda e: e.transpose(
                    pb[6][0:1, 0:16], red1[0:16, 0:1], idf[0:16, 0:16]),
                    waits=[("vv", cv1)], inc="mm")
                mk = em("tensor", lambda e: e.transpose(
                    pb[6][0:1, 16:144], red2[:, 1:2], idf[:]), inc="mm")
                mk = em("tensor", lambda e: e.transpose(
                    pb[6][0:1, 144:208], red1[0:64, 1:2], idf[0:64, 0:64]), inc="mm")
                v1 = em("vector", lambda e: e.tensor_copy(
                    out=srow[0:1, 0:16], in_=pb[6][0:1, 0:16]),
                    waits=[("mm", mk)], inc="vv")
                v1 = em("vector", lambda e: e.reduce_max(
                    out=srow[0:1, 16:17], in_=pb[6][0:1, 16:144], axis=AX.X), inc="vv")
                v1 = em("vector", lambda e: e.reduce_max(
                    out=srow[0:1, 17:18], in_=pb[6][0:1, 144:208], axis=AX.X),
                    waits=[("vv", v1)], inc="vv")
                v1 = em("vector", lambda e: e.tensor_single_scalar(
                    out=srow[0:1, 32:48], in_=srow[0:1, 0:16], scalar=srow[0:1, 16:17],
                    op=ALU.mult), waits=[("vv", v1)], inc="vv")
                v1 = em("vector", lambda e: e.tensor_single_scalar(
                    out=srow[0:1, 48:64], in_=srow[0:1, 0:16], scalar=srow[0:1, 17:18],
                    op=ALU.mult), waits=[("vv", v1)], inc="vv")
                aw = em("scalar", lambda e: e.activation(
                    out=srow[0:1, 32:64], in_=srow[0:1, 32:64], func=AF.Ln,
                    bias=epsln[:]), waits=[("vv", v1)], inc="aa")
                aw = em("scalar", lambda e: e.activation(
                    out=srow[0:1, 32:64], in_=srow[0:1, 32:64], func=AF.Exp, scale=0.5),
                    waits=[("aa", aw)], inc="aa")
                v1 = em("vector", lambda e: e.tensor_max(
                    out=srow[0:1, 32:48], in0=srow[0:1, 32:48], in1=sink_row[0:1, :]),
                    waits=[("aa", aw)], inc="vv")
                v1 = em("vector", lambda e: e.tensor_scalar(
                    out=srow[0:1, 32:64], in0=srow[0:1, 32:64], scalar1=-1.0,
                    scalar2=None, op0=ALU.mult), waits=[("vv", v1)], inc="vv")
                for hh in range(16):
                    em("tensor", lambda e, hh=hh: e.matmul(
                        pb[7][:, hh:hh + 1], onesrf[:], srow[0:1, 32 + hh:33 + hh],
                        start=True, stop=True),
                        waits=[("vv", v1)] if hh == 0 else [])
                    em("tensor", lambda e, hh=hh: e.matmul(
                        pb[7][:, 16 + hh:17 + hh], onesrf[:], srow[0:1, 48 + hh:49 + hh],
                        start=True, stop=True), inc="mm" if hh == 15 else None)
                mkb = P.val("mm")
                v1 = em("vector", lambda e: e.tensor_copy(out=nSW[:, :], in_=pb[7][:, 0:16]),
                        waits=[("mm", mkb)], inc="vv")
                v1 = em("vector", lambda e: e.tensor_copy(out=nSP[:, :], in_=pb[7][:, 16:32]),
                        inc="vv")
                v1 = em("vector", lambda e: e.tensor_add(
                    out=sinkexp[:, :], in0=nSW[:, :], in1=sink_cols[:, :]),
                    waits=[("vv", v1)], inc="vv")
                preM_aa = em("scalar", lambda e: e.activation(
                    out=sinkexp[:, :], in_=sinkexp[:, :], func=AF.Exp),
                    waits=[("vv", v1)], inc="aa")

                evac_marks = {}
                recip_marks = {}
                for it in range(H * NQT):
                    qt, hh = it // H, it % H
                    par = it % 2
                    X = pb[0 + 2 * par]
                    Y = pb[1 + 2 * par]
                    Z = pb[4 + par]
                    tsl = slice(qt * 128, qt * 128 + 128)
                    qap = qv[:, hh, tsl]
                    if it >= 2:
                        pw0 = [("aa", exp_marks[it - 2]), ("vv", n_marks[it - 2])]
                    else:
                        pw0 = [("aa", max(sel_done, preM_aa))]
                    for krel in range(5):
                        dst = X[:, krel * 128:krel * 128 + 128] if krel < 4 else Y[:, 0:128]
                        em("tensor", lambda e, krel=krel, qt=qt, dst=dst, qap=qap: e.matmul(
                            dst, kwin_fm[:, (qt + krel) * 128:(qt + krel) * 128 + 128], qap,
                            start=True, stop=True), waits=pw0 if krel == 0 else [])
                    mk_sc = em("tensor", lambda e, qap=qap, Y=Y: e.matmul(
                        Y[0:64, 128:256], ckv_fm[:, 0:64], qap, start=True, stop=True),
                        inc="mm")
                    a1 = em("scalar", lambda e, X=X, par=par, hh=hh: e.activation(
                        out=esw[:, par, 0:4, :], in_=X[:, :].rearrange("p (a b) -> p a b", b=128),
                        func=AF.Exp, bias=nSW[:, hh:hh + 1]),
                        waits=[("mm", mk_sc)], inc="aa")
                    a1 = em("scalar", lambda e, Y=Y, par=par, hh=hh: e.activation(
                        out=esw[:, par, 4, :], in_=Y[:, 0:128], func=AF.Exp,
                        bias=nSW[:, hh:hh + 1]), inc="aa")
                    a1 = em("scalar", lambda e, Y=Y, par=par, hh=hh: e.activation(
                        out=esp[:, par, :], in_=Y[0:64, 128:256], func=AF.Exp,
                        bias=nSP[0:64, hh:hh + 1]), inc="aa")
                    if it == 0:
                        exp_marks = {}
                        n_marks = {}
                        o_marks = {}
                    exp_marks[it] = a1
                    v1 = em("vector", lambda e, par=par, qt=qt: e.tensor_mul(
                        out=esw[:, par, :, :], in0=esw[:, par, :, :], in1=maskjm[:, qt, :, :]),
                        waits=[("aa", a1)], inc="vv")
                    mask_done = em("vector", lambda e, par=par, qt=qt: e.tensor_mul(
                        out=esp[:, par, :], in0=esp[:, par, :],
                        in1=selT[:, qt * 128:qt * 128 + 128]), inc="vv")
                    for krel in range(5):
                        em("tensor", lambda e, krel=krel, par=par, qt=qt, Z=Z: e.matmul(
                            Z[:, 128:256], esw[:, par, krel, :], vwin[:, qt + krel, :],
                            start=(krel == 0), stop=(krel == 4)),
                            waits=[("vv", mask_done)] if krel == 0 else [])
                        em("tensor", lambda e, krel=krel, par=par, Z=Z: e.matmul(
                            Z[:, 256:257], esw[:, par, krel, :], onescb[:],
                            start=(krel == 0), stop=(krel == 4)))
                    em("tensor", lambda e, par=par, Z=Z: e.matmul(
                        Z[:, 0:128], esp[:, par, :], ckv_nm[0:64, :],
                        start=True, stop=True))
                    mk_av = em("tensor", lambda e, par=par, Z=Z: e.matmul(
                        Z[:, 257:258], esp[:, par, :], onescb[0:64, :],
                        start=True, stop=True), inc="mm")
                    v1 = em("vector", lambda e, Z=Z, par=par, hh=hh: e.tensor_add(
                        out=ssw_c[:, par:par + 1], in0=Z[:, 256:257],
                        in1=sinkexp[:, hh:hh + 1]), waits=[("mm", mk_av)], inc="vv")
                    v1 = em("vector", lambda e, par=par: e.reciprocal(
                        out=rsw_c[:, par:par + 1], in_=ssw_c[:, par:par + 1]),
                        waits=[("vv", v1)], inc="vv")
                    v1 = em("vector", lambda e, Z=Z, par=par: e.tensor_scalar(
                        out=ssp_c[:, par:par + 1], in0=Z[:, 257:258], scalar1=1e-30,
                        scalar2=None, op0=ALU.add), inc="vv")
                    rec = em("vector", lambda e, par=par: e.reciprocal(
                        out=rsp_c[:, par:par + 1], in_=ssp_c[:, par:par + 1]),
                        waits=[("vv", v1)], inc="vv")
                    recip_marks[it] = rec
                    a1 = em("scalar", lambda e, Z=Z, par=par: e.activation(
                        out=avs[:, par, :], in_=Z[:, 128:256], func=AF.Copy,
                        scale=rsw_c[:, par:par + 1]), waits=[("vv", rec)], inc="aa")
    
                    pwN = [("aa", a1)]
                    if it >= 2:
                        pwN.append(("mm", o_marks[it - 2]))
                    v1 = em("vector", lambda e, Z=Z, par=par: e.scalar_tensor_tensor(
                        out=attn_tm[:, par, :], in0=Z[:, 0:128],
                        scalar=rsp_c[:, par:par + 1], in1=avs[:, par, :],
                        op0=ALU.mult, op1=ALU.add), waits=pwN, inc="vv")
                    n_marks[it] = v1
                    T_ = pb[6 + par]
                    pwO = [("vv", v1)]
                    if it >= 2:
                        pwO.append(("aa", evac_marks[it - 2]))
                    mk = em("tensor", lambda e, T_=T_, par=par: e.transpose(
                        T_[:, 0:128], attn_tm[:, par, :], idf[:]),
                        waits=pwO, inc="mm")
                    o_marks[it] = mk
                    evac_marks[it] = em("scalar", lambda e, T_=T_, hh=hh, tsl=tsl: e.activation(
                        out=Hb[:, hh, tsl], in_=T_[:, 0:128], func=AF.Copy),
                        waits=[("mm", mk)], inc="aa")
                attn_done_aa = P.val("aa")
                tap("attn", Hb[:, :, :].rearrange("p c t -> p (c t)"))

                em("gpsimd", lambda e: e.dma_start(out=Wv(0, (16, 512)), in_=gw_r),
                   waits=[("mm", P.val("mm")), ("aa", attn_done_aa),
                          ("vv", P.val("vv"))], inc="dG", amt=16)
                gw_ld = P.val("dG")
                og_ev = {}
                for g_ in range(4):
                    for ocg in range(4):
                        for tt in range(2):
                            it3 = (g_ * 4 + ocg) * 2 + tt
                            tsl = slice(tt * 512, tt * 512 + 512)
                            pband = pb[it3 % 2]
                            pw = [("dG", gw_ld), ("aa", attn_done_aa)]
                            if it3 >= 2:
                                pw.append(("aa", og_ev[it3 - 2]))
                            for ic in range(4):
                                mk = em("tensor", lambda e, g_=g_, ic=ic, ocg=ocg, tsl=tsl, pband=pband: e.matmul(
                                    pband[:, 0:512],
                                    Wv(0, (16, 4, 128))[:, g_ * 4 + ic, ocg],
                                    Hb[:, g_ * 4 + ic, tsl],
                                    start=(ic == 0), stop=(ic == 3)),
                                    waits=pw if ic == 0 else [],
                                    inc="mm" if ic == 3 else None)
                            og_ev[it3] = em("scalar", lambda e, g_=g_, ocg=ocg, tsl=tsl, pband=pband: e.activation(
                                out=qv[:, g_ * 4 + ocg, tsl], in_=pband[:, 0:512],
                                func=AF.Copy), waits=[("mm", mk)], inc="aa")
                og_done = P.val("aa")
                tap("og", qv[:, :, :].rearrange("p c t -> p (c t)"))

                fin_ev = {}
                for hf in range(2):
                    em("gpsimd", lambda e, hf=hf: e.dma_start(
                        out=Wv(0, (8, 2048)), in_=fin_r[:, hf * 8:hf * 8 + 8, :]),
                        waits=[("aa", og_done), ("mm", P.val("mm"))], inc="dG", amt=16)
                    fin_ld = P.val("dG")
                    for oc in range(16):
                        for tt in range(2):
                            it4 = oc * 2 + tt
                            tsl = slice(tt * 512, tt * 512 + 512)
                            pband = pb[it4 % 4]
                            pw2 = [("dG", fin_ld)]
                            if it4 >= 4:
                                pw2.append(("vv", fin_ev[(hf, it4 - 4)]))
                            elif hf == 1:
                                pw2.append(("vv", fin_ev[(0, 28 + it4)]))
                            for kc in range(8):
                                mk = em("tensor", lambda e, kc=kc, oc=oc, tsl=tsl, pband=pband, hf=hf: e.matmul(
                                    pband[:, 0:512],
                                    Wv(0, (8, 16, 128))[:, kc, oc],
                                    qv[:, hf * 8 + kc, tsl],
                                    start=(kc == 0), stop=(kc == 7)),
                                    waits=pw2 if kc == 0 else [],
                                    inc="mm" if kc == 7 else None)
                            fin_ev[(hf, it4)] = em("vector", lambda e, oc=oc, tsl=tsl, pband=pband: e.tensor_add(
                                out=R[:, oc, tsl], in0=R[:, oc, tsl], in1=pband[:, 0:512]),
                                waits=[("mm", mk)], inc="vv")
                final_marks = {"vv": P.val("vv"), "aa": P.val("aa")}

            if stage >= 4:
                x2_vv = final_marks["vv"]
                tap("x2", R[:, :, :].rearrange("p c t -> p (c t)"))
                ln3 = {0: ln_full(0, pre_mm=[("vv", x2_vv)]), 1: None}
                ln3[1] = ln_full(1)
                tap("r0", r0row[:])
                tap("r1", varrow[:])
                f2_vv, f2_mm = ffn_pass(ln3)
                tap("h3", Hb[:, :, :].rearrange("p c t -> p (c t)"))
                final_marks = {"vv": f2_vv, "aa": P.val("aa")}

            out_vv = final_marks["vv"]
            dS_out0 = P.val("dS")
            for i in range(8):
                for c in range(NCH):
                    pw = []
                    if i == 0 and c == 0:
                        pw = [("vv", out_vv), ("mm", P.val("mm")),
                              ("dG", P.val("dG"))]
                    else:
                        pw = [("aa", P.val("aa"))]
                    if c == 0 and i >= 2:
                        pw.append(("dS", dS_out0 + 16 * (i - 1)))
                    if c == 0 and i > 0:
                        pw.append(("vv", out_vv))
                    mk = em("tensor", lambda e, i=i, c=c: e.transpose(
                        pb[7][:, 0:128], R[:, c, i * 128:i * 128 + 128], idf[:]),
                        waits=pw, inc="mm")
                    em("scalar", lambda e, i=i, c=c: e.activation(
                        out=outbuf[:, i % 2, c * 128:c * 128 + 128], in_=pb[7][:, 0:128],
                        func=AF.Copy), waits=[("mm", mk)], inc="aa")
                aw = P.val("aa")
                em("sync", lambda e, i=i: e.dma_start(
                    out=y[i * 128:i * 128 + 128, :], in_=outbuf[:, i % 2, :]),
                    waits=[("aa", aw)], inc="dS", amt=16)
            st_done = P.val("dS")
            em("sync", None, waits=[("dS", st_done)])

            with nc.Block() as block:
                @block.sync
                def _(e):
                    P.replay("sync", e, sems)

                @block.tensor
                def _(e):
                    P.replay("tensor", e, sems)

                @block.vector
                def _(e):
                    P.replay("vector", e, sems)

                @block.scalar
                def _(e):
                    P.replay("scalar", e, sems)

                @block.gpsimd
                def _(e):
                    P.replay("gpsimd", e, sems)

    return nc



def _arrange(inputs):
    g = {}
    wg = np.asarray(inputs["L0_ffn_w_gate"])
    wu = np.asarray(inputs["L0_ffn_w_up"])
    wd = np.asarray(inputs["L0_ffn_w_down"])
    g["wg_r"] = np.ascontiguousarray(wg.reshape(16, 128, 32, 256).transpose(1, 2, 0, 3))
    g["wu_r"] = np.ascontiguousarray(wu.reshape(16, 128, 32, 256).transpose(1, 2, 0, 3))
    g["wd_r"] = np.ascontiguousarray(wd.reshape(16, 4, 128, 4, 512).transpose(2, 3, 0, 1, 4))
    g["sg_r"] = np.ascontiguousarray(np.asarray(inputs["L0_ffn_s_gate"]).reshape(64, 128).T)
    g["su_r"] = np.ascontiguousarray(np.asarray(inputs["L0_ffn_s_up"]).reshape(64, 128).T)
    g["sd_r"] = np.ascontiguousarray(np.asarray(inputs["L0_ffn_s_down"]).reshape(16, 128).T)
    g["wqc_r"] = np.ascontiguousarray(
        np.asarray(inputs["L0_attn_w_qc"]).reshape(16, 128, 1024).transpose(1, 0, 2))
    g["wqup_r"] = np.ascontiguousarray(
        np.asarray(inputs["L0_attn_w_qup"]).reshape(8, 128, 2048).transpose(1, 0, 2))
    for nm, key in (("wkvb_r", "L0_attn_w_kv_b"), ("wzb_r", "L0_attn_w_z_b"),
                    ("wk_r", "L0_attn_w_k"), ("wv_r", "L0_attn_w_v"),
                    ("wikv_r", "L0_attn_wi_kv"), ("wiz_r", "L0_attn_wi_z"),
                    ("wdq_r", "L0_attn_w_dq"), ("ww_r", "L0_attn_w_w")):
        w = np.asarray(inputs[key])
        g[nm] = np.ascontiguousarray(w.reshape(16, 128, w.shape[1]).transpose(1, 0, 2))
    g["wiuq_r"] = np.ascontiguousarray(np.asarray(inputs["L0_attn_w_iuq"]))
    g["gw_r"] = np.ascontiguousarray(
        np.asarray(inputs["L0_attn_group_w"]).reshape(4, 4, 128, 512)
        .transpose(2, 0, 1, 3).reshape(128, 16, 512))
    g["fin_r"] = np.ascontiguousarray(
        np.asarray(inputs["L0_attn_final_w"]).reshape(16, 128, 2048).transpose(1, 0, 2))
    g["qnw_r"] = np.ascontiguousarray(np.asarray(inputs["L0_attn_q_norm_w"]).reshape(1, 128))
    g["knw_r"] = np.ascontiguousarray(np.asarray(inputs["L0_attn_k_norm_w"]).reshape(1, 128))
    g["sink_r"] = np.ascontiguousarray(np.asarray(inputs["L0_attn_sink_logit"]).reshape(1, 16))
    g["biasb_r"] = np.ascontiguousarray(np.asarray(inputs["L0_attn_bias_b"]))
    g["biasi_r"] = np.ascontiguousarray(np.asarray(inputs["L0_attn_bias_i"]))
    g["idf_r"] = np.eye(128, dtype=np.float32)
    sg_ = np.arange(128)
    sig = np.where(sg_ < 32, sg_ + 32, np.where(sg_ < 64, sg_ - 32, sg_))
    sw_ = np.zeros((128, 128), BF)
    sw_[sig, np.arange(128)] = 1
    g["swp_r"] = sw_
    hs = np.zeros((128, 16, 16), BF)
    for c in range(16):
        hs[:, c, c] = 1
    g["hsel_r"] = hs
    qs = np.zeros((16, 16, 128), BF)
    for h_ in range(16):
        qs[h_, h_, :] = 1
    g["qsel_r"] = qs
    return g


def _percore_consts(hc):
    t0 = hc * T
    p = np.arange(t0, t0 + T)
    c = {}
    inv = 1.0 / (10000.0 ** (np.arange(0, RD, 2, dtype=np.float64) / RD))
    ang = p[:, None] * inv
    cos = np.cos(ang).astype(np.float32)
    sin = np.sin(ang).astype(np.float32)
    Cm = np.ones((128, 1024), np.float32)
    Sm = np.zeros((128, 1024), np.float32)
    Cm[0:32] = cos.T
    Cm[32:64] = cos.T
    Sm[0:32] = -sin.T
    Sm[32:64] = sin.T
    c["ropeC_r"] = Cm.astype(BF)
    c["ropeS_r"] = Sm.astype(BF)
    c["coskt_r"] = np.ascontiguousarray(cos.reshape(8, 128, 32).transpose(1, 0, 2)).astype(BF)
    c["sinkt_r"] = np.ascontiguousarray(sin.reshape(8, 128, 32).transpose(1, 0, 2)).astype(BF)
    jglob = np.concatenate([np.arange(t0 - 512, t0), np.arange(t0, t0 + T)])
    mk = np.zeros((128, NQT, 5, 128), np.float32)
    for qt in range(NQT):
        qpos = p[qt * 128:(qt + 1) * 128]
        for krel in range(5):
            kpos = jglob[(qt + krel) * 128:(qt + krel) * 128 + 128]
            valid = (kpos[:, None] >= 0) & (kpos[:, None] <= qpos[None, :]) & \
                    (qpos[None, :] - kpos[:, None] < SWIN)
            mk[:, qt, krel, :] = valid
    c["maskjm_r"] = mk.astype(BF)
    val = ((np.arange(NB) * M + (M - 1))[None, :] < p[:, None]).astype(np.float32)
    v01 = np.ascontiguousarray(val.reshape(NQT, 128, NB).transpose(1, 0, 2))
    c["valid01_r"] = v01.astype(BF)
    c["validadd_r"] = ((v01 - 1.0) * 1e30).astype(BF)
    return c


_TBROW = np.broadcast_to(
    (-np.arange(NB, dtype=np.float64) * 1e-20).astype(np.float32), (128, NB)).copy()

_NC_CACHE = {}


def kernel(**inputs):
    key = (STAGE, TAPS, SUB)
    if key not in _NC_CACHE:
        _NC_CACHE[key] = build_program(STAGE, TAPS, SUB)
    nc = _NC_CACHE[key]
    shared = _arrange(inputs)
    shared["tbrow_r"] = _TBROW
    x = np.asarray(inputs["L0_attn_h"])
    pc = [_percore_consts(0), _percore_consts(1)]
    in_maps = []
    for core in range(8):
        b, hc = core // 2, core % 2
        mmap = dict(shared)
        mmap.update(pc[hc])
        mmap["xin"] = np.ascontiguousarray(x[b, hc * T:(hc + 1) * T].T)
        in_maps.append(mmap)
    res = run_bass_kernel_spmd(nc, in_maps, core_ids=list(range(8)))
    out = np.empty((B, TSEQ, DM), np.float32)
    for core in range(8):
        b, hc = core // 2, core % 2
        out[b, hc * T:(hc + 1) * T] = res.results[core]["y"]
    return out


# version 04
# speedup vs baseline: 1.2082x; 1.1518x over previous
import os
import numpy as np
import ml_dtypes
from contextlib import ExitStack

import concourse.bass as bass
import concourse.mybir as mybir
from concourse.bass_utils import run_bass_kernel_spmd

F32 = mybir.dt.float32
BF16 = mybir.dt.bfloat16
I8 = mybir.dt.int8
AF = mybir.ActivationFunctionType
ALU = mybir.AluOpType
AX = mybir.AxisListType
BF = ml_dtypes.bfloat16

B, TSEQ, DM = 4, 2048, 2048
T = 1024
NCH = DM // 128
H, D, RD = 16, 128, 64
M, NB = 32, 64
G, DG, NIH, CI = 4, 512, 4, 64
SWIN = 512
INTER = 8192
NI = INTER // 128
FT = 512
NQT = T // 128
KW = 12
EPS_LN, EPS_RMS = 1e-5, 1e-6

SEC_K = 0
SEC_V = SEC_K + T * D
SEC_CKV = SEC_V + T * D
SEC_KCI = SEC_CKV + 128 * 32
BLOB_N = SEC_KCI + 64 * 32

STAGE = int(os.environ.get("MK_STAGE", "4"))
SUB = os.environ.get("MK_SUB", "full")
TAPS = tuple(t for t in os.environ.get("MK_TAPS", "").split(",") if t)


class Prog:
    ENGS = ("sync", "tensor", "vector", "scalar", "gpsimd")

    def __init__(self):
        self.items = {e: [] for e in self.ENGS}
        self.cnt = {}

    def val(self, sem):
        return self.cnt.get(sem, 0)

    def emit(self, eng, fn=None, waits=(), inc=None, amt=1):
        w = [(s, v) for (s, v) in waits if v and v > 0]
        self.items[eng].append((w, fn, inc, amt))
        if inc is not None:
            self.cnt[inc] = self.cnt.get(inc, 0) + amt
            return self.cnt[inc]
        return None

    def replay(self, eng_name, eng, sems):
        for w, fn, inc, amt in self.items[eng_name]:
            for s, v in w:
                eng.wait_ge(sems[s], v)
            ins = fn(eng) if fn is not None else None
            if inc is not None:
                if ins is None:
                    ins = eng.nop()
                ins.then_inc(sems[inc], amt)


def build_program(stage, taps, sub="full"):
    nc = bass.Bass("TRN2", target_bir_lowering=False)

    def din(name, shape, dtype):
        return nc.dram_tensor(name, shape, dtype, kind="ExternalInput").ap()

    xin = din("xin", [DM, T], F32)
    wg_r = din("wg_r", [128, 32, 16, 256], I8)
    wu_r = din("wu_r", [128, 32, 16, 256], I8)
    wd_r = din("wd_r", [128, 4, 16, 4, 512], I8)
    sg_r = din("sg_r", [128, 64], F32)
    su_r = din("su_r", [128, 64], F32)
    sd_r = din("sd_r", [128, 16], F32)
    wqc_r = din("wqc_r", [128, 16, 1024], F32)
    wqup_r = din("wqup_r", [128, 8, 2048], F32)
    wkvb_r = din("wkvb_r", [128, 16, 128], F32)
    wzb_r = din("wzb_r", [128, 16, 128], F32)
    wk_r = din("wk_r", [128, 16, 128], F32)
    wv_r = din("wv_r", [128, 16, 128], F32)
    wikv_r = din("wikv_r", [128, 16, 64], F32)
    wiz_r = din("wiz_r", [128, 16, 64], F32)
    wdq_r = din("wdq_r", [128, 16, 64], F32)
    ww_r = din("ww_r", [128, 16, 4], F32)
    wiuq_r = din("wiuq_r", [64, 256], F32)
    gw_r = din("gw_r", [128, 16, 512], F32)
    fin_r = din("fin_r", [128, 16, 2048], F32)
    qnw_r = din("qnw_r", [1, 128], F32)
    knw_r = din("knw_r", [1, 128], F32)
    sink_r = din("sink_r", [1, 16], F32)
    biasb_r = din("biasb_r", [32, 128], F32)
    biasi_r = din("biasi_r", [32, 64], F32)
    ropeC_r = din("ropeC_r", [128, 1024], BF16)
    ropeS_r = din("ropeS_r", [128, 1024], BF16)
    swp_r = din("swp_r", [128, 128], BF16)
    coskt_r = din("coskt_r", [128, 8, 32], BF16)
    sinkt_r = din("sinkt_r", [128, 8, 32], BF16)
    maskjm_r = din("maskjm_r", [128, 8, 5, 128], BF16)
    valid01_r = din("valid01_r", [128, 8, 64], BF16)
    validadd_r = din("validadd_r", [128, 8, 64], BF16)
    tbrow_r = din("tbrow_r", [128, 64], F32)
    idf_r = din("idf_r", [128, 128], F32)
    hsel_r = din("hsel_r", [128, 16, 16], BF16)
    qsel_r = din("qsel_r", [16, 16, 128], BF16)

    y = nc.dram_tensor("y", [T, DM], F32, kind="ExternalOutput").ap()
    dbg = {}
    tap_shapes = {"h": [128, NCH * 1024], "q": [128, 16 * 1024],
                  "kwin": [128, KW * 128], "vwin": [128, KW * 128],
                  "ckv": [128, 64], "kci": [64, 64], "selT": [64, NQT * 128],
                  "attn": [128, NCH * 1024], "og": [128, 16 * 1024],
                  "isc": [128, 64], "x2": [128, NCH * 1024], "h3": [128, NCH * 1024], "r0": [1, 512], "r1": [1, 512]}
    for tp in taps:
        dbg[tp] = nc.dram_tensor("dbg_" + tp, tap_shapes[tp], F32,
                                 kind="ExternalOutput").ap()

    cc_in = nc.dram_tensor("cc_in", [1, BLOB_N], BF16)
    cc_out = nc.dram_tensor("cc_out", [2, BLOB_N], BF16)

    P = Prog()
    em = P.emit
    ctx = ExitStack()
    sb = lambda n, s, d: ctx.enter_context(nc.sbuf_tensor(n, s, d))
    psT = lambda n, s, d: ctx.enter_context(nc.psum_tensor(n, s, d))

    with ctx:
        R = sb("R", [128, NCH, 1024], F32)
        Hb = sb("Hb", [128, NCH, 1024], BF16)
        A = sb("A", [128, 32768], BF16)
        W = sb("W", [128, 16384], BF16)
        fv = A[:, :].rearrange("p (c t) -> p c t", c=NI)
        qv = A[:, 0:16384].rearrange("p (c t) -> p c t", c=16)
        qcv = A[:, 16384:24576].rearrange("p (c t) -> p c t", c=8)
        ropeC = A[:, 16384:17408]
        ropeS = A[:, 17408:18432]
        tA = A[:, 18432:19456]
        qsel = A[0:16, 19456:21504].rearrange("p (h t) -> p h t", h=16)
        stats32 = A[0:16, 21504:23552].bitcast(F32)
        hsel = A[:, 23552:23808].rearrange("p (c h) -> p c h", c=16)
        rstd_hs = A[0:16, 23808:24832]
        zb32 = A[:, 16384:18432].bitcast(F32)
        cbf = A[:, 18432:19456]
        qi_v = A[0:64, 19456:23552].rearrange("p (c t) -> p c t", c=4)
        dq_v = A[0:64, 23552:24576]
        maskjm = A[:, 24832:29952].rearrange("p (q k t) -> p q k t", q=8, k=5)
        esw = W[:, 0:1920].rearrange("p (a k t) -> p a k t", a=3, k=5)
        esp = W[0:64, 1920:2304].rearrange("p (a t) -> p a t", a=3)
        avs = W[:, 2304:2560].rearrange("p (a t) -> p a t", a=2)
        attn_tm = W[:, 2560:3072].bitcast(F32).rearrange("p (a t) -> p a t", a=2)
        kwin_fm = W[:, 3072:4608]
        selT = W[0:64, 5376:6400]
        kwin_tm = W[:, 11328:14400].bitcast(F32).rearrange("p (k d) -> p k d", k=KW)
        vwin = W[:, 14400:15936].rearrange("p (k d) -> p k d", k=KW)
        outbuf = W[:, 0:8192].bitcast(F32).rearrange("p (a t) -> p a t", a=2)
        sq = sb("sq", [128, 512], BF16)
        usb = sb("usb", [128, 2, 512], BF16)
        sel32 = sb("sel32", [128, 64], F32)
        wgt_sb = sb("wgt_sb", [128, NQT, 4], F32)
        isc = sb("isc", [128, 64], F32)
        work = sb("work", [128, 64], F32)
        m01 = sb("m01", [128, 64], F32)
        mx8 = sb("mx8", [128, 8], F32)
        sgc = sb("sgc", [128, 64], F32)
        suc = sb("suc", [128, 64], F32)
        sdh = sb("sdh", [128, 16], F32)
        coskt = sb("coskt", [128, 8, 32], BF16)
        sinkt = sb("sinkt", [128, 8, 32], BF16)
        valid01 = A[:, 29952:30464].rearrange("p (q n) -> p q n", q=8)
        validadd = A[:, 30464:30976].rearrange("p (q n) -> p q n", q=8)
        tbrow = sb("tbrow", [128, 64], F32)
        idf = sb("idf", [128, 128], F32)
        swp = sb("swp", [128, 128], BF16)
        onescf = sb("onescf", [128, 1], F32)
        onescb = sb("onescb", [128, 1], BF16)
        onesrf = sb("onesrf", [1, 128], F32)
        onesrb = sb("onesrb", [1, 128], BF16)
        qnw_row = sb("qnw_row", [1, 128], F32)
        knw_row = sb("knw_row", [1, 128], F32)
        qnw_col = sb("qnw_col", [128, 1], F32)
        knw_bc = sb("knw_bc", [128, 128], F32)
        knw_col = sb("knw_col", [128, 1], F32)
        sink_row = sb("sink_row", [1, 16], F32)
        sink_cols = sb("sink_cols", [128, 16], F32)
        biasbT = sb("biasbT", [128, 32], F32)
        biasiT = sb("biasiT", [64, 32], F32)
        biasb_sb = A[0:32, 0:256].bitcast(F32)
        biasi_sb = A[0:32, 256:384].bitcast(F32)
        epsln = sb("epsln", [1, 1], F32)
        epsrms = sb("epsrms", [128, 1], F32)
        lnC = sb("lnC", [128, 1], F32)
        wiuq_sb = W[0:64, 15936:16192]
        r0row = sb("r0row", [1, 512], BF16)
        r1row = sb("r1row", [1, 512], BF16)
        varrow = sb("varrow", [1, 512], F32)
        rstd_r32 = sb("rstd_r32", [1, 32], F32)
        cmp32 = sb("cmp32", [128, 32], F32)
        gmx = sb("gmx", [128, 32], F32)
        gsm = sb("gsm", [128, 32], F32)
        grec = sb("grec", [128, 32], F32)
        rms_c = sb("rms_c", [128, 1], F32)
        ckv_fm = sb("ckv_fm", [128, 64], BF16)
        ckv_nm = sb("ckv_nm", [64, 128], BF16)
        kci_cm = sb("kci_cm", [64, 64], BF16)
        m11 = sb("m11", [1, 8], F32)
        srow = W[0:1, 6400:7424].bitcast(F32)
        nSW = sb("nSW", [128, 16], F32)
        nSP = sb("nSP", [128, 16], F32)
        sinkexp = sb("sinkexp", [128, 16], F32)
        hsel2 = A[:, 30976:31232].rearrange("p (c h) -> p c h", c=16)
        ssw_c = sb("ssw_c", [128, 2], F32)
        ssp_c = sb("ssp_c", [128, 2], F32)
        rsw_c = sb("rsw_c", [128, 2], F32)
        rsp_c = sb("rsp_c", [128, 2], F32)
        red1 = sb("red1", [128, 2], F32)
        red2 = sb("red2", [128, 2], F32)

        pb = [psT(f"pb{i}", [128, 512], F32) for i in range(8)]

        semctx = ExitStack()
        sems = {n: semctx.enter_context(nc.semaphore(n))
                for n in ("dS", "dG", "mm", "vv", "aa", "pp", "cc")}

        with semctx:
            def Wv(off, shape):
                n = int(np.prod(shape))
                v = W[:, off:off + n]
                if len(shape) > 1:
                    lets = "abcdef"[: len(shape)]
                    pat = f"p ({' '.join(lets)}) -> p {' '.join(lets)}"
                    v = v.rearrange(pat, **{lets[i]: shape[i] for i in range(len(shape))})
                return v

            def tap(name, src_ap):
                if name in dbg:
                    waits = [("aa", P.val("aa")), ("vv", P.val("vv")),
                             ("mm", P.val("mm")), ("pp", P.val("pp"))]
                    em("gpsimd", lambda e, s=src_ap, d=dbg[name]: e.dma_start(out=d, in_=s),
                       waits=waits, inc="dG", amt=16)

            for o, i in ((sgc[:], sg_r), (suc[:], su_r), (sdh[:], sd_r),
                         (coskt[:], coskt_r), (sinkt[:], sinkt_r),
                         (tbrow[:], tbrow_r), (idf[:], idf_r),
                         (swp[:], swp_r),
                         (qnw_row[:], qnw_r), (knw_row[:], knw_r),
                         (sink_row[:], sink_r),
                         (biasb_sb, biasb_r), (biasi_sb, biasi_r)):
                em("sync", lambda e, o=o, i=i: e.dma_start(out=o, in_=i), inc="dS", amt=16)
            em("sync", lambda e: e.dma_start(
                out=R[:], in_=xin.rearrange("(c p) t -> p c t", p=128)), inc="dS", amt=16)
            ld_done = P.val("dS")

            def memsets(e):
                e.memset(onescf[:], 1.0)
                e.memset(onescb[:], 1.0)
                e.memset(onesrf[:], 1.0)
                e.memset(onesrb[:], 1.0)
                e.memset(epsln[:], EPS_LN)
                e.memset(epsrms[:], EPS_RMS)
                return e.memset(lnC[:], float(np.log(1.0 / np.sqrt(128.0))))
            vmark = em("vector", memsets, inc="vv")
            vmark = em("vector", lambda e: e.tensor_scalar(
                out=sdh[:], in0=sdh[:], scalar1=0.5, scalar2=None, op0=ALU.mult),
                waits=[("dS", ld_done)], inc="vv")
            mk = em("tensor", lambda e: e.transpose(pb[6][:, 0:1], qnw_row[:], idf[0:1, 0:1]),
                    waits=[("dS", ld_done), ("vv", vmark)], inc="mm")
            mk = em("tensor", lambda e: e.transpose(pb[6][:, 128:129], knw_row[:], idf[0:1, 0:1]),
                    inc="mm")
            mk = em("tensor", lambda e: e.transpose(pb[6][:, 256:288], biasb_sb, idf[0:32, 0:32]),
                    inc="mm")
            mk = em("tensor", lambda e: e.transpose(pb[6][0:64, 288:320], biasi_sb, idf[0:32, 0:32]),
                    inc="mm")
            mk = em("tensor", lambda e: e.matmul(pb[6][:, 320:448], onesrf[:], knw_row[:],
                                                 start=True, stop=True), inc="mm")
            mk = em("tensor", lambda e: e.matmul(pb[6][:, 448:464], onesrf[:], sink_row[:],
                                                 start=True, stop=True), inc="mm")
            vmark = em("vector", lambda e: e.tensor_copy(out=qnw_col[:], in_=pb[6][:, 0:1]),
                       waits=[("mm", mk)], inc="vv")
            vmark = em("vector", lambda e: e.tensor_copy(out=knw_col[:], in_=pb[6][:, 128:129]), inc="vv")
            vmark = em("vector", lambda e: e.tensor_copy(out=biasbT[:, :], in_=pb[6][:, 256:288]), inc="vv")
            vmark = em("vector", lambda e: e.tensor_copy(out=biasiT[:, :], in_=pb[6][0:64, 288:320]), inc="vv")
            vmark = em("vector", lambda e: e.tensor_copy(out=knw_bc[:], in_=pb[6][:, 320:448]), inc="vv")
            init_done = em("vector", lambda e: e.tensor_copy(out=sink_cols[:], in_=pb[6][:, 448:464]), inc="vv")

            def ln_full(tau, pre_mm=(), pre_vv=(), pre_aa=()):
                ts = slice(tau * FT, tau * FT + FT)
                auto = [("vv", P.val("vv")), ("pp", P.val("pp")), ("aa", P.val("aa"))]
                entry_mm = P.val("mm")
                for c in range(NCH):
                    em("tensor", lambda e, c=c, ts=ts: e.matmul(
                        pb[4][0:1, :], onescf[:], R[:, c, ts],
                        start=(c == 0), stop=(c == NCH - 1)),
                        waits=(list(pre_mm) + auto if c == 0 else []),
                        inc="mm" if c == NCH - 1 else None)
                sqmk = entry_mm
                for c in range(NCH):
                    aw = em("scalar", lambda e, c=c, ts=ts: e.activation(
                        out=sq[:], in_=R[:, c, ts], func=AF.Square),
                        waits=(list(pre_aa) if c == 0 else []) + [("mm", sqmk)],
                        inc="aa")
                    sqmk = em("tensor", lambda e, c=c: e.matmul(
                        pb[5][0:1, :], onescb[:], sq[:],
                        start=(c == 0), stop=(c == NCH - 1)),
                        waits=[("aa", aw)], inc="mm")
                mm_sums = P.val("mm")
                vw = em("vector", lambda e: e.tensor_scalar(
                    out=r0row[:], in0=pb[4][0:1, :], scalar1=1.0 / DM, scalar2=None,
                    op0=ALU.mult), waits=[("mm", mm_sums)] + list(pre_vv), inc="vv")
                vw = em("vector", lambda e: e.tensor_mul(out=r1row[:], in0=r0row[:], in1=r0row[:]),
                        waits=[("vv", vw)], inc="vv")
                vw = em("vector", lambda e: e.scalar_tensor_tensor(
                    out=varrow[:], in0=pb[5][0:1, :], scalar=1.0 / DM, in1=r1row[:],
                    op0=ALU.mult, op1=ALU.subtract), waits=[("vv", vw)], inc="vv")
                aw = em("scalar", lambda e: e.activation(
                    out=varrow[:], in_=varrow[:], func=AF.Ln, bias=epsln[:]),
                    waits=[("vv", vw)], inc="aa")
                aw = em("scalar", lambda e: e.activation(
                    out=varrow[:], in_=varrow[:], func=AF.Exp, scale=-0.5),
                    waits=[("aa", aw)], inc="aa")
                mk1 = em("tensor", lambda e: e.matmul(pb[6][:, :], onesrb[:], r0row[:],
                                                      start=True, stop=True),
                         waits=[("vv", vw)], inc="mm")
                mk1 = em("tensor", lambda e: e.matmul(pb[7][:, :], onesrf[:], varrow[:],
                                                      start=True, stop=True),
                         waits=[("aa", aw)], inc="mm")
                for c in range(NCH):
                    em("vector", lambda e, c=c, ts=ts: e.tensor_sub(
                        out=Hb[:, c, ts], in0=R[:, c, ts], in1=pb[6][:, :]),
                        waits=[("mm", mk1)] if c == 0 else [], inc="vv")
                    em("vector", lambda e, c=c, ts=ts: e.tensor_mul(
                        out=Hb[:, c, ts], in0=Hb[:, c, ts], in1=pb[7][:, :]), inc="vv")
                return P.val("vv"), P.val("pp")

            def ffn_pass(ln_marks):
                prev_tile_vv = None
                prev_tile_aa = None
                ntau = 1 if SUB.startswith("z") else 2
                for tau in range(ntau):
                    ts = slice(tau * FT, tau * FT + FT)
                    lnvv, lnpp = ln_marks[tau]
                    pe_umark = {}
                    evac = {}
                    dma_mark = {}
                    def emit_wdma(blk):
                        par = blk % 2
                        if blk >= 2:
                            wwaits = [("mm", pe_umark[2 * (blk - 2) + 1])]
                        else:
                            wwaits = [("mm", P.val("mm"))]
                        em("gpsimd", lambda e, blk=blk, par=par: e.dma_start(
                            out=Wv(8192 * par, (16, 256)), in_=wg_r[:, blk]),
                            waits=wwaits, inc="dG", amt=16)
                        dma_mark[blk] = em("gpsimd", lambda e, blk=blk, par=par: e.dma_start(
                            out=Wv(8192 * par + 4096, (16, 256)), in_=wu_r[:, blk]),
                            inc="dG", amt=16)
                    emit_wdma(0)
                    emit_wdma(1)
                    if SUB == "dma":
                        for blk in range(2, 32):
                            par = blk % 2
                            dw = [("dG", P.val("dG") - 32)] if blk >= 4 else []
                            em("gpsimd", lambda e, blk=blk, par=par: e.dma_start(
                                out=Wv(8192 * par, (16, 256)), in_=wg_r[:, blk]),
                                waits=dw, inc="dG", amt=16)
                            em("gpsimd", lambda e, blk=blk, par=par: e.dma_start(
                                out=Wv(8192 * par + 4096, (16, 256)), in_=wu_r[:, blk]),
                                inc="dG", amt=16)
                        em("gpsimd", lambda e: None, waits=[("dG", P.val("dG"))])
                        ffn_tile_done = (P.val("vv"), P.val("mm"))
                        continue
                    nblk_eff = 1 if (SUB.startswith("a1") or SUB.startswith("z")) else 32
                    for blk in range(nblk_eff):
                        par = blk % 2
                        for sub in range(2):
                            oc = blk * 2 + sub
                            pg = pb[0 + (oc % 2)]
                            pu = pb[2 + (oc % 2)]
                            pw = [("dG", dma_mark[blk])]
                            if oc >= 2:
                                ag, au, vf = evac[oc - 2]
                                pw += [("aa", au), ("vv", vf)]
                            if oc in (0, 1):
                                if oc == 0:
                                    pw += [("vv", lnvv), ("pp", lnpp)]
                                if prev_tile_vv is not None:
                                    pw += [("vv", prev_tile_vv), ("aa", prev_tile_aa)]
                            for c in range(NCH):
                                mk_g = em("tensor", lambda e, c=c, sub=sub, par=par, pg=pg, ts=ts: e.matmul(
                                    pg[:, 0:FT],
                                    Wv(8192 * par, (16, 2, 128))[:, c, sub],
                                    Hb[:, c, ts], start=(c == 0), stop=(c == NCH - 1)),
                                    waits=pw if c == 0 else [],
                                    inc="mm" if c == NCH - 1 else None)
                            for c in range(NCH):
                                mk_u = em("tensor", lambda e, c=c, sub=sub, par=par, pu=pu, ts=ts: e.matmul(
                                    pu[:, 0:FT],
                                    Wv(8192 * par + 4096, (16, 2, 128))[:, c, sub],
                                    Hb[:, c, ts], start=(c == 0), stop=(c == NCH - 1)),
                                    inc="mm" if c == NCH - 1 else None)
                            pe_umark[oc] = mk_u
                            if SUB.startswith("z") or SUB in ("a1s", "a1su"):
                                if SUB == "z0":
                                    ag = em("scalar", lambda e, oc=oc, pg=pg: e.activation(
                                        out=fv[:, oc, :], in_=pg[:, 0:FT], func=AF.Silu),
                                        waits=[("mm", mk_g)], inc="aa")
                                elif SUB == "z2":
                                    ag = em("scalar", lambda e, oc=oc, pg=pg: e.activation(
                                        out=usb[:, oc % 2, :], in_=pg[:, 0:FT], func=AF.Silu,
                                        scale=sgc[:, oc:oc + 1]), waits=[("mm", mk_g)], inc="aa")
                                elif SUB == "z3":
                                    ag = em("scalar", lambda e, oc=oc, pg=pg: e.activation(
                                        out=fv[:, oc, :], in_=pg[:, 0:FT], func=AF.Copy,
                                        scale=sgc[:, oc:oc + 1]), waits=[("mm", mk_g)], inc="aa")
                                else:
                                    ag = em("scalar", lambda e, oc=oc, pg=pg: e.activation(
                                        out=fv[:, oc, :], in_=pg[:, 0:FT], func=AF.Silu,
                                        scale=sgc[:, oc:oc + 1]), waits=[("mm", mk_g)], inc="aa")
                                if SUB == "a1su":
                                    au = em("scalar", lambda e, oc=oc, pu=pu: e.activation(
                                        out=usb[:, oc % 2, :], in_=pu[:, 0:FT], func=AF.Copy,
                                        scale=suc[:, oc:oc + 1]), waits=[("mm", mk_u)], inc="aa")
                                else:
                                    au = em("scalar", lambda e: None, waits=[("mm", mk_u)], inc="aa")
                                vf = em("vector", lambda e: None, waits=[("aa", au)], inc="vv")
                                evac[oc] = (ag, au, vf)
                                continue
                            if SUB == "a1m":
                                ag = em("scalar", lambda e: None, waits=[("mm", mk_g)], inc="aa")
                                au = em("scalar", lambda e: None, waits=[("mm", mk_u)], inc="aa")
                                vf = em("vector", lambda e: None, waits=[("aa", au)], inc="vv")
                                evac[oc] = (ag, au, vf)
                                continue
                            ag = em("scalar", lambda e, oc=oc, pg=pg: e.activation(
                                out=fv[:, oc, :], in_=pg[:, 0:FT], func=AF.Silu,
                                scale=sgc[:, oc:oc + 1]), waits=[("mm", mk_g)], inc="aa")
                            au = em("scalar", lambda e, oc=oc, pu=pu: e.activation(
                                out=usb[:, oc % 2, :], in_=pu[:, 0:FT], func=AF.Copy,
                                scale=suc[:, oc:oc + 1]), waits=[("mm", mk_u)], inc="aa")
                            vf = em("vector", lambda e, oc=oc: e.tensor_mul(
                                out=fv[:, oc, :], in0=fv[:, oc, :], in1=usb[:, oc % 2, :]),
                                waits=[("aa", au)], inc="vv")
                            evac[oc] = (ag, au, vf)
                        if blk + 2 < nblk_eff:
                            emit_wdma(blk + 2)
                    f_done = P.val("vv")
                    if SUB == "a" or SUB.startswith("a1") or SUB.startswith("z"):
                        prev_tile_vv = P.val("vv")
                        prev_tile_aa = P.val("aa")
                        ffn_tile_done = (P.val("vv"), P.val("mm"))
                        continue
                    slab_mark = {}
                    yev = {}
                    for oq in range(4):
                        base = 0 if oq % 2 == 0 else 4
                        accs = [pb[base + j] for j in range(4)]
                        for s in range(16):
                            par = s % 2
                            if oq == 0 and s < 2:
                                wwaits = [("vv", f_done)]
                            else:
                                prev_use = (oq, s - 2) if s >= 2 else (oq - 1, 14 + s)
                                wwaits = [("mm", slab_mark[prev_use])]
                            em("gpsimd", lambda e, oq=oq, s=s, par=par: e.dma_start(
                                out=Wv(4096 * par, (4, 512)), in_=wd_r[:, oq, s]),
                                waits=wwaits, inc="dG", amt=16)
                            dmk = P.val("dG")
                            pw = [("dG", dmk)]
                            if s == 0:
                                if oq >= 2:
                                    pw.append(("vv", yev[oq - 2]))
                                else:
                                    pw.append(("vv", f_done))
                            for c in range(4):
                                for j in range(4):
                                    slab_mark[(oq, s)] = em("tensor", lambda e, c=c, j=j, s=s, par=par, accs=accs: e.matmul(
                                        accs[j][:, 0:FT],
                                        Wv(4096 * par, (4, 4, 128))[:, c, j],
                                        fv[:, s * 4 + c, :],
                                        start=(s == 0 and c == 0), stop=(s == 15 and c == 3)),
                                        waits=pw if (c == 0 and j == 0) else [],
                                        inc="mm" if (c == 3 and j == 3) else None)
                        mk = slab_mark[(oq, 15)]
                        for j in range(4):
                            oc = oq * 4 + j
                            vw = em("vector", lambda e, oc=oc, accs=accs, j=j, ts=ts: e.scalar_tensor_tensor(
                                out=R[:, oc, ts], in0=accs[j][:, 0:FT],
                                scalar=sdh[:, oc:oc + 1], in1=R[:, oc, ts],
                                op0=ALU.mult, op1=ALU.add),
                                waits=[("mm", mk)] if j == 0 else [], inc="vv")
                        yev[oq] = vw
                    prev_tile_vv = P.val("vv")
                    prev_tile_aa = P.val("aa")
                    ffn_tile_done = (P.val("vv"), P.val("mm"))
                return ffn_tile_done

            if stage >= 1:
                ln1 = {0: ln_full(0, pre_mm=[("dS", ld_done)], pre_vv=[("vv", init_done)]),
                       1: None}
                if SUB.startswith("z"):
                    ln1[1] = ln1[0]
                else:
                    ln1[1] = ln_full(1)
                if sub == "ln":
                    final_marks = {"vv": ln1[1][0], "aa": P.val("aa")}
                else:
                    f1_vv, f1_mm = ffn_pass(ln1)
                    final_marks = {"vv": f1_vv, "aa": P.val("aa")}
            else:
                final_marks = {"vv": init_done, "aa": P.val("aa")}
                if stage >= -1:
                    pass

            if stage >= 2:
                ln2 = {0: ln_full(0, pre_mm=[("vv", f1_vv)]), 1: None}
                ln2[1] = ln_full(1)
                h_vv, h_pp = ln2[1]
                tap("h", Hb[:, :, :].rearrange("p c t -> p (c t)"))

                em("gpsimd", lambda e: e.dma_start(out=Wv(0, (16, 1024)), in_=wqc_r),
                   waits=[("mm", P.val("mm"))], inc="dG", amt=16)
                wqc_ld = P.val("dG")
                qc_ev = {}
                for co in range(8):
                    for tt in range(2):
                        it2 = co * 2 + tt
                        tsl = slice(tt * 512, tt * 512 + 512)
                        pband = pb[it2 % 2]
                        pw = [("dG", wqc_ld), ("vv", h_vv), ("pp", h_pp)]
                        if it2 >= 2:
                            pw.append(("aa", qc_ev[it2 - 2]))
                        for c in range(NCH):
                            mk = em("tensor", lambda e, c=c, co=co, tsl=tsl, pband=pband: e.matmul(
                                pband[:, 0:512], Wv(0, (16, 8, 128))[:, c, co],
                                Hb[:, c, tsl], start=(c == 0), stop=(c == NCH - 1)),
                                waits=pw if c == 0 else [],
                                inc="mm" if c == NCH - 1 else None)
                        qc_ev[it2] = em("scalar", lambda e, co=co, tsl=tsl, pband=pband: e.activation(
                            out=qcv[:, co, tsl], in_=pband[:, 0:512], func=AF.Copy),
                            waits=[("mm", mk)], inc="aa")
                qc_done = P.val("aa")

                em("gpsimd", lambda e: e.dma_start(out=Wv(0, (8, 2048)), in_=wqup_r),
                   waits=[("aa", qc_done)], inc="dG", amt=16)
                wqup_ld = P.val("dG")
                q_ev = {}
                for hh in range(16):
                    for tt in range(2):
                        it2 = hh * 2 + tt
                        tsl = slice(tt * 512, tt * 512 + 512)
                        pband = pb[2 + it2 % 2]
                        pw = [("dG", wqup_ld)]
                        if it2 >= 2:
                            pw.append(("aa", q_ev[it2 - 2]))
                        for c in range(8):
                            mk = em("tensor", lambda e, c=c, hh=hh, tsl=tsl, pband=pband: e.matmul(
                                pband[:, 0:512], Wv(0, (8, 16, 128))[:, c, hh],
                                qcv[:, c, tsl], start=(c == 0), stop=(c == 7)),
                                waits=pw if c == 0 else [],
                                inc="mm" if c == 7 else None)
                        q_ev[it2] = em("scalar", lambda e, hh=hh, tsl=tsl, pband=pband: e.activation(
                            out=qv[:, hh, tsl], in_=pband[:, 0:512], func=AF.Copy),
                            waits=[("mm", mk)], inc="aa")
                q_done = P.val("aa")

                hs_ld = em("sync", lambda e: e.dma_start(out=hsel[:, :, :], in_=hsel_r),
                           waits=[("aa", q_done)], inc="dS", amt=16)
                for hh in range(16):
                    aw = em("scalar", lambda e, hh=hh: e.activation(
                        out=sq[:, 0:512], in_=qv[:, hh, 0:512], func=AF.Square),
                        waits=[("aa", q_done), ("dS", hs_ld)] if hh == 0 else [("mm", P.val("mm"))], inc="aa")
                    em("tensor", lambda e, hh=hh: e.matmul(
                        pb[4][0:16, 0:512], hsel[:, hh, :], sq[:, 0:512],
                        start=(hh == 0), stop=(hh == 15)), waits=[("aa", aw)], inc="mm")
                    aw = em("scalar", lambda e, hh=hh: e.activation(
                        out=sq[:, 0:512], in_=qv[:, hh, 512:1024], func=AF.Square),
                        waits=[("mm", P.val("mm"))], inc="aa")
                    em("tensor", lambda e, hh=hh: e.matmul(
                        pb[5][0:16, 0:512], hsel[:, hh, :], sq[:, 0:512],
                        start=(hh == 0), stop=(hh == 15)), waits=[("aa", aw)], inc="mm")
                mkq = P.val("mm")
                aw = em("scalar", lambda e: e.activation(
                    out=stats32[:, 0:512], in_=pb[4][0:16, 0:512], func=AF.Ln,
                    bias=epsrms[0:16, :], scale=1.0 / 128.0), waits=[("mm", mkq)], inc="aa")
                aw = em("scalar", lambda e: e.activation(
                    out=stats32[:, 512:1024], in_=pb[5][0:16, 0:512], func=AF.Ln,
                    bias=epsrms[0:16, :], scale=1.0 / 128.0), inc="aa")
                rstd_done = em("scalar", lambda e: e.activation(
                    out=rstd_hs[:, :], in_=stats32[:, :], func=AF.Exp,
                    bias=lnC[0:16, :], scale=-0.5), waits=[("aa", aw)], inc="aa")

                em("sync", lambda e: e.dma_start(out=ropeC, in_=ropeC_r),
                   waits=[("aa", q_done)], inc="dS", amt=16)
                rp_ld = em("sync", lambda e: e.dma_start(out=ropeS, in_=ropeS_r),
                           inc="dS", amt=16)
                qs_ld = em("sync", lambda e: e.dma_start(out=qsel[:, :, :], in_=qsel_r),
                           inc="dS", amt=16)
                qn_prev = None
                qn_prev2 = {}
                for it2 in range(32):
                    hh, half = it2 // 2, it2 % 2
                    hsl = slice(half * 512, half * 512 + 512)
                    pA_ = pb[4 + it2 % 2]
                    pB_ = pb[6 + it2 % 2]
                    if it2 == 0:
                        pw = [("dS", qs_ld), ("aa", rstd_done), ("dS", rp_ld)]
                    elif it2 == 1:
                        pw = []
                    else:
                        pw = [("vv", qn_prev2[it2 - 2])]
                    mka = em("tensor", lambda e, hh=hh, hsl=hsl, pA_=pA_: e.matmul(
                        pA_[:, 0:512], swp[:], qv[:, hh, hsl],
                        start=True, stop=True), waits=pw, inc="mm")
                    mkb = em("tensor", lambda e, hh=hh, hsl=hsl, pB_=pB_: e.matmul(
                        pB_[:, 0:512], qsel[:, hh, :], rstd_hs[:, hsl],
                        start=True, stop=True), inc="mm")
                    ts_ = tA[:, 0:512] if it2 % 2 == 0 else tA[:, 512:1024]
                    vw = em("vector", lambda e, hh=hh, hsl=hsl, ts_=ts_: e.tensor_mul(
                        out=ts_, in0=qv[:, hh, hsl], in1=ropeC[:, hsl]),
                        waits=[("mm", mkb)], inc="vv")
                    vw = em("vector", lambda e, hh=hh, hsl=hsl, pA_=pA_: e.tensor_tensor(
                        out=qv[:, hh, hsl], in0=pA_[:, 0:512], in1=ropeS[:, hsl],
                        op=ALU.mult), inc="vv")
                    vw = em("vector", lambda e, hh=hh, hsl=hsl, ts_=ts_: e.tensor_add(
                        out=qv[:, hh, hsl], in0=qv[:, hh, hsl], in1=ts_), inc="vv")
                    vw = em("vector", lambda e, hh=hh, hsl=hsl, pB_=pB_: e.scalar_tensor_tensor(
                        out=qv[:, hh, hsl], in0=qv[:, hh, hsl], scalar=qnw_col[:],
                        in1=pB_[:, 0:512], op0=ALU.mult, op1=ALU.mult), inc="vv")
                    qn_prev2[it2] = vw
                    qn_prev = vw
                qn_done = qn_prev
                tap("q", qv[:, :, :].rearrange("p c t -> p (c t)"))
                em("sync", lambda e: e.dma_start(out=maskjm[:, :, :, :], in_=maskjm_r),
                   waits=[("vv", qn_done)], inc="dS", amt=16)
                em("sync", lambda e: e.dma_start(out=valid01[:, :, :], in_=valid01_r),
                   inc="dS", amt=16)
                valid_ld = em("sync", lambda e: e.dma_start(out=validadd[:, :, :], in_=validadd_r),
                              inc="dS", amt=16)

                offs = {}
                off = 0
                first = True
                for nm, src, csz in (("wkvb", wkvb_r, 128), ("wzb", wzb_r, 128),
                                     ("wk", wk_r, 128), ("wv", wv_r, 128),
                                     ("wikv", wikv_r, 64), ("wiz", wiz_r, 64),
                                     ("wdq", wdq_r, 64), ("ww", ww_r, 4)):
                    offs[nm] = off
                    em("gpsimd", lambda e, src=src, off=off, csz=csz: e.dma_start(
                        out=Wv(off, (16, csz)), in_=src),
                        waits=[("vv", qn_done)] if first else [], inc="dG", amt=16)
                    first = False
                    off += 16 * csz
                em("gpsimd", lambda e: e.dma_start(out=wiuq_sb, in_=wiuq_r), inc="dG", amt=16)
                smalls_ld = P.val("dG")

                vev = None
                kev = None
                for i in range(8):
                    tsl = slice(i * 128, i * 128 + 128)
                    for c in range(NCH):
                        pw = []
                        if c == 0:
                            pw = [("dG", smalls_ld)]
                            if vev is not None:
                                pw.append(("aa", vev))
                        mkv = em("tensor", lambda e, c=c, tsl=tsl: e.matmul(
                            pb[0][:, 0:128], Hb[:, c, tsl], Wv(offs["wv"], (16, 128))[:, c, :],
                            start=(c == 0), stop=(c == NCH - 1)), waits=pw,
                            inc="mm" if c == NCH - 1 else None)
                    vev = em("scalar", lambda e, i=i: e.activation(
                        out=vwin[:, 4 + i, :], in_=pb[0][:, 0:128], func=AF.Copy),
                        waits=[("mm", mkv)], inc="aa")
                    for c in range(NCH):
                        pw = [("vv", kev)] if (c == 0 and kev is not None) else []
                        mkk = em("tensor", lambda e, c=c, tsl=tsl: e.matmul(
                            pb[1][:, 0:128], Hb[:, c, tsl], Wv(offs["wk"], (16, 128))[:, c, :],
                            start=(c == 0), stop=(c == NCH - 1)), waits=pw,
                            inc="mm" if c == NCH - 1 else None)
                    aw = em("scalar", lambda e: e.activation(
                        out=sq[:, 0:128], in_=pb[1][:, 0:128], func=AF.Square),
                        waits=[("mm", mkk)], inc="aa")
                    vw = em("vector", lambda e: e.reduce_sum(
                        out=rms_c[:], in_=sq[:, 0:128], axis=AX.X), waits=[("aa", aw)], inc="vv")
                    aw = em("scalar", lambda e: e.activation(
                        out=rms_c[:], in_=rms_c[:], func=AF.Ln, bias=epsrms[:], scale=1.0 / 128.0),
                        waits=[("vv", vw)], inc="aa")
                    aw = em("scalar", lambda e: e.activation(
                        out=rms_c[:], in_=rms_c[:], func=AF.Exp, scale=-0.5),
                        waits=[("aa", aw)], inc="aa")
                    vw = em("vector", lambda e: e.scalar_tensor_tensor(
                        out=cbf[:, 0:128], in0=pb[1][:, 0:128], scalar=rms_c[:],
                        in1=knw_bc[:], op0=ALU.mult, op1=ALU.mult),
                        waits=[("aa", aw)], inc="vv")
                    ci_ = coskt[:, i, :]
                    si_ = sinkt[:, i, :]
                    em("vector", lambda e, ci_=ci_: e.tensor_mul(
                        out=sel32[:, 0:32], in0=cbf[:, 0:32], in1=ci_), inc="vv")
                    em("vector", lambda e, si_=si_: e.tensor_mul(
                        out=sel32[:, 32:64], in0=cbf[:, 32:64], in1=si_), inc="vv")
                    em("vector", lambda e, i=i: e.tensor_sub(
                        out=kwin_tm[:, 4 + i, 0:32], in0=sel32[:, 0:32], in1=sel32[:, 32:64]), inc="vv")
                    em("vector", lambda e, si_=si_: e.tensor_mul(
                        out=sel32[:, 0:32], in0=cbf[:, 0:32], in1=si_), inc="vv")
                    em("vector", lambda e, ci_=ci_: e.tensor_mul(
                        out=sel32[:, 32:64], in0=cbf[:, 32:64], in1=ci_), inc="vv")
                    em("vector", lambda e, i=i: e.tensor_add(
                        out=kwin_tm[:, 4 + i, 32:64], in0=sel32[:, 0:32], in1=sel32[:, 32:64]), inc="vv")
                    kev = em("vector", lambda e, i=i: e.tensor_copy(
                        out=kwin_tm[:, 4 + i, 64:128], in_=cbf[:, 64:128]), inc="vv")
                kv_done_vv = P.val("vv")
                kv_done_aa = P.val("aa")

                def compress(wc, wz, odim, biasT_ap, out_ap, do_rms, pre_vv):
                    last = pre_vv
                    for tt in range(2):
                        tsl = slice(tt * 512, tt * 512 + 512)
                        for c in range(NCH):
                            pw = [("vv", last)] if c == 0 else []
                            mk = em("tensor", lambda e, c=c, tsl=tsl, wc=wc, odim=odim: e.matmul(
                                pb[2][0:odim, 0:512], Wv(wc, (16, odim))[:, c, :],
                                Hb[:, c, tsl], start=(c == 0), stop=(c == NCH - 1)), waits=pw,
                                inc="mm" if c == NCH - 1 else None)
                        aw = em("scalar", lambda e, tt=tt, odim=odim: e.activation(
                            out=cbf[0:odim, tt * 512:tt * 512 + 512], in_=pb[2][0:odim, 0:512],
                            func=AF.Copy), waits=[("mm", mk)], inc="aa")
                        for c in range(NCH):
                            mk = em("tensor", lambda e, c=c, tsl=tsl, wz=wz, odim=odim: e.matmul(
                                pb[3][0:odim, 0:512], Wv(wz, (16, odim))[:, c, :],
                                Hb[:, c, tsl], start=(c == 0), stop=(c == NCH - 1)),
                                waits=[("aa", aw)] if c == 0 else [],
                                inc="mm" if c == NCH - 1 else None)
                        last = em("vector", lambda e, tt=tt, odim=odim: e.tensor_copy(
                            out=zb32[0:odim, tt * 512:tt * 512 + 512], in_=pb[3][0:odim, 0:512]),
                            waits=[("mm", mk)], inc="vv")
                    z3 = lambda ap, od: ap[0:od, :].rearrange("p (n m) -> p n m", m=32)
                    vw = em("vector", lambda e, odim=odim, biasT_ap=biasT_ap: e.tensor_tensor(
                        out=z3(zb32, odim), in0=z3(zb32, odim),
                        in1=biasT_ap.rearrange("p (o m) -> p o m", o=1).broadcast_to([odim, 32, 32]),
                        op=ALU.add), inc="vv")
                    vw = em("vector", lambda e, odim=odim: e.tensor_reduce(
                        out=gmx[0:odim, :], in_=z3(zb32, odim), axis=AX.X, op=ALU.max,
                        negate=True), inc="vv")
                    vw = em("vector", lambda e, odim=odim: e.tensor_tensor(
                        out=z3(zb32, odim), in0=z3(zb32, odim),
                        in1=gmx[0:odim, :].rearrange("p (n o) -> p n o", o=1).broadcast_to([odim, 32, 32]),
                        op=ALU.add), waits=[("vv", vw)], inc="vv")
                    aw = em("scalar", lambda e, odim=odim: e.activation(
                        out=zb32[0:odim, :], in_=zb32[0:odim, :], func=AF.Exp),
                        waits=[("vv", vw)], inc="aa")
                    vw = em("vector", lambda e, odim=odim: e.tensor_reduce(
                        out=gsm[0:odim, :], in_=z3(zb32, odim), axis=AX.X, op=ALU.add),
                        waits=[("aa", aw)], inc="vv")
                    vw = em("vector", lambda e, odim=odim: e.tensor_mul(
                        out=zb32[0:odim, :], in0=zb32[0:odim, :], in1=cbf[0:odim, :]), inc="vv")
                    vw = em("vector", lambda e, odim=odim: e.tensor_reduce(
                        out=cmp32[0:odim, :], in_=z3(zb32, odim), axis=AX.X, op=ALU.add), inc="vv")
                    vw = em("vector", lambda e, odim=odim: e.reciprocal(
                        out=grec[0:odim, :], in_=gsm[0:odim, :]), waits=[("vv", vw)], inc="vv")
                    vw = em("vector", lambda e, odim=odim: e.tensor_mul(
                        out=cmp32[0:odim, :], in0=cmp32[0:odim, :], in1=grec[0:odim, :]),
                        waits=[("vv", vw)], inc="vv")
                    if do_rms:
                        aw = em("scalar", lambda e: e.activation(
                            out=sq[:, 0:32], in_=cmp32[:, :], func=AF.Square),
                            waits=[("vv", vw)], inc="aa")
                        mk = em("tensor", lambda e: e.matmul(
                            pb[2][0:1, 0:32], onescb[:], sq[:, 0:32], start=True, stop=True),
                            waits=[("aa", aw)], inc="mm")
                        aw = em("scalar", lambda e: e.activation(
                            out=rstd_r32[:], in_=pb[2][0:1, 0:32], func=AF.Ln,
                            bias=epsrms[0:1, :], scale=1.0 / 128.0),
                            waits=[("mm", mk)], inc="aa")
                        aw = em("scalar", lambda e: e.activation(
                            out=rstd_r32[:], in_=rstd_r32[:], func=AF.Exp, scale=-0.5), inc="aa")
                        mk = em("tensor", lambda e: e.matmul(
                            pb[3][:, 0:32], onesrf[:], rstd_r32[:], start=True, stop=True),
                            waits=[("aa", aw)], inc="mm")
                        vw = em("vector", lambda e, out_ap=out_ap: e.scalar_tensor_tensor(
                            out=out_ap, in0=cmp32[:, :], scalar=knw_col[:],
                            in1=pb[3][:, 0:32], op0=ALU.mult, op1=ALU.mult),
                            waits=[("mm", mk)], inc="vv")
                    else:
                        vw = em("vector", lambda e, odim=odim, out_ap=out_ap: e.tensor_copy(
                            out=out_ap, in_=cmp32[0:odim, :]), waits=[("vv", vw)], inc="vv")
                    return vw

                ckv_loc = compress(offs["wkvb"], offs["wzb"], 128, biasbT[:, :],
                                   ckv_fm[:, 0:32], True, kv_done_vv)
                kci_loc = compress(offs["wikv"], offs["wiz"], 64, biasiT[:, :],
                                   kci_cm[0:64, 0:32], False, ckv_loc)

                for tt in range(2):
                    tsl = slice(tt * 512, tt * 512 + 512)
                    for c in range(NCH):
                        pw = []
                        if c == 0:
                            pw = [("vv", kci_loc)] if tt == 0 else [("aa", P.val("aa"))]
                        mk = em("tensor", lambda e, c=c, tsl=tsl: e.matmul(
                            pb[2][0:64, 0:512], Wv(offs["wdq"], (16, 64))[:, c, :],
                            Hb[:, c, tsl], start=(c == 0), stop=(c == NCH - 1)),
                            waits=pw, inc="mm" if c == NCH - 1 else None)
                    em("scalar", lambda e, tt=tt: e.activation(
                        out=dq_v[0:64, tt * 512:tt * 512 + 512], in_=pb[2][0:64, 0:512],
                        func=AF.Copy), waits=[("mm", mk)], inc="aa")
                dq_done = P.val("aa")
                for chn in range(2):
                    for tt in range(2):
                        mk = em("tensor", lambda e, chn=chn, tt=tt: e.matmul(
                            pb[3][:, 0:512], wiuq_sb[0:64, chn * 128:chn * 128 + 128],
                            dq_v[0:64, tt * 512:tt * 512 + 512], start=True, stop=True),
                            waits=[("aa", P.val("aa"))], inc="mm")
                        em("scalar", lambda e, chn=chn, tt=tt: e.activation(
                            out=qi_v[0:64, 2 * chn, tt * 512:tt * 512 + 512],
                            in_=pb[3][0:64, 0:512], func=AF.Copy),
                            waits=[("mm", mk)], inc="aa")
                        em("scalar", lambda e, chn=chn, tt=tt: e.activation(
                            out=qi_v[0:64, 2 * chn + 1, tt * 512:tt * 512 + 512],
                            in_=pb[3][64:128, 0:512], func=AF.Copy), inc="aa")
                qi_done = P.val("aa")
                wgev = None
                for i in range(8):
                    tsl = slice(i * 128, i * 128 + 128)
                    for c in range(NCH):
                        pw = []
                        if c == 0:
                            pw = [("aa", qi_done)] if i == 0 else [("vv", wgev)]
                        mk = em("tensor", lambda e, c=c, tsl=tsl: e.matmul(
                            pb[2][:, 0:4], Hb[:, c, tsl], Wv(offs["ww"], (16, 4))[:, c, :],
                            start=(c == 0), stop=(c == NCH - 1)), waits=pw,
                            inc="mm" if c == NCH - 1 else None)
                    wgev = em("vector", lambda e, i=i: e.tensor_copy(
                        out=wgt_sb[:, i, :], in_=pb[2][:, 0:4]), waits=[("mm", mk)], inc="vv")
                wgt_done = P.val("vv")

                ccK = cc_in.ap()[0:1, SEC_K:SEC_K + T * D].rearrange(
                    "o (n p d) -> (o n) p d", p=128, d=128).rearrange("n p d -> p n d")
                ccV = cc_in.ap()[0:1, SEC_V:SEC_V + T * D].rearrange(
                    "o (n p d) -> (o n) p d", p=128, d=128).rearrange("n p d -> p n d")
                ccC = cc_in.ap()[0:1, SEC_CKV:SEC_CKV + 128 * 32].rearrange(
                    "o (p n) -> (o p) n", n=32)
                ccI = cc_in.ap()[0:1, SEC_KCI:SEC_KCI + 64 * 32].rearrange(
                    "o (p n) -> (o p) n", n=32)
                em("gpsimd", lambda e, ccK=ccK: e.dma_start(out=ccK, in_=kwin_tm[:, 4:12, :]),
                   waits=[("vv", kv_done_vv)], inc="dG", amt=16)
                em("gpsimd", lambda e, ccV=ccV: e.dma_start(out=ccV, in_=vwin[:, 4:12, :]),
                   waits=[("aa", kv_done_aa)], inc="dG", amt=16)
                em("gpsimd", lambda e, ccC=ccC: e.dma_start(out=ccC, in_=ckv_fm[:, 0:32]),
                   waits=[("vv", ckv_loc)], inc="dG", amt=16)
                blob_w = em("gpsimd", lambda e, ccI=ccI: e.dma_start(out=ccI, in_=kci_cm[0:64, 0:32]),
                            waits=[("vv", kci_loc)], inc="dG", amt=16)
                ccdone = em("gpsimd", lambda e: e.collective_compute(
                    "AllGather", ALU.bypass,
                    replica_groups=[[0, 1], [2, 3], [4, 5], [6, 7]],
                    ins=[cc_in.ap().opt()], outs=[cc_out.ap().opt()]),
                    waits=[("dG", blob_w)], inc="cc")
                rbK = cc_out.ap()[0:1, SEC_K:SEC_K + T * D].rearrange(
                    "o (n p d) -> (o n) p d", p=128, d=128).rearrange("n p d -> p n d")
                rbV = cc_out.ap()[0:1, SEC_V:SEC_V + T * D].rearrange(
                    "o (n p d) -> (o n) p d", p=128, d=128).rearrange("n p d -> p n d")
                em("gpsimd", lambda e, rbK=rbK: e.dma_start(
                    out=kwin_tm[:, 0:4, :], in_=rbK[:, 4:8, :]),
                    waits=[("cc", ccdone)], inc="dG", amt=16)
                em("gpsimd", lambda e, rbV=rbV: e.dma_start(
                    out=vwin[:, 0:4, :], in_=rbV[:, 4:8, :]), inc="dG", amt=16)
                for r_ in range(2):
                    cC = cc_out.ap()[r_:r_ + 1, SEC_CKV:SEC_CKV + 128 * 32].rearrange(
                        "o (p n) -> (o p) n", n=32)
                    cI = cc_out.ap()[r_:r_ + 1, SEC_KCI:SEC_KCI + 64 * 32].rearrange(
                        "o (p n) -> (o p) n", n=32)
                    em("gpsimd", lambda e, cC=cC, r_=r_: e.dma_start(
                        out=ckv_fm[:, r_ * 32:r_ * 32 + 32], in_=cC), inc="dG", amt=16)
                    em("gpsimd", lambda e, cI=cI, r_=r_: e.dma_start(
                        out=kci_cm[0:64, r_ * 32:r_ * 32 + 32], in_=cI), inc="dG", amt=16)
                rb_done = P.val("dG")

                ktv = None
                for k in range(KW):
                    pw = [("dG", rb_done)] if k == 0 else [("aa", ktv)]
                    mk = em("tensor", lambda e, k=k: e.transpose(
                        pb[2][:, 0:128], kwin_tm[:, k, :], idf[:]), waits=pw, inc="mm")
                    ktv = em("scalar", lambda e, k=k: e.activation(
                        out=kwin_fm[:, k * 128:k * 128 + 128], in_=pb[2][:, 0:128],
                        func=AF.Copy), waits=[("mm", mk)], inc="aa")
                vk = em("vector", lambda e: e.tensor_copy(out=sel32[:, 0:64], in_=ckv_fm[:, 0:64]),
                        waits=[("dG", rb_done)], inc="vv")
                mk = em("tensor", lambda e: e.transpose(pb[3][0:64, 0:128], sel32[:, 0:64], idf[:]),
                        waits=[("vv", vk)], inc="mm")
                em("scalar", lambda e: e.activation(
                    out=ckv_nm[0:64, :], in_=pb[3][0:64, 0:128], func=AF.Copy),
                    waits=[("mm", mk)], inc="aa")
                tap("kwin", kwin_tm[:, :, :].rearrange("p n d -> p (n d)"))
                tap("vwin", vwin[:, :, :].rearrange("p n d -> p (n d)"))
                tap("ckv", ckv_fm[:, :])
                tap("kci", kci_cm[:, :])

                selv = None
                for qt in range(NQT):
                    tsl = slice(qt * 128, qt * 128 + 128)
                    for ih in range(4):
                        pd = pb[4 + ih % 2]
                        lhs = qi_v[0:64, ih, tsl]
                        pw = [("dG", rb_done), ("vv", P.val("vv"))]
                        mk = em("tensor", lambda e, lhs=lhs, pd=pd: e.matmul(
                            pd[:, 0:64], lhs, kci_cm[0:64, 0:64], start=True, stop=True),
                            waits=pw, inc="mm")
                        vw = em("vector", lambda e, pd=pd: e.tensor_scalar_max(
                            out=m01[:], in0=pd[:, 0:64], scalar1=0.0),
                            waits=[("mm", mk)], inc="vv")
                        if ih == 0:
                            vw = em("vector", lambda e, qt=qt, ih=ih: e.tensor_scalar(
                                out=isc[:], in0=m01[:], scalar1=wgt_sb[:, qt, ih:ih + 1],
                                scalar2=None, op0=ALU.mult),
                                waits=[("vv", wgt_done)], inc="vv")
                        else:
                            vw = em("vector", lambda e, qt=qt, ih=ih: e.scalar_tensor_tensor(
                                out=isc[:], in0=m01[:], scalar=wgt_sb[:, qt, ih:ih + 1],
                                in1=isc[:], op0=ALU.mult, op1=ALU.add), inc="vv")
                    if "isc" in dbg and qt == 0:
                        tap("isc", isc[:, :])
                    vw = em("vector", lambda e, qt=qt: e.tensor_add(
                        out=work[:], in0=isc[:], in1=validadd[:, qt, :]),
                        waits=[("dS", valid_ld)] if qt == 0 else [], inc="vv")
                    vw = em("vector", lambda e: e.tensor_add(
                        out=work[:], in0=work[:], in1=tbrow[:]), inc="vv")
                    for r_ in range(4):
                        vw = em("vector", lambda e: e.max(out=mx8[:], in_=work[:]), inc="vv")
                        if r_ < 3:
                            vw = em("vector", lambda e: e.tensor_single_scalar(
                                out=m01[:], in_=work[:], scalar=mx8[:, 7:8], op=ALU.is_ge),
                                waits=[("vv", vw)], inc="vv")
                            vw = em("vector", lambda e: e.scalar_tensor_tensor(
                                out=work[:], in0=m01[:], scalar=-1e38, in1=work[:],
                                op0=ALU.mult, op1=ALU.add), inc="vv")
                    vw = em("vector", lambda e, qt=qt: e.tensor_add(
                        out=work[:], in0=isc[:], in1=validadd[:, qt, :]),
                        waits=[("vv", vw)], inc="vv")
                    vw = em("vector", lambda e: e.tensor_add(
                        out=work[:], in0=work[:], in1=tbrow[:]), inc="vv")
                    vw = em("vector", lambda e: e.tensor_single_scalar(
                        out=m01[:], in_=work[:], scalar=mx8[:, 7:8], op=ALU.is_ge), inc="vv")
                    vw = em("vector", lambda e, qt=qt: e.tensor_mul(
                        out=sel32[:, 0:64], in0=m01[:], in1=valid01[:, qt, :]), inc="vv")
                    mk = em("tensor", lambda e: e.transpose(pb[6][0:64, 0:128], sel32[:, 0:64], idf[:]),
                            waits=[("vv", vw)], inc="mm")
                    selv = em("scalar", lambda e, qt=qt: e.activation(
                        out=selT[:, qt * 128:qt * 128 + 128], in_=pb[6][0:64, 0:128],
                        func=AF.Copy), waits=[("mm", mk)], inc="aa")
                sel_done = P.val("aa")
                tap("selT", selT[:, :])
                final_marks = {"vv": P.val("vv"), "aa": P.val("aa")}

            if stage >= 3:
                hs_ld2 = em("sync", lambda e: e.dma_start(out=hsel2[:, :, :], in_=hsel_r),
                            waits=[("vv", qn_done)], inc="dS", amt=16)
                for hh in range(16):
                    aw = em("scalar", lambda e, hh=hh: e.activation(
                        out=sq[:, 0:512], in_=qv[:, hh, 0:512], func=AF.Square),
                        waits=[("vv", qn_done), ("dS", hs_ld2)] if hh == 0 else
                        [("mm", P.val("mm"))], inc="aa")
                    em("tensor", lambda e, hh=hh: e.matmul(
                        pb[6][0:16, 0:512], hsel2[:, hh, :], sq[:, 0:512],
                        start=(hh == 0), stop=(hh == 15)), waits=[("aa", aw)], inc="mm")
                    aw = em("scalar", lambda e, hh=hh: e.activation(
                        out=sq[:, 0:512], in_=qv[:, hh, 512:1024], func=AF.Square),
                        waits=[("mm", P.val("mm"))], inc="aa")
                    em("tensor", lambda e, hh=hh: e.matmul(
                        pb[7][0:16, 0:512], hsel2[:, hh, :], sq[:, 0:512],
                        start=(hh == 0), stop=(hh == 15)), waits=[("aa", aw)], inc="mm")
                mkq2 = P.val("mm")
                v1 = em("vector", lambda e: e.reduce_max(
                    out=red1[0:16, 0:1], in_=pb[6][0:16, 0:512], axis=AX.X),
                    waits=[("mm", mkq2)], inc="vv")
                v1 = em("vector", lambda e: e.reduce_max(
                    out=red2[0:16, 0:1], in_=pb[7][0:16, 0:512], axis=AX.X), inc="vv")
                v1 = em("vector", lambda e: e.tensor_max(
                    out=red1[0:16, 0:1], in0=red1[0:16, 0:1], in1=red2[0:16, 0:1]),
                    waits=[("vv", v1)], inc="vv")
                kv1 = None
                for k in range(KW):
                    aw = em("scalar", lambda e, k=k: e.activation(
                        out=sq[:, 0:128], in_=kwin_tm[:, k, :], func=AF.Square),
                        waits=[("vv", v1)] if k == 0 else [("vv", kv1)], inc="aa")
                    kv1 = em("vector", lambda e, k=k: e.reduce_sum(
                        out=red2[:, 0:1], in_=sq[:, 0:128], axis=AX.X),
                        waits=[("aa", aw)], inc="vv")
                    if k == 0:
                        kv1 = em("vector", lambda e: e.tensor_copy(
                            out=red2[:, 1:2], in_=red2[:, 0:1]), waits=[("vv", kv1)], inc="vv")
                    else:
                        kv1 = em("vector", lambda e: e.tensor_max(
                            out=red2[:, 1:2], in0=red2[:, 1:2], in1=red2[:, 0:1]),
                            waits=[("vv", kv1)], inc="vv")
                aw = em("scalar", lambda e: e.activation(
                    out=sq[0:64, 0:128], in_=ckv_nm[0:64, :], func=AF.Square),
                    waits=[("vv", kv1)], inc="aa")
                cv1 = em("vector", lambda e: e.reduce_sum(
                    out=red1[0:64, 1:2], in_=sq[0:64, 0:128], axis=AX.X),
                    waits=[("aa", aw)], inc="vv")
                mk = em("tensor", lambda e: e.transpose(
                    pb[6][0:1, 0:16], red1[0:16, 0:1], idf[0:16, 0:16]),
                    waits=[("vv", cv1)], inc="mm")
                mk = em("tensor", lambda e: e.transpose(
                    pb[6][0:1, 16:144], red2[:, 1:2], idf[:]), inc="mm")
                mk = em("tensor", lambda e: e.transpose(
                    pb[6][0:1, 144:208], red1[0:64, 1:2], idf[0:64, 0:64]), inc="mm")
                v1 = em("vector", lambda e: e.tensor_copy(
                    out=srow[0:1, 0:16], in_=pb[6][0:1, 0:16]),
                    waits=[("mm", mk)], inc="vv")
                v1 = em("vector", lambda e: e.reduce_max(
                    out=srow[0:1, 16:17], in_=pb[6][0:1, 16:144], axis=AX.X), inc="vv")
                v1 = em("vector", lambda e: e.reduce_max(
                    out=srow[0:1, 17:18], in_=pb[6][0:1, 144:208], axis=AX.X),
                    waits=[("vv", v1)], inc="vv")
                v1 = em("vector", lambda e: e.tensor_single_scalar(
                    out=srow[0:1, 32:48], in_=srow[0:1, 0:16], scalar=srow[0:1, 16:17],
                    op=ALU.mult), waits=[("vv", v1)], inc="vv")
                v1 = em("vector", lambda e: e.tensor_single_scalar(
                    out=srow[0:1, 48:64], in_=srow[0:1, 0:16], scalar=srow[0:1, 17:18],
                    op=ALU.mult), waits=[("vv", v1)], inc="vv")
                aw = em("scalar", lambda e: e.activation(
                    out=srow[0:1, 32:64], in_=srow[0:1, 32:64], func=AF.Ln,
                    bias=epsln[:]), waits=[("vv", v1)], inc="aa")
                aw = em("scalar", lambda e: e.activation(
                    out=srow[0:1, 32:64], in_=srow[0:1, 32:64], func=AF.Exp, scale=0.5),
                    waits=[("aa", aw)], inc="aa")
                v1 = em("vector", lambda e: e.tensor_max(
                    out=srow[0:1, 32:48], in0=srow[0:1, 32:48], in1=sink_row[0:1, :]),
                    waits=[("aa", aw)], inc="vv")
                v1 = em("vector", lambda e: e.tensor_scalar(
                    out=srow[0:1, 32:64], in0=srow[0:1, 32:64], scalar1=-1.0,
                    scalar2=None, op0=ALU.mult), waits=[("vv", v1)], inc="vv")
                for hh in range(16):
                    em("tensor", lambda e, hh=hh: e.matmul(
                        pb[7][:, hh:hh + 1], onesrf[:], srow[0:1, 32 + hh:33 + hh],
                        start=True, stop=True),
                        waits=[("vv", v1)] if hh == 0 else [])
                    em("tensor", lambda e, hh=hh: e.matmul(
                        pb[7][:, 16 + hh:17 + hh], onesrf[:], srow[0:1, 48 + hh:49 + hh],
                        start=True, stop=True), inc="mm" if hh == 15 else None)
                mkb = P.val("mm")
                v1 = em("vector", lambda e: e.tensor_copy(out=nSW[:, :], in_=pb[7][:, 0:16]),
                        waits=[("mm", mkb)], inc="vv")
                v1 = em("vector", lambda e: e.tensor_copy(out=nSP[:, :], in_=pb[7][:, 16:32]),
                        inc="vv")
                v1 = em("vector", lambda e: e.tensor_add(
                    out=sinkexp[:, :], in0=nSW[:, :], in1=sink_cols[:, :]),
                    waits=[("vv", v1)], inc="vv")
                preM_aa = em("scalar", lambda e: e.activation(
                    out=sinkexp[:, :], in_=sinkexp[:, :], func=AF.Exp),
                    waits=[("vv", v1)], inc="aa")

                NIT = H * NQT
                sc_mark = {}; exp_mark = {}; mask_mark = {}; av_mark = {}
                rec_mark = {}; avs_mark = {}; stt_mark = {}; tr_mark = {}; ev_mark = {}

                def s0(i):
                    qt, hh = i // H, i % H
                    X, Y = pb[0 + 2 * (i % 2)], pb[1 + 2 * (i % 2)]
                    qap = qv[:, hh, qt * 128:qt * 128 + 128]
                    pw0 = ([("aa", exp_mark[i - 2])] if i >= 2
                           else [("aa", max(sel_done, preM_aa))])
                    for krel in range(5):
                        dst = X[:, krel * 128:krel * 128 + 128] if krel < 4 else Y[:, 0:128]
                        em("tensor", lambda e, krel=krel, qt=qt, dst=dst, qap=qap: e.matmul(
                            dst, kwin_fm[:, (qt + krel) * 128:(qt + krel) * 128 + 128], qap,
                            start=True, stop=True), waits=pw0 if krel == 0 else [])
                    sc_mark[i] = em("tensor", lambda e, qap=qap, Y=Y: e.matmul(
                        Y[0:64, 128:256], ckv_fm[:, 0:64], qap, start=True, stop=True),
                        inc="mm")

                def s1(i):
                    hh = i % H
                    X, Y = pb[0 + 2 * (i % 2)], pb[1 + 2 * (i % 2)]
                    p3 = i % 3
                    mmw = av_mark[i - 3] if i >= 3 else sc_mark[i]
                    em("scalar", lambda e, X=X, p3=p3, hh=hh: e.activation(
                        out=esw[:, p3, 0:4, :], in_=X[:, :].rearrange("p (a b) -> p a b", b=128),
                        func=AF.Exp, bias=nSW[:, hh:hh + 1]),
                        waits=[("mm", mmw)], inc="aa")
                    em("scalar", lambda e, Y=Y, p3=p3, hh=hh: e.activation(
                        out=esw[:, p3, 4, :], in_=Y[:, 0:128], func=AF.Exp,
                        bias=nSW[:, hh:hh + 1]), inc="aa")
                    exp_mark[i] = em("scalar", lambda e, Y=Y, p3=p3, hh=hh: e.activation(
                        out=esp[:, p3, :], in_=Y[0:64, 128:256], func=AF.Exp,
                        bias=nSP[0:64, hh:hh + 1]), inc="aa")

                def s2(i):
                    qt, p3 = i // H, i % 3
                    em("vector", lambda e, p3=p3, qt=qt: e.tensor_mul(
                        out=esw[:, p3, :, :], in0=esw[:, p3, :, :], in1=maskjm[:, qt, :, :]),
                        waits=[("aa", exp_mark[i])], inc="vv")
                    mask_mark[i] = em("vector", lambda e, p3=p3, qt=qt: e.tensor_mul(
                        out=esp[:, p3, :], in0=esp[:, p3, :],
                        in1=selT[:, qt * 128:qt * 128 + 128]), inc="vv")

                def s3(i):
                    qt, p3 = i // H, i % 3
                    Z = pb[4 + i % 4]
                    for krel in range(5):
                        em("tensor", lambda e, krel=krel, p3=p3, qt=qt, Z=Z: e.matmul(
                            Z[:, 128:256], esw[:, p3, krel, :], vwin[:, qt + krel, :],
                            start=(krel == 0), stop=(krel == 4)),
                            waits=[("vv", mask_mark[i])] if krel == 0 else [])
                        em("tensor", lambda e, krel=krel, p3=p3, Z=Z: e.matmul(
                            Z[:, 256:257], esw[:, p3, krel, :], onescb[:],
                            start=(krel == 0), stop=(krel == 4)))
                    em("tensor", lambda e, p3=p3, Z=Z: e.matmul(
                        Z[:, 0:128], esp[:, p3, :], ckv_nm[0:64, :],
                        start=True, stop=True))
                    av_mark[i] = em("tensor", lambda e, p3=p3, Z=Z: e.matmul(
                        Z[:, 257:258], esp[:, p3, :], onescb[0:64, :],
                        start=True, stop=True), inc="mm")

                def s4(i):
                    hh, p2 = i % H, i % 2
                    Z = pb[4 + i % 4]
                    v1 = em("vector", lambda e, Z=Z, p2=p2, hh=hh: e.tensor_add(
                        out=ssw_c[:, p2:p2 + 1], in0=Z[:, 256:257],
                        in1=sinkexp[:, hh:hh + 1]), waits=[("mm", av_mark[i])], inc="vv")
                    rw = [("vv", v1)] + ([("aa", avs_mark[i - 2])] if i >= 2 else [])
                    v1 = em("vector", lambda e, p2=p2: e.reciprocal(
                        out=rsw_c[:, p2:p2 + 1], in_=ssw_c[:, p2:p2 + 1]),
                        waits=rw, inc="vv")
                    v1 = em("vector", lambda e, Z=Z, p2=p2: e.tensor_scalar(
                        out=ssp_c[:, p2:p2 + 1], in0=Z[:, 257:258], scalar1=1e-30,
                        scalar2=None, op0=ALU.add), inc="vv")
                    rec_mark[i] = em("vector", lambda e, p2=p2: e.reciprocal(
                        out=rsp_c[:, p2:p2 + 1], in_=ssp_c[:, p2:p2 + 1]),
                        waits=[("vv", v1)], inc="vv")

                def s5(i):
                    p2 = i % 2
                    Z = pb[4 + i % 4]
                    avs_mark[i] = em("scalar", lambda e, Z=Z, p2=p2: e.activation(
                        out=avs[:, p2, :], in_=Z[:, 128:256], func=AF.Copy,
                        scale=rsw_c[:, p2:p2 + 1]), waits=[("vv", rec_mark[i])], inc="aa")

                def s6(i):
                    p2 = i % 2
                    Z = pb[4 + i % 4]
                    w6 = [("aa", avs_mark[i])] + ([("mm", tr_mark[i - 2])] if i >= 2 else [])
                    stt_mark[i] = em("vector", lambda e, Z=Z, p2=p2: e.scalar_tensor_tensor(
                        out=attn_tm[:, p2, :], in0=Z[:, 0:128],
                        scalar=rsp_c[:, p2:p2 + 1], in1=avs[:, p2, :],
                        op0=ALU.mult, op1=ALU.add), waits=w6, inc="vv")

                def s7(i):
                    p2 = i % 2
                    TY = pb[1 + 2 * p2]
                    w7 = [("vv", stt_mark[i])] + ([("aa", ev_mark[i - 2])] if i >= 2 else [])
                    tr_mark[i] = em("tensor", lambda e, TY=TY, p2=p2: e.transpose(
                        TY[:, 256:384], attn_tm[:, p2, :], idf[:]),
                        waits=w7, inc="mm")

                def s8(i):
                    qt, hh, p2 = i // H, i % H, i % 2
                    TY = pb[1 + 2 * p2]
                    ev_mark[i] = em("scalar", lambda e, TY=TY, hh=hh, qt=qt: e.activation(
                        out=Hb[:, hh, qt * 128:qt * 128 + 128], in_=TY[:, 256:384],
                        func=AF.Copy), waits=[("mm", tr_mark[i])], inc="aa")

                for step in range(NIT + 9):
                    if step < NIT:
                        s0(step)
                    if 0 <= step - 1 < NIT:
                        s1(step - 1)
                    if 0 <= step - 6 < NIT:
                        s6(step - 6)
                    if 0 <= step - 4 < NIT:
                        s4(step - 4)
                    if 0 <= step - 2 < NIT:
                        s2(step - 2)
                    if 0 <= step - 3 < NIT:
                        s3(step - 3)
                    if 0 <= step - 5 < NIT:
                        s5(step - 5)
                    if 0 <= step - 7 < NIT:
                        s7(step - 7)
                    if 0 <= step - 8 < NIT:
                        s8(step - 8)
                attn_done_aa = P.val("aa")
                tap("attn", Hb[:, :, :].rearrange("p c t -> p (c t)"))

                em("gpsimd", lambda e: e.dma_start(out=Wv(0, (16, 512)), in_=gw_r),
                   waits=[("mm", P.val("mm")), ("aa", attn_done_aa),
                          ("vv", P.val("vv"))], inc="dG", amt=16)
                gw_ld = P.val("dG")
                og_ev = {}
                for g_ in range(4):
                    for ocg in range(4):
                        for tt in range(2):
                            it3 = (g_ * 4 + ocg) * 2 + tt
                            tsl = slice(tt * 512, tt * 512 + 512)
                            pband = pb[it3 % 2]
                            pw = [("dG", gw_ld), ("aa", attn_done_aa)]
                            if it3 >= 2:
                                pw.append(("aa", og_ev[it3 - 2]))
                            for ic in range(4):
                                mk = em("tensor", lambda e, g_=g_, ic=ic, ocg=ocg, tsl=tsl, pband=pband: e.matmul(
                                    pband[:, 0:512],
                                    Wv(0, (16, 4, 128))[:, g_ * 4 + ic, ocg],
                                    Hb[:, g_ * 4 + ic, tsl],
                                    start=(ic == 0), stop=(ic == 3)),
                                    waits=pw if ic == 0 else [],
                                    inc="mm" if ic == 3 else None)
                            og_ev[it3] = em("scalar", lambda e, g_=g_, ocg=ocg, tsl=tsl, pband=pband: e.activation(
                                out=qv[:, g_ * 4 + ocg, tsl], in_=pband[:, 0:512],
                                func=AF.Copy), waits=[("mm", mk)], inc="aa")
                og_done = P.val("aa")
                tap("og", qv[:, :, :].rearrange("p c t -> p (c t)"))

                fin_ev = {}
                for hf in range(2):
                    em("gpsimd", lambda e, hf=hf: e.dma_start(
                        out=Wv(0, (8, 2048)), in_=fin_r[:, hf * 8:hf * 8 + 8, :]),
                        waits=[("aa", og_done), ("mm", P.val("mm"))], inc="dG", amt=16)
                    fin_ld = P.val("dG")
                    for oc in range(16):
                        for tt in range(2):
                            it4 = oc * 2 + tt
                            tsl = slice(tt * 512, tt * 512 + 512)
                            pband = pb[it4 % 4]
                            pw2 = [("dG", fin_ld)]
                            if it4 >= 4:
                                pw2.append(("vv", fin_ev[(hf, it4 - 4)]))
                            elif hf == 1:
                                pw2.append(("vv", fin_ev[(0, 28 + it4)]))
                            for kc in range(8):
                                mk = em("tensor", lambda e, kc=kc, oc=oc, tsl=tsl, pband=pband, hf=hf: e.matmul(
                                    pband[:, 0:512],
                                    Wv(0, (8, 16, 128))[:, kc, oc],
                                    qv[:, hf * 8 + kc, tsl],
                                    start=(kc == 0), stop=(kc == 7)),
                                    waits=pw2 if kc == 0 else [],
                                    inc="mm" if kc == 7 else None)
                            fin_ev[(hf, it4)] = em("vector", lambda e, oc=oc, tsl=tsl, pband=pband: e.tensor_add(
                                out=R[:, oc, tsl], in0=R[:, oc, tsl], in1=pband[:, 0:512]),
                                waits=[("mm", mk)], inc="vv")
                final_marks = {"vv": P.val("vv"), "aa": P.val("aa")}

            if stage >= 4:
                x2_vv = final_marks["vv"]
                tap("x2", R[:, :, :].rearrange("p c t -> p (c t)"))
                ln3 = {0: ln_full(0, pre_mm=[("vv", x2_vv)]), 1: None}
                ln3[1] = ln_full(1)
                tap("r0", r0row[:])
                tap("r1", varrow[:])
                f2_vv, f2_mm = ffn_pass(ln3)
                tap("h3", Hb[:, :, :].rearrange("p c t -> p (c t)"))
                final_marks = {"vv": f2_vv, "aa": P.val("aa")}

            out_vv = final_marks["vv"]
            dS_out0 = P.val("dS")
            for i in range(8):
                for c in range(NCH):
                    pw = []
                    if i == 0 and c == 0:
                        pw = [("vv", out_vv), ("mm", P.val("mm")),
                              ("dG", P.val("dG"))]
                    else:
                        pw = [("aa", P.val("aa"))]
                    if c == 0 and i >= 2:
                        pw.append(("dS", dS_out0 + 16 * (i - 1)))
                    if c == 0 and i > 0:
                        pw.append(("vv", out_vv))
                    mk = em("tensor", lambda e, i=i, c=c: e.transpose(
                        pb[7][:, 0:128], R[:, c, i * 128:i * 128 + 128], idf[:]),
                        waits=pw, inc="mm")
                    em("scalar", lambda e, i=i, c=c: e.activation(
                        out=outbuf[:, i % 2, c * 128:c * 128 + 128], in_=pb[7][:, 0:128],
                        func=AF.Copy), waits=[("mm", mk)], inc="aa")
                aw = P.val("aa")
                em("sync", lambda e, i=i: e.dma_start(
                    out=y[i * 128:i * 128 + 128, :], in_=outbuf[:, i % 2, :]),
                    waits=[("aa", aw)], inc="dS", amt=16)
            st_done = P.val("dS")
            em("sync", None, waits=[("dS", st_done)])

            with nc.Block() as block:
                @block.sync
                def _(e):
                    P.replay("sync", e, sems)

                @block.tensor
                def _(e):
                    P.replay("tensor", e, sems)

                @block.vector
                def _(e):
                    P.replay("vector", e, sems)

                @block.scalar
                def _(e):
                    P.replay("scalar", e, sems)

                @block.gpsimd
                def _(e):
                    P.replay("gpsimd", e, sems)

    return nc



def _arrange(inputs):
    g = {}
    wg = np.asarray(inputs["L0_ffn_w_gate"])
    wu = np.asarray(inputs["L0_ffn_w_up"])
    wd = np.asarray(inputs["L0_ffn_w_down"])
    g["wg_r"] = np.ascontiguousarray(wg.reshape(16, 128, 32, 256).transpose(1, 2, 0, 3))
    g["wu_r"] = np.ascontiguousarray(wu.reshape(16, 128, 32, 256).transpose(1, 2, 0, 3))
    g["wd_r"] = np.ascontiguousarray(wd.reshape(16, 4, 128, 4, 512).transpose(2, 3, 0, 1, 4))
    g["sg_r"] = np.ascontiguousarray(np.asarray(inputs["L0_ffn_s_gate"]).reshape(64, 128).T)
    g["su_r"] = np.ascontiguousarray(np.asarray(inputs["L0_ffn_s_up"]).reshape(64, 128).T)
    g["sd_r"] = np.ascontiguousarray(np.asarray(inputs["L0_ffn_s_down"]).reshape(16, 128).T)
    g["wqc_r"] = np.ascontiguousarray(
        np.asarray(inputs["L0_attn_w_qc"]).reshape(16, 128, 1024).transpose(1, 0, 2))
    g["wqup_r"] = np.ascontiguousarray(
        np.asarray(inputs["L0_attn_w_qup"]).reshape(8, 128, 2048).transpose(1, 0, 2))
    for nm, key in (("wkvb_r", "L0_attn_w_kv_b"), ("wzb_r", "L0_attn_w_z_b"),
                    ("wk_r", "L0_attn_w_k"), ("wv_r", "L0_attn_w_v"),
                    ("wikv_r", "L0_attn_wi_kv"), ("wiz_r", "L0_attn_wi_z"),
                    ("wdq_r", "L0_attn_w_dq"), ("ww_r", "L0_attn_w_w")):
        w = np.asarray(inputs[key])
        g[nm] = np.ascontiguousarray(w.reshape(16, 128, w.shape[1]).transpose(1, 0, 2))
    g["wiuq_r"] = np.ascontiguousarray(np.asarray(inputs["L0_attn_w_iuq"]))
    g["gw_r"] = np.ascontiguousarray(
        np.asarray(inputs["L0_attn_group_w"]).reshape(4, 4, 128, 512)
        .transpose(2, 0, 1, 3).reshape(128, 16, 512))
    g["fin_r"] = np.ascontiguousarray(
        np.asarray(inputs["L0_attn_final_w"]).reshape(16, 128, 2048).transpose(1, 0, 2))
    g["qnw_r"] = np.ascontiguousarray(np.asarray(inputs["L0_attn_q_norm_w"]).reshape(1, 128))
    g["knw_r"] = np.ascontiguousarray(np.asarray(inputs["L0_attn_k_norm_w"]).reshape(1, 128))
    g["sink_r"] = np.ascontiguousarray(np.asarray(inputs["L0_attn_sink_logit"]).reshape(1, 16))
    g["biasb_r"] = np.ascontiguousarray(np.asarray(inputs["L0_attn_bias_b"]))
    g["biasi_r"] = np.ascontiguousarray(np.asarray(inputs["L0_attn_bias_i"]))
    g["idf_r"] = np.eye(128, dtype=np.float32)
    sg_ = np.arange(128)
    sig = np.where(sg_ < 32, sg_ + 32, np.where(sg_ < 64, sg_ - 32, sg_))
    sw_ = np.zeros((128, 128), BF)
    sw_[sig, np.arange(128)] = 1
    g["swp_r"] = sw_
    hs = np.zeros((128, 16, 16), BF)
    for c in range(16):
        hs[:, c, c] = 1
    g["hsel_r"] = hs
    qs = np.zeros((16, 16, 128), BF)
    for h_ in range(16):
        qs[h_, h_, :] = 1
    g["qsel_r"] = qs
    return g


def _percore_consts(hc):
    t0 = hc * T
    p = np.arange(t0, t0 + T)
    c = {}
    inv = 1.0 / (10000.0 ** (np.arange(0, RD, 2, dtype=np.float64) / RD))
    ang = p[:, None] * inv
    cos = np.cos(ang).astype(np.float32)
    sin = np.sin(ang).astype(np.float32)
    Cm = np.ones((128, 1024), np.float32)
    Sm = np.zeros((128, 1024), np.float32)
    Cm[0:32] = cos.T
    Cm[32:64] = cos.T
    Sm[0:32] = -sin.T
    Sm[32:64] = sin.T
    c["ropeC_r"] = Cm.astype(BF)
    c["ropeS_r"] = Sm.astype(BF)
    c["coskt_r"] = np.ascontiguousarray(cos.reshape(8, 128, 32).transpose(1, 0, 2)).astype(BF)
    c["sinkt_r"] = np.ascontiguousarray(sin.reshape(8, 128, 32).transpose(1, 0, 2)).astype(BF)
    jglob = np.concatenate([np.arange(t0 - 512, t0), np.arange(t0, t0 + T)])
    mk = np.zeros((128, NQT, 5, 128), np.float32)
    for qt in range(NQT):
        qpos = p[qt * 128:(qt + 1) * 128]
        for krel in range(5):
            kpos = jglob[(qt + krel) * 128:(qt + krel) * 128 + 128]
            valid = (kpos[:, None] >= 0) & (kpos[:, None] <= qpos[None, :]) & \
                    (qpos[None, :] - kpos[:, None] < SWIN)
            mk[:, qt, krel, :] = valid
    c["maskjm_r"] = mk.astype(BF)
    val = ((np.arange(NB) * M + (M - 1))[None, :] < p[:, None]).astype(np.float32)
    v01 = np.ascontiguousarray(val.reshape(NQT, 128, NB).transpose(1, 0, 2))
    c["valid01_r"] = v01.astype(BF)
    c["validadd_r"] = ((v01 - 1.0) * 1e30).astype(BF)
    return c


_TBROW = np.broadcast_to(
    (-np.arange(NB, dtype=np.float64) * 1e-20).astype(np.float32), (128, NB)).copy()

_NC_CACHE = {}


def kernel(**inputs):
    key = (STAGE, TAPS, SUB)
    if key not in _NC_CACHE:
        _NC_CACHE[key] = build_program(STAGE, TAPS, SUB)
    nc = _NC_CACHE[key]
    shared = _arrange(inputs)
    shared["tbrow_r"] = _TBROW
    x = np.asarray(inputs["L0_attn_h"])
    pc = [_percore_consts(0), _percore_consts(1)]
    in_maps = []
    for core in range(8):
        b, hc = core // 2, core % 2
        mmap = dict(shared)
        mmap.update(pc[hc])
        mmap["xin"] = np.ascontiguousarray(x[b, hc * T:(hc + 1) * T].T)
        in_maps.append(mmap)
    res = run_bass_kernel_spmd(nc, in_maps, core_ids=list(range(8)))
    out = np.empty((B, TSEQ, DM), np.float32)
    for core in range(8):
        b, hc = core // 2, core % 2
        out[b, hc * T:(hc + 1) * T] = res.results[core]["y"]
    return out


# version 07
# speedup vs baseline: 1.2282x; 1.0166x over previous
import os
import numpy as np
import ml_dtypes
from contextlib import ExitStack

import concourse.bass as bass
import concourse.mybir as mybir
from concourse.bass_utils import run_bass_kernel_spmd

F32 = mybir.dt.float32
BF16 = mybir.dt.bfloat16
I8 = mybir.dt.int8
AF = mybir.ActivationFunctionType
ALU = mybir.AluOpType
AX = mybir.AxisListType
BF = ml_dtypes.bfloat16

B, TSEQ, DM = 4, 2048, 2048
T = 1024
NCH = DM // 128
H, D, RD = 16, 128, 64
M, NB = 32, 64
G, DG, NIH, CI = 4, 512, 4, 64
SWIN = 512
INTER = 8192
NI = INTER // 128
FT = 512
NQT = T // 128
KW = 12
EPS_LN, EPS_RMS = 1e-5, 1e-6

SEC_K = 0
SEC_V = SEC_K + T * D
SEC_CKV = SEC_V + T * D
SEC_KCI = SEC_CKV + 128 * 32
BLOB_N = SEC_KCI + 64 * 32

STAGE = int(os.environ.get("MK_STAGE", "4"))
SUB = os.environ.get("MK_SUB", "full")
TAPS = tuple(t for t in os.environ.get("MK_TAPS", "").split(",") if t)


class Prog:
    ENGS = ("sync", "tensor", "vector", "scalar", "gpsimd")

    def __init__(self):
        self.items = {e: [] for e in self.ENGS}
        self.cnt = {}

    def val(self, sem):
        return self.cnt.get(sem, 0)

    def emit(self, eng, fn=None, waits=(), inc=None, amt=1):
        w = [(s, v) for (s, v) in waits if v and v > 0]
        self.items[eng].append((w, fn, inc, amt))
        if inc is not None:
            self.cnt[inc] = self.cnt.get(inc, 0) + amt
            return self.cnt[inc]
        return None

    def replay(self, eng_name, eng, sems):
        for w, fn, inc, amt in self.items[eng_name]:
            for s, v in w:
                eng.wait_ge(sems[s], v)
            ins = fn(eng) if fn is not None else None
            if inc is not None:
                if ins is None:
                    ins = eng.nop()
                ins.then_inc(sems[inc], amt)


def build_program(stage, taps, sub="full"):
    nc = bass.Bass("TRN2", target_bir_lowering=False)

    def din(name, shape, dtype):
        return nc.dram_tensor(name, shape, dtype, kind="ExternalInput").ap()

    xin = din("xin", [DM, T], F32)
    wg_r = din("wg_r", [128, 32, 16, 256], I8)
    wu_r = din("wu_r", [128, 32, 16, 256], I8)
    wd_r = din("wd_r", [128, 4, 16, 4, 512], I8)
    sg_r = din("sg_r", [128, 64], F32)
    su_r = din("su_r", [128, 64], F32)
    sd_r = din("sd_r", [128, 16], F32)
    wqc_r = din("wqc_r", [128, 16, 1024], F32)
    wqup_r = din("wqup_r", [128, 8, 2048], F32)
    wkvb_r = din("wkvb_r", [128, 16, 128], F32)
    wzb_r = din("wzb_r", [128, 16, 128], F32)
    wk_r = din("wk_r", [128, 16, 128], F32)
    wv_r = din("wv_r", [128, 16, 128], F32)
    wikv_r = din("wikv_r", [128, 16, 64], F32)
    wiz_r = din("wiz_r", [128, 16, 64], F32)
    wdq_r = din("wdq_r", [128, 16, 64], F32)
    ww_r = din("ww_r", [128, 16, 4], F32)
    wiuq_r = din("wiuq_r", [64, 256], F32)
    gw_r = din("gw_r", [128, 16, 512], F32)
    fin_r = din("fin_r", [128, 16, 2048], F32)
    qnw_r = din("qnw_r", [1, 128], F32)
    knw_r = din("knw_r", [1, 128], F32)
    sink_r = din("sink_r", [1, 16], F32)
    biasb_r = din("biasb_r", [32, 128], F32)
    biasi_r = din("biasi_r", [32, 64], F32)
    ropeC_r = din("ropeC_r", [128, 1024], BF16)
    ropeS_r = din("ropeS_r", [128, 1024], BF16)
    swp_r = din("swp_r", [128, 128], BF16)
    coskt_r = din("coskt_r", [128, 8, 32], BF16)
    sinkt_r = din("sinkt_r", [128, 8, 32], BF16)
    maskjm_r = din("maskjm_r", [128, 8, 5, 128], BF16)
    valid01_r = din("valid01_r", [128, 8, 64], BF16)
    validadd_r = din("validadd_r", [128, 8, 64], BF16)
    tbrow_r = din("tbrow_r", [128, 64], F32)
    idf_r = din("idf_r", [128, 128], F32)
    hsel_r = din("hsel_r", [128, 16, 16], BF16)
    qsel_r = din("qsel_r", [16, 16, 128], BF16)

    y = nc.dram_tensor("y", [T, DM], F32, kind="ExternalOutput").ap()
    dbg = {}
    tap_shapes = {"h": [128, NCH * 1024], "q": [128, 16 * 1024],
                  "kwin": [128, KW * 128], "vwin": [128, KW * 128],
                  "ckv": [128, 64], "kci": [64, 64], "selT": [64, NQT * 128],
                  "attn": [128, NCH * 1024], "og": [128, 16 * 1024],
                  "isc": [128, 64], "x2": [128, NCH * 1024], "h3": [128, NCH * 1024], "r0": [1, 512], "r1": [1, 512]}
    for tp in taps:
        dbg[tp] = nc.dram_tensor("dbg_" + tp, tap_shapes[tp], F32,
                                 kind="ExternalOutput").ap()

    cc_in = nc.dram_tensor("cc_in", [1, BLOB_N], BF16)
    cc_out = nc.dram_tensor("cc_out", [2, BLOB_N], BF16)

    P = Prog()
    em = P.emit
    ctx = ExitStack()
    sb = lambda n, s, d: ctx.enter_context(nc.sbuf_tensor(n, s, d))
    psT = lambda n, s, d: ctx.enter_context(nc.psum_tensor(n, s, d))

    with ctx:
        R = sb("R", [128, NCH, 1024], F32)
        Hb = sb("Hb", [128, NCH, 1024], BF16)
        A = sb("A", [128, 32768], BF16)
        W = sb("W", [128, 16384], BF16)
        fv = A[:, :].rearrange("p (c t) -> p c t", c=NI)
        qv = A[:, 0:16384].rearrange("p (c t) -> p c t", c=16)
        qcv = A[:, 16384:24576].rearrange("p (c t) -> p c t", c=8)
        ropeC = A[:, 16384:17408]
        ropeS = A[:, 17408:18432]
        tA = A[:, 18432:19456]
        qsel = A[0:16, 19456:21504].rearrange("p (h t) -> p h t", h=16)
        stats32 = A[0:16, 21504:23552].bitcast(F32)
        hsel = A[:, 23552:23808].rearrange("p (c h) -> p c h", c=16)
        rstd_hs = A[0:16, 23808:24832]
        zb32 = A[:, 16384:18432].bitcast(F32)
        cbf = A[:, 18432:19456]
        qi_v = A[0:64, 19456:23552].rearrange("p (c t) -> p c t", c=4)
        dq_v = A[0:64, 23552:24576]
        maskjm = A[:, 24832:29952].rearrange("p (q k t) -> p q k t", q=8, k=5)
        esw = W[:, 0:1920].rearrange("p (a k t) -> p a k t", a=3, k=5)
        esp = W[0:64, 1920:2304].rearrange("p (a t) -> p a t", a=3)
        avs = W[:, 2304:2560].rearrange("p (a t) -> p a t", a=2)
        attn_tm = W[:, 2560:3072].bitcast(F32).rearrange("p (a t) -> p a t", a=2)
        kwin_fm = W[:, 3072:4608]
        selT = W[0:64, 5376:6400]
        kwin_tm = W[:, 11328:14400].bitcast(F32).rearrange("p (k d) -> p k d", k=KW)
        vwin = W[:, 14400:15936].rearrange("p (k d) -> p k d", k=KW)
        outbuf = W[:, 0:8192].bitcast(F32).rearrange("p (a t) -> p a t", a=2)
        sq = sb("sq", [128, 512], BF16)
        usb = sb("usb", [128, 2, 512], BF16)
        sel32 = sb("sel32", [128, 64], F32)
        wgt_sb = sb("wgt_sb", [128, NQT, 4], F32)
        isc = sb("isc", [128, 64], F32)
        work = sb("work", [128, 64], F32)
        m01 = sb("m01", [128, 64], F32)
        mx8 = sb("mx8", [128, 8], F32)
        sgc = sb("sgc", [128, 64], F32)
        suc = sb("suc", [128, 64], F32)
        sdh = sb("sdh", [128, 16], F32)
        coskt = sb("coskt", [128, 8, 32], BF16)
        sinkt = sb("sinkt", [128, 8, 32], BF16)
        valid01 = A[:, 29952:30464].rearrange("p (q n) -> p q n", q=8)
        validadd = A[:, 30464:30976].rearrange("p (q n) -> p q n", q=8)
        tbrow = sb("tbrow", [128, 64], F32)
        idf = sb("idf", [128, 128], F32)
        swp = sb("swp", [128, 128], BF16)
        onescf = sb("onescf", [128, 1], F32)
        onescb = sb("onescb", [128, 1], BF16)
        onesrf = sb("onesrf", [1, 128], F32)
        onesrb = sb("onesrb", [1, 128], BF16)
        qnw_row = sb("qnw_row", [1, 128], F32)
        knw_row = sb("knw_row", [1, 128], F32)
        qnw_col = sb("qnw_col", [128, 1], F32)
        knw_bc = sb("knw_bc", [128, 128], F32)
        knw_col = sb("knw_col", [128, 1], F32)
        sink_row = sb("sink_row", [1, 16], F32)
        sink_cols = sb("sink_cols", [128, 16], F32)
        biasbT = sb("biasbT", [128, 32], F32)
        biasiT = sb("biasiT", [64, 32], F32)
        biasb_sb = A[0:32, 0:256].bitcast(F32)
        biasi_sb = A[0:32, 256:384].bitcast(F32)
        epsln = sb("epsln", [1, 1], F32)
        epsrms = sb("epsrms", [128, 1], F32)
        lnC = sb("lnC", [128, 1], F32)
        wiuq_sb = W[0:64, 15936:16192]
        r0row = sb("r0row", [1, 512], BF16)
        r1row = sb("r1row", [1, 512], BF16)
        varrow = sb("varrow", [1, 512], F32)
        rstd_r32 = sb("rstd_r32", [1, 32], F32)
        cmp32 = sb("cmp32", [128, 32], F32)
        gmx = sb("gmx", [128, 32], F32)
        gsm = sb("gsm", [128, 32], F32)
        grec = sb("grec", [128, 32], F32)
        rms_c = sb("rms_c", [128, 1], F32)
        ckv_fm = sb("ckv_fm", [128, 64], BF16)
        ckv_nm = sb("ckv_nm", [64, 128], BF16)
        kci_cm = sb("kci_cm", [64, 64], BF16)
        m11 = sb("m11", [1, 8], F32)
        srow = W[0:1, 6400:7424].bitcast(F32)
        nSW = sb("nSW", [128, 16], F32)
        nSP = sb("nSP", [128, 16], F32)
        sinkexp = sb("sinkexp", [128, 16], F32)
        hsel2 = A[:, 30976:31232].rearrange("p (c h) -> p c h", c=16)
        ssw_c = sb("ssw_c", [128, 2], F32)
        ssp_c = sb("ssp_c", [128, 2], F32)
        rsw_c = sb("rsw_c", [128, 2], F32)
        rsp_c = sb("rsp_c", [128, 2], F32)
        red1 = sb("red1", [128, 2], F32)
        red2 = sb("red2", [128, 2], F32)

        pb = [psT(f"pb{i}", [128, 512], F32) for i in range(8)]

        semctx = ExitStack()
        sems = {n: semctx.enter_context(nc.semaphore(n))
                for n in ("dS", "dG", "mm", "vv", "aa", "pp", "cc")}

        with semctx:
            def Wv(off, shape):
                n = int(np.prod(shape))
                v = W[:, off:off + n]
                if len(shape) > 1:
                    lets = "abcdef"[: len(shape)]
                    pat = f"p ({' '.join(lets)}) -> p {' '.join(lets)}"
                    v = v.rearrange(pat, **{lets[i]: shape[i] for i in range(len(shape))})
                return v

            def tap(name, src_ap):
                if name in dbg:
                    waits = [("aa", P.val("aa")), ("vv", P.val("vv")),
                             ("mm", P.val("mm")), ("pp", P.val("pp"))]
                    em("gpsimd", lambda e, s=src_ap, d=dbg[name]: e.dma_start(out=d, in_=s),
                       waits=waits, inc="dG", amt=16)

            for o, i in ((sgc[:], sg_r), (suc[:], su_r), (sdh[:], sd_r),
                         (coskt[:], coskt_r), (sinkt[:], sinkt_r),
                         (tbrow[:], tbrow_r), (idf[:], idf_r),
                         (swp[:], swp_r),
                         (qnw_row[:], qnw_r), (knw_row[:], knw_r),
                         (sink_row[:], sink_r),
                         (biasb_sb, biasb_r), (biasi_sb, biasi_r)):
                em("sync", lambda e, o=o, i=i: e.dma_start(out=o, in_=i), inc="dS", amt=16)
            em("sync", lambda e: e.dma_start(
                out=R[:], in_=xin.rearrange("(c p) t -> p c t", p=128)), inc="dS", amt=16)
            ld_done = P.val("dS")

            def memsets(e):
                e.memset(onescf[:], 1.0)
                e.memset(onescb[:], 1.0)
                e.memset(onesrf[:], 1.0)
                e.memset(onesrb[:], 1.0)
                e.memset(epsln[:], EPS_LN)
                e.memset(epsrms[:], EPS_RMS)
                return e.memset(lnC[:], float(np.log(1.0 / np.sqrt(128.0))))
            vmark = em("vector", memsets, inc="vv")
            vmark = em("vector", lambda e: e.tensor_scalar(
                out=sdh[:], in0=sdh[:], scalar1=0.5, scalar2=None, op0=ALU.mult),
                waits=[("dS", ld_done)], inc="vv")
            mk = em("tensor", lambda e: e.transpose(pb[6][:, 0:1], qnw_row[:], idf[0:1, 0:1]),
                    waits=[("dS", ld_done), ("vv", vmark)], inc="mm")
            mk = em("tensor", lambda e: e.transpose(pb[6][:, 128:129], knw_row[:], idf[0:1, 0:1]),
                    inc="mm")
            mk = em("tensor", lambda e: e.transpose(pb[6][:, 256:288], biasb_sb, idf[0:32, 0:32]),
                    inc="mm")
            mk = em("tensor", lambda e: e.transpose(pb[6][0:64, 288:320], biasi_sb, idf[0:32, 0:32]),
                    inc="mm")
            mk = em("tensor", lambda e: e.matmul(pb[6][:, 320:448], onesrf[:], knw_row[:],
                                                 start=True, stop=True), inc="mm")
            mk = em("tensor", lambda e: e.matmul(pb[6][:, 448:464], onesrf[:], sink_row[:],
                                                 start=True, stop=True), inc="mm")
            vmark = em("vector", lambda e: e.tensor_copy(out=qnw_col[:], in_=pb[6][:, 0:1]),
                       waits=[("mm", mk)], inc="vv")
            vmark = em("vector", lambda e: e.tensor_copy(out=knw_col[:], in_=pb[6][:, 128:129]), inc="vv")
            vmark = em("vector", lambda e: e.tensor_copy(out=biasbT[:, :], in_=pb[6][:, 256:288]), inc="vv")
            vmark = em("vector", lambda e: e.tensor_copy(out=biasiT[:, :], in_=pb[6][0:64, 288:320]), inc="vv")
            vmark = em("vector", lambda e: e.tensor_copy(out=knw_bc[:], in_=pb[6][:, 320:448]), inc="vv")
            init_done = em("vector", lambda e: e.tensor_copy(out=sink_cols[:], in_=pb[6][:, 448:464]), inc="vv")

            def ln_full(tau, pre_mm=(), pre_vv=(), pre_aa=()):
                ts = slice(tau * FT, tau * FT + FT)
                auto = [("vv", P.val("vv")), ("pp", P.val("pp")), ("aa", P.val("aa"))]
                entry_mm = P.val("mm")
                for c in range(NCH):
                    em("tensor", lambda e, c=c, ts=ts: e.matmul(
                        pb[4][0:1, :], onescf[:], R[:, c, ts],
                        start=(c == 0), stop=(c == NCH - 1)),
                        waits=(list(pre_mm) + auto if c == 0 else []),
                        inc="mm" if c == NCH - 1 else None)
                sqmk = entry_mm
                for c in range(NCH):
                    aw = em("scalar", lambda e, c=c, ts=ts: e.activation(
                        out=sq[:], in_=R[:, c, ts], func=AF.Square),
                        waits=(list(pre_aa) if c == 0 else []) + [("mm", sqmk)],
                        inc="aa")
                    sqmk = em("tensor", lambda e, c=c: e.matmul(
                        pb[5][0:1, :], onescb[:], sq[:],
                        start=(c == 0), stop=(c == NCH - 1)),
                        waits=[("aa", aw)], inc="mm")
                mm_sums = P.val("mm")
                vw = em("vector", lambda e: e.tensor_scalar(
                    out=r0row[:], in0=pb[4][0:1, :], scalar1=1.0 / DM, scalar2=None,
                    op0=ALU.mult), waits=[("mm", mm_sums)] + list(pre_vv), inc="vv")
                vw = em("vector", lambda e: e.tensor_mul(out=r1row[:], in0=r0row[:], in1=r0row[:]),
                        waits=[("vv", vw)], inc="vv")
                vw = em("vector", lambda e: e.scalar_tensor_tensor(
                    out=varrow[:], in0=pb[5][0:1, :], scalar=1.0 / DM, in1=r1row[:],
                    op0=ALU.mult, op1=ALU.subtract), waits=[("vv", vw)], inc="vv")
                aw = em("scalar", lambda e: e.activation(
                    out=varrow[:], in_=varrow[:], func=AF.Ln, bias=epsln[:]),
                    waits=[("vv", vw)], inc="aa")
                aw = em("scalar", lambda e: e.activation(
                    out=varrow[:], in_=varrow[:], func=AF.Exp, scale=-0.5),
                    waits=[("aa", aw)], inc="aa")
                mk1 = em("tensor", lambda e: e.matmul(pb[6][:, :], onesrb[:], r0row[:],
                                                      start=True, stop=True),
                         waits=[("vv", vw)], inc="mm")
                mk1 = em("tensor", lambda e: e.matmul(pb[7][:, :], onesrf[:], varrow[:],
                                                      start=True, stop=True),
                         waits=[("aa", aw)], inc="mm")
                for c in range(NCH):
                    em("vector", lambda e, c=c, ts=ts: e.tensor_sub(
                        out=Hb[:, c, ts], in0=R[:, c, ts], in1=pb[6][:, :]),
                        waits=[("mm", mk1)] if c == 0 else [], inc="vv")
                    em("vector", lambda e, c=c, ts=ts: e.tensor_mul(
                        out=Hb[:, c, ts], in0=Hb[:, c, ts], in1=pb[7][:, :]), inc="vv")
                return P.val("vv"), P.val("pp")

            def ffn_pass(ln_marks, first_w_mm=None):
                fvh = A[:, :].rearrange("p (c t) -> p c t", c=32)
                lnvv = max(ln_marks[0][0], ln_marks[1][0])
                lnpp = max(ln_marks[0][1], ln_marks[1][1])
                pe_umark = {}
                evac = {}
                dma_mark = {}
                prevB_mm = None
                prevB_vv = None

                def emit_wdma(blk):
                    nonlocal prevB_mm
                    par = blk % 2
                    if blk >= 2:
                        wwaits = [("mm", pe_umark[2 * (blk - 2) + 1])]
                    else:
                        wwaits = [("mm", first_w_mm if first_w_mm is not None
                                  else P.val("mm"))]
                    if par == 1 and prevB_mm is not None and blk % 16 == 1:
                        wwaits.append(("mm", prevB_mm))
                    em("gpsimd", lambda e, blk=blk, par=par: e.dma_start(
                        out=Wv(8192 * par, (16, 256)), in_=wg_r[:, blk]),
                        waits=wwaits, inc="dG", amt=16)
                    dma_mark[blk] = em("gpsimd", lambda e, blk=blk, par=par: e.dma_start(
                        out=Wv(8192 * par + 4096, (16, 256)), in_=wu_r[:, blk]),
                        inc="dG", amt=16)

                for ih in range(2):
                    emit_wdma(ih * 16)
                    emit_wdma(ih * 16 + 1)
                    for blk in range(ih * 16, ih * 16 + 16):
                        par = blk % 2
                        for sub in range(2):
                            oc = blk * 2 + sub
                            ol = oc - ih * 32
                            pgs = [pb[(oc % 2) * 4 + th] for th in range(2)]
                            pus = [pb[(oc % 2) * 4 + 2 + th] for th in range(2)]
                            pw = [("dG", dma_mark[blk])]
                            if oc >= 2:
                                ag, vf = evac[oc - 2]
                                pw += [("aa", ag), ("vv", vf)]
                            if oc in (0, 1):
                                pw += [("vv", lnvv), ("pp", lnpp)]
                            if ol in (0, 1) and prevB_vv is not None:
                                pw += [("vv", prevB_vv)]
                            for c in range(NCH):
                                for th in range(2):
                                    mk_g = em("tensor", lambda e, c=c, th=th, sub=sub, par=par, pgs=pgs: e.matmul(
                                        pgs[th][:, 0:FT],
                                        Wv(8192 * par, (16, 2, 128))[:, c, sub],
                                        Hb[:, c, th * FT:th * FT + FT],
                                        start=(c == 0), stop=(c == NCH - 1)),
                                        waits=pw if (c == 0 and th == 0) else [],
                                        inc="mm" if (c == NCH - 1 and th == 1) else None)
                            for c in range(NCH):
                                for th in range(2):
                                    mk_u = em("tensor", lambda e, c=c, th=th, sub=sub, par=par, pus=pus: e.matmul(
                                        pus[th][:, 0:FT],
                                        Wv(8192 * par + 4096, (16, 2, 128))[:, c, sub],
                                        Hb[:, c, th * FT:th * FT + FT],
                                        start=(c == 0), stop=(c == NCH - 1)),
                                        inc="mm" if (c == NCH - 1 and th == 1) else None)
                            pe_umark[oc] = mk_u
                            ag = None
                            for th in range(2):
                                ag = em("scalar", lambda e, oc=oc, ol=ol, th=th, pgs=pgs: e.activation(
                                    out=fvh[:, ol, th * FT:th * FT + FT], in_=pgs[th][:, 0:FT],
                                    func=AF.Silu, scale=sgc[:, oc:oc + 1]),
                                    waits=[("mm", mk_g)] if th == 0 else [], inc="aa")
                            vf = None
                            for th in range(2):
                                vf = em("vector", lambda e, oc=oc, ol=ol, th=th, pus=pus: e.scalar_tensor_tensor(
                                    out=fvh[:, ol, th * FT:th * FT + FT], in0=pus[th][:, 0:FT],
                                    scalar=suc[:, oc:oc + 1], in1=fvh[:, ol, th * FT:th * FT + FT],
                                    op0=ALU.mult, op1=ALU.mult),
                                    waits=[("mm", mk_u), ("aa", ag)] if th == 0 else [], inc="vv")
                            evac[oc] = (ag, vf)
                        if blk + 2 < ih * 16 + 16:
                            emit_wdma(blk + 2)
                    f_done = P.val("vv")
                    slab_mark = {}
                    yev = {}
                    for oq in range(4):
                        accs = [pb[j * 2 + th] for j in range(4) for th in range(2)]
                        for si in range(8):
                            s = ih * 8 + si
                            par = si % 2
                            if oq == 0 and si < 2:
                                wwaits = [("vv", f_done)]
                            else:
                                prev_use = (oq, si - 2) if si >= 2 else (oq - 1, 6 + si)
                                wwaits = [("mm", slab_mark[prev_use])]
                            em("gpsimd", lambda e, oq=oq, s=s, par=par: e.dma_start(
                                out=Wv(8192 + 4096 * par, (4, 512)), in_=wd_r[:, oq, s]),
                                waits=wwaits, inc="dG", amt=16)
                            dmk = P.val("dG")
                            pw = [("dG", dmk)]
                            if si == 0:
                                if oq == 0:
                                    pw.append(("vv", f_done))
                                else:
                                    pw.append(("vv", yev[oq - 1]))
                            for c in range(4):
                                for j in range(4):
                                    for th in range(2):
                                        slab_mark[(oq, si)] = em("tensor", lambda e, c=c, j=j, th=th, si=si, s=s, par=par: e.matmul(
                                            pb[j * 2 + th][:, 0:FT],
                                            Wv(8192 + 4096 * par, (4, 4, 128))[:, c, j],
                                            fvh[:, si * 4 + c, th * FT:th * FT + FT],
                                            start=(si == 0 and c == 0), stop=(si == 7 and c == 3)),
                                            waits=pw if (c == 0 and j == 0 and th == 0) else [],
                                            inc="mm" if (c == 3 and j == 3 and th == 1) else None)
                        mk = slab_mark[(oq, 7)]
                        vw = None
                        for j in range(4):
                            oc = oq * 4 + j
                            for th in range(2):
                                vw = em("vector", lambda e, oc=oc, j=j, th=th: e.scalar_tensor_tensor(
                                    out=R[:, oc, th * FT:th * FT + FT], in0=pb[j * 2 + th][:, 0:FT],
                                    scalar=sdh[:, oc:oc + 1], in1=R[:, oc, th * FT:th * FT + FT],
                                    op0=ALU.mult, op1=ALU.add),
                                    waits=[("mm", mk)] if (j == 0 and th == 0) else [], inc="vv")
                        yev[oq] = vw
                    prevB_mm = slab_mark[(3, 7)]
                    prevB_vv = yev[3]
                ffn_tile_done = (P.val("vv"), P.val("mm"))
                return ffn_tile_done

            if stage >= 1:
                ln1 = {0: ln_full(0, pre_mm=[("dS", ld_done)], pre_vv=[("vv", init_done)]),
                       1: None}
                if SUB.startswith("z"):
                    ln1[1] = ln1[0]
                else:
                    ln1[1] = ln_full(1)
                if sub == "ln":
                    final_marks = {"vv": ln1[1][0], "aa": P.val("aa")}
                else:
                    f1_vv, f1_mm = ffn_pass(ln1, first_w_mm=0)
                    final_marks = {"vv": f1_vv, "aa": P.val("aa")}
            else:
                final_marks = {"vv": init_done, "aa": P.val("aa")}
                if stage >= -1:
                    pass

            if stage >= 2:
                ln2 = {0: ln_full(0, pre_mm=[("vv", f1_vv)]), 1: None}
                ln2[1] = ln_full(1)
                h_vv, h_pp = ln2[1]
                tap("h", Hb[:, :, :].rearrange("p c t -> p (c t)"))

                em("gpsimd", lambda e: e.dma_start(out=Wv(0, (16, 1024)), in_=wqc_r),
                   waits=[("mm", P.val("mm"))], inc="dG", amt=16)
                wqc_ld = P.val("dG")
                qc_ev = {}
                for co in range(8):
                    for tt in range(2):
                        it2 = co * 2 + tt
                        tsl = slice(tt * 512, tt * 512 + 512)
                        pband = pb[it2 % 2]
                        pw = [("dG", wqc_ld), ("vv", h_vv), ("pp", h_pp)]
                        if it2 >= 2:
                            pw.append(("aa", qc_ev[it2 - 2]))
                        for c in range(NCH):
                            mk = em("tensor", lambda e, c=c, co=co, tsl=tsl, pband=pband: e.matmul(
                                pband[:, 0:512], Wv(0, (16, 8, 128))[:, c, co],
                                Hb[:, c, tsl], start=(c == 0), stop=(c == NCH - 1)),
                                waits=pw if c == 0 else [],
                                inc="mm" if c == NCH - 1 else None)
                        qc_ev[it2] = em("scalar", lambda e, co=co, tsl=tsl, pband=pband: e.activation(
                            out=qcv[:, co, tsl], in_=pband[:, 0:512], func=AF.Copy),
                            waits=[("mm", mk)], inc="aa")
                qc_done = P.val("aa")

                em("gpsimd", lambda e: e.dma_start(out=Wv(0, (8, 2048)), in_=wqup_r),
                   waits=[("aa", qc_done)], inc="dG", amt=16)
                wqup_ld = P.val("dG")
                q_ev = {}
                for hh in range(16):
                    for tt in range(2):
                        it2 = hh * 2 + tt
                        tsl = slice(tt * 512, tt * 512 + 512)
                        pband = pb[2 + it2 % 2]
                        pw = [("dG", wqup_ld)]
                        if it2 >= 2:
                            pw.append(("aa", q_ev[it2 - 2]))
                        for c in range(8):
                            mk = em("tensor", lambda e, c=c, hh=hh, tsl=tsl, pband=pband: e.matmul(
                                pband[:, 0:512], Wv(0, (8, 16, 128))[:, c, hh],
                                qcv[:, c, tsl], start=(c == 0), stop=(c == 7)),
                                waits=pw if c == 0 else [],
                                inc="mm" if c == 7 else None)
                        q_ev[it2] = em("scalar", lambda e, hh=hh, tsl=tsl, pband=pband: e.activation(
                            out=qv[:, hh, tsl], in_=pband[:, 0:512], func=AF.Copy),
                            waits=[("mm", mk)], inc="aa")
                q_done = P.val("aa")

                hs_ld = em("sync", lambda e: e.dma_start(out=hsel[:, :, :], in_=hsel_r),
                           waits=[("aa", q_done)], inc="dS", amt=16)
                for hh in range(16):
                    aw = em("scalar", lambda e, hh=hh: e.activation(
                        out=sq[:, 0:512], in_=qv[:, hh, 0:512], func=AF.Square),
                        waits=[("aa", q_done), ("dS", hs_ld)] if hh == 0 else [("mm", P.val("mm"))], inc="aa")
                    em("tensor", lambda e, hh=hh: e.matmul(
                        pb[4][0:16, 0:512], hsel[:, hh, :], sq[:, 0:512],
                        start=(hh == 0), stop=(hh == 15)), waits=[("aa", aw)], inc="mm")
                    aw = em("scalar", lambda e, hh=hh: e.activation(
                        out=sq[:, 0:512], in_=qv[:, hh, 512:1024], func=AF.Square),
                        waits=[("mm", P.val("mm"))], inc="aa")
                    em("tensor", lambda e, hh=hh: e.matmul(
                        pb[5][0:16, 0:512], hsel[:, hh, :], sq[:, 0:512],
                        start=(hh == 0), stop=(hh == 15)), waits=[("aa", aw)], inc="mm")
                mkq = P.val("mm")
                aw = em("scalar", lambda e: e.activation(
                    out=stats32[:, 0:512], in_=pb[4][0:16, 0:512], func=AF.Ln,
                    bias=epsrms[0:16, :], scale=1.0 / 128.0), waits=[("mm", mkq)], inc="aa")
                aw = em("scalar", lambda e: e.activation(
                    out=stats32[:, 512:1024], in_=pb[5][0:16, 0:512], func=AF.Ln,
                    bias=epsrms[0:16, :], scale=1.0 / 128.0), inc="aa")
                rstd_done = em("scalar", lambda e: e.activation(
                    out=rstd_hs[:, :], in_=stats32[:, :], func=AF.Exp,
                    bias=lnC[0:16, :], scale=-0.5), waits=[("aa", aw)], inc="aa")

                em("sync", lambda e: e.dma_start(out=ropeC, in_=ropeC_r),
                   waits=[("aa", q_done)], inc="dS", amt=16)
                rp_ld = em("sync", lambda e: e.dma_start(out=ropeS, in_=ropeS_r),
                           inc="dS", amt=16)
                qs_ld = em("sync", lambda e: e.dma_start(out=qsel[:, :, :], in_=qsel_r),
                           inc="dS", amt=16)
                qn_prev = None
                qn_prev2 = {}
                for it2 in range(32):
                    hh, half = it2 // 2, it2 % 2
                    hsl = slice(half * 512, half * 512 + 512)
                    pA_ = pb[4 + it2 % 2]
                    pB_ = pb[6 + it2 % 2]
                    if it2 == 0:
                        pw = [("dS", qs_ld), ("aa", rstd_done), ("dS", rp_ld)]
                    elif it2 == 1:
                        pw = []
                    else:
                        pw = [("vv", qn_prev2[it2 - 2])]
                    mka = em("tensor", lambda e, hh=hh, hsl=hsl, pA_=pA_: e.matmul(
                        pA_[:, 0:512], swp[:], qv[:, hh, hsl],
                        start=True, stop=True), waits=pw, inc="mm")
                    mkb = em("tensor", lambda e, hh=hh, hsl=hsl, pB_=pB_: e.matmul(
                        pB_[:, 0:512], qsel[:, hh, :], rstd_hs[:, hsl],
                        start=True, stop=True), inc="mm")
                    ts_ = tA[:, 0:512] if it2 % 2 == 0 else tA[:, 512:1024]
                    vw = em("vector", lambda e, hh=hh, hsl=hsl, ts_=ts_: e.tensor_mul(
                        out=ts_, in0=qv[:, hh, hsl], in1=ropeC[:, hsl]),
                        waits=[("mm", mkb)], inc="vv")
                    vw = em("vector", lambda e, hh=hh, hsl=hsl, pA_=pA_: e.tensor_tensor(
                        out=qv[:, hh, hsl], in0=pA_[:, 0:512], in1=ropeS[:, hsl],
                        op=ALU.mult), inc="vv")
                    vw = em("vector", lambda e, hh=hh, hsl=hsl, ts_=ts_: e.tensor_add(
                        out=qv[:, hh, hsl], in0=qv[:, hh, hsl], in1=ts_), inc="vv")
                    vw = em("vector", lambda e, hh=hh, hsl=hsl, pB_=pB_: e.scalar_tensor_tensor(
                        out=qv[:, hh, hsl], in0=qv[:, hh, hsl], scalar=qnw_col[:],
                        in1=pB_[:, 0:512], op0=ALU.mult, op1=ALU.mult), inc="vv")
                    qn_prev2[it2] = vw
                    qn_prev = vw
                qn_done = qn_prev
                tap("q", qv[:, :, :].rearrange("p c t -> p (c t)"))
                em("sync", lambda e: e.dma_start(out=maskjm[:, :, :, :], in_=maskjm_r),
                   waits=[("vv", qn_done)], inc="dS", amt=16)
                em("sync", lambda e: e.dma_start(out=valid01[:, :, :], in_=valid01_r),
                   inc="dS", amt=16)
                valid_ld = em("sync", lambda e: e.dma_start(out=validadd[:, :, :], in_=validadd_r),
                              inc="dS", amt=16)

                offs = {}
                off = 0
                first = True
                for nm, src, csz in (("wkvb", wkvb_r, 128), ("wzb", wzb_r, 128),
                                     ("wk", wk_r, 128), ("wv", wv_r, 128),
                                     ("wikv", wikv_r, 64), ("wiz", wiz_r, 64),
                                     ("wdq", wdq_r, 64), ("ww", ww_r, 4)):
                    offs[nm] = off
                    em("gpsimd", lambda e, src=src, off=off, csz=csz: e.dma_start(
                        out=Wv(off, (16, csz)), in_=src),
                        waits=[("vv", qn_done)] if first else [], inc="dG", amt=16)
                    first = False
                    off += 16 * csz
                em("gpsimd", lambda e: e.dma_start(out=wiuq_sb, in_=wiuq_r), inc="dG", amt=16)
                smalls_ld = P.val("dG")

                vev = None
                kev = None
                for i in range(8):
                    tsl = slice(i * 128, i * 128 + 128)
                    for c in range(NCH):
                        pw = []
                        if c == 0:
                            pw = [("dG", smalls_ld)]
                            if vev is not None:
                                pw.append(("aa", vev))
                        mkv = em("tensor", lambda e, c=c, tsl=tsl: e.matmul(
                            pb[0][:, 0:128], Hb[:, c, tsl], Wv(offs["wv"], (16, 128))[:, c, :],
                            start=(c == 0), stop=(c == NCH - 1)), waits=pw,
                            inc="mm" if c == NCH - 1 else None)
                    vev = em("scalar", lambda e, i=i: e.activation(
                        out=vwin[:, 4 + i, :], in_=pb[0][:, 0:128], func=AF.Copy),
                        waits=[("mm", mkv)], inc="aa")
                    for c in range(NCH):
                        pw = [("vv", kev)] if (c == 0 and kev is not None) else []
                        mkk = em("tensor", lambda e, c=c, tsl=tsl: e.matmul(
                            pb[1][:, 0:128], Hb[:, c, tsl], Wv(offs["wk"], (16, 128))[:, c, :],
                            start=(c == 0), stop=(c == NCH - 1)), waits=pw,
                            inc="mm" if c == NCH - 1 else None)
                    aw = em("scalar", lambda e: e.activation(
                        out=sq[:, 0:128], in_=pb[1][:, 0:128], func=AF.Square),
                        waits=[("mm", mkk)], inc="aa")
                    vw = em("vector", lambda e: e.reduce_sum(
                        out=rms_c[:], in_=sq[:, 0:128], axis=AX.X), waits=[("aa", aw)], inc="vv")
                    aw = em("scalar", lambda e: e.activation(
                        out=rms_c[:], in_=rms_c[:], func=AF.Ln, bias=epsrms[:], scale=1.0 / 128.0),
                        waits=[("vv", vw)], inc="aa")
                    aw = em("scalar", lambda e: e.activation(
                        out=rms_c[:], in_=rms_c[:], func=AF.Exp, scale=-0.5),
                        waits=[("aa", aw)], inc="aa")
                    vw = em("vector", lambda e: e.scalar_tensor_tensor(
                        out=cbf[:, 0:128], in0=pb[1][:, 0:128], scalar=rms_c[:],
                        in1=knw_bc[:], op0=ALU.mult, op1=ALU.mult),
                        waits=[("aa", aw)], inc="vv")
                    ci_ = coskt[:, i, :]
                    si_ = sinkt[:, i, :]
                    em("vector", lambda e, ci_=ci_: e.tensor_mul(
                        out=sel32[:, 0:32], in0=cbf[:, 0:32], in1=ci_), inc="vv")
                    em("vector", lambda e, si_=si_: e.tensor_mul(
                        out=sel32[:, 32:64], in0=cbf[:, 32:64], in1=si_), inc="vv")
                    em("vector", lambda e, i=i: e.tensor_sub(
                        out=kwin_tm[:, 4 + i, 0:32], in0=sel32[:, 0:32], in1=sel32[:, 32:64]), inc="vv")
                    em("vector", lambda e, si_=si_: e.tensor_mul(
                        out=sel32[:, 0:32], in0=cbf[:, 0:32], in1=si_), inc="vv")
                    em("vector", lambda e, ci_=ci_: e.tensor_mul(
                        out=sel32[:, 32:64], in0=cbf[:, 32:64], in1=ci_), inc="vv")
                    em("vector", lambda e, i=i: e.tensor_add(
                        out=kwin_tm[:, 4 + i, 32:64], in0=sel32[:, 0:32], in1=sel32[:, 32:64]), inc="vv")
                    kev = em("vector", lambda e, i=i: e.tensor_copy(
                        out=kwin_tm[:, 4 + i, 64:128], in_=cbf[:, 64:128]), inc="vv")
                kv_done_vv = P.val("vv")
                kv_done_aa = P.val("aa")

                def compress(wc, wz, odim, biasT_ap, out_ap, do_rms, pre_vv):
                    last = pre_vv
                    for tt in range(2):
                        tsl = slice(tt * 512, tt * 512 + 512)
                        for c in range(NCH):
                            pw = [("vv", last)] if c == 0 else []
                            mk = em("tensor", lambda e, c=c, tsl=tsl, wc=wc, odim=odim: e.matmul(
                                pb[2][0:odim, 0:512], Wv(wc, (16, odim))[:, c, :],
                                Hb[:, c, tsl], start=(c == 0), stop=(c == NCH - 1)), waits=pw,
                                inc="mm" if c == NCH - 1 else None)
                        aw = em("scalar", lambda e, tt=tt, odim=odim: e.activation(
                            out=cbf[0:odim, tt * 512:tt * 512 + 512], in_=pb[2][0:odim, 0:512],
                            func=AF.Copy), waits=[("mm", mk)], inc="aa")
                        for c in range(NCH):
                            mk = em("tensor", lambda e, c=c, tsl=tsl, wz=wz, odim=odim: e.matmul(
                                pb[3][0:odim, 0:512], Wv(wz, (16, odim))[:, c, :],
                                Hb[:, c, tsl], start=(c == 0), stop=(c == NCH - 1)),
                                waits=[("aa", aw)] if c == 0 else [],
                                inc="mm" if c == NCH - 1 else None)
                        last = em("vector", lambda e, tt=tt, odim=odim: e.tensor_copy(
                            out=zb32[0:odim, tt * 512:tt * 512 + 512], in_=pb[3][0:odim, 0:512]),
                            waits=[("mm", mk)], inc="vv")
                    z3 = lambda ap, od: ap[0:od, :].rearrange("p (n m) -> p n m", m=32)
                    vw = em("vector", lambda e, odim=odim, biasT_ap=biasT_ap: e.tensor_tensor(
                        out=z3(zb32, odim), in0=z3(zb32, odim),
                        in1=biasT_ap.rearrange("p (o m) -> p o m", o=1).broadcast_to([odim, 32, 32]),
                        op=ALU.add), inc="vv")
                    vw = em("vector", lambda e, odim=odim: e.tensor_reduce(
                        out=gmx[0:odim, :], in_=z3(zb32, odim), axis=AX.X, op=ALU.max,
                        negate=True), inc="vv")
                    vw = em("vector", lambda e, odim=odim: e.tensor_tensor(
                        out=z3(zb32, odim), in0=z3(zb32, odim),
                        in1=gmx[0:odim, :].rearrange("p (n o) -> p n o", o=1).broadcast_to([odim, 32, 32]),
                        op=ALU.add), waits=[("vv", vw)], inc="vv")
                    aw = em("scalar", lambda e, odim=odim: e.activation(
                        out=zb32[0:odim, :], in_=zb32[0:odim, :], func=AF.Exp),
                        waits=[("vv", vw)], inc="aa")
                    vw = em("vector", lambda e, odim=odim: e.tensor_reduce(
                        out=gsm[0:odim, :], in_=z3(zb32, odim), axis=AX.X, op=ALU.add),
                        waits=[("aa", aw)], inc="vv")
                    vw = em("vector", lambda e, odim=odim: e.tensor_mul(
                        out=zb32[0:odim, :], in0=zb32[0:odim, :], in1=cbf[0:odim, :]), inc="vv")
                    vw = em("vector", lambda e, odim=odim: e.tensor_reduce(
                        out=cmp32[0:odim, :], in_=z3(zb32, odim), axis=AX.X, op=ALU.add), inc="vv")
                    vw = em("vector", lambda e, odim=odim: e.reciprocal(
                        out=grec[0:odim, :], in_=gsm[0:odim, :]), waits=[("vv", vw)], inc="vv")
                    vw = em("vector", lambda e, odim=odim: e.tensor_mul(
                        out=cmp32[0:odim, :], in0=cmp32[0:odim, :], in1=grec[0:odim, :]),
                        waits=[("vv", vw)], inc="vv")
                    if do_rms:
                        aw = em("scalar", lambda e: e.activation(
                            out=sq[:, 0:32], in_=cmp32[:, :], func=AF.Square),
                            waits=[("vv", vw)], inc="aa")
                        mk = em("tensor", lambda e: e.matmul(
                            pb[2][0:1, 0:32], onescb[:], sq[:, 0:32], start=True, stop=True),
                            waits=[("aa", aw)], inc="mm")
                        aw = em("scalar", lambda e: e.activation(
                            out=rstd_r32[:], in_=pb[2][0:1, 0:32], func=AF.Ln,
                            bias=epsrms[0:1, :], scale=1.0 / 128.0),
                            waits=[("mm", mk)], inc="aa")
                        aw = em("scalar", lambda e: e.activation(
                            out=rstd_r32[:], in_=rstd_r32[:], func=AF.Exp, scale=-0.5), inc="aa")
                        mk = em("tensor", lambda e: e.matmul(
                            pb[3][:, 0:32], onesrf[:], rstd_r32[:], start=True, stop=True),
                            waits=[("aa", aw)], inc="mm")
                        vw = em("vector", lambda e, out_ap=out_ap: e.scalar_tensor_tensor(
                            out=out_ap, in0=cmp32[:, :], scalar=knw_col[:],
                            in1=pb[3][:, 0:32], op0=ALU.mult, op1=ALU.mult),
                            waits=[("mm", mk)], inc="vv")
                    else:
                        vw = em("vector", lambda e, odim=odim, out_ap=out_ap: e.tensor_copy(
                            out=out_ap, in_=cmp32[0:odim, :]), waits=[("vv", vw)], inc="vv")
                    return vw

                ckv_loc = compress(offs["wkvb"], offs["wzb"], 128, biasbT[:, :],
                                   ckv_fm[:, 0:32], True, kv_done_vv)
                kci_loc = compress(offs["wikv"], offs["wiz"], 64, biasiT[:, :],
                                   kci_cm[0:64, 0:32], False, ckv_loc)

                for tt in range(2):
                    tsl = slice(tt * 512, tt * 512 + 512)
                    for c in range(NCH):
                        pw = []
                        if c == 0:
                            pw = [("vv", kci_loc)] if tt == 0 else [("aa", P.val("aa"))]
                        mk = em("tensor", lambda e, c=c, tsl=tsl: e.matmul(
                            pb[2][0:64, 0:512], Wv(offs["wdq"], (16, 64))[:, c, :],
                            Hb[:, c, tsl], start=(c == 0), stop=(c == NCH - 1)),
                            waits=pw, inc="mm" if c == NCH - 1 else None)
                    em("scalar", lambda e, tt=tt: e.activation(
                        out=dq_v[0:64, tt * 512:tt * 512 + 512], in_=pb[2][0:64, 0:512],
                        func=AF.Copy), waits=[("mm", mk)], inc="aa")
                dq_done = P.val("aa")
                for chn in range(2):
                    for tt in range(2):
                        mk = em("tensor", lambda e, chn=chn, tt=tt: e.matmul(
                            pb[3][:, 0:512], wiuq_sb[0:64, chn * 128:chn * 128 + 128],
                            dq_v[0:64, tt * 512:tt * 512 + 512], start=True, stop=True),
                            waits=[("aa", P.val("aa"))], inc="mm")
                        em("scalar", lambda e, chn=chn, tt=tt: e.activation(
                            out=qi_v[0:64, 2 * chn, tt * 512:tt * 512 + 512],
                            in_=pb[3][0:64, 0:512], func=AF.Copy),
                            waits=[("mm", mk)], inc="aa")
                        em("scalar", lambda e, chn=chn, tt=tt: e.activation(
                            out=qi_v[0:64, 2 * chn + 1, tt * 512:tt * 512 + 512],
                            in_=pb[3][64:128, 0:512], func=AF.Copy), inc="aa")
                qi_done = P.val("aa")
                wgev = None
                for i in range(8):
                    tsl = slice(i * 128, i * 128 + 128)
                    for c in range(NCH):
                        pw = []
                        if c == 0:
                            pw = [("aa", qi_done)] if i == 0 else [("vv", wgev)]
                        mk = em("tensor", lambda e, c=c, tsl=tsl: e.matmul(
                            pb[2][:, 0:4], Hb[:, c, tsl], Wv(offs["ww"], (16, 4))[:, c, :],
                            start=(c == 0), stop=(c == NCH - 1)), waits=pw,
                            inc="mm" if c == NCH - 1 else None)
                    wgev = em("vector", lambda e, i=i: e.tensor_copy(
                        out=wgt_sb[:, i, :], in_=pb[2][:, 0:4]), waits=[("mm", mk)], inc="vv")
                wgt_done = P.val("vv")

                ccK = cc_in.ap()[0:1, SEC_K:SEC_K + T * D].rearrange(
                    "o (n p d) -> (o n) p d", p=128, d=128).rearrange("n p d -> p n d")
                ccV = cc_in.ap()[0:1, SEC_V:SEC_V + T * D].rearrange(
                    "o (n p d) -> (o n) p d", p=128, d=128).rearrange("n p d -> p n d")
                ccC = cc_in.ap()[0:1, SEC_CKV:SEC_CKV + 128 * 32].rearrange(
                    "o (p n) -> (o p) n", n=32)
                ccI = cc_in.ap()[0:1, SEC_KCI:SEC_KCI + 64 * 32].rearrange(
                    "o (p n) -> (o p) n", n=32)
                em("gpsimd", lambda e, ccK=ccK: e.dma_start(out=ccK, in_=kwin_tm[:, 4:12, :]),
                   waits=[("vv", kv_done_vv)], inc="dG", amt=16)
                em("gpsimd", lambda e, ccV=ccV: e.dma_start(out=ccV, in_=vwin[:, 4:12, :]),
                   waits=[("aa", kv_done_aa)], inc="dG", amt=16)
                em("gpsimd", lambda e, ccC=ccC: e.dma_start(out=ccC, in_=ckv_fm[:, 0:32]),
                   waits=[("vv", ckv_loc)], inc="dG", amt=16)
                blob_w = em("gpsimd", lambda e, ccI=ccI: e.dma_start(out=ccI, in_=kci_cm[0:64, 0:32]),
                            waits=[("vv", kci_loc)], inc="dG", amt=16)
                ccdone = em("gpsimd", lambda e: e.collective_compute(
                    "AllGather", ALU.bypass,
                    replica_groups=[[0, 1], [2, 3], [4, 5], [6, 7]],
                    ins=[cc_in.ap().opt()], outs=[cc_out.ap().opt()]),
                    waits=[("dG", blob_w)], inc="cc")
                rbK = cc_out.ap()[0:1, SEC_K:SEC_K + T * D].rearrange(
                    "o (n p d) -> (o n) p d", p=128, d=128).rearrange("n p d -> p n d")
                rbV = cc_out.ap()[0:1, SEC_V:SEC_V + T * D].rearrange(
                    "o (n p d) -> (o n) p d", p=128, d=128).rearrange("n p d -> p n d")
                em("gpsimd", lambda e, rbK=rbK: e.dma_start(
                    out=kwin_tm[:, 0:4, :], in_=rbK[:, 4:8, :]),
                    waits=[("cc", ccdone)], inc="dG", amt=16)
                em("gpsimd", lambda e, rbV=rbV: e.dma_start(
                    out=vwin[:, 0:4, :], in_=rbV[:, 4:8, :]), inc="dG", amt=16)
                for r_ in range(2):
                    cC = cc_out.ap()[r_:r_ + 1, SEC_CKV:SEC_CKV + 128 * 32].rearrange(
                        "o (p n) -> (o p) n", n=32)
                    cI = cc_out.ap()[r_:r_ + 1, SEC_KCI:SEC_KCI + 64 * 32].rearrange(
                        "o (p n) -> (o p) n", n=32)
                    em("gpsimd", lambda e, cC=cC, r_=r_: e.dma_start(
                        out=ckv_fm[:, r_ * 32:r_ * 32 + 32], in_=cC), inc="dG", amt=16)
                    em("gpsimd", lambda e, cI=cI, r_=r_: e.dma_start(
                        out=kci_cm[0:64, r_ * 32:r_ * 32 + 32], in_=cI), inc="dG", amt=16)
                rb_done = P.val("dG")

                ktv = None
                for k in range(KW):
                    pw = [("dG", rb_done)] if k == 0 else [("aa", ktv)]
                    mk = em("tensor", lambda e, k=k: e.transpose(
                        pb[2][:, 0:128], kwin_tm[:, k, :], idf[:]), waits=pw, inc="mm")
                    ktv = em("scalar", lambda e, k=k: e.activation(
                        out=kwin_fm[:, k * 128:k * 128 + 128], in_=pb[2][:, 0:128],
                        func=AF.Copy), waits=[("mm", mk)], inc="aa")
                vk = em("vector", lambda e: e.tensor_copy(out=sel32[:, 0:64], in_=ckv_fm[:, 0:64]),
                        waits=[("dG", rb_done)], inc="vv")
                mk = em("tensor", lambda e: e.transpose(pb[3][0:64, 0:128], sel32[:, 0:64], idf[:]),
                        waits=[("vv", vk)], inc="mm")
                em("scalar", lambda e: e.activation(
                    out=ckv_nm[0:64, :], in_=pb[3][0:64, 0:128], func=AF.Copy),
                    waits=[("mm", mk)], inc="aa")
                tap("kwin", kwin_tm[:, :, :].rearrange("p n d -> p (n d)"))
                tap("vwin", vwin[:, :, :].rearrange("p n d -> p (n d)"))
                tap("ckv", ckv_fm[:, :])
                tap("kci", kci_cm[:, :])

                selv = None
                for qt in range(NQT):
                    tsl = slice(qt * 128, qt * 128 + 128)
                    for ih in range(4):
                        pd = pb[4 + ih % 2]
                        lhs = qi_v[0:64, ih, tsl]
                        pw = [("dG", rb_done), ("vv", P.val("vv"))]
                        mk = em("tensor", lambda e, lhs=lhs, pd=pd: e.matmul(
                            pd[:, 0:64], lhs, kci_cm[0:64, 0:64], start=True, stop=True),
                            waits=pw, inc="mm")
                        vw = em("vector", lambda e, pd=pd: e.tensor_scalar_max(
                            out=m01[:], in0=pd[:, 0:64], scalar1=0.0),
                            waits=[("mm", mk)], inc="vv")
                        if ih == 0:
                            vw = em("vector", lambda e, qt=qt, ih=ih: e.tensor_scalar(
                                out=isc[:], in0=m01[:], scalar1=wgt_sb[:, qt, ih:ih + 1],
                                scalar2=None, op0=ALU.mult),
                                waits=[("vv", wgt_done)], inc="vv")
                        else:
                            vw = em("vector", lambda e, qt=qt, ih=ih: e.scalar_tensor_tensor(
                                out=isc[:], in0=m01[:], scalar=wgt_sb[:, qt, ih:ih + 1],
                                in1=isc[:], op0=ALU.mult, op1=ALU.add), inc="vv")
                    if "isc" in dbg and qt == 0:
                        tap("isc", isc[:, :])
                    vw = em("vector", lambda e, qt=qt: e.tensor_add(
                        out=work[:], in0=isc[:], in1=validadd[:, qt, :]),
                        waits=[("dS", valid_ld)] if qt == 0 else [], inc="vv")
                    vw = em("vector", lambda e: e.tensor_add(
                        out=work[:], in0=work[:], in1=tbrow[:]), inc="vv")
                    for r_ in range(4):
                        vw = em("vector", lambda e: e.max(out=mx8[:], in_=work[:]), inc="vv")
                        if r_ < 3:
                            vw = em("vector", lambda e: e.tensor_single_scalar(
                                out=m01[:], in_=work[:], scalar=mx8[:, 7:8], op=ALU.is_ge),
                                waits=[("vv", vw)], inc="vv")
                            vw = em("vector", lambda e: e.scalar_tensor_tensor(
                                out=work[:], in0=m01[:], scalar=-1e38, in1=work[:],
                                op0=ALU.mult, op1=ALU.add), inc="vv")
                    vw = em("vector", lambda e, qt=qt: e.tensor_add(
                        out=work[:], in0=isc[:], in1=validadd[:, qt, :]),
                        waits=[("vv", vw)], inc="vv")
                    vw = em("vector", lambda e: e.tensor_add(
                        out=work[:], in0=work[:], in1=tbrow[:]), inc="vv")
                    vw = em("vector", lambda e: e.tensor_single_scalar(
                        out=m01[:], in_=work[:], scalar=mx8[:, 7:8], op=ALU.is_ge), inc="vv")
                    vw = em("vector", lambda e, qt=qt: e.tensor_mul(
                        out=sel32[:, 0:64], in0=m01[:], in1=valid01[:, qt, :]), inc="vv")
                    mk = em("tensor", lambda e: e.transpose(pb[6][0:64, 0:128], sel32[:, 0:64], idf[:]),
                            waits=[("vv", vw)], inc="mm")
                    selv = em("scalar", lambda e, qt=qt: e.activation(
                        out=selT[:, qt * 128:qt * 128 + 128], in_=pb[6][0:64, 0:128],
                        func=AF.Copy), waits=[("mm", mk)], inc="aa")
                sel_done = P.val("aa")
                tap("selT", selT[:, :])
                final_marks = {"vv": P.val("vv"), "aa": P.val("aa")}

            if stage >= 3:
                hs_ld2 = em("sync", lambda e: e.dma_start(out=hsel2[:, :, :], in_=hsel_r),
                            waits=[("vv", qn_done)], inc="dS", amt=16)
                for hh in range(16):
                    aw = em("scalar", lambda e, hh=hh: e.activation(
                        out=sq[:, 0:512], in_=qv[:, hh, 0:512], func=AF.Square),
                        waits=[("vv", qn_done), ("dS", hs_ld2)] if hh == 0 else
                        [("mm", P.val("mm"))], inc="aa")
                    em("tensor", lambda e, hh=hh: e.matmul(
                        pb[6][0:16, 0:512], hsel2[:, hh, :], sq[:, 0:512],
                        start=(hh == 0), stop=(hh == 15)), waits=[("aa", aw)], inc="mm")
                    aw = em("scalar", lambda e, hh=hh: e.activation(
                        out=sq[:, 0:512], in_=qv[:, hh, 512:1024], func=AF.Square),
                        waits=[("mm", P.val("mm"))], inc="aa")
                    em("tensor", lambda e, hh=hh: e.matmul(
                        pb[7][0:16, 0:512], hsel2[:, hh, :], sq[:, 0:512],
                        start=(hh == 0), stop=(hh == 15)), waits=[("aa", aw)], inc="mm")
                mkq2 = P.val("mm")
                v1 = em("vector", lambda e: e.reduce_max(
                    out=red1[0:16, 0:1], in_=pb[6][0:16, 0:512], axis=AX.X),
                    waits=[("mm", mkq2)], inc="vv")
                v1 = em("vector", lambda e: e.reduce_max(
                    out=red2[0:16, 0:1], in_=pb[7][0:16, 0:512], axis=AX.X), inc="vv")
                v1 = em("vector", lambda e: e.tensor_max(
                    out=red1[0:16, 0:1], in0=red1[0:16, 0:1], in1=red2[0:16, 0:1]),
                    waits=[("vv", v1)], inc="vv")
                kv1 = None
                for k in range(KW):
                    aw = em("scalar", lambda e, k=k: e.activation(
                        out=sq[:, 0:128], in_=kwin_tm[:, k, :], func=AF.Square),
                        waits=[("vv", v1)] if k == 0 else [("vv", kv1)], inc="aa")
                    kv1 = em("vector", lambda e, k=k: e.reduce_sum(
                        out=red2[:, 0:1], in_=sq[:, 0:128], axis=AX.X),
                        waits=[("aa", aw)], inc="vv")
                    if k == 0:
                        kv1 = em("vector", lambda e: e.tensor_copy(
                            out=red2[:, 1:2], in_=red2[:, 0:1]), waits=[("vv", kv1)], inc="vv")
                    else:
                        kv1 = em("vector", lambda e: e.tensor_max(
                            out=red2[:, 1:2], in0=red2[:, 1:2], in1=red2[:, 0:1]),
                            waits=[("vv", kv1)], inc="vv")
                aw = em("scalar", lambda e: e.activation(
                    out=sq[0:64, 0:128], in_=ckv_nm[0:64, :], func=AF.Square),
                    waits=[("vv", kv1)], inc="aa")
                cv1 = em("vector", lambda e: e.reduce_sum(
                    out=red1[0:64, 1:2], in_=sq[0:64, 0:128], axis=AX.X),
                    waits=[("aa", aw)], inc="vv")
                mk = em("tensor", lambda e: e.transpose(
                    pb[6][0:1, 0:16], red1[0:16, 0:1], idf[0:16, 0:16]),
                    waits=[("vv", cv1)], inc="mm")
                mk = em("tensor", lambda e: e.transpose(
                    pb[6][0:1, 16:144], red2[:, 1:2], idf[:]), inc="mm")
                mk = em("tensor", lambda e: e.transpose(
                    pb[6][0:1, 144:208], red1[0:64, 1:2], idf[0:64, 0:64]), inc="mm")
                v1 = em("vector", lambda e: e.tensor_copy(
                    out=srow[0:1, 0:16], in_=pb[6][0:1, 0:16]),
                    waits=[("mm", mk)], inc="vv")
                v1 = em("vector", lambda e: e.reduce_max(
                    out=srow[0:1, 16:17], in_=pb[6][0:1, 16:144], axis=AX.X), inc="vv")
                v1 = em("vector", lambda e: e.reduce_max(
                    out=srow[0:1, 17:18], in_=pb[6][0:1, 144:208], axis=AX.X),
                    waits=[("vv", v1)], inc="vv")
                v1 = em("vector", lambda e: e.tensor_single_scalar(
                    out=srow[0:1, 32:48], in_=srow[0:1, 0:16], scalar=srow[0:1, 16:17],
                    op=ALU.mult), waits=[("vv", v1)], inc="vv")
                v1 = em("vector", lambda e: e.tensor_single_scalar(
                    out=srow[0:1, 48:64], in_=srow[0:1, 0:16], scalar=srow[0:1, 17:18],
                    op=ALU.mult), waits=[("vv", v1)], inc="vv")
                aw = em("scalar", lambda e: e.activation(
                    out=srow[0:1, 32:64], in_=srow[0:1, 32:64], func=AF.Ln,
                    bias=epsln[:]), waits=[("vv", v1)], inc="aa")
                aw = em("scalar", lambda e: e.activation(
                    out=srow[0:1, 32:64], in_=srow[0:1, 32:64], func=AF.Exp, scale=0.5),
                    waits=[("aa", aw)], inc="aa")
                v1 = em("vector", lambda e: e.tensor_max(
                    out=srow[0:1, 32:48], in0=srow[0:1, 32:48], in1=sink_row[0:1, :]),
                    waits=[("aa", aw)], inc="vv")
                v1 = em("vector", lambda e: e.tensor_scalar(
                    out=srow[0:1, 32:64], in0=srow[0:1, 32:64], scalar1=-1.0,
                    scalar2=None, op0=ALU.mult), waits=[("vv", v1)], inc="vv")
                for hh in range(16):
                    em("tensor", lambda e, hh=hh: e.matmul(
                        pb[7][:, hh:hh + 1], onesrf[:], srow[0:1, 32 + hh:33 + hh],
                        start=True, stop=True),
                        waits=[("vv", v1)] if hh == 0 else [])
                    em("tensor", lambda e, hh=hh: e.matmul(
                        pb[7][:, 16 + hh:17 + hh], onesrf[:], srow[0:1, 48 + hh:49 + hh],
                        start=True, stop=True), inc="mm" if hh == 15 else None)
                mkb = P.val("mm")
                v1 = em("vector", lambda e: e.tensor_copy(out=nSW[:, :], in_=pb[7][:, 0:16]),
                        waits=[("mm", mkb)], inc="vv")
                v1 = em("vector", lambda e: e.tensor_copy(out=nSP[:, :], in_=pb[7][:, 16:32]),
                        inc="vv")
                v1 = em("vector", lambda e: e.tensor_add(
                    out=sinkexp[:, :], in0=nSW[:, :], in1=sink_cols[:, :]),
                    waits=[("vv", v1)], inc="vv")
                preM_aa = em("scalar", lambda e: e.activation(
                    out=sinkexp[:, :], in_=sinkexp[:, :], func=AF.Exp),
                    waits=[("vv", v1)], inc="aa")

                NIT = H * NQT
                sc_mark = {}; exp_mark = {}; mask_mark = {}; av_mark = {}
                rec_mark = {}; avs_mark = {}; stt_mark = {}; tr_mark = {}; ev_mark = {}

                def s0(i):
                    qt, hh = i // H, i % H
                    X, Y = pb[0 + 2 * (i % 2)], pb[1 + 2 * (i % 2)]
                    qap = qv[:, hh, qt * 128:qt * 128 + 128]
                    pw0 = ([("aa", exp_mark[i - 2])] if i >= 2
                           else [("aa", max(sel_done, preM_aa))])
                    for krel in range(5):
                        dst = X[:, krel * 128:krel * 128 + 128] if krel < 4 else Y[:, 0:128]
                        em("tensor", lambda e, krel=krel, qt=qt, dst=dst, qap=qap: e.matmul(
                            dst, kwin_fm[:, (qt + krel) * 128:(qt + krel) * 128 + 128], qap,
                            start=True, stop=True), waits=pw0 if krel == 0 else [])
                    sc_mark[i] = em("tensor", lambda e, qap=qap, Y=Y: e.matmul(
                        Y[0:64, 128:256], ckv_fm[:, 0:64], qap, start=True, stop=True),
                        inc="mm")

                def s1(i):
                    hh = i % H
                    X, Y = pb[0 + 2 * (i % 2)], pb[1 + 2 * (i % 2)]
                    p3 = i % 3
                    mmw = av_mark[i - 3] if i >= 3 else sc_mark[i]
                    em("scalar", lambda e, X=X, p3=p3, hh=hh: e.activation(
                        out=esw[:, p3, 0:4, :], in_=X[:, :].rearrange("p (a b) -> p a b", b=128),
                        func=AF.Exp, bias=nSW[:, hh:hh + 1]),
                        waits=[("mm", mmw)], inc="aa")
                    em("scalar", lambda e, Y=Y, p3=p3, hh=hh: e.activation(
                        out=esw[:, p3, 4, :], in_=Y[:, 0:128], func=AF.Exp,
                        bias=nSW[:, hh:hh + 1]), inc="aa")
                    exp_mark[i] = em("scalar", lambda e, Y=Y, p3=p3, hh=hh: e.activation(
                        out=esp[:, p3, :], in_=Y[0:64, 128:256], func=AF.Exp,
                        bias=nSP[0:64, hh:hh + 1]), inc="aa")

                def s2(i):
                    qt, p3 = i // H, i % 3
                    em("vector", lambda e, p3=p3, qt=qt: e.tensor_mul(
                        out=esw[:, p3, :, :], in0=esw[:, p3, :, :], in1=maskjm[:, qt, :, :]),
                        waits=[("aa", exp_mark[i])], inc="vv")
                    mask_mark[i] = em("vector", lambda e, p3=p3, qt=qt: e.tensor_mul(
                        out=esp[:, p3, :], in0=esp[:, p3, :],
                        in1=selT[:, qt * 128:qt * 128 + 128]), inc="vv")

                def s3(i):
                    qt, p3 = i // H, i % 3
                    Z = pb[4 + i % 4]
                    for krel in range(5):
                        em("tensor", lambda e, krel=krel, p3=p3, qt=qt, Z=Z: e.matmul(
                            Z[:, 128:256], esw[:, p3, krel, :], vwin[:, qt + krel, :],
                            start=(krel == 0), stop=(krel == 4)),
                            waits=[("vv", mask_mark[i])] if krel == 0 else [])
                        em("tensor", lambda e, krel=krel, p3=p3, Z=Z: e.matmul(
                            Z[:, 256:257], esw[:, p3, krel, :], onescb[:],
                            start=(krel == 0), stop=(krel == 4)))
                    em("tensor", lambda e, p3=p3, Z=Z: e.matmul(
                        Z[:, 0:128], esp[:, p3, :], ckv_nm[0:64, :],
                        start=True, stop=True))
                    av_mark[i] = em("tensor", lambda e, p3=p3, Z=Z: e.matmul(
                        Z[:, 257:258], esp[:, p3, :], onescb[0:64, :],
                        start=True, stop=True), inc="mm")

                def s4(i):
                    hh, p2 = i % H, i % 2
                    Z = pb[4 + i % 4]
                    v1 = em("vector", lambda e, Z=Z, p2=p2, hh=hh: e.tensor_add(
                        out=ssw_c[:, p2:p2 + 1], in0=Z[:, 256:257],
                        in1=sinkexp[:, hh:hh + 1]), waits=[("mm", av_mark[i])], inc="vv")
                    rw = [("vv", v1)] + ([("aa", avs_mark[i - 2])] if i >= 2 else [])
                    v1 = em("vector", lambda e, p2=p2: e.reciprocal(
                        out=rsw_c[:, p2:p2 + 1], in_=ssw_c[:, p2:p2 + 1]),
                        waits=rw, inc="vv")
                    v1 = em("vector", lambda e, Z=Z, p2=p2: e.tensor_scalar(
                        out=ssp_c[:, p2:p2 + 1], in0=Z[:, 257:258], scalar1=1e-30,
                        scalar2=None, op0=ALU.add), inc="vv")
                    rec_mark[i] = em("vector", lambda e, p2=p2: e.reciprocal(
                        out=rsp_c[:, p2:p2 + 1], in_=ssp_c[:, p2:p2 + 1]),
                        waits=[("vv", v1)], inc="vv")

                def s5(i):
                    p2 = i % 2
                    Z = pb[4 + i % 4]
                    avs_mark[i] = em("scalar", lambda e, Z=Z, p2=p2: e.activation(
                        out=avs[:, p2, :], in_=Z[:, 128:256], func=AF.Copy,
                        scale=rsw_c[:, p2:p2 + 1]), waits=[("vv", rec_mark[i])], inc="aa")

                def s6(i):
                    p2 = i % 2
                    Z = pb[4 + i % 4]
                    w6 = [("aa", avs_mark[i])] + ([("mm", tr_mark[i - 2])] if i >= 2 else [])
                    stt_mark[i] = em("vector", lambda e, Z=Z, p2=p2: e.scalar_tensor_tensor(
                        out=attn_tm[:, p2, :], in0=Z[:, 0:128],
                        scalar=rsp_c[:, p2:p2 + 1], in1=avs[:, p2, :],
                        op0=ALU.mult, op1=ALU.add), waits=w6, inc="vv")

                def s7(i):
                    p2 = i % 2
                    TY = pb[1 + 2 * p2]
                    w7 = [("vv", stt_mark[i])] + ([("aa", ev_mark[i - 2])] if i >= 2 else [])
                    tr_mark[i] = em("tensor", lambda e, TY=TY, p2=p2: e.transpose(
                        TY[:, 256:384], attn_tm[:, p2, :], idf[:]),
                        waits=w7, inc="mm")

                def s8(i):
                    qt, hh, p2 = i // H, i % H, i % 2
                    TY = pb[1 + 2 * p2]
                    ev_mark[i] = em("scalar", lambda e, TY=TY, hh=hh, qt=qt: e.activation(
                        out=Hb[:, hh, qt * 128:qt * 128 + 128], in_=TY[:, 256:384],
                        func=AF.Copy), waits=[("mm", tr_mark[i])], inc="aa")

                for step in range(NIT + 9):
                    if step < NIT:
                        s0(step)
                    if 0 <= step - 1 < NIT:
                        s1(step - 1)
                    if 0 <= step - 6 < NIT:
                        s6(step - 6)
                    if 0 <= step - 4 < NIT:
                        s4(step - 4)
                    if 0 <= step - 2 < NIT:
                        s2(step - 2)
                    if 0 <= step - 3 < NIT:
                        s3(step - 3)
                    if 0 <= step - 5 < NIT:
                        s5(step - 5)
                    if 0 <= step - 7 < NIT:
                        s7(step - 7)
                    if 0 <= step - 8 < NIT:
                        s8(step - 8)
                attn_done_aa = P.val("aa")
                tap("attn", Hb[:, :, :].rearrange("p c t -> p (c t)"))

                em("gpsimd", lambda e: e.dma_start(out=Wv(0, (16, 512)), in_=gw_r),
                   waits=[("mm", P.val("mm")), ("aa", attn_done_aa),
                          ("vv", P.val("vv"))], inc="dG", amt=16)
                gw_ld = P.val("dG")
                og_ev = {}
                for g_ in range(4):
                    for ocg in range(4):
                        for tt in range(2):
                            it3 = (g_ * 4 + ocg) * 2 + tt
                            tsl = slice(tt * 512, tt * 512 + 512)
                            pband = pb[it3 % 2]
                            pw = [("dG", gw_ld), ("aa", attn_done_aa)]
                            if it3 >= 2:
                                pw.append(("aa", og_ev[it3 - 2]))
                            for ic in range(4):
                                mk = em("tensor", lambda e, g_=g_, ic=ic, ocg=ocg, tsl=tsl, pband=pband: e.matmul(
                                    pband[:, 0:512],
                                    Wv(0, (16, 4, 128))[:, g_ * 4 + ic, ocg],
                                    Hb[:, g_ * 4 + ic, tsl],
                                    start=(ic == 0), stop=(ic == 3)),
                                    waits=pw if ic == 0 else [],
                                    inc="mm" if ic == 3 else None)
                            og_ev[it3] = em("scalar", lambda e, g_=g_, ocg=ocg, tsl=tsl, pband=pband: e.activation(
                                out=qv[:, g_ * 4 + ocg, tsl], in_=pband[:, 0:512],
                                func=AF.Copy), waits=[("mm", mk)], inc="aa")
                og_done = P.val("aa")
                tap("og", qv[:, :, :].rearrange("p c t -> p (c t)"))

                fin_ev = {}
                for hf in range(2):
                    em("gpsimd", lambda e, hf=hf: e.dma_start(
                        out=Wv(0, (8, 2048)), in_=fin_r[:, hf * 8:hf * 8 + 8, :]),
                        waits=[("aa", og_done), ("mm", P.val("mm"))], inc="dG", amt=16)
                    fin_ld = P.val("dG")
                    for oc in range(16):
                        for tt in range(2):
                            it4 = oc * 2 + tt
                            tsl = slice(tt * 512, tt * 512 + 512)
                            pband = pb[it4 % 4]
                            pw2 = [("dG", fin_ld)]
                            if it4 >= 4:
                                pw2.append(("vv", fin_ev[(hf, it4 - 4)]))
                            elif hf == 1:
                                pw2.append(("vv", fin_ev[(0, 28 + it4)]))
                            for kc in range(8):
                                mk = em("tensor", lambda e, kc=kc, oc=oc, tsl=tsl, pband=pband, hf=hf: e.matmul(
                                    pband[:, 0:512],
                                    Wv(0, (8, 16, 128))[:, kc, oc],
                                    qv[:, hf * 8 + kc, tsl],
                                    start=(kc == 0), stop=(kc == 7)),
                                    waits=pw2 if kc == 0 else [],
                                    inc="mm" if kc == 7 else None)
                            fin_ev[(hf, it4)] = em("vector", lambda e, oc=oc, tsl=tsl, pband=pband: e.tensor_add(
                                out=R[:, oc, tsl], in0=R[:, oc, tsl], in1=pband[:, 0:512]),
                                waits=[("mm", mk)], inc="vv")
                fin_mm = P.val("mm")
                final_marks = {"vv": P.val("vv"), "aa": P.val("aa")}

            if stage >= 4:
                x2_vv = final_marks["vv"]
                tap("x2", R[:, :, :].rearrange("p c t -> p (c t)"))
                ln3 = {0: ln_full(0, pre_mm=[("vv", x2_vv)]), 1: None}
                ln3[1] = ln_full(1)
                tap("r0", r0row[:])
                tap("r1", varrow[:])
                f2_vv, f2_mm = ffn_pass(ln3, first_w_mm=fin_mm)
                tap("h3", Hb[:, :, :].rearrange("p c t -> p (c t)"))
                final_marks = {"vv": f2_vv, "aa": P.val("aa")}

            out_vv = final_marks["vv"]
            ocp = {}
            odma = {}
            for i in range(8):
                for c in range(NCH):
                    bank = pb[6 + (c % 2)]
                    pw = []
                    if i == 0 and c == 0:
                        pw = [("vv", out_vv), ("mm", P.val("mm")),
                              ("dG", P.val("dG"))]
                    elif (i, c) != (0, 1):
                        pc = (i, c - 2) if c >= 2 else (i - 1, 14 + c)
                        pw = [("aa", ocp[pc])]
                    if c == 0 and i >= 2:
                        pw.append(odma[i - 2])
                    mk = em("tensor", lambda e, i=i, c=c, bank=bank: e.transpose(
                        bank[:, 0:128], R[:, c, i * 128:i * 128 + 128], idf[:]),
                        waits=pw, inc="mm")
                    ocp[(i, c)] = em("scalar", lambda e, i=i, c=c, bank=bank: e.activation(
                        out=outbuf[:, i % 2, c * 128:c * 128 + 128], in_=bank[:, 0:128],
                        func=AF.Copy), waits=[("mm", mk)], inc="aa")
                qe, qs = ("sync", "dS") if i % 2 == 0 else ("gpsimd", "dG")
                odma[i] = (qs, em(qe, lambda e, i=i: e.dma_start(
                    out=y[i * 128:i * 128 + 128, :], in_=outbuf[:, i % 2, :]),
                    waits=[("aa", ocp[(i, NCH - 1)])], inc=qs, amt=16))
            em("sync", None, waits=[("dS", P.val("dS")), ("dG", P.val("dG"))])

            with nc.Block() as block:
                @block.sync
                def _(e):
                    P.replay("sync", e, sems)

                @block.tensor
                def _(e):
                    P.replay("tensor", e, sems)

                @block.vector
                def _(e):
                    P.replay("vector", e, sems)

                @block.scalar
                def _(e):
                    P.replay("scalar", e, sems)

                @block.gpsimd
                def _(e):
                    P.replay("gpsimd", e, sems)

    return nc



def _arrange(inputs):
    g = {}
    wg = np.asarray(inputs["L0_ffn_w_gate"])
    wu = np.asarray(inputs["L0_ffn_w_up"])
    wd = np.asarray(inputs["L0_ffn_w_down"])
    g["wg_r"] = np.ascontiguousarray(wg.reshape(16, 128, 32, 256).transpose(1, 2, 0, 3))
    g["wu_r"] = np.ascontiguousarray(wu.reshape(16, 128, 32, 256).transpose(1, 2, 0, 3))
    g["wd_r"] = np.ascontiguousarray(wd.reshape(16, 4, 128, 4, 512).transpose(2, 3, 0, 1, 4))
    g["sg_r"] = np.ascontiguousarray(np.asarray(inputs["L0_ffn_s_gate"]).reshape(64, 128).T)
    g["su_r"] = np.ascontiguousarray(np.asarray(inputs["L0_ffn_s_up"]).reshape(64, 128).T)
    g["sd_r"] = np.ascontiguousarray(np.asarray(inputs["L0_ffn_s_down"]).reshape(16, 128).T)
    g["wqc_r"] = np.ascontiguousarray(
        np.asarray(inputs["L0_attn_w_qc"]).reshape(16, 128, 1024).transpose(1, 0, 2))
    g["wqup_r"] = np.ascontiguousarray(
        np.asarray(inputs["L0_attn_w_qup"]).reshape(8, 128, 2048).transpose(1, 0, 2))
    for nm, key in (("wkvb_r", "L0_attn_w_kv_b"), ("wzb_r", "L0_attn_w_z_b"),
                    ("wk_r", "L0_attn_w_k"), ("wv_r", "L0_attn_w_v"),
                    ("wikv_r", "L0_attn_wi_kv"), ("wiz_r", "L0_attn_wi_z"),
                    ("wdq_r", "L0_attn_w_dq"), ("ww_r", "L0_attn_w_w")):
        w = np.asarray(inputs[key])
        g[nm] = np.ascontiguousarray(w.reshape(16, 128, w.shape[1]).transpose(1, 0, 2))
    g["wiuq_r"] = np.ascontiguousarray(np.asarray(inputs["L0_attn_w_iuq"]))
    g["gw_r"] = np.ascontiguousarray(
        np.asarray(inputs["L0_attn_group_w"]).reshape(4, 4, 128, 512)
        .transpose(2, 0, 1, 3).reshape(128, 16, 512))
    g["fin_r"] = np.ascontiguousarray(
        np.asarray(inputs["L0_attn_final_w"]).reshape(16, 128, 2048).transpose(1, 0, 2))
    g["qnw_r"] = np.ascontiguousarray(np.asarray(inputs["L0_attn_q_norm_w"]).reshape(1, 128))
    g["knw_r"] = np.ascontiguousarray(np.asarray(inputs["L0_attn_k_norm_w"]).reshape(1, 128))
    g["sink_r"] = np.ascontiguousarray(np.asarray(inputs["L0_attn_sink_logit"]).reshape(1, 16))
    g["biasb_r"] = np.ascontiguousarray(np.asarray(inputs["L0_attn_bias_b"]))
    g["biasi_r"] = np.ascontiguousarray(np.asarray(inputs["L0_attn_bias_i"]))
    g["idf_r"] = np.eye(128, dtype=np.float32)
    sg_ = np.arange(128)
    sig = np.where(sg_ < 32, sg_ + 32, np.where(sg_ < 64, sg_ - 32, sg_))
    sw_ = np.zeros((128, 128), BF)
    sw_[sig, np.arange(128)] = 1
    g["swp_r"] = sw_
    hs = np.zeros((128, 16, 16), BF)
    for c in range(16):
        hs[:, c, c] = 1
    g["hsel_r"] = hs
    qs = np.zeros((16, 16, 128), BF)
    for h_ in range(16):
        qs[h_, h_, :] = 1
    g["qsel_r"] = qs
    return g


def _percore_consts(hc):
    t0 = hc * T
    p = np.arange(t0, t0 + T)
    c = {}
    inv = 1.0 / (10000.0 ** (np.arange(0, RD, 2, dtype=np.float64) / RD))
    ang = p[:, None] * inv
    cos = np.cos(ang).astype(np.float32)
    sin = np.sin(ang).astype(np.float32)
    Cm = np.ones((128, 1024), np.float32)
    Sm = np.zeros((128, 1024), np.float32)
    Cm[0:32] = cos.T
    Cm[32:64] = cos.T
    Sm[0:32] = -sin.T
    Sm[32:64] = sin.T
    c["ropeC_r"] = Cm.astype(BF)
    c["ropeS_r"] = Sm.astype(BF)
    c["coskt_r"] = np.ascontiguousarray(cos.reshape(8, 128, 32).transpose(1, 0, 2)).astype(BF)
    c["sinkt_r"] = np.ascontiguousarray(sin.reshape(8, 128, 32).transpose(1, 0, 2)).astype(BF)
    jglob = np.concatenate([np.arange(t0 - 512, t0), np.arange(t0, t0 + T)])
    mk = np.zeros((128, NQT, 5, 128), np.float32)
    for qt in range(NQT):
        qpos = p[qt * 128:(qt + 1) * 128]
        for krel in range(5):
            kpos = jglob[(qt + krel) * 128:(qt + krel) * 128 + 128]
            valid = (kpos[:, None] >= 0) & (kpos[:, None] <= qpos[None, :]) & \
                    (qpos[None, :] - kpos[:, None] < SWIN)
            mk[:, qt, krel, :] = valid
    c["maskjm_r"] = mk.astype(BF)
    val = ((np.arange(NB) * M + (M - 1))[None, :] < p[:, None]).astype(np.float32)
    v01 = np.ascontiguousarray(val.reshape(NQT, 128, NB).transpose(1, 0, 2))
    c["valid01_r"] = v01.astype(BF)
    c["validadd_r"] = ((v01 - 1.0) * 1e30).astype(BF)
    return c


_TBROW = np.broadcast_to(
    (-np.arange(NB, dtype=np.float64) * 1e-20).astype(np.float32), (128, NB)).copy()

_NC_CACHE = {}


def kernel(**inputs):
    key = (STAGE, TAPS, SUB)
    if key not in _NC_CACHE:
        _NC_CACHE[key] = build_program(STAGE, TAPS, SUB)
    nc = _NC_CACHE[key]
    shared = _arrange(inputs)
    shared["tbrow_r"] = _TBROW
    x = np.asarray(inputs["L0_attn_h"])
    pc = [_percore_consts(0), _percore_consts(1)]
    in_maps = []
    for core in range(8):
        b, hc = core // 2, core % 2
        mmap = dict(shared)
        mmap.update(pc[hc])
        mmap["xin"] = np.ascontiguousarray(x[b, hc * T:(hc + 1) * T].T)
        in_maps.append(mmap)
    res = run_bass_kernel_spmd(nc, in_maps, core_ids=list(range(8)))
    out = np.empty((B, TSEQ, DM), np.float32)
    for core in range(8):
        b, hc = core // 2, core % 2
        out[b, hc * T:(hc + 1) * T] = res.results[core]["y"]
    return out


# version 08
# speedup vs baseline: 1.2743x; 1.0375x over previous
import os
import numpy as np
import ml_dtypes
from contextlib import ExitStack

import concourse.bass as bass
import concourse.mybir as mybir
from concourse.bass_utils import run_bass_kernel_spmd

F32 = mybir.dt.float32
BF16 = mybir.dt.bfloat16
I8 = mybir.dt.int8
AF = mybir.ActivationFunctionType
ALU = mybir.AluOpType
AX = mybir.AxisListType
BF = ml_dtypes.bfloat16

B, TSEQ, DM = 4, 2048, 2048
T = 1024
NCH = DM // 128
H, D, RD = 16, 128, 64
M, NB = 32, 64
G, DG, NIH, CI = 4, 512, 4, 64
SWIN = 512
INTER = 8192
NI = INTER // 128
FT = 512
NQT = T // 128
KW = 12
EPS_LN, EPS_RMS = 1e-5, 1e-6

SEC_K = 0
SEC_V = SEC_K + T * D
SEC_CKV = SEC_V + T * D
SEC_KCI = SEC_CKV + 128 * 32
BLOB_N = SEC_KCI + 64 * 32

STAGE = int(os.environ.get("MK_STAGE", "4"))
SUB = os.environ.get("MK_SUB", "full")
TAPS = tuple(t for t in os.environ.get("MK_TAPS", "").split(",") if t)


class Prog:
    ENGS = ("sync", "tensor", "vector", "scalar", "gpsimd")

    def __init__(self):
        self.items = {e: [] for e in self.ENGS}
        self.cnt = {}

    def val(self, sem):
        return self.cnt.get(sem, 0)

    def emit(self, eng, fn=None, waits=(), inc=None, amt=1):
        w = [(s, v) for (s, v) in waits if v and v > 0]
        self.items[eng].append((w, fn, inc, amt))
        if inc is not None:
            self.cnt[inc] = self.cnt.get(inc, 0) + amt
            return self.cnt[inc]
        return None

    def replay(self, eng_name, eng, sems):
        for w, fn, inc, amt in self.items[eng_name]:
            for s, v in w:
                eng.wait_ge(sems[s], v)
            ins = fn(eng) if fn is not None else None
            if inc is not None:
                if ins is None:
                    ins = eng.nop()
                ins.then_inc(sems[inc], amt)


def build_program(stage, taps, sub="full"):
    nc = bass.Bass("TRN2", target_bir_lowering=False)

    def din(name, shape, dtype):
        return nc.dram_tensor(name, shape, dtype, kind="ExternalInput").ap()

    xin = din("xin", [DM, T], F32)
    wg_r = din("wg_r", [128, 32, 16, 256], I8)
    wu_r = din("wu_r", [128, 32, 16, 256], I8)
    wd_r = din("wd_r", [128, 4, 16, 4, 512], I8)
    sg_r = din("sg_r", [128, 64], F32)
    su_r = din("su_r", [128, 64], F32)
    sd_r = din("sd_r", [128, 16], F32)
    wqc_r = din("wqc_r", [128, 16, 1024], F32)
    wqup_r = din("wqup_r", [128, 8, 2048], F32)
    wkvb_r = din("wkvb_r", [128, 16, 128], F32)
    wzb_r = din("wzb_r", [128, 16, 128], F32)
    wk_r = din("wk_r", [128, 16, 128], F32)
    wv_r = din("wv_r", [128, 16, 128], F32)
    wikv_r = din("wikv_r", [128, 16, 64], F32)
    wiz_r = din("wiz_r", [128, 16, 64], F32)
    wdq_r = din("wdq_r", [128, 16, 64], F32)
    ww_r = din("ww_r", [128, 16, 4], F32)
    wiuq_r = din("wiuq_r", [64, 256], F32)
    gw_r = din("gw_r", [128, 16, 512], F32)
    fin_r = din("fin_r", [128, 16, 2048], F32)
    qnw_r = din("qnw_r", [1, 128], F32)
    knw_r = din("knw_r", [1, 128], F32)
    sink_r = din("sink_r", [1, 16], F32)
    biasb_r = din("biasb_r", [32, 128], F32)
    biasi_r = din("biasi_r", [32, 64], F32)
    ropeC_r = din("ropeC_r", [128, 1024], BF16)
    ropeS_r = din("ropeS_r", [128, 1024], BF16)
    swp_r = din("swp_r", [128, 128], BF16)
    coskt_r = din("coskt_r", [128, 8, 32], BF16)
    sinkt_r = din("sinkt_r", [128, 8, 32], BF16)
    maskjm_r = din("maskjm_r", [128, 8, 5, 128], BF16)
    valid01_r = din("valid01_r", [128, 8, 64], BF16)
    validadd_r = din("validadd_r", [128, 8, 64], BF16)
    tbrow_r = din("tbrow_r", [128, 64], F32)
    idf_r = din("idf_r", [128, 128], F32)
    hsel_r = din("hsel_r", [128, 16, 16], BF16)
    qsel_r = din("qsel_r", [16, 16, 128], BF16)

    y = nc.dram_tensor("y", [T, DM], F32, kind="ExternalOutput").ap()
    dbg = {}
    tap_shapes = {"h": [128, NCH * 1024], "q": [128, 16 * 1024],
                  "kwin": [128, KW * 128], "vwin": [128, KW * 128],
                  "ckv": [128, 64], "kci": [64, 64], "selT": [64, NQT * 128],
                  "attn": [128, NCH * 1024], "og": [128, 16 * 1024],
                  "isc": [128, 64], "x2": [128, NCH * 1024], "h3": [128, NCH * 1024], "r0": [1, 512], "r1": [1, 512]}
    for tp in taps:
        dbg[tp] = nc.dram_tensor("dbg_" + tp, tap_shapes[tp], F32,
                                 kind="ExternalOutput").ap()

    cc_in = nc.dram_tensor("cc_in", [1, BLOB_N], BF16)
    cc_out = nc.dram_tensor("cc_out", [2, BLOB_N], BF16)

    P = Prog()
    em = P.emit
    ctx = ExitStack()
    sb = lambda n, s, d: ctx.enter_context(nc.sbuf_tensor(n, s, d))
    psT = lambda n, s, d: ctx.enter_context(nc.psum_tensor(n, s, d))

    with ctx:
        R = sb("R", [128, NCH, 1024], F32)
        Hb = sb("Hb", [128, NCH, 1024], BF16)
        A = sb("A", [128, 32768], BF16)
        W = sb("W", [128, 16384], BF16)
        fv = A[:, :].rearrange("p (c t) -> p c t", c=NI)
        qv = A[:, 0:16384].rearrange("p (c t) -> p c t", c=16)
        qcv = A[:, 16384:24576].rearrange("p (c t) -> p c t", c=8)
        ropeC = A[:, 16384:17408]
        ropeS = A[:, 17408:18432]
        tA = A[:, 18432:19456]
        qsel = A[0:16, 19456:21504].rearrange("p (h t) -> p h t", h=16)
        stats32 = A[0:16, 21504:23552].bitcast(F32)
        hsel = A[:, 23552:23808].rearrange("p (c h) -> p c h", c=16)
        rstd_hs = A[0:16, 23808:24832]
        zb32 = A[:, 16384:18432].bitcast(F32)
        cbf = A[:, 18432:19456]
        qi_v = A[0:64, 19456:23552].rearrange("p (c t) -> p c t", c=4)
        dq_v = A[0:64, 23552:24576]
        maskjm = A[:, 24832:29952].rearrange("p (q k t) -> p q k t", q=8, k=5)
        esw = W[:, 0:1920].rearrange("p (a k t) -> p a k t", a=3, k=5)
        esp = W[0:64, 1920:2304].rearrange("p (a t) -> p a t", a=3)
        avs = W[:, 2304:2560].rearrange("p (a t) -> p a t", a=2)
        attn_tm = W[:, 2560:3072].bitcast(F32).rearrange("p (a t) -> p a t", a=2)
        kwin_fm = W[:, 3072:4608]
        selT = W[0:64, 5376:6400]
        kwin_tm = W[:, 11328:14400].bitcast(F32).rearrange("p (k d) -> p k d", k=KW)
        vwin = W[:, 14400:15936].rearrange("p (k d) -> p k d", k=KW)
        outbuf = W[:, 0:8192].bitcast(F32).rearrange("p (a t) -> p a t", a=2)
        sq = sb("sq", [128, 512], BF16)
        usb = sb("usb", [128, 2, 512], BF16)
        sel32 = sb("sel32", [128, 64], F32)
        wgt_sb = sb("wgt_sb", [128, NQT, 4], F32)
        isc = sb("isc", [128, 64], F32)
        work = sb("work", [128, 64], F32)
        m01 = sb("m01", [128, 64], F32)
        mx8 = sb("mx8", [128, 8], F32)
        sgc = sb("sgc", [128, 64], F32)
        suc = sb("suc", [128, 64], F32)
        sdh = sb("sdh", [128, 16], F32)
        coskt = sb("coskt", [128, 8, 32], BF16)
        sinkt = sb("sinkt", [128, 8, 32], BF16)
        valid01 = A[:, 29952:30464].rearrange("p (q n) -> p q n", q=8)
        validadd = A[:, 30464:30976].rearrange("p (q n) -> p q n", q=8)
        tbrow = sb("tbrow", [128, 64], F32)
        idf = sb("idf", [128, 128], F32)
        swp = sb("swp", [128, 128], BF16)
        onescf = sb("onescf", [128, 1], F32)
        onescb = sb("onescb", [128, 1], BF16)
        onesrf = sb("onesrf", [1, 128], F32)
        onesrb = sb("onesrb", [1, 128], BF16)
        qnw_row = sb("qnw_row", [1, 128], F32)
        knw_row = sb("knw_row", [1, 128], F32)
        qnw_col = sb("qnw_col", [128, 1], F32)
        knw_bc = sb("knw_bc", [128, 128], F32)
        knw_col = sb("knw_col", [128, 1], F32)
        sink_row = sb("sink_row", [1, 16], F32)
        sink_cols = sb("sink_cols", [128, 16], F32)
        biasbT = sb("biasbT", [128, 32], F32)
        biasiT = sb("biasiT", [64, 32], F32)
        biasb_sb = A[0:32, 0:256].bitcast(F32)
        biasi_sb = A[0:32, 256:384].bitcast(F32)
        epsln = sb("epsln", [1, 1], F32)
        epsrms = sb("epsrms", [128, 1], F32)
        lnC = sb("lnC", [128, 1], F32)
        wiuq_sb = W[0:64, 15936:16192]
        r0row = sb("r0row", [1, 512], BF16)
        r1row = sb("r1row", [1, 512], BF16)
        varrow = sb("varrow", [1, 512], F32)
        rstd_r32 = sb("rstd_r32", [1, 32], F32)
        cmp32 = sb("cmp32", [128, 32], F32)
        gmx = sb("gmx", [128, 32], F32)
        gsm = sb("gsm", [128, 32], F32)
        grec = sb("grec", [128, 32], F32)
        rms_c = sb("rms_c", [128, 1], F32)
        ckv_fm = sb("ckv_fm", [128, 64], BF16)
        ckv_nm = sb("ckv_nm", [64, 128], BF16)
        kci_cm = sb("kci_cm", [64, 64], BF16)
        m11 = sb("m11", [1, 8], F32)
        srow = W[0:1, 6400:7424].bitcast(F32)
        nSW = sb("nSW", [128, 16], F32)
        nSP = sb("nSP", [128, 16], F32)
        sinkexp = sb("sinkexp", [128, 16], F32)
        hsel2 = A[:, 30976:31232].rearrange("p (c h) -> p c h", c=16)
        ssw_c = sb("ssw_c", [128, 2], F32)
        ssp_c = sb("ssp_c", [128, 2], F32)
        rsw_c = sb("rsw_c", [128, 2], F32)
        rsp_c = sb("rsp_c", [128, 2], F32)
        red1 = sb("red1", [128, 2], F32)
        red2 = sb("red2", [128, 2], F32)

        pb = [psT(f"pb{i}", [128, 512], F32) for i in range(8)]

        semctx = ExitStack()
        sems = {n: semctx.enter_context(nc.semaphore(n))
                for n in ("dS", "dG", "mm", "vv", "aa", "pp", "cc")}

        with semctx:
            def Wv(off, shape):
                n = int(np.prod(shape))
                v = W[:, off:off + n]
                if len(shape) > 1:
                    lets = "abcdef"[: len(shape)]
                    pat = f"p ({' '.join(lets)}) -> p {' '.join(lets)}"
                    v = v.rearrange(pat, **{lets[i]: shape[i] for i in range(len(shape))})
                return v

            def tap(name, src_ap):
                if name in dbg:
                    waits = [("aa", P.val("aa")), ("vv", P.val("vv")),
                             ("mm", P.val("mm")), ("pp", P.val("pp"))]
                    em("gpsimd", lambda e, s=src_ap, d=dbg[name]: e.dma_start(out=d, in_=s),
                       waits=waits, inc="dG", amt=16)

            for o, i in ((sgc[:], sg_r), (suc[:], su_r), (sdh[:], sd_r),
                         (coskt[:], coskt_r), (sinkt[:], sinkt_r),
                         (tbrow[:], tbrow_r), (idf[:], idf_r),
                         (swp[:], swp_r),
                         (qnw_row[:], qnw_r), (knw_row[:], knw_r),
                         (sink_row[:], sink_r),
                         (biasb_sb, biasb_r), (biasi_sb, biasi_r)):
                em("sync", lambda e, o=o, i=i: e.dma_start(out=o, in_=i), inc="dS", amt=16)
            xld = {}
            for g in range(4):
                xld[g] = em("sync", lambda e, g=g: e.dma_start(
                    out=R[:, 4 * g:4 * g + 4, :],
                    in_=xin.rearrange("(c p) t -> p c t", p=128)[:, 4 * g:4 * g + 4, :]),
                    inc="dS", amt=16)
            ld_done = P.val("dS")

            def memsets(e):
                e.memset(onescf[:], 1.0)
                e.memset(onescb[:], 1.0)
                e.memset(onesrf[:], 1.0)
                e.memset(onesrb[:], 1.0)
                e.memset(epsln[:], EPS_LN)
                e.memset(epsrms[:], EPS_RMS)
                return e.memset(lnC[:], float(np.log(1.0 / np.sqrt(128.0))))
            vmark = em("vector", memsets, inc="vv")
            vmark = em("vector", lambda e: e.tensor_scalar(
                out=sdh[:], in0=sdh[:], scalar1=0.5, scalar2=None, op0=ALU.mult),
                waits=[("dS", ld_done)], inc="vv")
            mk = em("tensor", lambda e: e.transpose(pb[6][:, 0:1], qnw_row[:], idf[0:1, 0:1]),
                    waits=[("dS", ld_done), ("vv", vmark)], inc="mm")
            mk = em("tensor", lambda e: e.transpose(pb[6][:, 128:129], knw_row[:], idf[0:1, 0:1]),
                    inc="mm")
            mk = em("tensor", lambda e: e.transpose(pb[6][:, 256:288], biasb_sb, idf[0:32, 0:32]),
                    inc="mm")
            mk = em("tensor", lambda e: e.transpose(pb[6][0:64, 288:320], biasi_sb, idf[0:32, 0:32]),
                    inc="mm")
            mk = em("tensor", lambda e: e.matmul(pb[6][:, 320:448], onesrf[:], knw_row[:],
                                                 start=True, stop=True), inc="mm")
            mk = em("tensor", lambda e: e.matmul(pb[6][:, 448:464], onesrf[:], sink_row[:],
                                                 start=True, stop=True), inc="mm")
            vmark = em("vector", lambda e: e.tensor_copy(out=qnw_col[:], in_=pb[6][:, 0:1]),
                       waits=[("mm", mk)], inc="vv")
            vmark = em("vector", lambda e: e.tensor_copy(out=knw_col[:], in_=pb[6][:, 128:129]), inc="vv")
            vmark = em("vector", lambda e: e.tensor_copy(out=biasbT[:, :], in_=pb[6][:, 256:288]), inc="vv")
            vmark = em("vector", lambda e: e.tensor_copy(out=biasiT[:, :], in_=pb[6][0:64, 288:320]), inc="vv")
            vmark = em("vector", lambda e: e.tensor_copy(out=knw_bc[:], in_=pb[6][:, 320:448]), inc="vv")
            init_done = em("vector", lambda e: e.tensor_copy(out=sink_cols[:], in_=pb[6][:, 448:464]), inc="vv")

            def ln_full(tau, pre_mm=(), pre_vv=(), pre_aa=(), chunk_w=None,
                        entry=None):
                ts = slice(tau * FT, tau * FT + FT)
                if entry is None:
                    entry = [("vv", P.val("vv")), ("pp", P.val("pp")),
                             ("aa", P.val("aa"))]
                xmk = {}
                smk = {}
                prevx, prevs = ln_full.prev
                for c in range(NCH):
                    cw = list(chunk_w[c]) if chunk_w is not None else []
                    w1 = (list(pre_aa) if c == 0 else []) + cw
                    if c >= 2:
                        w1.append(("mm", xmk[c - 2]))
                    elif prevx is not None:
                        w1.append(("mm", prevx))
                    aw1 = em("scalar", lambda e, c=c, ts=ts: e.activation(
                        out=usb[:, c % 2, :], in_=R[:, c, ts], func=AF.Copy),
                        waits=w1, inc="aa")
                    xmk[c] = em("tensor", lambda e, c=c: e.matmul(
                        pb[4][0:1, :], onescb[:], usb[:, c % 2, :],
                        start=(c == 0), stop=(c == NCH - 1)),
                        waits=[("aa", aw1)] + (list(pre_mm) + entry if c == 0 else []),
                        inc="mm")
                    w2 = list(cw)
                    if c >= 1:
                        w2.append(("mm", smk[c - 1]))
                    elif prevs is not None:
                        w2.append(("mm", prevs))
                    aw2 = em("scalar", lambda e, c=c, ts=ts: e.activation(
                        out=sq[:, 0:FT], in_=R[:, c, ts], func=AF.Square),
                        waits=w2, inc="aa")
                    smk[c] = em("tensor", lambda e, c=c: e.matmul(
                        pb[5][0:1, :], onescb[:], sq[:, 0:FT],
                        start=(c == 0), stop=(c == NCH - 1)),
                        waits=[("aa", aw2)], inc="mm")
                ln_full.prev = (xmk[NCH - 1], smk[NCH - 1])
                mm_sums = P.val("mm")
                vw = em("vector", lambda e: e.tensor_scalar(
                    out=r0row[:], in0=pb[4][0:1, :], scalar1=1.0 / DM, scalar2=None,
                    op0=ALU.mult), waits=[("mm", mm_sums)] + list(pre_vv), inc="vv")
                vw = em("vector", lambda e: e.tensor_mul(out=r1row[:], in0=r0row[:], in1=r0row[:]),
                        waits=[("vv", vw)], inc="vv")
                vw = em("vector", lambda e: e.scalar_tensor_tensor(
                    out=varrow[:], in0=pb[5][0:1, :], scalar=1.0 / DM, in1=r1row[:],
                    op0=ALU.mult, op1=ALU.subtract), waits=[("vv", vw)], inc="vv")
                row_vv = vw
                aw = em("scalar", lambda e: e.activation(
                    out=varrow[:], in_=varrow[:], func=AF.Ln, bias=epsln[:]),
                    waits=[("vv", vw)], inc="aa")
                aw = em("scalar", lambda e: e.activation(
                    out=varrow[:], in_=varrow[:], func=AF.Exp, scale=-0.5),
                    waits=[("aa", aw)], inc="aa")
                mk1 = em("tensor", lambda e: e.matmul(pb[6][:, :], onesrb[:], r0row[:],
                                                      start=True, stop=True),
                         waits=[("vv", vw)], inc="mm")
                mk1 = em("tensor", lambda e: e.matmul(pb[7][:, :], onesrf[:], varrow[:],
                                                      start=True, stop=True),
                         waits=[("aa", aw)], inc="mm")
                amarks = {}
                for c in range(NCH):
                    em("vector", lambda e, c=c, ts=ts: e.tensor_sub(
                        out=Hb[:, c, ts], in0=R[:, c, ts], in1=pb[6][:, :]),
                        waits=[("mm", mk1)] if c == 0 else [], inc="vv")
                    amarks[c] = em("vector", lambda e, c=c, ts=ts: e.tensor_mul(
                        out=Hb[:, c, ts], in0=Hb[:, c, ts], in1=pb[7][:, :]), inc="vv")
                return P.val("vv"), P.val("pp"), amarks, row_vv

            ln_full.prev = (None, None)

            def ffn_pass(ln_marks, first_w_mm=None):
                fvh = A[:, :].rearrange("p (c t) -> p c t", c=32)
                amA = [ln_marks[0][2], ln_marks[1][2]]
                pe_umark = {}
                evac = {}
                dma_mark = {}
                prevB_mm = None
                prevB_vv = None

                def emit_wdma(blk):
                    nonlocal prevB_mm
                    par = blk % 2
                    if blk >= 2:
                        wwaits = [("mm", pe_umark[2 * (blk - 2) + 1])]
                    else:
                        wwaits = [("mm", first_w_mm if first_w_mm is not None
                                  else P.val("mm"))]
                    if par == 1 and prevB_mm is not None and blk % 16 == 1:
                        wwaits.append(("mm", prevB_mm))
                    em("gpsimd", lambda e, blk=blk, par=par: e.dma_start(
                        out=Wv(8192 * par, (16, 256)), in_=wg_r[:, blk]),
                        waits=wwaits, inc="dG", amt=16)
                    dma_mark[blk] = em("gpsimd", lambda e, blk=blk, par=par: e.dma_start(
                        out=Wv(8192 * par + 4096, (16, 256)), in_=wu_r[:, blk]),
                        inc="dG", amt=16)

                for ih in range(2):
                    emit_wdma(ih * 16)
                    emit_wdma(ih * 16 + 1)
                    for blk in range(ih * 16, ih * 16 + 16):
                        par = blk % 2
                        for sub in range(2):
                            oc = blk * 2 + sub
                            ol = oc - ih * 32
                            pgs = [pb[(oc % 2) * 4 + th] for th in range(2)]
                            pus = [pb[(oc % 2) * 4 + 2 + th] for th in range(2)]
                            pw = [("dG", dma_mark[blk])]
                            if oc >= 2:
                                ag, vf = evac[oc - 2]
                                pw += [("aa", ag), ("vv", vf)]
                            if ol in (0, 1) and prevB_vv is not None:
                                pw += [("vv", prevB_vv)]
                            for c in range(NCH):
                                for th in range(2):
                                    cw = [("vv", amA[th][c])] if oc == 0 else []
                                    mk_g = em("tensor", lambda e, c=c, th=th, sub=sub, par=par, pgs=pgs: e.matmul(
                                        pgs[th][:, 0:FT],
                                        Wv(8192 * par, (16, 2, 128))[:, c, sub],
                                        Hb[:, c, th * FT:th * FT + FT],
                                        start=(c == 0), stop=(c == NCH - 1)),
                                        waits=(pw if (c == 0 and th == 0) else []) + cw,
                                        inc="mm" if (c == NCH - 1 and th == 1) else None)
                            for c in range(NCH):
                                for th in range(2):
                                    mk_u = em("tensor", lambda e, c=c, th=th, sub=sub, par=par, pus=pus: e.matmul(
                                        pus[th][:, 0:FT],
                                        Wv(8192 * par + 4096, (16, 2, 128))[:, c, sub],
                                        Hb[:, c, th * FT:th * FT + FT],
                                        start=(c == 0), stop=(c == NCH - 1)),
                                        inc="mm" if (c == NCH - 1 and th == 1) else None)
                            pe_umark[oc] = mk_u
                            ag = None
                            for th in range(2):
                                ag = em("scalar", lambda e, oc=oc, ol=ol, th=th, pgs=pgs: e.activation(
                                    out=fvh[:, ol, th * FT:th * FT + FT], in_=pgs[th][:, 0:FT],
                                    func=AF.Silu, scale=sgc[:, oc:oc + 1]),
                                    waits=[("mm", mk_g)] if th == 0 else [], inc="aa")
                            vf = None
                            for th in range(2):
                                vf = em("vector", lambda e, oc=oc, ol=ol, th=th, pus=pus: e.scalar_tensor_tensor(
                                    out=fvh[:, ol, th * FT:th * FT + FT], in0=pus[th][:, 0:FT],
                                    scalar=suc[:, oc:oc + 1], in1=fvh[:, ol, th * FT:th * FT + FT],
                                    op0=ALU.mult, op1=ALU.mult),
                                    waits=[("mm", mk_u), ("aa", ag)] if th == 0 else [], inc="vv")
                            evac[oc] = (ag, vf)
                        if blk + 2 < ih * 16 + 16:
                            emit_wdma(blk + 2)
                    f_done = P.val("vv")
                    slab_mark = {}
                    yev = {}
                    for oq in range(4):
                        accs = [pb[j * 2 + th] for j in range(4) for th in range(2)]
                        for si in range(8):
                            s = ih * 8 + si
                            par = si % 2
                            if oq == 0 and si < 2:
                                wwaits = [("vv", f_done)]
                            else:
                                prev_use = (oq, si - 2) if si >= 2 else (oq - 1, 6 + si)
                                wwaits = [("mm", slab_mark[prev_use])]
                            em("gpsimd", lambda e, oq=oq, s=s, par=par: e.dma_start(
                                out=Wv(8192 + 4096 * par, (4, 512)), in_=wd_r[:, oq, s]),
                                waits=wwaits, inc="dG", amt=16)
                            dmk = P.val("dG")
                            pw = [("dG", dmk)]
                            if si == 0:
                                if oq == 0:
                                    pw.append(("vv", f_done))
                                else:
                                    pw.append(("vv", yev[oq - 1]))
                            for c in range(4):
                                for j in range(4):
                                    for th in range(2):
                                        slab_mark[(oq, si)] = em("tensor", lambda e, c=c, j=j, th=th, si=si, s=s, par=par: e.matmul(
                                            pb[j * 2 + th][:, 0:FT],
                                            Wv(8192 + 4096 * par, (4, 4, 128))[:, c, j],
                                            fvh[:, si * 4 + c, th * FT:th * FT + FT],
                                            start=(si == 0 and c == 0), stop=(si == 7 and c == 3)),
                                            waits=pw if (c == 0 and j == 0 and th == 0) else [],
                                            inc="mm" if (c == 3 and j == 3 and th == 1) else None)
                        mk = slab_mark[(oq, 7)]
                        vw = None
                        for j in range(4):
                            oc = oq * 4 + j
                            for th in range(2):
                                vw = em("vector", lambda e, oc=oc, j=j, th=th: e.scalar_tensor_tensor(
                                    out=R[:, oc, th * FT:th * FT + FT], in0=pb[j * 2 + th][:, 0:FT],
                                    scalar=sdh[:, oc:oc + 1], in1=R[:, oc, th * FT:th * FT + FT],
                                    op0=ALU.mult, op1=ALU.add),
                                    waits=[("mm", mk)] if (j == 0 and th == 0) else [], inc="vv")
                        yev[oq] = vw
                    prevB_mm = slab_mark[(3, 7)]
                    prevB_vv = yev[3]
                ffn_tile_done = (P.val("vv"), P.val("mm"))
                return ffn_tile_done

            if stage >= 1:
                ln1cw = {c: [("dS", xld[c // 4])] for c in range(NCH)}
                ln1 = {0: ln_full(0, pre_vv=[("vv", init_done)], chunk_w=ln1cw),
                       1: None}
                ln1[1] = ln_full(1, chunk_w=ln1cw,
                                 entry=[("vv", ln1[0][3]), ("aa", P.val("aa"))])
                if sub == "ln":
                    final_marks = {"vv": ln1[1][0], "aa": P.val("aa")}
                else:
                    f1_vv, f1_mm = ffn_pass(ln1, first_w_mm=0)
                    final_marks = {"vv": f1_vv, "aa": P.val("aa")}
            else:
                final_marks = {"vv": init_done, "aa": P.val("aa")}
                if stage >= -1:
                    pass

            if stage >= 2:
                ln2 = {0: ln_full(0, pre_mm=[("vv", f1_vv)]), 1: None}
                ln2[1] = ln_full(1, entry=[("vv", ln2[0][3]), ("aa", P.val("aa"))])
                h_vv, h_pp = ln2[1][0], ln2[1][1]
                tap("h", Hb[:, :, :].rearrange("p c t -> p (c t)"))

                em("gpsimd", lambda e: e.dma_start(out=Wv(0, (16, 1024)), in_=wqc_r),
                   waits=[("mm", P.val("mm"))], inc="dG", amt=16)
                wqc_ld = P.val("dG")
                qc_ev = {}
                for co in range(8):
                    for tt in range(2):
                        it2 = co * 2 + tt
                        tsl = slice(tt * 512, tt * 512 + 512)
                        pband = pb[it2 % 2]
                        pw = [("dG", wqc_ld), ("vv", h_vv), ("pp", h_pp)]
                        if it2 >= 2:
                            pw.append(("aa", qc_ev[it2 - 2]))
                        for c in range(NCH):
                            mk = em("tensor", lambda e, c=c, co=co, tsl=tsl, pband=pband: e.matmul(
                                pband[:, 0:512], Wv(0, (16, 8, 128))[:, c, co],
                                Hb[:, c, tsl], start=(c == 0), stop=(c == NCH - 1)),
                                waits=pw if c == 0 else [],
                                inc="mm" if c == NCH - 1 else None)
                        qc_ev[it2] = em("scalar", lambda e, co=co, tsl=tsl, pband=pband: e.activation(
                            out=qcv[:, co, tsl], in_=pband[:, 0:512], func=AF.Copy),
                            waits=[("mm", mk)], inc="aa")
                qc_done = P.val("aa")

                em("gpsimd", lambda e: e.dma_start(out=Wv(0, (8, 2048)), in_=wqup_r),
                   waits=[("aa", qc_done)], inc="dG", amt=16)
                wqup_ld = P.val("dG")
                q_ev = {}
                for hh in range(16):
                    for tt in range(2):
                        it2 = hh * 2 + tt
                        tsl = slice(tt * 512, tt * 512 + 512)
                        pband = pb[2 + it2 % 2]
                        pw = [("dG", wqup_ld)]
                        if it2 >= 2:
                            pw.append(("aa", q_ev[it2 - 2]))
                        for c in range(8):
                            mk = em("tensor", lambda e, c=c, hh=hh, tsl=tsl, pband=pband: e.matmul(
                                pband[:, 0:512], Wv(0, (8, 16, 128))[:, c, hh],
                                qcv[:, c, tsl], start=(c == 0), stop=(c == 7)),
                                waits=pw if c == 0 else [],
                                inc="mm" if c == 7 else None)
                        q_ev[it2] = em("scalar", lambda e, hh=hh, tsl=tsl, pband=pband: e.activation(
                            out=qv[:, hh, tsl], in_=pband[:, 0:512], func=AF.Copy),
                            waits=[("mm", mk)], inc="aa")
                q_done = P.val("aa")

                hs_ld = em("sync", lambda e: e.dma_start(out=hsel[:, :, :], in_=hsel_r),
                           waits=[("aa", q_done)], inc="dS", amt=16)
                for hh in range(16):
                    aw = em("scalar", lambda e, hh=hh: e.activation(
                        out=sq[:, 0:512], in_=qv[:, hh, 0:512], func=AF.Square),
                        waits=[("aa", q_done), ("dS", hs_ld)] if hh == 0 else [("mm", P.val("mm"))], inc="aa")
                    em("tensor", lambda e, hh=hh: e.matmul(
                        pb[4][0:16, 0:512], hsel[:, hh, :], sq[:, 0:512],
                        start=(hh == 0), stop=(hh == 15)), waits=[("aa", aw)], inc="mm")
                    aw = em("scalar", lambda e, hh=hh: e.activation(
                        out=sq[:, 0:512], in_=qv[:, hh, 512:1024], func=AF.Square),
                        waits=[("mm", P.val("mm"))], inc="aa")
                    em("tensor", lambda e, hh=hh: e.matmul(
                        pb[5][0:16, 0:512], hsel[:, hh, :], sq[:, 0:512],
                        start=(hh == 0), stop=(hh == 15)), waits=[("aa", aw)], inc="mm")
                mkq = P.val("mm")
                aw = em("scalar", lambda e: e.activation(
                    out=stats32[:, 0:512], in_=pb[4][0:16, 0:512], func=AF.Ln,
                    bias=epsrms[0:16, :], scale=1.0 / 128.0), waits=[("mm", mkq)], inc="aa")
                aw = em("scalar", lambda e: e.activation(
                    out=stats32[:, 512:1024], in_=pb[5][0:16, 0:512], func=AF.Ln,
                    bias=epsrms[0:16, :], scale=1.0 / 128.0), inc="aa")
                rstd_done = em("scalar", lambda e: e.activation(
                    out=rstd_hs[:, :], in_=stats32[:, :], func=AF.Exp,
                    bias=lnC[0:16, :], scale=-0.5), waits=[("aa", aw)], inc="aa")

                em("sync", lambda e: e.dma_start(out=ropeC, in_=ropeC_r),
                   waits=[("aa", q_done)], inc="dS", amt=16)
                rp_ld = em("sync", lambda e: e.dma_start(out=ropeS, in_=ropeS_r),
                           inc="dS", amt=16)
                qs_ld = em("sync", lambda e: e.dma_start(out=qsel[:, :, :], in_=qsel_r),
                           inc="dS", amt=16)
                qn_prev = None
                qn_prev2 = {}
                for it2 in range(32):
                    hh, half = it2 // 2, it2 % 2
                    hsl = slice(half * 512, half * 512 + 512)
                    pA_ = pb[4 + it2 % 2]
                    pB_ = pb[6 + it2 % 2]
                    if it2 == 0:
                        pw = [("dS", qs_ld), ("aa", rstd_done), ("dS", rp_ld)]
                    elif it2 == 1:
                        pw = []
                    else:
                        pw = [("vv", qn_prev2[it2 - 2])]
                    mka = em("tensor", lambda e, hh=hh, hsl=hsl, pA_=pA_: e.matmul(
                        pA_[:, 0:512], swp[:], qv[:, hh, hsl],
                        start=True, stop=True), waits=pw, inc="mm")
                    mkb = em("tensor", lambda e, hh=hh, hsl=hsl, pB_=pB_: e.matmul(
                        pB_[:, 0:512], qsel[:, hh, :], rstd_hs[:, hsl],
                        start=True, stop=True), inc="mm")
                    ts_ = tA[:, 0:512] if it2 % 2 == 0 else tA[:, 512:1024]
                    vw = em("vector", lambda e, hh=hh, hsl=hsl, ts_=ts_: e.tensor_mul(
                        out=ts_, in0=qv[:, hh, hsl], in1=ropeC[:, hsl]),
                        waits=[("mm", mkb)], inc="vv")
                    vw = em("vector", lambda e, hh=hh, hsl=hsl, pA_=pA_: e.tensor_tensor(
                        out=qv[:, hh, hsl], in0=pA_[:, 0:512], in1=ropeS[:, hsl],
                        op=ALU.mult), inc="vv")
                    vw = em("vector", lambda e, hh=hh, hsl=hsl, ts_=ts_: e.tensor_add(
                        out=qv[:, hh, hsl], in0=qv[:, hh, hsl], in1=ts_), inc="vv")
                    vw = em("vector", lambda e, hh=hh, hsl=hsl, pB_=pB_: e.scalar_tensor_tensor(
                        out=qv[:, hh, hsl], in0=qv[:, hh, hsl], scalar=qnw_col[:],
                        in1=pB_[:, 0:512], op0=ALU.mult, op1=ALU.mult), inc="vv")
                    qn_prev2[it2] = vw
                    qn_prev = vw
                qn_done = qn_prev
                tap("q", qv[:, :, :].rearrange("p c t -> p (c t)"))
                em("sync", lambda e: e.dma_start(out=maskjm[:, :, :, :], in_=maskjm_r),
                   waits=[("vv", qn_done)], inc="dS", amt=16)
                em("sync", lambda e: e.dma_start(out=valid01[:, :, :], in_=valid01_r),
                   inc="dS", amt=16)
                valid_ld = em("sync", lambda e: e.dma_start(out=validadd[:, :, :], in_=validadd_r),
                              inc="dS", amt=16)

                offs = {}
                off = 0
                first = True
                for nm, src, csz in (("wkvb", wkvb_r, 128), ("wzb", wzb_r, 128),
                                     ("wk", wk_r, 128), ("wv", wv_r, 128),
                                     ("wikv", wikv_r, 64), ("wiz", wiz_r, 64),
                                     ("wdq", wdq_r, 64), ("ww", ww_r, 4)):
                    offs[nm] = off
                    em("gpsimd", lambda e, src=src, off=off, csz=csz: e.dma_start(
                        out=Wv(off, (16, csz)), in_=src),
                        waits=[("vv", qn_done)] if first else [], inc="dG", amt=16)
                    first = False
                    off += 16 * csz
                em("gpsimd", lambda e: e.dma_start(out=wiuq_sb, in_=wiuq_r), inc="dG", amt=16)
                smalls_ld = P.val("dG")

                vev = None
                kev = None
                for i in range(8):
                    tsl = slice(i * 128, i * 128 + 128)
                    for c in range(NCH):
                        pw = []
                        if c == 0:
                            pw = [("dG", smalls_ld)]
                            if vev is not None:
                                pw.append(("aa", vev))
                        mkv = em("tensor", lambda e, c=c, tsl=tsl: e.matmul(
                            pb[0][:, 0:128], Hb[:, c, tsl], Wv(offs["wv"], (16, 128))[:, c, :],
                            start=(c == 0), stop=(c == NCH - 1)), waits=pw,
                            inc="mm" if c == NCH - 1 else None)
                    vev = em("scalar", lambda e, i=i: e.activation(
                        out=vwin[:, 4 + i, :], in_=pb[0][:, 0:128], func=AF.Copy),
                        waits=[("mm", mkv)], inc="aa")
                    for c in range(NCH):
                        pw = [("vv", kev)] if (c == 0 and kev is not None) else []
                        mkk = em("tensor", lambda e, c=c, tsl=tsl: e.matmul(
                            pb[1][:, 0:128], Hb[:, c, tsl], Wv(offs["wk"], (16, 128))[:, c, :],
                            start=(c == 0), stop=(c == NCH - 1)), waits=pw,
                            inc="mm" if c == NCH - 1 else None)
                    aw = em("scalar", lambda e: e.activation(
                        out=sq[:, 0:128], in_=pb[1][:, 0:128], func=AF.Square),
                        waits=[("mm", mkk)], inc="aa")
                    vw = em("vector", lambda e: e.reduce_sum(
                        out=rms_c[:], in_=sq[:, 0:128], axis=AX.X), waits=[("aa", aw)], inc="vv")
                    aw = em("scalar", lambda e: e.activation(
                        out=rms_c[:], in_=rms_c[:], func=AF.Ln, bias=epsrms[:], scale=1.0 / 128.0),
                        waits=[("vv", vw)], inc="aa")
                    aw = em("scalar", lambda e: e.activation(
                        out=rms_c[:], in_=rms_c[:], func=AF.Exp, scale=-0.5),
                        waits=[("aa", aw)], inc="aa")
                    vw = em("vector", lambda e: e.scalar_tensor_tensor(
                        out=cbf[:, 0:128], in0=pb[1][:, 0:128], scalar=rms_c[:],
                        in1=knw_bc[:], op0=ALU.mult, op1=ALU.mult),
                        waits=[("aa", aw)], inc="vv")
                    ci_ = coskt[:, i, :]
                    si_ = sinkt[:, i, :]
                    em("vector", lambda e, ci_=ci_: e.tensor_mul(
                        out=sel32[:, 0:32], in0=cbf[:, 0:32], in1=ci_), inc="vv")
                    em("vector", lambda e, si_=si_: e.tensor_mul(
                        out=sel32[:, 32:64], in0=cbf[:, 32:64], in1=si_), inc="vv")
                    em("vector", lambda e, i=i: e.tensor_sub(
                        out=kwin_tm[:, 4 + i, 0:32], in0=sel32[:, 0:32], in1=sel32[:, 32:64]), inc="vv")
                    em("vector", lambda e, si_=si_: e.tensor_mul(
                        out=sel32[:, 0:32], in0=cbf[:, 0:32], in1=si_), inc="vv")
                    em("vector", lambda e, ci_=ci_: e.tensor_mul(
                        out=sel32[:, 32:64], in0=cbf[:, 32:64], in1=ci_), inc="vv")
                    em("vector", lambda e, i=i: e.tensor_add(
                        out=kwin_tm[:, 4 + i, 32:64], in0=sel32[:, 0:32], in1=sel32[:, 32:64]), inc="vv")
                    kev = em("vector", lambda e, i=i: e.tensor_copy(
                        out=kwin_tm[:, 4 + i, 64:128], in_=cbf[:, 64:128]), inc="vv")
                kv_done_vv = P.val("vv")
                kv_done_aa = P.val("aa")

                def compress(wc, wz, odim, biasT_ap, out_ap, do_rms, pre_vv):
                    last = pre_vv
                    for tt in range(2):
                        tsl = slice(tt * 512, tt * 512 + 512)
                        for c in range(NCH):
                            pw = [("vv", last)] if c == 0 else []
                            mk = em("tensor", lambda e, c=c, tsl=tsl, wc=wc, odim=odim: e.matmul(
                                pb[2][0:odim, 0:512], Wv(wc, (16, odim))[:, c, :],
                                Hb[:, c, tsl], start=(c == 0), stop=(c == NCH - 1)), waits=pw,
                                inc="mm" if c == NCH - 1 else None)
                        aw = em("scalar", lambda e, tt=tt, odim=odim: e.activation(
                            out=cbf[0:odim, tt * 512:tt * 512 + 512], in_=pb[2][0:odim, 0:512],
                            func=AF.Copy), waits=[("mm", mk)], inc="aa")
                        for c in range(NCH):
                            mk = em("tensor", lambda e, c=c, tsl=tsl, wz=wz, odim=odim: e.matmul(
                                pb[3][0:odim, 0:512], Wv(wz, (16, odim))[:, c, :],
                                Hb[:, c, tsl], start=(c == 0), stop=(c == NCH - 1)),
                                waits=[("aa", aw)] if c == 0 else [],
                                inc="mm" if c == NCH - 1 else None)
                        last = em("vector", lambda e, tt=tt, odim=odim: e.tensor_copy(
                            out=zb32[0:odim, tt * 512:tt * 512 + 512], in_=pb[3][0:odim, 0:512]),
                            waits=[("mm", mk)], inc="vv")
                    z3 = lambda ap, od: ap[0:od, :].rearrange("p (n m) -> p n m", m=32)
                    vw = em("vector", lambda e, odim=odim, biasT_ap=biasT_ap: e.tensor_tensor(
                        out=z3(zb32, odim), in0=z3(zb32, odim),
                        in1=biasT_ap.rearrange("p (o m) -> p o m", o=1).broadcast_to([odim, 32, 32]),
                        op=ALU.add), inc="vv")
                    vw = em("vector", lambda e, odim=odim: e.tensor_reduce(
                        out=gmx[0:odim, :], in_=z3(zb32, odim), axis=AX.X, op=ALU.max,
                        negate=True), inc="vv")
                    vw = em("vector", lambda e, odim=odim: e.tensor_tensor(
                        out=z3(zb32, odim), in0=z3(zb32, odim),
                        in1=gmx[0:odim, :].rearrange("p (n o) -> p n o", o=1).broadcast_to([odim, 32, 32]),
                        op=ALU.add), waits=[("vv", vw)], inc="vv")
                    aw = em("scalar", lambda e, odim=odim: e.activation(
                        out=zb32[0:odim, :], in_=zb32[0:odim, :], func=AF.Exp),
                        waits=[("vv", vw)], inc="aa")
                    vw = em("vector", lambda e, odim=odim: e.tensor_reduce(
                        out=gsm[0:odim, :], in_=z3(zb32, odim), axis=AX.X, op=ALU.add),
                        waits=[("aa", aw)], inc="vv")
                    vw = em("vector", lambda e, odim=odim: e.tensor_mul(
                        out=zb32[0:odim, :], in0=zb32[0:odim, :], in1=cbf[0:odim, :]), inc="vv")
                    vw = em("vector", lambda e, odim=odim: e.tensor_reduce(
                        out=cmp32[0:odim, :], in_=z3(zb32, odim), axis=AX.X, op=ALU.add), inc="vv")
                    vw = em("vector", lambda e, odim=odim: e.reciprocal(
                        out=grec[0:odim, :], in_=gsm[0:odim, :]), waits=[("vv", vw)], inc="vv")
                    vw = em("vector", lambda e, odim=odim: e.tensor_mul(
                        out=cmp32[0:odim, :], in0=cmp32[0:odim, :], in1=grec[0:odim, :]),
                        waits=[("vv", vw)], inc="vv")
                    if do_rms:
                        aw = em("scalar", lambda e: e.activation(
                            out=sq[:, 0:32], in_=cmp32[:, :], func=AF.Square),
                            waits=[("vv", vw)], inc="aa")
                        mk = em("tensor", lambda e: e.matmul(
                            pb[2][0:1, 0:32], onescb[:], sq[:, 0:32], start=True, stop=True),
                            waits=[("aa", aw)], inc="mm")
                        aw = em("scalar", lambda e: e.activation(
                            out=rstd_r32[:], in_=pb[2][0:1, 0:32], func=AF.Ln,
                            bias=epsrms[0:1, :], scale=1.0 / 128.0),
                            waits=[("mm", mk)], inc="aa")
                        aw = em("scalar", lambda e: e.activation(
                            out=rstd_r32[:], in_=rstd_r32[:], func=AF.Exp, scale=-0.5), inc="aa")
                        mk = em("tensor", lambda e: e.matmul(
                            pb[3][:, 0:32], onesrf[:], rstd_r32[:], start=True, stop=True),
                            waits=[("aa", aw)], inc="mm")
                        vw = em("vector", lambda e, out_ap=out_ap: e.scalar_tensor_tensor(
                            out=out_ap, in0=cmp32[:, :], scalar=knw_col[:],
                            in1=pb[3][:, 0:32], op0=ALU.mult, op1=ALU.mult),
                            waits=[("mm", mk)], inc="vv")
                    else:
                        vw = em("vector", lambda e, odim=odim, out_ap=out_ap: e.tensor_copy(
                            out=out_ap, in_=cmp32[0:odim, :]), waits=[("vv", vw)], inc="vv")
                    return vw

                ckv_loc = compress(offs["wkvb"], offs["wzb"], 128, biasbT[:, :],
                                   ckv_fm[:, 0:32], True, kv_done_vv)
                kci_loc = compress(offs["wikv"], offs["wiz"], 64, biasiT[:, :],
                                   kci_cm[0:64, 0:32], False, ckv_loc)

                for tt in range(2):
                    tsl = slice(tt * 512, tt * 512 + 512)
                    for c in range(NCH):
                        pw = []
                        if c == 0:
                            pw = [("vv", kci_loc)] if tt == 0 else [("aa", P.val("aa"))]
                        mk = em("tensor", lambda e, c=c, tsl=tsl: e.matmul(
                            pb[2][0:64, 0:512], Wv(offs["wdq"], (16, 64))[:, c, :],
                            Hb[:, c, tsl], start=(c == 0), stop=(c == NCH - 1)),
                            waits=pw, inc="mm" if c == NCH - 1 else None)
                    em("scalar", lambda e, tt=tt: e.activation(
                        out=dq_v[0:64, tt * 512:tt * 512 + 512], in_=pb[2][0:64, 0:512],
                        func=AF.Copy), waits=[("mm", mk)], inc="aa")
                dq_done = P.val("aa")
                for chn in range(2):
                    for tt in range(2):
                        mk = em("tensor", lambda e, chn=chn, tt=tt: e.matmul(
                            pb[3][:, 0:512], wiuq_sb[0:64, chn * 128:chn * 128 + 128],
                            dq_v[0:64, tt * 512:tt * 512 + 512], start=True, stop=True),
                            waits=[("aa", P.val("aa"))], inc="mm")
                        em("scalar", lambda e, chn=chn, tt=tt: e.activation(
                            out=qi_v[0:64, 2 * chn, tt * 512:tt * 512 + 512],
                            in_=pb[3][0:64, 0:512], func=AF.Copy),
                            waits=[("mm", mk)], inc="aa")
                        em("scalar", lambda e, chn=chn, tt=tt: e.activation(
                            out=qi_v[0:64, 2 * chn + 1, tt * 512:tt * 512 + 512],
                            in_=pb[3][64:128, 0:512], func=AF.Copy), inc="aa")
                qi_done = P.val("aa")
                wgev = None
                for i in range(8):
                    tsl = slice(i * 128, i * 128 + 128)
                    for c in range(NCH):
                        pw = []
                        if c == 0:
                            pw = [("aa", qi_done)] if i == 0 else [("vv", wgev)]
                        mk = em("tensor", lambda e, c=c, tsl=tsl: e.matmul(
                            pb[2][:, 0:4], Hb[:, c, tsl], Wv(offs["ww"], (16, 4))[:, c, :],
                            start=(c == 0), stop=(c == NCH - 1)), waits=pw,
                            inc="mm" if c == NCH - 1 else None)
                    wgev = em("vector", lambda e, i=i: e.tensor_copy(
                        out=wgt_sb[:, i, :], in_=pb[2][:, 0:4]), waits=[("mm", mk)], inc="vv")
                wgt_done = P.val("vv")

                ccK = cc_in.ap()[0:1, SEC_K:SEC_K + T * D].rearrange(
                    "o (n p d) -> (o n) p d", p=128, d=128).rearrange("n p d -> p n d")
                ccV = cc_in.ap()[0:1, SEC_V:SEC_V + T * D].rearrange(
                    "o (n p d) -> (o n) p d", p=128, d=128).rearrange("n p d -> p n d")
                ccC = cc_in.ap()[0:1, SEC_CKV:SEC_CKV + 128 * 32].rearrange(
                    "o (p n) -> (o p) n", n=32)
                ccI = cc_in.ap()[0:1, SEC_KCI:SEC_KCI + 64 * 32].rearrange(
                    "o (p n) -> (o p) n", n=32)
                em("gpsimd", lambda e, ccK=ccK: e.dma_start(out=ccK, in_=kwin_tm[:, 4:12, :]),
                   waits=[("vv", kv_done_vv)], inc="dG", amt=16)
                em("gpsimd", lambda e, ccV=ccV: e.dma_start(out=ccV, in_=vwin[:, 4:12, :]),
                   waits=[("aa", kv_done_aa)], inc="dG", amt=16)
                em("gpsimd", lambda e, ccC=ccC: e.dma_start(out=ccC, in_=ckv_fm[:, 0:32]),
                   waits=[("vv", ckv_loc)], inc="dG", amt=16)
                blob_w = em("gpsimd", lambda e, ccI=ccI: e.dma_start(out=ccI, in_=kci_cm[0:64, 0:32]),
                            waits=[("vv", kci_loc)], inc="dG", amt=16)
                ccdone = em("gpsimd", lambda e: e.collective_compute(
                    "AllGather", ALU.bypass,
                    replica_groups=[[0, 1], [2, 3], [4, 5], [6, 7]],
                    ins=[cc_in.ap().opt()], outs=[cc_out.ap().opt()]),
                    waits=[("dG", blob_w)], inc="cc")
                rbK = cc_out.ap()[0:1, SEC_K:SEC_K + T * D].rearrange(
                    "o (n p d) -> (o n) p d", p=128, d=128).rearrange("n p d -> p n d")
                rbV = cc_out.ap()[0:1, SEC_V:SEC_V + T * D].rearrange(
                    "o (n p d) -> (o n) p d", p=128, d=128).rearrange("n p d -> p n d")
                em("gpsimd", lambda e, rbK=rbK: e.dma_start(
                    out=kwin_tm[:, 0:4, :], in_=rbK[:, 4:8, :]),
                    waits=[("cc", ccdone)], inc="dG", amt=16)
                em("gpsimd", lambda e, rbV=rbV: e.dma_start(
                    out=vwin[:, 0:4, :], in_=rbV[:, 4:8, :]), inc="dG", amt=16)
                for r_ in range(2):
                    cC = cc_out.ap()[r_:r_ + 1, SEC_CKV:SEC_CKV + 128 * 32].rearrange(
                        "o (p n) -> (o p) n", n=32)
                    cI = cc_out.ap()[r_:r_ + 1, SEC_KCI:SEC_KCI + 64 * 32].rearrange(
                        "o (p n) -> (o p) n", n=32)
                    em("gpsimd", lambda e, cC=cC, r_=r_: e.dma_start(
                        out=ckv_fm[:, r_ * 32:r_ * 32 + 32], in_=cC), inc="dG", amt=16)
                    em("gpsimd", lambda e, cI=cI, r_=r_: e.dma_start(
                        out=kci_cm[0:64, r_ * 32:r_ * 32 + 32], in_=cI), inc="dG", amt=16)
                rb_done = P.val("dG")

                ktv = None
                for k in range(KW):
                    pw = [("dG", rb_done)] if k == 0 else [("aa", ktv)]
                    mk = em("tensor", lambda e, k=k: e.transpose(
                        pb[2][:, 0:128], kwin_tm[:, k, :], idf[:]), waits=pw, inc="mm")
                    ktv = em("scalar", lambda e, k=k: e.activation(
                        out=kwin_fm[:, k * 128:k * 128 + 128], in_=pb[2][:, 0:128],
                        func=AF.Copy), waits=[("mm", mk)], inc="aa")
                vk = em("vector", lambda e: e.tensor_copy(out=sel32[:, 0:64], in_=ckv_fm[:, 0:64]),
                        waits=[("dG", rb_done)], inc="vv")
                mk = em("tensor", lambda e: e.transpose(pb[3][0:64, 0:128], sel32[:, 0:64], idf[:]),
                        waits=[("vv", vk)], inc="mm")
                em("scalar", lambda e: e.activation(
                    out=ckv_nm[0:64, :], in_=pb[3][0:64, 0:128], func=AF.Copy),
                    waits=[("mm", mk)], inc="aa")
                tap("kwin", kwin_tm[:, :, :].rearrange("p n d -> p (n d)"))
                tap("vwin", vwin[:, :, :].rearrange("p n d -> p (n d)"))
                tap("ckv", ckv_fm[:, :])
                tap("kci", kci_cm[:, :])

                selv = None
                for qt in range(NQT):
                    tsl = slice(qt * 128, qt * 128 + 128)
                    for ih in range(4):
                        pd = pb[4 + ih % 2]
                        lhs = qi_v[0:64, ih, tsl]
                        pw = [("dG", rb_done), ("vv", P.val("vv"))]
                        mk = em("tensor", lambda e, lhs=lhs, pd=pd: e.matmul(
                            pd[:, 0:64], lhs, kci_cm[0:64, 0:64], start=True, stop=True),
                            waits=pw, inc="mm")
                        vw = em("vector", lambda e, pd=pd: e.tensor_scalar_max(
                            out=m01[:], in0=pd[:, 0:64], scalar1=0.0),
                            waits=[("mm", mk)], inc="vv")
                        if ih == 0:
                            vw = em("vector", lambda e, qt=qt, ih=ih: e.tensor_scalar(
                                out=isc[:], in0=m01[:], scalar1=wgt_sb[:, qt, ih:ih + 1],
                                scalar2=None, op0=ALU.mult),
                                waits=[("vv", wgt_done)], inc="vv")
                        else:
                            vw = em("vector", lambda e, qt=qt, ih=ih: e.scalar_tensor_tensor(
                                out=isc[:], in0=m01[:], scalar=wgt_sb[:, qt, ih:ih + 1],
                                in1=isc[:], op0=ALU.mult, op1=ALU.add), inc="vv")
                    if "isc" in dbg and qt == 0:
                        tap("isc", isc[:, :])
                    vw = em("vector", lambda e, qt=qt: e.tensor_add(
                        out=work[:], in0=isc[:], in1=validadd[:, qt, :]),
                        waits=[("dS", valid_ld)] if qt == 0 else [], inc="vv")
                    vw = em("vector", lambda e: e.tensor_add(
                        out=work[:], in0=work[:], in1=tbrow[:]), inc="vv")
                    for r_ in range(4):
                        vw = em("vector", lambda e: e.max(out=mx8[:], in_=work[:]), inc="vv")
                        if r_ < 3:
                            vw = em("vector", lambda e: e.tensor_single_scalar(
                                out=m01[:], in_=work[:], scalar=mx8[:, 7:8], op=ALU.is_ge),
                                waits=[("vv", vw)], inc="vv")
                            vw = em("vector", lambda e: e.scalar_tensor_tensor(
                                out=work[:], in0=m01[:], scalar=-1e38, in1=work[:],
                                op0=ALU.mult, op1=ALU.add), inc="vv")
                    vw = em("vector", lambda e, qt=qt: e.tensor_add(
                        out=work[:], in0=isc[:], in1=validadd[:, qt, :]),
                        waits=[("vv", vw)], inc="vv")
                    vw = em("vector", lambda e: e.tensor_add(
                        out=work[:], in0=work[:], in1=tbrow[:]), inc="vv")
                    vw = em("vector", lambda e: e.tensor_single_scalar(
                        out=m01[:], in_=work[:], scalar=mx8[:, 7:8], op=ALU.is_ge), inc="vv")
                    vw = em("vector", lambda e, qt=qt: e.tensor_mul(
                        out=sel32[:, 0:64], in0=m01[:], in1=valid01[:, qt, :]), inc="vv")
                    mk = em("tensor", lambda e: e.transpose(pb[6][0:64, 0:128], sel32[:, 0:64], idf[:]),
                            waits=[("vv", vw)], inc="mm")
                    selv = em("scalar", lambda e, qt=qt: e.activation(
                        out=selT[:, qt * 128:qt * 128 + 128], in_=pb[6][0:64, 0:128],
                        func=AF.Copy), waits=[("mm", mk)], inc="aa")
                sel_done = P.val("aa")
                tap("selT", selT[:, :])
                final_marks = {"vv": P.val("vv"), "aa": P.val("aa")}

            if stage >= 3:
                hs_ld2 = em("sync", lambda e: e.dma_start(out=hsel2[:, :, :], in_=hsel_r),
                            waits=[("vv", qn_done)], inc="dS", amt=16)
                for hh in range(16):
                    aw = em("scalar", lambda e, hh=hh: e.activation(
                        out=sq[:, 0:512], in_=qv[:, hh, 0:512], func=AF.Square),
                        waits=[("vv", qn_done), ("dS", hs_ld2)] if hh == 0 else
                        [("mm", P.val("mm"))], inc="aa")
                    em("tensor", lambda e, hh=hh: e.matmul(
                        pb[6][0:16, 0:512], hsel2[:, hh, :], sq[:, 0:512],
                        start=(hh == 0), stop=(hh == 15)), waits=[("aa", aw)], inc="mm")
                    aw = em("scalar", lambda e, hh=hh: e.activation(
                        out=sq[:, 0:512], in_=qv[:, hh, 512:1024], func=AF.Square),
                        waits=[("mm", P.val("mm"))], inc="aa")
                    em("tensor", lambda e, hh=hh: e.matmul(
                        pb[7][0:16, 0:512], hsel2[:, hh, :], sq[:, 0:512],
                        start=(hh == 0), stop=(hh == 15)), waits=[("aa", aw)], inc="mm")
                mkq2 = P.val("mm")
                v1 = em("vector", lambda e: e.reduce_max(
                    out=red1[0:16, 0:1], in_=pb[6][0:16, 0:512], axis=AX.X),
                    waits=[("mm", mkq2)], inc="vv")
                v1 = em("vector", lambda e: e.reduce_max(
                    out=red2[0:16, 0:1], in_=pb[7][0:16, 0:512], axis=AX.X), inc="vv")
                v1 = em("vector", lambda e: e.tensor_max(
                    out=red1[0:16, 0:1], in0=red1[0:16, 0:1], in1=red2[0:16, 0:1]),
                    waits=[("vv", v1)], inc="vv")
                kv1 = None
                for k in range(KW):
                    aw = em("scalar", lambda e, k=k: e.activation(
                        out=sq[:, 0:128], in_=kwin_tm[:, k, :], func=AF.Square),
                        waits=[("vv", v1)] if k == 0 else [("vv", kv1)], inc="aa")
                    kv1 = em("vector", lambda e, k=k: e.reduce_sum(
                        out=red2[:, 0:1], in_=sq[:, 0:128], axis=AX.X),
                        waits=[("aa", aw)], inc="vv")
                    if k == 0:
                        kv1 = em("vector", lambda e: e.tensor_copy(
                            out=red2[:, 1:2], in_=red2[:, 0:1]), waits=[("vv", kv1)], inc="vv")
                    else:
                        kv1 = em("vector", lambda e: e.tensor_max(
                            out=red2[:, 1:2], in0=red2[:, 1:2], in1=red2[:, 0:1]),
                            waits=[("vv", kv1)], inc="vv")
                aw = em("scalar", lambda e: e.activation(
                    out=sq[0:64, 0:128], in_=ckv_nm[0:64, :], func=AF.Square),
                    waits=[("vv", kv1)], inc="aa")
                cv1 = em("vector", lambda e: e.reduce_sum(
                    out=red1[0:64, 1:2], in_=sq[0:64, 0:128], axis=AX.X),
                    waits=[("aa", aw)], inc="vv")
                mk = em("tensor", lambda e: e.transpose(
                    pb[6][0:1, 0:16], red1[0:16, 0:1], idf[0:16, 0:16]),
                    waits=[("vv", cv1)], inc="mm")
                mk = em("tensor", lambda e: e.transpose(
                    pb[6][0:1, 16:144], red2[:, 1:2], idf[:]), inc="mm")
                mk = em("tensor", lambda e: e.transpose(
                    pb[6][0:1, 144:208], red1[0:64, 1:2], idf[0:64, 0:64]), inc="mm")
                v1 = em("vector", lambda e: e.tensor_copy(
                    out=srow[0:1, 0:16], in_=pb[6][0:1, 0:16]),
                    waits=[("mm", mk)], inc="vv")
                v1 = em("vector", lambda e: e.reduce_max(
                    out=srow[0:1, 16:17], in_=pb[6][0:1, 16:144], axis=AX.X), inc="vv")
                v1 = em("vector", lambda e: e.reduce_max(
                    out=srow[0:1, 17:18], in_=pb[6][0:1, 144:208], axis=AX.X),
                    waits=[("vv", v1)], inc="vv")
                v1 = em("vector", lambda e: e.tensor_single_scalar(
                    out=srow[0:1, 32:48], in_=srow[0:1, 0:16], scalar=srow[0:1, 16:17],
                    op=ALU.mult), waits=[("vv", v1)], inc="vv")
                v1 = em("vector", lambda e: e.tensor_single_scalar(
                    out=srow[0:1, 48:64], in_=srow[0:1, 0:16], scalar=srow[0:1, 17:18],
                    op=ALU.mult), waits=[("vv", v1)], inc="vv")
                aw = em("scalar", lambda e: e.activation(
                    out=srow[0:1, 32:64], in_=srow[0:1, 32:64], func=AF.Ln,
                    bias=epsln[:]), waits=[("vv", v1)], inc="aa")
                aw = em("scalar", lambda e: e.activation(
                    out=srow[0:1, 32:64], in_=srow[0:1, 32:64], func=AF.Exp, scale=0.5),
                    waits=[("aa", aw)], inc="aa")
                v1 = em("vector", lambda e: e.tensor_max(
                    out=srow[0:1, 32:48], in0=srow[0:1, 32:48], in1=sink_row[0:1, :]),
                    waits=[("aa", aw)], inc="vv")
                v1 = em("vector", lambda e: e.tensor_scalar(
                    out=srow[0:1, 32:64], in0=srow[0:1, 32:64], scalar1=-1.0,
                    scalar2=None, op0=ALU.mult), waits=[("vv", v1)], inc="vv")
                for hh in range(16):
                    em("tensor", lambda e, hh=hh: e.matmul(
                        pb[7][:, hh:hh + 1], onesrf[:], srow[0:1, 32 + hh:33 + hh],
                        start=True, stop=True),
                        waits=[("vv", v1)] if hh == 0 else [])
                    em("tensor", lambda e, hh=hh: e.matmul(
                        pb[7][:, 16 + hh:17 + hh], onesrf[:], srow[0:1, 48 + hh:49 + hh],
                        start=True, stop=True), inc="mm" if hh == 15 else None)
                mkb = P.val("mm")
                v1 = em("vector", lambda e: e.tensor_copy(out=nSW[:, :], in_=pb[7][:, 0:16]),
                        waits=[("mm", mkb)], inc="vv")
                v1 = em("vector", lambda e: e.tensor_copy(out=nSP[:, :], in_=pb[7][:, 16:32]),
                        inc="vv")
                v1 = em("vector", lambda e: e.tensor_add(
                    out=sinkexp[:, :], in0=nSW[:, :], in1=sink_cols[:, :]),
                    waits=[("vv", v1)], inc="vv")
                preM_aa = em("scalar", lambda e: e.activation(
                    out=sinkexp[:, :], in_=sinkexp[:, :], func=AF.Exp),
                    waits=[("vv", v1)], inc="aa")

                NIT = H * NQT
                sc_mark = {}; exp_mark = {}; mask_mark = {}; av_mark = {}
                rec_mark = {}; avs_mark = {}; stt_mark = {}; tr_mark = {}; ev_mark = {}

                def s0(i):
                    qt, hh = i // H, i % H
                    X, Y = pb[0 + 2 * (i % 2)], pb[1 + 2 * (i % 2)]
                    qap = qv[:, hh, qt * 128:qt * 128 + 128]
                    pw0 = ([("aa", exp_mark[i - 2])] if i >= 2
                           else [("aa", max(sel_done, preM_aa))])
                    for krel in range(5):
                        dst = X[:, krel * 128:krel * 128 + 128] if krel < 4 else Y[:, 0:128]
                        em("tensor", lambda e, krel=krel, qt=qt, dst=dst, qap=qap: e.matmul(
                            dst, kwin_fm[:, (qt + krel) * 128:(qt + krel) * 128 + 128], qap,
                            start=True, stop=True), waits=pw0 if krel == 0 else [])
                    sc_mark[i] = em("tensor", lambda e, qap=qap, Y=Y: e.matmul(
                        Y[0:64, 128:256], ckv_fm[:, 0:64], qap, start=True, stop=True),
                        inc="mm")

                def s1(i):
                    hh = i % H
                    X, Y = pb[0 + 2 * (i % 2)], pb[1 + 2 * (i % 2)]
                    p3 = i % 3
                    mmw = av_mark[i - 3] if i >= 3 else sc_mark[i]
                    em("scalar", lambda e, X=X, p3=p3, hh=hh: e.activation(
                        out=esw[:, p3, 0:4, :], in_=X[:, :].rearrange("p (a b) -> p a b", b=128),
                        func=AF.Exp, bias=nSW[:, hh:hh + 1]),
                        waits=[("mm", mmw)], inc="aa")
                    em("scalar", lambda e, Y=Y, p3=p3, hh=hh: e.activation(
                        out=esw[:, p3, 4, :], in_=Y[:, 0:128], func=AF.Exp,
                        bias=nSW[:, hh:hh + 1]), inc="aa")
                    exp_mark[i] = em("scalar", lambda e, Y=Y, p3=p3, hh=hh: e.activation(
                        out=esp[:, p3, :], in_=Y[0:64, 128:256], func=AF.Exp,
                        bias=nSP[0:64, hh:hh + 1]), inc="aa")

                def s2(i):
                    qt, p3 = i // H, i % 3
                    em("vector", lambda e, p3=p3, qt=qt: e.tensor_mul(
                        out=esw[:, p3, :, :], in0=esw[:, p3, :, :], in1=maskjm[:, qt, :, :]),
                        waits=[("aa", exp_mark[i])], inc="vv")
                    mask_mark[i] = em("vector", lambda e, p3=p3, qt=qt: e.tensor_mul(
                        out=esp[:, p3, :], in0=esp[:, p3, :],
                        in1=selT[:, qt * 128:qt * 128 + 128]), inc="vv")

                def s3(i):
                    qt, p3 = i // H, i % 3
                    Z = pb[4 + i % 4]
                    for krel in range(5):
                        em("tensor", lambda e, krel=krel, p3=p3, qt=qt, Z=Z: e.matmul(
                            Z[:, 128:256], esw[:, p3, krel, :], vwin[:, qt + krel, :],
                            start=(krel == 0), stop=(krel == 4)),
                            waits=[("vv", mask_mark[i])] if krel == 0 else [])
                        em("tensor", lambda e, krel=krel, p3=p3, Z=Z: e.matmul(
                            Z[:, 256:257], esw[:, p3, krel, :], onescb[:],
                            start=(krel == 0), stop=(krel == 4)))
                    em("tensor", lambda e, p3=p3, Z=Z: e.matmul(
                        Z[:, 0:128], esp[:, p3, :], ckv_nm[0:64, :],
                        start=True, stop=True))
                    av_mark[i] = em("tensor", lambda e, p3=p3, Z=Z: e.matmul(
                        Z[:, 257:258], esp[:, p3, :], onescb[0:64, :],
                        start=True, stop=True), inc="mm")

                def s4(i):
                    hh, p2 = i % H, i % 2
                    Z = pb[4 + i % 4]
                    v1 = em("vector", lambda e, Z=Z, p2=p2, hh=hh: e.tensor_add(
                        out=ssw_c[:, p2:p2 + 1], in0=Z[:, 256:257],
                        in1=sinkexp[:, hh:hh + 1]), waits=[("mm", av_mark[i])], inc="vv")
                    rw = [("vv", v1)] + ([("aa", avs_mark[i - 2])] if i >= 2 else [])
                    v1 = em("vector", lambda e, p2=p2: e.reciprocal(
                        out=rsw_c[:, p2:p2 + 1], in_=ssw_c[:, p2:p2 + 1]),
                        waits=rw, inc="vv")
                    v1 = em("vector", lambda e, Z=Z, p2=p2: e.tensor_scalar(
                        out=ssp_c[:, p2:p2 + 1], in0=Z[:, 257:258], scalar1=1e-30,
                        scalar2=None, op0=ALU.add), inc="vv")
                    rec_mark[i] = em("vector", lambda e, p2=p2: e.reciprocal(
                        out=rsp_c[:, p2:p2 + 1], in_=ssp_c[:, p2:p2 + 1]),
                        waits=[("vv", v1)], inc="vv")

                def s5(i):
                    p2 = i % 2
                    Z = pb[4 + i % 4]
                    avs_mark[i] = em("scalar", lambda e, Z=Z, p2=p2: e.activation(
                        out=avs[:, p2, :], in_=Z[:, 128:256], func=AF.Copy,
                        scale=rsw_c[:, p2:p2 + 1]), waits=[("vv", rec_mark[i])], inc="aa")

                def s6(i):
                    p2 = i % 2
                    Z = pb[4 + i % 4]
                    w6 = [("aa", avs_mark[i])] + ([("mm", tr_mark[i - 2])] if i >= 2 else [])
                    stt_mark[i] = em("vector", lambda e, Z=Z, p2=p2: e.scalar_tensor_tensor(
                        out=attn_tm[:, p2, :], in0=Z[:, 0:128],
                        scalar=rsp_c[:, p2:p2 + 1], in1=avs[:, p2, :],
                        op0=ALU.mult, op1=ALU.add), waits=w6, inc="vv")

                def s7(i):
                    p2 = i % 2
                    TY = pb[1 + 2 * p2]
                    w7 = [("vv", stt_mark[i])] + ([("aa", ev_mark[i - 2])] if i >= 2 else [])
                    tr_mark[i] = em("tensor", lambda e, TY=TY, p2=p2: e.transpose(
                        TY[:, 256:384], attn_tm[:, p2, :], idf[:]),
                        waits=w7, inc="mm")

                def s8(i):
                    qt, hh, p2 = i // H, i % H, i % 2
                    TY = pb[1 + 2 * p2]
                    ev_mark[i] = em("scalar", lambda e, TY=TY, hh=hh, qt=qt: e.activation(
                        out=Hb[:, hh, qt * 128:qt * 128 + 128], in_=TY[:, 256:384],
                        func=AF.Copy), waits=[("mm", tr_mark[i])], inc="aa")

                for step in range(NIT + 9):
                    if step < NIT:
                        s0(step)
                    if 0 <= step - 1 < NIT:
                        s1(step - 1)
                    if 0 <= step - 6 < NIT:
                        s6(step - 6)
                    if 0 <= step - 4 < NIT:
                        s4(step - 4)
                    if 0 <= step - 2 < NIT:
                        s2(step - 2)
                    if 0 <= step - 3 < NIT:
                        s3(step - 3)
                    if 0 <= step - 5 < NIT:
                        s5(step - 5)
                    if 0 <= step - 7 < NIT:
                        s7(step - 7)
                    if 0 <= step - 8 < NIT:
                        s8(step - 8)
                attn_done_aa = P.val("aa")
                attn_psum_free = [("vv", stt_mark[NIT - 1]), ("aa", avs_mark[NIT - 1])]
                tap("attn", Hb[:, :, :].rearrange("p c t -> p (c t)"))

                em("gpsimd", lambda e: e.dma_start(out=Wv(0, (16, 512)), in_=gw_r),
                   waits=[("mm", P.val("mm")), ("aa", attn_done_aa),
                          ("vv", P.val("vv"))], inc="dG", amt=16)
                gw_ld = P.val("dG")
                og_ev = {}
                for g_ in range(4):
                    for ocg in range(4):
                        for tt in range(2):
                            it3 = (g_ * 4 + ocg) * 2 + tt
                            tsl = slice(tt * 512, tt * 512 + 512)
                            pband = pb[it3 % 2]
                            pw = [("dG", gw_ld), ("aa", attn_done_aa)]
                            if it3 >= 2:
                                pw.append(("aa", og_ev[it3 - 2]))
                            for ic in range(4):
                                mk = em("tensor", lambda e, g_=g_, ic=ic, ocg=ocg, tsl=tsl, pband=pband: e.matmul(
                                    pband[:, 0:512],
                                    Wv(0, (16, 4, 128))[:, g_ * 4 + ic, ocg],
                                    Hb[:, g_ * 4 + ic, tsl],
                                    start=(ic == 0), stop=(ic == 3)),
                                    waits=pw if ic == 0 else [],
                                    inc="mm" if ic == 3 else None)
                            og_ev[it3] = em("scalar", lambda e, g_=g_, ocg=ocg, tsl=tsl, pband=pband: e.activation(
                                out=qv[:, g_ * 4 + ocg, tsl], in_=pband[:, 0:512],
                                func=AF.Copy), waits=[("mm", mk)], inc="aa")
                og_done = P.val("aa")
                tap("og", qv[:, :, :].rearrange("p c t -> p (c t)"))

                fin_ev = {}
                for hf in range(2):
                    em("gpsimd", lambda e, hf=hf: e.dma_start(
                        out=Wv(0, (8, 2048)), in_=fin_r[:, hf * 8:hf * 8 + 8, :]),
                        waits=[("aa", og_done), ("mm", P.val("mm"))], inc="dG", amt=16)
                    fin_ld = P.val("dG")
                    for oc in range(16):
                        for tt in range(2):
                            it4 = oc * 2 + tt
                            tsl = slice(tt * 512, tt * 512 + 512)
                            pband = pb[it4 % 4]
                            pw2 = [("dG", fin_ld)]
                            if it4 >= 4:
                                pw2.append(("vv", fin_ev[(hf, it4 - 4)]))
                            elif hf == 1:
                                pw2.append(("vv", fin_ev[(0, 28 + it4)]))
                            for kc in range(8):
                                mk = em("tensor", lambda e, kc=kc, oc=oc, tsl=tsl, pband=pband, hf=hf: e.matmul(
                                    pband[:, 0:512],
                                    Wv(0, (8, 16, 128))[:, kc, oc],
                                    qv[:, hf * 8 + kc, tsl],
                                    start=(kc == 0), stop=(kc == 7)),
                                    waits=pw2 if kc == 0 else [],
                                    inc="mm" if kc == 7 else None)
                            fin_ev[(hf, it4)] = em("vector", lambda e, oc=oc, tsl=tsl, pband=pband: e.tensor_add(
                                out=R[:, oc, tsl], in0=R[:, oc, tsl], in1=pband[:, 0:512]),
                                waits=[("mm", mk)], inc="vv")
                fin_mm = P.val("mm")
                final_marks = {"vv": P.val("vv"), "aa": P.val("aa")}

            if stage >= 4:
                tap("x2", R[:, :, :].rearrange("p c t -> p (c t)"))
                ln3cw = {0: {c: [("vv", fin_ev[(1, 2 * c + 0)])] for c in range(NCH)},
                         1: {c: [("vv", fin_ev[(1, 2 * c + 1)])] for c in range(NCH)}}
                ln3 = {0: ln_full(0, chunk_w=ln3cw[0], entry=attn_psum_free), 1: None}
                ln3[1] = ln_full(1, chunk_w=ln3cw[1],
                                 entry=[("vv", ln3[0][3]), ("aa", P.val("aa"))])
                tap("r0", r0row[:])
                tap("r1", varrow[:])
                f2_vv, f2_mm = ffn_pass(ln3, first_w_mm=fin_mm)
                tap("h3", Hb[:, :, :].rearrange("p c t -> p (c t)"))
                final_marks = {"vv": f2_vv, "aa": P.val("aa")}

            out_vv = final_marks["vv"]
            ocp = {}
            odma = {}
            for i in range(8):
                for c in range(NCH):
                    bank = pb[6 + (c % 2)]
                    pw = []
                    if i == 0 and c == 0:
                        pw = [("vv", out_vv), ("mm", P.val("mm")),
                              ("dG", P.val("dG"))]
                    elif (i, c) != (0, 1):
                        pc = (i, c - 2) if c >= 2 else (i - 1, 14 + c)
                        pw = [("aa", ocp[pc])]
                    if c == 0 and i >= 2:
                        pw.append(odma[i - 2])
                    mk = em("tensor", lambda e, i=i, c=c, bank=bank: e.transpose(
                        bank[:, 0:128], R[:, c, i * 128:i * 128 + 128], idf[:]),
                        waits=pw, inc="mm")
                    ocp[(i, c)] = em("scalar", lambda e, i=i, c=c, bank=bank: e.activation(
                        out=outbuf[:, i % 2, c * 128:c * 128 + 128], in_=bank[:, 0:128],
                        func=AF.Copy), waits=[("mm", mk)], inc="aa")
                qe, qs = ("sync", "dS") if i % 2 == 0 else ("gpsimd", "dG")
                odma[i] = (qs, em(qe, lambda e, i=i: e.dma_start(
                    out=y[i * 128:i * 128 + 128, :], in_=outbuf[:, i % 2, :]),
                    waits=[("aa", ocp[(i, NCH - 1)])], inc=qs, amt=16))
            em("sync", None, waits=[("dS", P.val("dS")), ("dG", P.val("dG"))])

            with nc.Block() as block:
                @block.sync
                def _(e):
                    P.replay("sync", e, sems)

                @block.tensor
                def _(e):
                    P.replay("tensor", e, sems)

                @block.vector
                def _(e):
                    P.replay("vector", e, sems)

                @block.scalar
                def _(e):
                    P.replay("scalar", e, sems)

                @block.gpsimd
                def _(e):
                    P.replay("gpsimd", e, sems)

    return nc



def _arrange(inputs):
    g = {}
    wg = np.asarray(inputs["L0_ffn_w_gate"])
    wu = np.asarray(inputs["L0_ffn_w_up"])
    wd = np.asarray(inputs["L0_ffn_w_down"])
    g["wg_r"] = np.ascontiguousarray(wg.reshape(16, 128, 32, 256).transpose(1, 2, 0, 3))
    g["wu_r"] = np.ascontiguousarray(wu.reshape(16, 128, 32, 256).transpose(1, 2, 0, 3))
    g["wd_r"] = np.ascontiguousarray(wd.reshape(16, 4, 128, 4, 512).transpose(2, 3, 0, 1, 4))
    g["sg_r"] = np.ascontiguousarray(np.asarray(inputs["L0_ffn_s_gate"]).reshape(64, 128).T)
    g["su_r"] = np.ascontiguousarray(np.asarray(inputs["L0_ffn_s_up"]).reshape(64, 128).T)
    g["sd_r"] = np.ascontiguousarray(np.asarray(inputs["L0_ffn_s_down"]).reshape(16, 128).T)
    g["wqc_r"] = np.ascontiguousarray(
        np.asarray(inputs["L0_attn_w_qc"]).reshape(16, 128, 1024).transpose(1, 0, 2))
    g["wqup_r"] = np.ascontiguousarray(
        np.asarray(inputs["L0_attn_w_qup"]).reshape(8, 128, 2048).transpose(1, 0, 2))
    for nm, key in (("wkvb_r", "L0_attn_w_kv_b"), ("wzb_r", "L0_attn_w_z_b"),
                    ("wk_r", "L0_attn_w_k"), ("wv_r", "L0_attn_w_v"),
                    ("wikv_r", "L0_attn_wi_kv"), ("wiz_r", "L0_attn_wi_z"),
                    ("wdq_r", "L0_attn_w_dq"), ("ww_r", "L0_attn_w_w")):
        w = np.asarray(inputs[key])
        g[nm] = np.ascontiguousarray(w.reshape(16, 128, w.shape[1]).transpose(1, 0, 2))
    g["wiuq_r"] = np.ascontiguousarray(np.asarray(inputs["L0_attn_w_iuq"]))
    g["gw_r"] = np.ascontiguousarray(
        np.asarray(inputs["L0_attn_group_w"]).reshape(4, 4, 128, 512)
        .transpose(2, 0, 1, 3).reshape(128, 16, 512))
    g["fin_r"] = np.ascontiguousarray(
        np.asarray(inputs["L0_attn_final_w"]).reshape(16, 128, 2048).transpose(1, 0, 2))
    g["qnw_r"] = np.ascontiguousarray(np.asarray(inputs["L0_attn_q_norm_w"]).reshape(1, 128))
    g["knw_r"] = np.ascontiguousarray(np.asarray(inputs["L0_attn_k_norm_w"]).reshape(1, 128))
    g["sink_r"] = np.ascontiguousarray(np.asarray(inputs["L0_attn_sink_logit"]).reshape(1, 16))
    g["biasb_r"] = np.ascontiguousarray(np.asarray(inputs["L0_attn_bias_b"]))
    g["biasi_r"] = np.ascontiguousarray(np.asarray(inputs["L0_attn_bias_i"]))
    g["idf_r"] = np.eye(128, dtype=np.float32)
    sg_ = np.arange(128)
    sig = np.where(sg_ < 32, sg_ + 32, np.where(sg_ < 64, sg_ - 32, sg_))
    sw_ = np.zeros((128, 128), BF)
    sw_[sig, np.arange(128)] = 1
    g["swp_r"] = sw_
    hs = np.zeros((128, 16, 16), BF)
    for c in range(16):
        hs[:, c, c] = 1
    g["hsel_r"] = hs
    qs = np.zeros((16, 16, 128), BF)
    for h_ in range(16):
        qs[h_, h_, :] = 1
    g["qsel_r"] = qs
    return g


def _percore_consts(hc):
    t0 = hc * T
    p = np.arange(t0, t0 + T)
    c = {}
    inv = 1.0 / (10000.0 ** (np.arange(0, RD, 2, dtype=np.float64) / RD))
    ang = p[:, None] * inv
    cos = np.cos(ang).astype(np.float32)
    sin = np.sin(ang).astype(np.float32)
    Cm = np.ones((128, 1024), np.float32)
    Sm = np.zeros((128, 1024), np.float32)
    Cm[0:32] = cos.T
    Cm[32:64] = cos.T
    Sm[0:32] = -sin.T
    Sm[32:64] = sin.T
    c["ropeC_r"] = Cm.astype(BF)
    c["ropeS_r"] = Sm.astype(BF)
    c["coskt_r"] = np.ascontiguousarray(cos.reshape(8, 128, 32).transpose(1, 0, 2)).astype(BF)
    c["sinkt_r"] = np.ascontiguousarray(sin.reshape(8, 128, 32).transpose(1, 0, 2)).astype(BF)
    jglob = np.concatenate([np.arange(t0 - 512, t0), np.arange(t0, t0 + T)])
    mk = np.zeros((128, NQT, 5, 128), np.float32)
    for qt in range(NQT):
        qpos = p[qt * 128:(qt + 1) * 128]
        for krel in range(5):
            kpos = jglob[(qt + krel) * 128:(qt + krel) * 128 + 128]
            valid = (kpos[:, None] >= 0) & (kpos[:, None] <= qpos[None, :]) & \
                    (qpos[None, :] - kpos[:, None] < SWIN)
            mk[:, qt, krel, :] = valid
    c["maskjm_r"] = mk.astype(BF)
    val = ((np.arange(NB) * M + (M - 1))[None, :] < p[:, None]).astype(np.float32)
    v01 = np.ascontiguousarray(val.reshape(NQT, 128, NB).transpose(1, 0, 2))
    c["valid01_r"] = v01.astype(BF)
    c["validadd_r"] = ((v01 - 1.0) * 1e30).astype(BF)
    return c


_TBROW = np.broadcast_to(
    (-np.arange(NB, dtype=np.float64) * 1e-20).astype(np.float32), (128, NB)).copy()

_NC_CACHE = {}


def kernel(**inputs):
    key = (STAGE, TAPS, SUB)
    if key not in _NC_CACHE:
        _NC_CACHE[key] = build_program(STAGE, TAPS, SUB)
    nc = _NC_CACHE[key]
    shared = _arrange(inputs)
    shared["tbrow_r"] = _TBROW
    x = np.asarray(inputs["L0_attn_h"])
    pc = [_percore_consts(0), _percore_consts(1)]
    in_maps = []
    for core in range(8):
        b, hc = core // 2, core % 2
        mmap = dict(shared)
        mmap.update(pc[hc])
        mmap["xin"] = np.ascontiguousarray(x[b, hc * T:(hc + 1) * T].T)
        in_maps.append(mmap)
    res = run_bass_kernel_spmd(nc, in_maps, core_ids=list(range(8)))
    out = np.empty((B, TSEQ, DM), np.float32)
    for core in range(8):
        b, hc = core // 2, core % 2
        out[b, hc * T:(hc + 1) * T] = res.results[core]["y"]
    return out
